# version 12
# baseline (speedup 1.0000x reference)
import contextlib
from collections import defaultdict
import numpy as np
import ml_dtypes
import concourse.bass as bass
import concourse.mybir as mybir
from concourse.bass_utils import run_bass_kernel_spmd

F32 = mybir.dt.float32
BF16 = mybir.dt.bfloat16
AF = mybir.ActivationFunctionType
ALU = mybir.AluOpType
BF = ml_dtypes.bfloat16


class Prog:

    def __init__(self):
        self.ops = []
        self.cnt = defaultdict(int)

    def add(self, eng, fn, inc=None, wait=()):
        if inc is not None and isinstance(inc, str):
            inc = (inc, 1)
        wait = tuple((s, v) for (s, v) in wait if v > 0)
        self.ops.append((eng, fn, inc, wait))
        if inc:
            self.cnt[inc[0]] += inc[1]
            return self.cnt[inc[0]]
        return None

    def v(self, sem):
        return self.cnt[sem]

    def barrier(self):
        self.ops.append(("BARRIER", dict(self.cnt), None, ()))

    def emit(self, block, sems):
        def mk(eng_name):
            def run(eng):
                last = defaultdict(int)
                pend = None
                for e, fn, inc, waits in self.ops:
                    if e == "BARRIER":
                        pend = fn
                        continue
                    if e != eng_name:
                        continue
                    if pend is not None:
                        waits = list(waits) + [(s, v) for s, v in pend.items()]
                        pend = None
                    for s, val in waits:
                        if val > last[s]:
                            eng.wait_ge(sems[s], val)
                            last[s] = val
                    instr = fn(eng)
                    if inc is not None:
                        assert instr is not None, f"op on {e} returned None but has inc {inc}"
                        instr.then_inc(sems[inc[0]], inc[1])
            return run
        block.sync(mk("sync"))
        block.scalar(mk("scalar"))
        block.vector(mk("vector"))
        block.gpsimd(mk("gpsimd"))
        block.tensor(mk("tensor"))


def finish(p, es, nc):
    names = sorted(set(p.cnt.keys()) | {"PE", "ACT", "DVE", "POOL"})
    final = [(n, p.cnt[n]) for n in names if ("O_" in n or n == "OUT") and p.cnt[n] > 0]
    sems = {n: es.enter_context(nc.semaphore(f"sem{i}")) for i, n in enumerate(names)}
    def fin(g):
        ins = None
        for n, v in final:
            ins = g.wait_ge(sems[n], v)
        return ins
    p.add("gpsimd", fin)
    block = es.enter_context(nc.Block())
    p.emit(block, sems)


TC = 512
NCH = 4096 // TC
KT = 8
IT = 8
SG = 2
NSG = 64 // SG


def build_B():
    nc = bass.Bass("TRN2", target_bir_lowering=False, debug=False)
    io = {}
    for nm, sh, dt in [("xb", [1024, 4096], F32), ("w_ip", [1024, 2240], F32),
                       ("w_conv", [128, 32], F32), ("b_conv", [128, 8], F32),
                       ("w_dt", [64, 1024], F32), ("b_dt", [128, 8], F32),
                       ("a_log", [128, 64], F32), ("d_prm", [128, 8], F32),
                       ("w_op", [1024, 1024], F32), ("eye64", [64, 64], BF16),
                       ("cst", [128, 4], F32)]:
        io[nm] = nc.dram_tensor(nm, sh, dt, kind="ExternalInput").ap()
    s_par = nc.dram_tensor("s_par", [1024, 4096], F32, kind="ExternalOutput").ap()
    io["xn_out"] = nc.dram_tensor("xn_out", [1024, 4096], BF16, kind="ExternalOutput").ap()
    io["spar_st"] = lambda dcol, t0: s_par[dcol * 128:(dcol + 1) * 128, t0:t0 + TC]
    p = Prog()
    with contextlib.ExitStack() as es0:
        _emit_B(nc, p, io)
        finish(p, es0, nc)
    return nc


def _emit_B(nc, p, io):
    for _ in _emit_B_gen(nc, p, io):
        pass


def _emit_B_gen(nc, p, io):
    xb, w_ip, w_conv, b_conv = io["xb"], io["w_ip"], io["w_conv"], io["b_conv"]
    w_dt, b_dt, a_log, d_prm = io["w_dt"], io["b_dt"], io["a_log"], io["d_prm"]
    w_op, eye64, cst, xn_out = io["w_op"], io["eye64"], io["cst"], io["xn_out"]
    spar_st = io["spar_st"]
    PF = io.get("pref", "")
    with contextlib.ExitStack() as es:
        _pf = io.get("pref", "")
        T = lambda name, shape, dt: es.enter_context(nc.sbuf_tensor(_pf + name, shape, dt))
        PT = lambda name, shape, dt: es.enter_context(nc.psum_tensor(_pf + name, shape, dt))
        wip = T("wip", [128, KT * 2240], BF16)
        wop = T("wop", [128, KT * 1024], BF16)
        wdt = T("wdtb", [64, 1024], BF16)
        wcv = T("wcv", [128, 32], F32)
        bcv = T("bcv", [128, 8], F32)
        bdt = T("bdtb", [128, 8], F32)
        dpr = T("dpr", [128, 8], F32)
        arep = T("arep", [128, 64], F32)
        eyes = T("eyes", [64, 64], BF16)
        ones_c = T("ones_c", [128, 1], F32)
        ones_r = T("ones_r", [1, 128], F32)
        wst = T("wst", [128, 1120], F32)
        cstt = T("cstt", [128, 4], F32)
        wdst = T("wdst", [64, 512], F32)
        xfs = [T(f"xf{i}", [128, TC], F32) for i in range(2)]
        xsq = [T(f"xsq{i}", [128, TC], F32) for i in range(2)]
        nm = T("nm", [1, TC], F32)
        m2 = T("m2", [1, TC], F32)
        var = T("var", [1, TC], F32)
        rs = T("rs", [1, TC], F32)
        nmr = T("nmr", [1, TC], F32)
        xnt = [T(f"xnt{i}", [128, TC], F32) for i in range(1)]
        xn = T("xn", [128, KT * TC], BF16)
        xin = T("xin", [128, IT * (TC + 3)], BF16)
        zb = T("zb", [128, IT * TC], BF16)
        bcb = T("bcb", [64, TC], BF16)
        ccb = T("ccb", [64, TC], BF16)
        drw = T("drw", [64, TC], BF16)
        spv = [T(f"spv{i}", [128, TC], F32) for i in range(1)]
        sav = [T(f"sav{i}", [128, TC], F32) for i in range(1)]
        delta = T("delta", [128, IT * TC], BF16)
        xc = T("xc", [128, IT * TC], BF16)
        cacc = [T(f"cacc{i}", [128, TC], F32) for i in range(1)]
        dxc = T("dxc", [128, IT * TC], BF16)
        brep = T("brep", [128, SG * TC], BF16)
        crep = T("crep", [128, SG * TC], BF16)
        dab = [T(f"dab{i}", [128, SG * (TC + 1)], BF16) for i in range(1)]
        ub = [T(f"ub{i}", [128, SG * (TC + 1)], BF16) for i in range(1)]
        hb = [T(f"hb{i}", [128, SG * (TC + 1)], BF16) for i in range(2)]
        wbf = [T(f"wbf{i}", [128, SG * TC], BF16) for i in range(1)]
        ysg = [T(f"ysg{i}", [128, TC], F32) for i in range(1)]
        yacc = T("yacc", [128, IT * TC], F32)
        hstate = T("hstate", [128, IT * 64], BF16)
        szb = [T(f"szb{i}", [128, TC], BF16) for i in range(1)]
        y1f = [T(f"y1f{i}", [128, TC], F32) for i in range(1)]
        ypp = T("ypp", [128, IT * TC], BF16)
        spo = [T(f"spo{i}", [128, TC], F32) for i in range(1)]
        pa = PT("pa", [128, TC], F32)
        pb = PT("pb", [128, TC], F32)
        brp = PT("brp", [128, SG * TC], F32)
        mps = pa[0:1, 0:TC]
        sps = pb[0:1, 0:TC]
        pst = [pa, pb]

        A = p.add
        V = p.v

        A("vector", lambda v: v.memset(ones_c[:], 1.0), inc="DVE")
        A("vector", lambda v: v.memset(ones_r[:], 1.0), inc="DVE")
        for k in range(IT):
            A("vector", lambda v, k=k: v.memset(xin[:, k * (TC + 3):k * (TC + 3) + 3], 0.0), inc="DVE")
        A("sync", lambda s: s.dma_start(out=eyes[:], in_=eye64), inc=(PF + "L_stp", 16))
        A("sync", lambda s: s.dma_start(out=cstt[:], in_=cst), inc=(PF + "L_stp", 16))
        A("sync", lambda s: s.dma_start(out=arep[:], in_=a_log), inc=(PF + "L_stp", 16))
        A("sync", lambda s: s.dma_start(out=bcv[:], in_=b_conv), inc=(PF + "L_stp", 16))
        A("sync", lambda s: s.dma_start(out=bdt[:], in_=b_dt), inc=(PF + "L_stp", 16))
        A("sync", lambda s: s.dma_start(out=dpr[:], in_=d_prm), inc=(PF + "L_stp", 16))
        A("sync", lambda s: s.dma_start(out=wcv[:], in_=w_conv), inc=(PF + "L_stp", 16))
        ld0v = V(PF + "L_stp")
        ld0 = (PF + "L_stp", ld0v)
        A("scalar", lambda s: s.activation(out=arep[:], in_=arep[:], func=AF.Exp),
          inc="ACT", wait=[ld0])
        A("vector", lambda v: v.tensor_scalar_mul(out=arep[:], in0=arep[:], scalar1=-1.0),
          inc="DVE", wait=[("ACT", V("ACT"))])
        for h in range(2):
            w_cast = V("DVE")
            A("sync", lambda s, h=h: s.dma_start(out=wdst[:], in_=w_dt[:, h * 512:(h + 1) * 512]),
              inc=(PF + "L_wdst", 16), wait=[("DVE", w_cast)])
            A("vector", lambda v, h=h: v.tensor_copy(out=wdt[:, h * 512:(h + 1) * 512], in_=wdst[:]),
              inc="DVE", wait=[(PF + "L_wdst", V(PF + "L_wdst"))])
        for k in range(KT):
            for h in range(2):
                c0, c1 = h * 1120, min(2240, (h + 1) * 1120)
                w_cast = V("ACT")
                A("sync", lambda s, k=k, c0=c0, c1=c1: s.dma_start(
                    out=wst[:, :c1 - c0], in_=w_ip[k * 128:(k + 1) * 128, c0:c1]),
                  inc=(PF + "L_wst", 16), wait=[("ACT", w_cast)])
                A("scalar", lambda s, k=k, c0=c0, c1=c1: s.copy(
                    out=wip[:, k * 2240 + c0:k * 2240 + c1], in_=wst[:, :c1 - c0]),
                  inc="ACT", wait=[(PF + "L_wst", V(PF + "L_wst"))])
        for k in range(KT):
            w_cast = V("ACT")
            A("sync", lambda s, k=k: s.dma_start(out=wst[:, :1024], in_=w_op[k * 128:(k + 1) * 128, :]),
              inc=(PF + "L_wst", 16), wait=[("ACT", w_cast)])
            A("scalar", lambda s, k=k: s.copy(out=wop[:, k * 1024:(k + 1) * 1024], in_=wst[:, :1024]),
              inc="ACT", wait=[(PF + "L_wst", V(PF + "L_wst"))])
        A("vector", lambda v: v.memset(dab[0][:], 0.0), inc="DVE")
        A("gpsimd", lambda g: g.memset(hstate[:], 0.0), inc="POOL")
        pool_hst0 = V("POOL")
        setup_act = V("ACT")
        setup_dve = V("DVE")

        war = defaultdict(tuple)
        for j in range(IT):
            war[("hst", j)] = [("POOL", pool_hst0)]
        yield "setup"

        for c in range(NCH):
            t0 = c * TC
            for k in range(KT):
                par = k % 2
                A("sync", lambda s, k=k, par=par, t0=t0: s.dma_start(
                    out=xfs[par][:], in_=xb[k * 128:(k + 1) * 128, t0:t0 + TC]),
                  inc=(PF + f"L_xf{par}", 16), wait=war[("xf", par)])
                ldw = (PF + f"L_xf{par}", V(PF + f"L_xf{par}"))
                A("scalar", lambda s, par=par: s.activation(out=xsq[par][:], in_=xfs[par][:], func=AF.Square),
                  inc="ACT", wait=[ldw] + list(war[("xsq", par)]))
                actv = V("ACT")
                A("tensor", lambda t, k=k, par=par: t.matmul(
                    out=mps[:], lhsT=ones_c[:], rhs=xfs[par][:], start=(k == 0), stop=(k == KT - 1)),
                  wait=([ldw] + list(war[("mps",)]) + list(war[("pa",)]) + list(war[("pb",)])) if k == 0 else [ldw])
                A("tensor", lambda t, k=k, par=par: t.matmul(
                    out=sps[:], lhsT=ones_c[:], rhs=xsq[par][:], start=(k == 0), stop=(k == KT - 1)),
                  inc="PE",
                  wait=[("ACT", actv)])
                war[("xf", par)] = [("PE", V("PE"))]
                war[("xsq", par)] = [("PE", V("PE"))]
            pe_stats = V("PE")
            A("vector", lambda v: v.tensor_scalar_mul(out=nm[:], in0=mps[:], scalar1=-1.0 / 1024.0),
              inc="DVE", wait=[("PE", pe_stats)])
            A("vector", lambda v: v.tensor_mul(out=m2[:], in0=nm[:], in1=nm[:]), inc="DVE")
            A("vector", lambda v: v.scalar_tensor_tensor(
                out=var[:], in0=sps[:], scalar=1.0 / 1024.0, in1=m2[:], op0=ALU.mult, op1=ALU.subtract),
              inc="DVE", wait=[("PE", pe_stats)])
            dve_var = V("DVE")
            war[("mps",)] = [("DVE", dve_var)]
            A("scalar", lambda s: s.activation(out=rs[:], in_=var[:], func=AF.Ln, bias=cstt[0:1, 0:1]),
              inc="ACT", wait=[("DVE", dve_var)])
            A("scalar", lambda s: s.activation(out=rs[:], in_=rs[:], func=AF.Exp, scale=cstt[0:1, 1:2]), inc="ACT")
            act_rs = V("ACT")
            A("vector", lambda v: v.tensor_mul(out=nmr[:], in0=nm[:], in1=rs[:]),
              inc="DVE", wait=[("ACT", act_rs)])
            dve_nmr = V("DVE")
            A("tensor", lambda t: t.matmul(out=pa[:], lhsT=ones_r[:], rhs=rs[:], start=True, stop=True),
              wait=[("ACT", act_rs)] + list(war[("pa",)]))
            A("tensor", lambda t: t.matmul(out=pb[:], lhsT=ones_r[:], rhs=nmr[:], start=True, stop=True),
              inc="PE", wait=[("DVE", dve_nmr)] + list(war[("pb",)]))
            pe_bc = V("PE")
            for k in range(KT):
                par = k % 2
                A("sync", lambda s, k=k, par=par, t0=t0: s.dma_start(
                    out=xfs[par][:], in_=xb[k * 128:(k + 1) * 128, t0:t0 + TC]),
                  inc=(PF + f"L_xf{par}", 16), wait=war[("xf", par)])
                ldw = (PF + f"L_xf{par}", V(PF + f"L_xf{par}"))
                A("vector", lambda v, par=par: v.tensor_mul(out=xnt[0][:], in0=xfs[par][:], in1=pa[:]),
                  inc="DVE", wait=[ldw, ("PE", pe_bc)])
                A("vector", lambda v, k=k, par=par: v.tensor_add(
                    out=xn[:, k * TC:(k + 1) * TC], in0=xnt[0][:], in1=pb[:]),
                  inc="DVE", wait=war[("xnk", k)])
                dv = V("DVE")
                war[("xf", par)] = [("DVE", dv)]
                A("gpsimd", lambda g, k=k, t0=t0: g.dma_start(
                    out=xn_out[k * 128:(k + 1) * 128, t0:t0 + TC], in_=xn[:, k * TC:(k + 1) * TC]),
                  inc=(PF + "O_xn", 16), wait=[("DVE", dv)])
            oxn_end = V(PF + "O_xn")
            for k in range(KT):
                war[("xnk", k)] = [(PF + "O_xn", oxn_end)]
            dve_xn = V("DVE")
            war[("pa",)] = [("DVE", dve_xn)]
            war[("pb",)] = [("DVE", dve_xn)]

            groups = [("xin", j) for j in range(IT)] + [("z", j) for j in range(IT)] + [("bc", 0), ("draw", 0)]
            for gi, (gname, j) in enumerate(groups):
                ps = pst[gi % 2]
                pkey = ("pa",) if gi % 2 == 0 else ("pb",)
                col0 = {"xin": j * 128, "z": 1024 + j * 128, "bc": 2048, "draw": 2176}[gname]
                npart = 64 if gname == "draw" else 128
                for k in range(KT):
                    A("tensor", lambda t, k=k, col0=col0, ps=ps, npart=npart: t.matmul(
                        out=ps[:npart, :], lhsT=wip[:, k * 2240 + col0:k * 2240 + col0 + npart],
                        rhs=xn[:, k * TC:(k + 1) * TC], start=(k == 0), stop=(k == KT - 1)),
                      inc="PE",
                      wait=(list(war[pkey]) + [("ACT", setup_act), ("DVE", dve_xn)]) if k == 0 else [])
                pev = V("PE")
                if gname == "xin":
                    A("scalar", lambda s, j=j, ps=ps: s.copy(
                        out=xin[:, j * (TC + 3) + 3:(j + 1) * (TC + 3)], in_=ps[:]),
                      inc="ACT", wait=[("PE", pev)] + list(war[("xin", j)]))
                elif gname == "z":
                    A("scalar", lambda s, j=j, ps=ps: s.copy(out=zb[:, j * TC:(j + 1) * TC], in_=ps[:]),
                      inc="ACT", wait=[("PE", pev)] + list(war[("z", j)]))
                elif gname == "bc":
                    A("scalar", lambda s, ps=ps: s.copy(out=bcb[:], in_=ps[0:64, :]),
                      inc="ACT", wait=[("PE", pev)] + list(war[("bcb",)]))
                    A("scalar", lambda s, ps=ps: s.copy(out=ccb[:], in_=ps[64:128, :]),
                      inc="ACT")
                else:
                    A("scalar", lambda s, ps=ps: s.copy(out=drw[:], in_=ps[:64, :]),
                      inc="ACT", wait=[("PE", pev)] + list(war[("drw",)]))
                war[pkey] = [("ACT", V("ACT"))]
            act_front = V("ACT")

            for j in range(IT):
                par = 0
                ps = pst[j % 2]
                pkey = ("pa",) if j % 2 == 0 else ("pb",)
                A("tensor", lambda t, j=j, ps=ps: t.matmul(
                    out=ps[:], lhsT=wdt[:, j * 128:(j + 1) * 128], rhs=drw[:],
                    start=True, stop=True),
                  inc="PE",
                  wait=[("ACT", act_front), ("DVE", setup_dve)] + list(war[pkey]))
                pev = V("PE")
                A("vector", lambda v, j=j, par=par, ps=ps: v.tensor_scalar_add(
                    out=spv[par][:], in0=ps[:], scalar1=bdt[:, j:j + 1]),
                  inc="DVE", wait=[("PE", pev)] + list(war[("spv", par)]))
                dve_v = V("DVE")
                war[pkey] = [("DVE", dve_v)]
                A("scalar", lambda s, par=par: s.activation(out=sav[par][:], in_=spv[par][:], func=AF.Abs),
                  inc="ACT", wait=[("DVE", dve_v)] + list(war[("sav", par)]))
                A("scalar", lambda s, par=par: s.activation(out=sav[par][:], in_=sav[par][:], func=AF.Exp, scale=-1.0),
                  inc="ACT")
                A("scalar", lambda s, par=par: s.activation(out=sav[par][:], in_=sav[par][:], func=AF.Ln, bias=1.0),
                  inc="ACT")
                act_sp = V("ACT")
                A("vector", lambda v, j=j, par=par: v.scalar_tensor_tensor(
                    out=delta[:, j * TC:(j + 1) * TC], in0=spv[par][:], scalar=0.0, in1=sav[par][:],
                    op0=ALU.max, op1=ALU.add),
                  inc="DVE", wait=[("ACT", act_sp)] + list(war[("delta", j)]))
                war[("spv", par)] = [("ACT", act_sp)]
                war[("sav", par)] = [("DVE", V("DVE"))]

            for j in range(IT):
                par = 0
                x0 = j * (TC + 3)
                A("vector", lambda v, j=j, par=par, x0=x0: v.tensor_scalar_mul(
                    out=cacc[par][:], in0=xin[:, x0:x0 + TC], scalar1=wcv[:, j * 4:j * 4 + 1]),
                  inc="DVE", wait=[("ACT", act_front)] + list(war[("cacc", par)]))
                for kk in range(1, 4):
                    A("vector", lambda v, par=par, x0=x0, kk=kk, j=j: v.scalar_tensor_tensor(
                        out=cacc[par][:], in0=xin[:, x0 + kk:x0 + kk + TC],
                        scalar=wcv[:, j * 4 + kk:j * 4 + kk + 1], in1=cacc[par][:],
                        op0=ALU.mult, op1=ALU.add),
                      inc="DVE")
                dve_c = V("DVE")
                A("scalar", lambda s, j=j, par=par: s.activation(
                    out=xc[:, j * TC:(j + 1) * TC], in_=cacc[par][:], func=AF.Silu,
                    bias=bcv[:, j:j + 1]),
                  inc="ACT", wait=[("DVE", dve_c)] + list(war[("xck", j)]))
                act_xc = V("ACT")
                war[("cacc", par)] = [("ACT", act_xc)]
                A("vector", lambda v, x0=x0: v.tensor_copy(
                    out=xin[:, x0:x0 + 3], in_=xin[:, x0 + TC:x0 + TC + 3]),
                  inc="DVE")
                war[("xin", j)] = [("DVE", V("DVE"))]
                A("vector", lambda v, j=j: v.tensor_mul(
                    out=dxc[:, j * TC:(j + 1) * TC], in0=delta[:, j * TC:(j + 1) * TC],
                    in1=xc[:, j * TC:(j + 1) * TC]),
                  inc="DVE", wait=[("ACT", act_xc)])
            dve_front = V("DVE")

            A("gpsimd", lambda g: g.memset(yacc[:], 0.0), inc="POOL", wait=war[("yacc",)])
            pool_ymem = V("POOL")

            act_cr = 0
            for sg in range(NSG):
                for sl in range(SG):
                    s_ = sg * SG + sl
                    A("tensor", lambda t, s_=s_, sl=sl: t.matmul(
                        out=brp[:, sl * TC:(sl + 1) * TC],
                        lhsT=eyes[:, s_:s_ + 1].broadcast_to([64, 128]),
                        rhs=bcb[:], start=True, stop=True),
                      inc="PE",
                      wait=([("ACT", act_front)] + list(war[("brp",)])) if sl == 0 else [])
                pe_b = V("PE")
                A("scalar", lambda s: s.copy(out=brep[:], in_=brp[:]),
                  inc="ACT", wait=[("PE", pe_b)] + list(war[("brep",)]))
                act_b = V("ACT")
                war[("brp",)] = [("ACT", act_b)]

                for j in range(IT):
                    par = 0
                    hpar = j % 2
                    SEG = TC + 1
                    for sl in range(SG):
                        s_ = sg * SG + sl
                        A("scalar", lambda s, j=j, par=par, sl=sl, s_=s_: s.activation(
                            out=dab[par][:, sl * SEG + 1:(sl + 1) * SEG], in_=delta[:, j * TC:(j + 1) * TC],
                            func=AF.Exp, scale=arep[:, s_:s_ + 1]),
                          inc="ACT" if sl == SG - 1 else None,
                          wait=(list(war[("dab", par)]) + [("DVE", dve_front), ("DVE", setup_dve)]) if sl == 0 else [])
                    act_da = V("ACT")
                    A("vector", lambda v, j=j, par=par: v.tensor_mul(
                        out=ub[par][:].rearrange("p (s t) -> p s t", s=SG)[:, :, 1:],
                        in0=dxc[:, j * TC:(j + 1) * TC].unsqueeze(1).broadcast_to([128, SG, TC]),
                        in1=brep[:].rearrange("p (s t) -> p s t", s=SG)),
                      inc="DVE", wait=[("ACT", act_b)])
                    A("gpsimd", lambda g, j=j, par=par, sg=sg: g.tensor_copy(
                        out=ub[par][:, 0::SEG],
                        in_=hstate[:, j * 64 + sg * SG:j * 64 + (sg + 1) * SG]),
                      inc="POOL", wait=list(war[("hst", j)]) + [("DVE", V("DVE"))])
                    pool_inj = V("POOL")
                    A("vector", lambda v, par=par, hpar=hpar: v.tensor_tensor_scan(
                        out=hb[hpar][:], data0=dab[par][:], data1=ub[par][:],
                        initial=0.0, op0=ALU.mult, op1=ALU.add),
                      inc="DVE",
                      wait=[("ACT", act_da), ("POOL", pool_inj)] + list(war[("hb", hpar)]))
                    dve_scan = V("DVE")
                    war[("dab", par)] = [("DVE", dve_scan)]
                    A("gpsimd", lambda g, j=j, hpar=hpar, sg=sg: g.tensor_copy(
                        out=hstate[:, j * 64 + sg * SG:j * 64 + (sg + 1) * SG],
                        in_=hb[hpar][:, TC::SEG]),
                      inc="POOL", wait=[("DVE", dve_scan)])
                    war[("hst", j)] = [("POOL", V("POOL"))]
                    if j == 0:
                        for sl in range(SG):
                            s_ = sg * SG + sl
                            A("tensor", lambda t, s_=s_, sl=sl: t.matmul(
                                out=brp[:, sl * TC:(sl + 1) * TC],
                                lhsT=eyes[:, s_:s_ + 1].broadcast_to([64, 128]),
                                rhs=ccb[:], start=True, stop=True),
                              inc="PE",
                              wait=([("ACT", act_b)] + list(war[("brp",)])) if sl == 0 else [])
                        pe_cr = V("PE")
                        A("scalar", lambda s: s.copy(out=crep[:], in_=brp[:]),
                          inc="ACT", wait=[("PE", pe_cr)] + list(war[("crep",)]))
                        act_cr = V("ACT")
                        war[("brp",)] = [("ACT", act_cr)]
                    A("vector", lambda v, par=par, hpar=hpar: v.tensor_mul(
                        out=wbf[par][:].rearrange("p (s t) -> p s t", s=SG),
                        in0=hb[hpar][:].rearrange("p (s t) -> p s t", s=SG)[:, :, 1:],
                        in1=crep[:].rearrange("p (s t) -> p s t", s=SG)),
                      inc="DVE", wait=[("ACT", act_cr)] + list(war[("wbf", par)]))
                    dve_w = V("DVE")
                    war[("hb", hpar)] = [("POOL", p.cnt["POOL"]), ("DVE", dve_w)]
                    A("gpsimd", lambda g, par=par: g.tensor_add(
                        out=ysg[par][:], in0=wbf[par][:, 0:TC], in1=wbf[par][:, TC:2 * TC]),
                      inc="POOL", wait=[("DVE", dve_w)])
                    A("gpsimd", lambda g, j=j, par=par: g.tensor_add(
                        out=yacc[:, j * TC:(j + 1) * TC], in0=yacc[:, j * TC:(j + 1) * TC],
                        in1=ysg[par][:]),
                      inc="POOL", wait=[("POOL", pool_ymem)])
                    pool_acc = V("POOL")
                    war[("wbf", par)] = [("POOL", pool_acc)]
                    war[("yacck", j)] = [("POOL", pool_acc)]
                war[("brep",)] = [("DVE", V("DVE"))]
                war[("crep",)] = [("DVE", V("DVE"))]
            war[("bcb",)] = [("PE", V("PE"))]
            war[("drw",)] = [("PE", V("PE"))]
            for j in range(IT):
                war[("delta", j)] = [("ACT", V("ACT")), ("DVE", V("DVE"))]

            for j in range(IT):
                par = 0
                A("scalar", lambda s, j=j, par=par: s.activation(
                    out=szb[par][:], in_=zb[:, j * TC:(j + 1) * TC], func=AF.Silu),
                  inc="ACT", wait=list(war[("szb", par)]))
                act_sz = V("ACT")
                A("vector", lambda v, j=j, par=par: v.scalar_tensor_tensor(
                    out=y1f[par][:], in0=xc[:, j * TC:(j + 1) * TC], scalar=dpr[:, j:j + 1],
                    in1=yacc[:, j * TC:(j + 1) * TC], op0=ALU.mult, op1=ALU.add),
                  inc="DVE", wait=list(war[("yacck", j)]) + list(war[("y1", par)]))
                A("vector", lambda v, j=j, par=par: v.tensor_mul(
                    out=ypp[:, j * TC:(j + 1) * TC], in0=y1f[par][:], in1=szb[par][:]),
                  inc="DVE", wait=[("ACT", act_sz)] + list(war[("yppk", j)]))
                dve_ypp = V("DVE")
                war[("y1", par)] = [("DVE", dve_ypp)]
                war[("szb", par)] = [("DVE", dve_ypp)]
                war[("z", j)] = [("ACT", act_sz)]
                war[("xck", j)] = [("DVE", dve_ypp)]
                war[("yacc",)] = [("DVE", dve_ypp)]
            dve_yppall = V("DVE")
            for dcol in range(8):
                ps = pst[dcol % 2]
                pkey = ("pa",) if dcol % 2 == 0 else ("pb",)
                par = 0
                for k in range(IT):
                    A("tensor", lambda t, dcol=dcol, k=k, ps=ps: t.matmul(
                        out=ps[:], lhsT=wop[:, k * 1024 + dcol * 128:k * 1024 + dcol * 128 + 128],
                        rhs=ypp[:, k * TC:(k + 1) * TC], start=(k == 0), stop=(k == IT - 1)),
                      inc="PE",
                      wait=([("DVE", dve_yppall), ("ACT", setup_act)] + list(war[pkey])) if k == 0 else [])
                pev = V("PE")
                A("scalar", lambda s, par=par, ps=ps: s.copy(out=spo[par][:], in_=ps[:]),
                  inc="ACT", wait=[("PE", pev)] + list(war[("spo", par)]))
                actv = V("ACT")
                war[pkey] = [("ACT", actv)]
                A("gpsimd", lambda g, dcol=dcol, par=par, t0=t0: g.dma_start(
                    out=spar_st(dcol, t0), in_=spo[par][:]),
                  inc=(PF + "O_sp", 16), wait=[("ACT", actv)])
                war[("spo", par)] = [(PF + "O_sp", V(PF + "O_sp"))]
            pe_end = V("PE")
            for j in range(IT):
                war[("yppk", j)] = [("PE", pe_end)]
            for k in range(KT):
                war[("xnk", k)] = list(war[("xnk", k)]) + [("PE", pe_end)]
            war[("spo", 0)] = list(war[("spo", 0)])
            yield ("chunk", c)


def _col(a):
    return np.ascontiguousarray(np.asarray(a).reshape(8, 128).T)


def stage_B(d):
    x = np.asarray(d["L0_attn_x"])
    wipT = np.ascontiguousarray(np.asarray(d["L0_attn_in_proj_w"]).T)
    opT = np.ascontiguousarray(np.asarray(d["L0_attn_out_proj_w"]).T)
    dtw = np.asarray(d["L0_attn_dt_proj_w"])
    eye = np.eye(64).astype(BF)
    maps = []
    for c in range(8):
        b, ih = c // 2, c % 2
        sl = slice(ih * 1024, (ih + 1) * 1024)
        m = {
            "xb": np.ascontiguousarray(x[b].T),
            "w_ip": np.ascontiguousarray(np.concatenate(
                [wipT[:, ih * 1024:(ih + 1) * 1024],
                 wipT[:, 2048 + ih * 1024:2048 + (ih + 1) * 1024],
                 wipT[:, 4096:4288]], axis=1)),
            "w_conv": np.ascontiguousarray(
                np.asarray(d["L0_attn_conv_w"])[sl, 0, :].reshape(8, 128, 4).transpose(1, 0, 2).reshape(128, 32)),
            "b_conv": _col(d["L0_attn_conv_b"][sl]),
            "w_dt": np.ascontiguousarray(dtw[sl].T),
            "b_dt": _col(d["L0_attn_dt_proj_b"][sl]),
            "a_log": np.ascontiguousarray(np.broadcast_to(
                np.asarray(d["L0_attn_A_log"])[None, :], (128, 64))),
            "d_prm": _col(d["L0_attn_D_param"][sl]),
            "w_op": np.ascontiguousarray(opT[sl, :]),
            "eye64": eye,
            "cst": np.ascontiguousarray(np.broadcast_to(
                np.array([1e-5, -0.5, 0.0, 0.0], np.float32)[None, :], (128, 4))),
        }
        maps.append(m)
    return maps


TCM = 512
NTCM = 2048 // TCM
NTT = 16


def build_C():
    return build_tail(0)


def build_D2():
    return build_tail(1)


def build_tail(lay=0):
    nc = bass.Bass("TRN2", target_bir_lowering=False, debug=False)
    io = {"lay": lay, "s_mode": "dual" if lay == 0 else "single_bf16"}
    xn_in = nc.dram_tensor("xn_in", [1024, 2048], BF16, kind="ExternalInput").ap()
    x_own = nc.dram_tensor("x_own", [1024, 2048], F32, kind="ExternalInput").ap()
    if lay == 0:
        s_a = nc.dram_tensor("s_a", [1024, 2048], F32, kind="ExternalInput").ap()
        s_b = nc.dram_tensor("s_b", [1024, 2048], F32, kind="ExternalInput").ap()
        io["sa_ld"] = lambda j, tm0: s_a[j * 128:(j + 1) * 128, tm0:tm0 + TCM]
        io["sb_ld"] = lambda j, tm0: s_b[j * 128:(j + 1) * 128, tm0:tm0 + TCM]
    else:
        s_in = nc.dram_tensor("s_in", [1024, 2048], BF16, kind="ExternalInput").ap()
        io["bgr"] = nc.dram_tensor("bgr", [128, 8], F32, kind="ExternalInput").ap()
        io["s_ld"] = lambda j, tm0: s_in[j * 128:(j + 1) * 128, tm0:tm0 + TCM]
    for nm, sh, dt in [("wgt", [128, 64], F32), ("w1r", [8, 32, 128, 1024], F32),
                       ("w2r", [8, 8, 128, 4096], F32), ("b1c", [128, 256], F32),
                       ("b2c", [128, 64], F32), ("eye128", [128, 128], BF16),
                       ("eye8", [8, 8], BF16)]:
        io[nm] = nc.dram_tensor(nm, sh, dt, kind="ExternalInput").ap()
    x1 = nc.dram_tensor("x1", [1024, 2048], F32, kind="ExternalOutput").ap()
    io["xn_ld"] = lambda k, tm0: xn_in[k * 128:(k + 1) * 128, tm0:tm0 + TCM]
    io["x_ld"] = lambda j, tm0: x_own[j * 128:(j + 1) * 128, tm0:tm0 + TCM]
    io["out_st"] = lambda j, tm0: x1[j * 128:(j + 1) * 128, tm0:tm0 + TCM]
    io["w1b_dr"] = nc.dram_tensor("w1b_dr", [8, 32, 128, 1024], BF16).ap()
    io["w2b_dr"] = nc.dram_tensor("w2b_dr", [8, 8, 128, 4096], BF16).ap()
    p = Prog()
    with contextlib.ExitStack() as es0:
        _emit_tail(nc, p, io)
        finish(p, es0, nc)
    return nc


def _emit_tail(nc, p, io):
    for _ in _emit_tail_gen(nc, p, io):
        pass


def _emit_tail_gen(nc, p, io):
    lay = io["lay"]
    s_mode = io["s_mode"]
    wgt, b1c, b2c = io["wgt"], io["b1c"], io["b2c"]
    w1r, w2r = io["w1r"], io["w2r"]
    w1bd, w2bd = io["w1b_dr"], io["w2b_dr"]
    eye128, eye8 = io["eye128"], io["eye8"]
    xn_ld, x_ld, out_st = io.get("xn_ld"), io["x_ld"], io["out_st"]
    PF = io.get("pref", "")
    with contextlib.ExitStack() as es:
        _pf = io.get("pref", "")
        T = lambda name, shape, dt: es.enter_context(nc.sbuf_tensor(_pf + name, shape, dt))
        PT = lambda name, shape, dt: es.enter_context(nc.psum_tensor(_pf + name, shape, dt))
        wgb = T("wgb", [128, 64], BF16)
        wgf = T("wgf", [128, 64], F32)
        b1s = T("b1s", [128, 256], F32)
        b2s = T("b2s", [128, 64], F32)
        id128 = T("id128", [128, 128], BF16)
        id8 = T("id8", [8, 8], BF16)
        xns = T("xns", [128, 8 * 512], BF16)
        gsb = T("gsb", [128, 32], F32)
        mm1a = T("mm1a", [128, 4], F32)
        mm2a = T("mm2a", [128, 4], F32)
        mk1a = T("mk1a", [128, 32], F32)
        gg2a = T("gg2a", [128, 32], F32)
        sela = T("sela", [128, 32], F32)
        ee1a = T("ee1a", [128, 32], F32)
        eesa = T("eesa", [128, 32], F32)
        ssma = T("ssma", [128, 4], F32)
        rcpa = T("rcpa", [128, 4], F32)
        msk_tm = T("msk_tm", [128, 128], BF16)
        mask_fm = T("mask_fm", [8, 2048], BF16)
        mrep = T("mrep", [128, 512], BF16)
        if lay == 1:
            bgs = T("bgs", [128, 8], F32)
            sss = T("sss", [128, 512], BF16)
        w1st = [T(f"w1st{i}", [128, 1024], F32) for i in range(io.get("nst", 1))]
        w1b = [T(f"w1b{i}", [128, 1024], BF16) for i in range(2)]
        w2st = [T(f"w2st{i}", [128, 2048], F32) for i in range(1)]
        w2b = [T(f"w2b{i}", [128, 2048], BF16) for i in range(2)]
        hsb = T("hsb", [128, 32 * 512], BF16)
        ytmp = [T(f"ytmp{i}", [128, 512], F32) for i in range(2)]
        tmp2 = T("tmp2", [128, 512], BF16)
        yacc = T("yaccC", [128, 8 * 512], F32 if lay == 0 else BF16)
        gbf = [T(f"gbf{i}", [128, 512], BF16) for i in range(1)]
        xsl = [T(f"xsl{i}", [128, 512], F32) for i in range(1)]
        sas = [T(f"sas{i}", [128, 512], F32) for i in range(1)]
        sbs = [T(f"sbs{i}", [128, 512], F32) for i in range(1)]
        x1s = [T(f"x1s{i}", [128, 512], F32) for i in range(1)]
        ptr = PT("ptr", [8, 128], BF16)
        pc = [PT(f"pc{i}", [128, 512], F32) for i in range(2)]
        pm = PT("pm", [128, 512], F32)
        gpsA = pm[:, 0:32]

        A = p.add
        V = p.v
        war = defaultdict(tuple)

        A("sync", lambda s: s.dma_start(out=id128[:], in_=eye128), inc=(PF + "L_stp", 16))
        A("sync", lambda s: s.dma_start(out=id8[:], in_=eye8), inc=(PF + "L_stp", 16))
        A("sync", lambda s: s.dma_start(out=wgf[:], in_=wgt), inc=(PF + "L_stp", 16))
        A("sync", lambda s: s.dma_start(out=b1s[:], in_=b1c), inc=(PF + "L_stp", 16))
        A("sync", lambda s: s.dma_start(out=b2s[:], in_=b2c), inc=(PF + "L_stp", 16))
        if lay == 1:
            A("sync", lambda s: s.dma_start(out=bgs[:], in_=io["bgr"]), inc=(PF + "L_stp", 16))
        ld0 = (PF + "L_stp", V(PF + "L_stp"))
        A("vector", lambda v: v.tensor_copy(out=wgb[:], in_=wgf[:]), inc="DVE", wait=[ld0])
        setup_dve = V("DVE")

        ln_out = 0
        if io.get("ln_x") is not None:
            lnx = io["ln_x"]
            lnc = io["ln_cst"]
            xn_st = io["xn_st"]
            xfs2 = [ytmp[0][:], ytmp[1][:]]
            xsq2 = [xsl[0][:], sas[0][:]]
            nmT = yacc[0:1, 0:512]
            m2T = yacc[0:1, 512:1024]
            varT = yacc[0:1, 1024:1536]
            rsT = yacc[0:1, 1536:2048]
            nmrT = yacc[0:1, 2048:2560]
            xntT = x1s[0][:]
            xobT = gbf[0][:]
            onescT = T("onescT", [128, 1], F32)
            onesrT = T("onesrT", [1, 128], F32)
            csttT = T("csttT", [128, 4], F32)
            mpsT = pc[0][0:1, 0:512]
            spsT = pc[1][0:1, 0:512]
            A("vector", lambda v: v.memset(onescT[:], 1.0), inc="DVE")
            A("vector", lambda v: v.memset(onesrT[:], 1.0), inc="DVE")
            A("sync", lambda s: s.dma_start(out=csttT[:], in_=lnc), inc=(PF + "L_stp", 16))
            setup_dve = V("DVE")
            ldc = (PF + "L_stp", V(PF + "L_stp"))
            for c in range(4):
                t0 = c * 512
                for k in range(8):
                    par = k % 2
                    A("sync", lambda s, k=k, par=par, t0=t0: s.dma_start(
                        out=xfs2[par], in_=lnx[k * 128:(k + 1) * 128, t0:t0 + 512]),
                      inc=(PF + f"L_lnx{par}", 16), wait=war[("xfT", par)])
                    ldw = (PF + f"L_lnx{par}", V(PF + f"L_lnx{par}"))
                    A("scalar", lambda s, par=par: s.activation(out=xsq2[par], in_=xfs2[par], func=AF.Square),
                      inc="ACT", wait=[ldw] + list(war[("xsqT", par)]))
                    actv = V("ACT")
                    A("tensor", lambda t, k=k, par=par: t.matmul(
                        out=mpsT, lhsT=onescT[:], rhs=xfs2[par], start=(k == 0), stop=(k == 7)),
                      inc="PE",
                      wait=([ldw, ("DVE", setup_dve)] + list(war[("mpsT",)])) if k == 0 else [ldw])
                    A("tensor", lambda t, k=k, par=par: t.matmul(
                        out=spsT, lhsT=onescT[:], rhs=xsq2[par], start=(k == 0), stop=(k == 7)),
                      inc="PE", wait=[("ACT", actv)])
                    war[("xfT", par)] = [("PE", V("PE"))]
                    war[("xsqT", par)] = [("PE", V("PE"))]
                pe_stats = V("PE")
                A("vector", lambda v: v.tensor_scalar_mul(out=nmT, in0=mpsT, scalar1=-1.0 / 1024.0),
                  inc="DVE", wait=[("PE", pe_stats)])
                A("vector", lambda v: v.tensor_mul(out=m2T, in0=nmT, in1=nmT), inc="DVE")
                A("vector", lambda v: v.scalar_tensor_tensor(
                    out=varT, in0=spsT, scalar=1.0 / 1024.0, in1=m2T, op0=ALU.mult, op1=ALU.subtract),
                  inc="DVE", wait=[("PE", pe_stats)])
                dve_var = V("DVE")
                war[("mpsT",)] = [("DVE", dve_var)]
                A("scalar", lambda s: s.activation(out=rsT, in_=varT, func=AF.Ln, bias=csttT[0:1, 0:1]),
                  inc="ACT", wait=[("DVE", dve_var), ldc])
                A("scalar", lambda s: s.activation(out=rsT, in_=rsT, func=AF.Exp, scale=csttT[0:1, 1:2]),
                  inc="ACT")
                act_rs = V("ACT")
                A("vector", lambda v: v.tensor_mul(out=nmrT, in0=nmT, in1=rsT),
                  inc="DVE", wait=[("ACT", act_rs)])
                dve_nmr = V("DVE")
                A("tensor", lambda t: t.matmul(out=pm[:], lhsT=onesrT[:], rhs=rsT, start=True, stop=True),
                  inc="PE", wait=[("ACT", act_rs)] + list(war[("pmLN",)]))
                pe_b1 = V("PE")
                A("tensor", lambda t: t.matmul(out=pc[0][:], lhsT=onesrT[:], rhs=nmrT, start=True, stop=True),
                  inc="PE", wait=[("DVE", dve_nmr), ("DVE", V("DVE"))])
                pe_bc = V("PE")
                for k in range(8):
                    par = k % 2
                    A("sync", lambda s, k=k, par=par, t0=t0: s.dma_start(
                        out=xfs2[par], in_=lnx[k * 128:(k + 1) * 128, t0:t0 + 512]),
                      inc=(PF + f"L_lnx{par}", 16), wait=war[("xfT", par)])
                    ldw = (PF + f"L_lnx{par}", V(PF + f"L_lnx{par}"))
                    A("vector", lambda v, par=par: v.tensor_mul(out=xntT, in0=xfs2[par], in1=pm[:]),
                      inc="DVE", wait=[ldw, ("PE", pe_bc)])
                    A("vector", lambda v, par=par: v.tensor_add(out=xobT, in0=xntT, in1=pc[0][:]),
                      inc="DVE", wait=war[("xobT",)])
                    dv = V("DVE")
                    war[("xfT", par)] = [("DVE", dv)]
                    A("gpsimd", lambda g, k=k, t0=t0: g.dma_start(out=xn_st(k, t0), in_=xobT),
                      inc=(PF + "O_gbf", 16), wait=[("DVE", dv)])
                    war[("xobT",)] = [(PF + "O_gbf", V(PF + "O_gbf"))]
                war[("pmLN",)] = [("DVE", V("DVE"))]
                war[("pm",)] = [("DVE", V("DVE"))]
                war[("pc", 0)] = [("DVE", V("DVE"))]
                war[("mpsT",)] = [("DVE", V("DVE"))]
            ln_out = V(PF + "O_gbf")
            setup_dve = V("DVE")
            war[("ytmp", 0)] = [("PE", V("PE")), ("DVE", setup_dve), ("ACT", V("ACT"))]
            war[("ytmp", 1)] = list(war[("ytmp", 0)])
            war[("xsl", 0)] = list(war[("ytmp", 0)])
            war[("sas", 0)] = list(war[("ytmp", 0)])
            war[("x1s", 0)] = [("DVE", setup_dve)]
            war[("gbf", 0)] = [(PF + "O_gbf", ln_out)]
            for j in range(8):
                war[("yacck", j)] = [("PE", V("PE"))]
        yield "setup"

        g3 = lambda t_: t_.rearrange("p (t e) -> p t e", e=8)
        bc3 = lambda t_: t_.unsqueeze(2).broadcast_to([128, 4, 8])
        first_e_pe = {}
        for tcm in range(4):
            tm0 = tcm * 512
            xnw = list(war[("xns",)])
            if io.get("ln_x") is not None:
                xnw += [(PF + "O_gbf", ln_out)]
            else:
                xnw += list(io.get("xn_wait", ()))
            for k in range(8):
                A("sync", lambda s, k=k, tm0=tm0: s.dma_start(
                    out=xns[:, k * 512:(k + 1) * 512], in_=xn_ld(k, tm0)),
                  inc=(PF + "L_xns", 16), wait=xnw if k == 0 else [])
            ld_xn = (PF + "L_xns", V(PF + "L_xns"))
            for tt in range(4):
                for k in range(8):
                    A("tensor", lambda t, tt=tt, k=k: t.matmul(
                        out=gpsA[:, tt * 8:(tt + 1) * 8],
                        lhsT=xns[:, k * 512 + tt * 128:k * 512 + tt * 128 + 128],
                        rhs=wgb[:, k * 8:(k + 1) * 8], start=(k == 0), stop=(k == 7)),
                      inc="PE",
                      wait=([("DVE", setup_dve), ld_xn] + list(war[("gpsA",)])) if (k == 0 and tt == 0) else [])
            pe_gate = V("PE")
            A("scalar", lambda s: s.copy(out=gsb[:], in_=gpsA[:]), inc="ACT",
              wait=[("PE", pe_gate)] + list(war[("gsb",)]))
            act_gsb = V("ACT")
            if lay == 0:
                A("vector", lambda v: v.tensor_reduce(out=mm1a[:].unsqueeze(2), in_=g3(gsb[:]),
                                                      axis=mybir.AxisListType.X, op=ALU.max),
                  inc="DVE", wait=[("ACT", act_gsb)])
                A("scalar", lambda s: s.activation(out=ee1a[:], in_=gsb[:], func=AF.Exp),
                  inc="ACT")
                A("vector", lambda v: v.tensor_tensor(out=g3(mk1a[:]), in0=g3(gsb[:]), in1=bc3(mm1a[:]),
                                                      op=ALU.is_ge), inc="DVE", wait=[("DVE", V("DVE"))])
                A("vector", lambda v: v.scalar_tensor_tensor(out=gg2a[:], in0=mk1a[:], scalar=-30000.0,
                                                             in1=gsb[:], op0=ALU.mult, op1=ALU.add),
                  inc="DVE")
                A("vector", lambda v: v.tensor_reduce(out=mm2a[:].unsqueeze(2), in_=g3(gg2a[:]),
                                                      axis=mybir.AxisListType.X, op=ALU.max), inc="DVE",
                  wait=[("DVE", V("DVE"))])
                A("vector", lambda v: v.tensor_tensor(out=g3(sela[:]), in0=g3(gsb[:]), in1=bc3(mm2a[:]),
                                                      op=ALU.is_ge), inc="DVE", wait=[("DVE", V("DVE"))])
                A("vector", lambda v: v.tensor_mul(out=eesa[:], in0=ee1a[:], in1=sela[:]),
                  inc="DVE", wait=[("ACT", V("ACT"))])
                A("vector", lambda v: v.tensor_reduce(out=ssma[:].unsqueeze(2), in_=g3(eesa[:]),
                                                      axis=mybir.AxisListType.X, op=ALU.add), inc="DVE",
                  wait=[("DVE", V("DVE"))])
                A("vector", lambda v: v.reciprocal(out=rcpa[:], in_=ssma[:]), inc="DVE",
                  wait=[("DVE", V("DVE"))])
                A("vector", lambda v, tcm=tcm: v.scalar_tensor_tensor(
                    out=g3(msk_tm[:, tcm * 32:(tcm + 1) * 32]), in0=g3(eesa[:]),
                    scalar=0.125, in1=bc3(rcpa[:]),
                    op0=ALU.mult, op1=ALU.mult), inc="DVE",
                  wait=[("DVE", V("DVE"))])
            else:
                A("vector", lambda v: v.tensor_tensor(
                    out=g3(gsb[:]), in0=g3(gsb[:]),
                    in1=bgs[:].unsqueeze(1).broadcast_to([128, 4, 8]), op=ALU.add),
                  inc="DVE", wait=[("ACT", act_gsb), ld0])
                dve_lg = V("DVE")
                A("scalar", lambda s: s.activation(out=ee1a[:], in_=gsb[:], func=AF.Exp),
                  inc="ACT", wait=[("DVE", dve_lg)])
                A("vector", lambda v: v.tensor_reduce(out=ssma[:].unsqueeze(2), in_=g3(ee1a[:]),
                                                      axis=mybir.AxisListType.X, op=ALU.add), inc="DVE",
                  wait=[("ACT", V("ACT"))])
                A("vector", lambda v: v.reciprocal(out=rcpa[:], in_=ssma[:]), inc="DVE",
                  wait=[("DVE", V("DVE"))])
                A("vector", lambda v: v.tensor_tensor(out=g3(gsb[:]), in0=g3(ee1a[:]), in1=bc3(rcpa[:]),
                                                      op=ALU.mult), inc="DVE", wait=[("DVE", V("DVE"))])
                A("vector", lambda v: v.tensor_reduce(out=mm1a[:].unsqueeze(2), in_=g3(gsb[:]),
                                                      axis=mybir.AxisListType.X, op=ALU.max), inc="DVE",
                  wait=[("DVE", V("DVE"))])
                A("vector", lambda v: v.tensor_tensor(out=g3(mk1a[:]), in0=g3(gsb[:]), in1=bc3(mm1a[:]),
                                                      op=ALU.is_ge), inc="DVE", wait=[("DVE", V("DVE"))])
                A("vector", lambda v: v.scalar_tensor_tensor(out=gg2a[:], in0=mk1a[:], scalar=-30000.0,
                                                             in1=gsb[:], op0=ALU.mult, op1=ALU.add),
                  inc="DVE")
                A("vector", lambda v: v.tensor_reduce(out=mm2a[:].unsqueeze(2), in_=g3(gg2a[:]),
                                                      axis=mybir.AxisListType.X, op=ALU.max), inc="DVE",
                  wait=[("DVE", V("DVE"))])
                A("vector", lambda v: v.tensor_tensor(out=g3(sela[:]), in0=g3(gsb[:]), in1=bc3(mm2a[:]),
                                                      op=ALU.is_ge), inc="DVE", wait=[("DVE", V("DVE"))])
                A("vector", lambda v: v.tensor_mul(out=eesa[:], in0=gsb[:], in1=sela[:]), inc="DVE")
                A("vector", lambda v: v.tensor_reduce(out=ssma[:].unsqueeze(2), in_=g3(eesa[:]),
                                                      axis=mybir.AxisListType.X, op=ALU.add), inc="DVE",
                  wait=[("DVE", V("DVE"))])
                A("vector", lambda v: v.tensor_scalar_add(out=ssma[:], in0=ssma[:], scalar1=1e-8),
                  inc="DVE", wait=[("DVE", V("DVE"))])
                A("vector", lambda v: v.reciprocal(out=rcpa[:], in_=ssma[:]), inc="DVE",
                  wait=[("DVE", V("DVE"))])
                A("vector", lambda v, tcm=tcm: v.tensor_tensor(
                    out=g3(msk_tm[:, tcm * 32:(tcm + 1) * 32]), in0=g3(eesa[:]),
                    in1=bc3(rcpa[:]), op=ALU.mult), inc="DVE",
                  wait=[("DVE", V("DVE"))])
            dve_msk = V("DVE")
            war[("gsb",)] = [("DVE", dve_msk)]
            for tt in range(4):
                A("tensor", lambda t, tt=tt, tcm=tcm: t.transpose(
                    out=ptr[:], in_=msk_tm[:, tcm * 32 + tt * 8:tcm * 32 + (tt + 1) * 8], identity=id128[:]),
                  inc="PE", wait=[("DVE", dve_msk), ld0] + list(war[("ptr",)]))
                pev = V("PE")
                A("scalar", lambda s, tt=tt, tm0=tm0: s.copy(
                    out=mask_fm[:, tm0 + tt * 128:tm0 + (tt + 1) * 128], in_=ptr[:]),
                  inc="ACT", wait=[("PE", pev)])
                war[("ptr",)] = [("ACT", V("ACT"))]
            act_mask = V("ACT")
            war[("gpsA",)] = [("ACT", act_mask)]
            yield ("gating", tcm)

            for e in range(8):
                A("tensor", lambda t, e=e, tm0=tm0: t.matmul(
                    out=pm[:], lhsT=id8[:, e:e + 1].broadcast_to([8, 128]),
                    rhs=mask_fm[:, tm0:tm0 + 512], start=True, stop=True),
                  inc="PE",
                  wait=[("ACT", act_mask)] + list(war[("pm",)]))
                pev = V("PE")
                A("scalar", lambda s: s.copy(out=mrep[:], in_=pm[:]),
                  inc="ACT", wait=[("PE", pev)] + list(war[("mrep",)]))
                act_mr = V("ACT")
                war[("pm",)] = [("ACT", act_mr)]
                for hcol in range(32):
                    par = hcol % 2
                    wp = hcol % 2
                    pp = pc[par]
                    if tcm == 0:
                        A("sync", lambda s, e=e, hcol=hcol, wp=wp: s.dma_start(
                            out=w1st[wp % len(w1st)][:], in_=w1r[e, hcol]),
                          inc=(PF + f"L_w1s{wp}", 16), wait=war[("w1st", wp % len(w1st))])
                        ldv = (PF + f"L_w1s{wp}", V(PF + f"L_w1s{wp}"))
                        A("scalar", lambda s2, wp=wp: s2.copy(out=w1b[wp][:], in_=w1st[wp % len(w1st)][:]),
                          inc="ACT", wait=[ldv] + list(war[("w1b", wp)]))
                        dve_w1 = V("ACT")
                        war[("w1st", wp % len(w1st))] = [("ACT", dve_w1)]
                        A("gpsimd", lambda g, e=e, hcol=hcol, wp=wp: g.dma_start(
                            out=w1bd[e, hcol], in_=w1b[wp][:]),
                          inc=(PF + f"O_w1b{wp}", 16), wait=[("ACT", dve_w1)])
                        wb_out = V(PF + f"O_w1b{wp}")
                        ld_w1 = [("ACT", dve_w1)]
                        war[("w1b", wp)] = [(PF + f"O_w1b{wp}", wb_out)]
                    else:
                        A("sync", lambda s, e=e, hcol=hcol, wp=wp: s.dma_start(
                            out=w1b[wp][:], in_=w1bd[e, hcol]),
                          inc=(PF + f"L_w1b{wp}", 16), wait=list(war[("w1b", wp)]) + list(war[("wb_done",)]))
                        ld_w1 = [(PF + f"L_w1b{wp}", V(PF + f"L_w1b{wp}"))]
                    for k in range(8):
                        A("tensor", lambda t, k=k, wp=wp, tm0=tm0, pp=pp: t.matmul(
                            out=pp[:], lhsT=w1b[wp][:, k * 128:(k + 1) * 128],
                            rhs=xns[:, k * 512:(k + 1) * 512],
                            start=(k == 0), stop=(k == 7)),
                          inc="PE",
                          wait=(ld_w1 + list(war[("pc", par)])) if k == 0 else [])
                    pev = V("PE")
                    war[("w1b", wp)] = list(war[("w1b", wp)]) + [("PE", pev)] if tcm == 0 else [("PE", pev)]
                    A("scalar", lambda s, e=e, hcol=hcol, pp=pp: s.activation(
                        out=hsb[:, hcol * 512:(hcol + 1) * 512], in_=pp[:],
                        func=AF.Relu if lay == 0 else AF.Gelu_apprx_tanh,
                        bias=b1s[:, e * 32 + hcol:e * 32 + hcol + 1]),
                      inc="ACT", wait=[("PE", pev), ld0] + list(war[("hsbk", hcol)]))
                    war[("pc", par)] = [("ACT", V("ACT"))]
                act_h = V("ACT")
                for dcol in range(8):
                    par = dcol % 2
                    pp = pc[par]
                    for hh in range(2):
                        sp = hh
                        if tcm == 0:
                            A("sync", lambda s, e=e, dcol=dcol, hh=hh, sp=sp: s.dma_start(
                                out=w2st[0][:], in_=w2r[e, dcol][:, hh * 2048:(hh + 1) * 2048]),
                              inc=(PF + f"L_w2s{sp}", 16), wait=war[("w2st", 0)])
                            ldv = (PF + f"L_w2s{sp}", V(PF + f"L_w2s{sp}"))
                            A("scalar", lambda s2, sp=sp: s2.copy(out=w2b[sp][:], in_=w2st[0][:]),
                              inc="ACT", wait=[ldv] + list(war[("w2b", sp)]))
                            dve_w2 = V("ACT")
                            war[("w2st", 0)] = [("ACT", dve_w2)]
                            A("gpsimd", lambda g, e=e, dcol=dcol, hh=hh, sp=sp: g.dma_start(
                                out=w2bd[e, dcol][:, hh * 2048:(hh + 1) * 2048], in_=w2b[sp][:]),
                              inc=(PF + f"O_w2b{sp}", 16), wait=[("ACT", dve_w2)])
                            war[("w2b", sp)] = [(PF + f"O_w2b{sp}", V(PF + f"O_w2b{sp}"))]
                            ld_w2 = [("ACT", dve_w2)]
                        else:
                            A("sync", lambda s, e=e, dcol=dcol, hh=hh, sp=sp: s.dma_start(
                                out=w2b[sp][:], in_=w2bd[e, dcol][:, hh * 2048:(hh + 1) * 2048]),
                              inc=(PF + f"L_w2b{sp}", 16), wait=list(war[("w2b", sp)]) + list(war[("wb_done",)]))
                            ld_w2 = [(PF + f"L_w2b{sp}", V(PF + f"L_w2b{sp}"))]
                        for kh in range(16):
                            A("tensor", lambda t, kh=kh, hh=hh, sp=sp, pp=pp: t.matmul(
                                out=pp[:], lhsT=w2b[sp][:, kh * 128:(kh + 1) * 128],
                                rhs=hsb[:, (hh * 16 + kh) * 512:(hh * 16 + kh + 1) * 512],
                                start=(hh == 0 and kh == 0), stop=(hh == 1 and kh == 15)),
                              inc="PE",
                              wait=(ld_w2 + ([("ACT", act_h)] + list(war[("pc", par)]) if hh == 0 else [])) if kh == 0 else [])
                        pev = V("PE")
                        war[("w2b", sp)] = list(war[("w2b", sp)]) + [("PE", pev)] if tcm == 0 else [("PE", pev)]
                    A("scalar", lambda s, e=e, dcol=dcol, par=par, pp=pp: s.activation(
                        out=ytmp[par][:], in_=pp[:], func=AF.Identity,
                        bias=b2s[:, e * 8 + dcol:e * 8 + dcol + 1]),
                      inc="ACT", wait=[("PE", pev), ld0] + list(war[("ytmp", par)]))
                    war[("pc", par)] = [("ACT", V("ACT"))]
                    act_yt = V("ACT")
                    if e == 0:
                        A("gpsimd", lambda v, dcol=dcol, par=par: v.tensor_mul(
                            out=yacc[:, dcol * 512:(dcol + 1) * 512], in0=ytmp[par][:], in1=mrep[:]),
                          inc="POOL", wait=[("ACT", act_yt), ("ACT", act_mr)] + list(war[("yacck", dcol)]))
                    else:
                        A("gpsimd", lambda v, dcol=dcol, par=par: v.tensor_mul(
                            out=tmp2[:], in0=ytmp[par][:], in1=mrep[:]),
                          inc="POOL", wait=[("ACT", act_yt), ("ACT", act_mr)])
                        A("gpsimd", lambda v, dcol=dcol: v.tensor_add(
                            out=yacc[:, dcol * 512:(dcol + 1) * 512],
                            in0=yacc[:, dcol * 512:(dcol + 1) * 512], in1=tmp2[:]),
                          inc="POOL")
                    war[("ytmp", par)] = [("POOL", V("POOL"))]
                war[("mrep",)] = [("POOL", V("POOL"))]
                yield ("block", tcm, e)
            if tcm == 0:
                war[("wb_done",)] = [(PF + "O_w1b0", V(PF + "O_w1b0")), (PF + "O_w1b1", V(PF + "O_w1b1")),
                                     (PF + "O_w2b0", V(PF + "O_w2b0")), (PF + "O_w2b1", V(PF + "O_w2b1"))]
            for hcol in range(32):
                war[("hsbk", hcol)] = [("PE", V("PE"))]
            for j in range(8):
                par = 0
                A("scalar", lambda s, j=j, par=par: s.activation(
                    out=gbf[par][:], in_=yacc[:, j * 512:(j + 1) * 512], func=AF.Silu),
                  inc="ACT", wait=[("POOL", V("POOL"))] + list(war[("gbf", par)]))
                act_g = V("ACT")
                A("sync", lambda s, j=j, par=par, tm0=tm0: s.dma_start(
                    out=xsl[par][:], in_=x_ld(j, tm0)),
                  inc=(PF + "L_xsl", 16), wait=war[("xsl", par)])
                ld_xsl = (PF + "L_xsl", V(PF + "L_xsl"))
                if s_mode == "dual":
                    A("sync", lambda s, j=j, par=par, tm0=tm0: s.dma_start(
                        out=sas[par][:], in_=io["sa_ld"](j, tm0)),
                      inc=(PF + "L_sas", 16), wait=war[("sas", par)])
                    A("sync", lambda s, j=j, par=par, tm0=tm0: s.dma_start(
                        out=sbs[par][:], in_=io["sb_ld"](j, tm0)),
                      inc=(PF + "L_sbs", 16), wait=war[("sbs", par)])
                    ldv = [(PF + "L_sas", V(PF + "L_sas")), (PF + "L_sbs", V(PF + "L_sbs"))]
                    A("gpsimd", lambda v, par=par: v.tensor_add(out=sas[par][:], in0=sas[par][:], in1=sbs[par][:]),
                      inc="POOL", wait=ldv)
                elif s_mode == "single_f32":
                    A("sync", lambda s, j=j, par=par, tm0=tm0: s.dma_start(
                        out=sas[par][:], in_=io["s_ld"](j, tm0)),
                      inc=(PF + "L_sas", 16), wait=tuple(war[("sas", par)]) + tuple(io.get("s_wait", ())))
                    ldv = [(PF + "L_sas", V(PF + "L_sas"))]
                else:
                    A("sync", lambda s, j=j, tm0=tm0: s.dma_start(
                        out=sss[:], in_=io["s_ld"](j, tm0)),
                      inc=(PF + "L_sss", 16), wait=list(war[("sas", par)]) + list(io.get("s_wait", ())))
                    ldv = [(PF + "L_sss", V(PF + "L_sss"))]
                    A("gpsimd", lambda v, par=par: v.tensor_copy(out=sas[par][:], in_=sss[:]),
                      inc="POOL", wait=ldv)
                A("gpsimd", lambda v, par=par: v.tensor_mul(out=sas[par][:], in0=sas[par][:], in1=gbf[par][:]),
                  inc="POOL", wait=[("ACT", act_g)] + ldv)
                A("gpsimd", lambda v, par=par: v.tensor_add(out=x1s[par][:], in0=xsl[par][:], in1=sas[par][:]),
                  inc="POOL", wait=[ld_xsl] + list(war[("x1s", par)]))
                dve_x1 = V("POOL")
                war[("sbs", par)] = [("POOL", dve_x1)]
                war[("sas", par)] = [("POOL", dve_x1)]
                war[("xsl", par)] = [("POOL", dve_x1)]
                war[("gbf", par)] = [("POOL", dve_x1)]
                A("gpsimd", lambda g, j=j, par=par, tm0=tm0: g.dma_start(
                    out=out_st(j, tm0), in_=x1s[par][:]),
                  inc=(PF + "O_x1s", 16), wait=[("POOL", dve_x1)])
                war[("x1s", par)] = [(PF + "O_x1s", V(PF + "O_x1s"))]
                war[("yacck", j)] = [("ACT", act_g)]
            war[("xns",)] = [("PE", V("PE"))]
            yield ("combine", tcm)


def stage_C(d, res_b):
    x = np.asarray(d["L0_attn_x"])
    w1 = np.asarray(d["L0_ffn_W1"])
    w2 = np.asarray(d["L0_ffn_W2"])
    w1r = np.ascontiguousarray(w1.reshape(8, 8, 128, 32, 128).transpose(0, 3, 2, 1, 4).reshape(8, 32, 128, 1024))
    w2r = np.ascontiguousarray(w2.reshape(8, 32, 128, 8, 128).transpose(0, 3, 2, 1, 4).reshape(8, 8, 128, 4096))
    b1cv = np.ascontiguousarray(np.asarray(d["L0_ffn_b1"]).reshape(8, 32, 128).transpose(2, 0, 1).reshape(128, 256))
    b2cv = np.ascontiguousarray(np.asarray(d["L0_ffn_b2"]).reshape(8, 8, 128).transpose(2, 0, 1).reshape(128, 64))
    wgtv = np.ascontiguousarray(np.asarray(d["L0_ffn_W_gate"]).reshape(8, 128, 8).transpose(1, 0, 2).reshape(128, 64))
    e128 = np.eye(128).astype(BF)
    e8 = np.eye(8).astype(BF)
    maps = []
    for c in range(8):
        b, ih = c // 2, c % 2
        tsl = slice(ih * 2048, (ih + 1) * 2048)
        partner = 2 * b + (1 - ih)
        m = {
            "xn_in": np.ascontiguousarray(res_b[c]["xn_out"][:, tsl]),
            "x_own": np.ascontiguousarray(x[b].T[:, tsl]),
            "s_a": np.ascontiguousarray(res_b[c]["s_par"][:, tsl]),
            "s_b": np.ascontiguousarray(res_b[partner]["s_par"][:, tsl]),
            "wgt": wgtv, "w1r": w1r, "w2r": w2r, "b1c": b1cv, "b2c": b2cv,
            "eye128": e128, "eye8": e8,
        }
        maps.append(m)
    return maps


def build_D1():
    nc = bass.Bass("TRN2", target_bir_lowering=False, debug=False)
    io = {}
    for nm, sh, dt in [("x1in", [1024, 2048], F32), ("nwc", [128, 8], F32),
                       ("w_xp", [1024, 192], F32), ("w_dt1", [64, 1024], F32),
                       ("b_dt1", [128, 8], F32), ("a1in", [128, 512], F32),
                       ("d1c", [128, 8], F32), ("w_op1r", [8, 128, 1024], F32),
                       ("eye64", [64, 64], BF16), ("cst", [128, 4], F32)]:
        io[nm] = nc.dram_tensor(nm, sh, dt, kind="ExternalInput").ap()
    io["s1o"] = nc.dram_tensor("s1o", [1024, 2048], BF16, kind="ExternalOutput").ap()
    io["xn1o"] = nc.dram_tensor("xn1o", [1024, 2048], BF16, kind="ExternalOutput").ap()
    p = Prog()
    with contextlib.ExitStack() as es0:
        _emit_D1(nc, p, io)
        finish(p, es0, nc)
    return nc


def _emit_D1(nc, p, io):
    for _ in _emit_D1_gen(nc, p, io):
        pass


def _emit_D1_gen(nc, p, io):
    x1in, nwc, w_xp, w_dt1 = io["x1in"], io["nwc"], io["w_xp"], io["w_dt1"]
    b_dt1, a1in, d1c, w_op1r = io["b_dt1"], io["a1in"], io["d1c"], io["w_op1r"]
    eye64, cst, s1o, xn1o = io["eye64"], io["cst"], io["s1o"], io["xn1o"]
    NCH1 = 2048 // TC
    PF = io.get("pref", "")
    with contextlib.ExitStack() as es:
        _pf = io.get("pref", "")
        T = lambda name, shape, dt: es.enter_context(nc.sbuf_tensor(_pf + name, shape, dt))
        PT = lambda name, shape, dt: es.enter_context(nc.psum_tensor(_pf + name, shape, dt))
        nws = T("nws", [128, 8], F32)
        wxpb = T("wxpb", [128, 8 * 192], BF16)
        wdt1 = T("wdt1b", [64, 1024], BF16)
        bdt1 = T("bdt1", [128, 8], F32)
        a1c = T("a1c", [128, 512], F32)
        d1s = T("d1s", [128, 8], F32)
        eyes = T("eyes", [64, 64], BF16)
        cstt = T("cstt", [128, 4], F32)
        ones_c = T("ones_c", [128, 1], F32)
        ones_r = T("ones_r", [1, 128], F32)
        wst = T("wstD", [128, 1024], F32)
        wopb = [T(f"wopb{i}", [128, 1024], BF16) for i in range(1)]
        xfs = [T(f"xfD{i}", [128, TC], F32) for i in range(2)]
        xsq = [T(f"xsqD{i}", [128, TC], F32) for i in range(1)]
        nm = T("nmD", [1, TC], F32)
        m2 = T("m2D", [1, TC], F32)
        var = T("varD", [1, TC], F32)
        rs = T("rsD", [1, TC], F32)
        nmr = T("nmrD", [1, TC], F32)
        xnt = T("xntD", [128, TC], F32)
        xn1 = T("xn1", [128, KT * TC], BF16)
        xnr = T("xnr", [128, KT * TC], BF16)
        dlo_t = T("dlo_t", [64, TC], BF16)
        bs_t = T("bs_t", [64, TC], BF16)
        cs_t = T("cs_t", [64, TC], BF16)
        spv = T("spvD", [128, TC], F32)
        sav = T("savD", [128, TC], F32)
        delta1 = T("delta1", [128, KT * TC], BF16)
        dxn = T("dxn", [128, KT * TC], BF16)
        brep = T("brepD", [128, SG * TC], BF16)
        crep = T("crepD", [128, SG * TC], BF16)
        dab = [T(f"dabD{i}", [128, SG * TC], BF16) for i in range(1)]
        ub = [T(f"ubD{i}", [128, SG * TC], BF16) for i in range(1)]
        hb = [T(f"hbD{i}", [128, SG * TC], BF16) for i in range(2)]
        wbf = [T(f"wbfD{i}", [128, SG * TC], BF16) for i in range(1)]
        ysg = [T(f"ysgD{i}", [128, TC], F32) for i in range(1)]
        yacc = T("yaccD", [128, KT * TC], BF16)
        ob = [T(f"ob{i}", [128, KT * TC], BF16) for i in range(2)]
        spo = [T(f"spoD{i}", [128, TC], BF16) for i in range(2)]
        pa = PT("paD", [128, TC], F32)
        pb = PT("pbD", [128, TC], F32)
        brp = PT("brpD", [128, SG * TC], F32)
        mps = pa[0:1, 0:TC]
        sps = pb[0:1, 0:TC]
        pst = [pa, pb]

        A = p.add
        V = p.v
        war = defaultdict(tuple)

        A("vector", lambda v: v.memset(ones_c[:], 1.0), inc="DVE")
        A("vector", lambda v: v.memset(ones_r[:], 1.0), inc="DVE")
        A("sync", lambda s: s.dma_start(out=eyes[:], in_=eye64), inc=(PF + "L_stp", 16))
        A("sync", lambda s: s.dma_start(out=cstt[:], in_=cst), inc=(PF + "L_stp", 16))
        A("sync", lambda s: s.dma_start(out=nws[:], in_=nwc), inc=(PF + "L_stp", 16))
        A("sync", lambda s: s.dma_start(out=bdt1[:], in_=b_dt1), inc=(PF + "L_stp", 16))
        A("sync", lambda s: s.dma_start(out=d1s[:], in_=d1c), inc=(PF + "L_stp", 16))
        A("sync", lambda s: s.dma_start(out=a1c[:], in_=a1in), inc=(PF + "L_stp", 16))
        ld0v = V(PF + "L_stp")
        ld0 = (PF + "L_stp", ld0v)
        A("scalar", lambda s: s.activation(out=a1c[:], in_=a1c[:], func=AF.Exp),
          inc="ACT", wait=[ld0])
        A("vector", lambda v: v.tensor_scalar_mul(out=a1c[:], in0=a1c[:], scalar1=-1.0),
          inc="DVE", wait=[("ACT", V("ACT"))])
        for k in range(KT):
            w_cast = V("ACT")
            A("sync", lambda s, k=k: s.dma_start(out=wst[:, :192], in_=w_xp[k * 128:(k + 1) * 128, :]),
              inc=(PF + "L_wst", 16), wait=[("ACT", w_cast)])
            A("scalar", lambda s, k=k: s.copy(out=wxpb[:, k * 192:(k + 1) * 192], in_=wst[:, :192]),
              inc="ACT", wait=[(PF + "L_wst", V(PF + "L_wst"))])
        for h in range(2):
            w_cast = V("ACT")
            A("sync", lambda s, h=h: s.dma_start(out=wst[:64, :512], in_=w_dt1[:, h * 512:(h + 1) * 512]),
              inc=(PF + "L_wst", 16), wait=[("ACT", w_cast)])
            A("scalar", lambda s, h=h: s.copy(out=wdt1[:, h * 512:(h + 1) * 512], in_=wst[:64, :512]),
              inc="ACT", wait=[(PF + "L_wst", V(PF + "L_wst"))])
        setup_act = V("ACT")
        setup_dve = V("DVE")
        yield "setup"

        obdve = {}

        def emit_back(c):
            t0 = c * TC
            obb = ob[c % 2]
            for dcol in range(8):
                ps = pst[dcol % 2]
                pkey = ("pa",) if dcol % 2 == 0 else ("pb",)
                par = dcol % 2
                wp = 0
                wld = V("ACT")
                A("sync", lambda s, dcol=dcol, wp=wp: s.dma_start(
                    out=wst[:], in_=w_op1r[dcol]),
                  inc=(PF + "L_wst", 16), wait=[("ACT", wld)] + list(war[("wopst",)]))
                ldv = (PF + "L_wst", V(PF + "L_wst"))
                A("scalar", lambda s, wp=wp: s.copy(out=wopb[wp][:], in_=wst[:]),
                  inc="ACT", wait=[ldv] + list(war[("wopb",)]))
                act_wc = V("ACT")
                war[("wopst",)] = [("ACT", act_wc)]
                for k in range(KT):
                    A("tensor", lambda t, k=k, wp=wp, ps=ps, obb=obb: t.matmul(
                        out=ps[:], lhsT=wopb[wp][:, k * 128:(k + 1) * 128],
                        rhs=obb[:, k * TC:(k + 1) * TC], start=(k == 0), stop=(k == KT - 1)),
                      inc="PE",
                      wait=([("DVE", obdve[c]), ("ACT", act_wc)] + list(war[pkey])) if k == 0 else [])
                pev = V("PE")
                war[("wopb",)] = [("PE", pev)]
                A("scalar", lambda s, par=par, ps=ps: s.copy(out=spo[par][:], in_=ps[:]),
                  inc="ACT", wait=[("PE", pev)] + list(war[("spo", par)]))
                actv = V("ACT")
                war[pkey] = [("ACT", actv)]
                A("gpsimd", lambda g, dcol=dcol, par=par, t0=t0: g.dma_start(
                    out=io["s1o"][dcol * 128:(dcol + 1) * 128, t0:t0 + TC], in_=spo[par][:]),
                  inc=(PF + f"O_sp{par}", 16), wait=[("ACT", actv)])
                war[("spo", par)] = [(PF + f"O_sp{par}", V(PF + f"O_sp{par}"))]
            pe_end = V("PE")
            war[("ob", c % 2)] = [("PE", pe_end)]

        for c in range(NCH1):
            t0 = c * TC
            for k in range(KT):
                par = k % 2
                A("sync", lambda s, k=k, par=par, t0=t0: s.dma_start(
                    out=xfs[par][:], in_=x1in[k * 128:(k + 1) * 128, t0:t0 + TC]),
                  inc=(PF + f"L_xf{par}", 16), wait=war[("xf", par)])
                ldw = (PF + f"L_xf{par}", V(PF + f"L_xf{par}"))
                A("scalar", lambda s, par=par: s.activation(out=xsq[0][:], in_=xfs[par][:], func=AF.Square),
                  inc="ACT", wait=[ldw] + list(war[("xsq", 0)]))
                actv = V("ACT")
                A("tensor", lambda t, k=k, par=par: t.matmul(
                    out=mps[:], lhsT=ones_c[:], rhs=xfs[par][:], start=(k == 0), stop=(k == KT - 1)),
                  inc="PE",
                  wait=([ldw, ("DVE", setup_dve)] + list(war[("mps",)]) + list(war[("pa",)]) + list(war[("pb",)])) if k == 0 else [ldw])
                A("tensor", lambda t, k=k, par=par: t.matmul(
                    out=sps[:], lhsT=ones_c[:], rhs=xsq[0][:], start=(k == 0), stop=(k == KT - 1)),
                  inc="PE", wait=[("ACT", actv)])
                war[("xf", par)] = [("PE", V("PE"))]
                war[("xsq", 0)] = [("PE", V("PE"))]
            pe_stats = V("PE")
            A("vector", lambda v: v.tensor_scalar_mul(out=nm[:], in0=mps[:], scalar1=-1.0 / 1024.0),
              inc="DVE", wait=[("PE", pe_stats)])
            A("vector", lambda v: v.tensor_mul(out=m2[:], in0=nm[:], in1=nm[:]), inc="DVE")
            A("vector", lambda v: v.scalar_tensor_tensor(
                out=var[:], in0=sps[:], scalar=1.0 / 1024.0, in1=m2[:], op0=ALU.mult, op1=ALU.subtract),
              inc="DVE", wait=[("PE", pe_stats)])
            dve_var = V("DVE")
            war[("mps",)] = [("DVE", dve_var)]
            A("scalar", lambda s: s.activation(out=rs[:], in_=var[:], func=AF.Ln, bias=cstt[0:1, 0:1]),
              inc="ACT", wait=[("DVE", dve_var)])
            A("scalar", lambda s: s.activation(out=rs[:], in_=rs[:], func=AF.Exp, scale=cstt[0:1, 1:2]), inc="ACT")
            act_rs = V("ACT")
            A("vector", lambda v: v.tensor_mul(out=nmr[:], in0=nm[:], in1=rs[:]),
              inc="DVE", wait=[("ACT", act_rs)])
            dve_nmr = V("DVE")
            A("tensor", lambda t: t.matmul(out=pa[:], lhsT=ones_r[:], rhs=rs[:], start=True, stop=True),
              inc="PE", wait=[("ACT", act_rs)] + list(war[("pa",)]))
            A("tensor", lambda t: t.matmul(out=pb[:], lhsT=ones_r[:], rhs=nmr[:], start=True, stop=True),
              inc="PE", wait=[("DVE", dve_nmr)] + list(war[("pb",)]))
            pe_bc = V("PE")
            for k in range(KT):
                par = k % 2
                A("sync", lambda s, k=k, par=par, t0=t0: s.dma_start(
                    out=xfs[par][:], in_=x1in[k * 128:(k + 1) * 128, t0:t0 + TC]),
                  inc=(PF + f"L_xf{par}", 16), wait=war[("xf", par)])
                ldw = (PF + f"L_xf{par}", V(PF + f"L_xf{par}"))
                A("vector", lambda v, par=par: v.tensor_mul(out=xnt[:], in0=xfs[par][:], in1=pa[:]),
                  inc="DVE", wait=[ldw, ("PE", pe_bc)])
                A("vector", lambda v, k=k, par=par, t0=t0: v.tensor_add(
                    out=xn1[:, k * TC:(k + 1) * TC], in0=xnt[:], in1=pb[:]),
                  inc="DVE", wait=war[("xnk", k)])
                dv = V("DVE")
                war[("xf", par)] = [("DVE", dv)]
                A("gpsimd", lambda g, k=k, t0=t0: g.dma_start(
                    out=io["xn1o"][k * 128:(k + 1) * 128, t0:t0 + TC], in_=xn1[:, k * TC:(k + 1) * TC]),
                  inc=(PF + "O_xn1", 16), wait=[("DVE", dv)])
            oxn_end = V(PF + "O_xn1")
            for k in range(KT):
                war[("xnk", k)] = [(PF + "O_xn1", oxn_end)]
            dve_xn = V("DVE")
            war[("pa",)] = [("DVE", dve_xn)]
            war[("pb",)] = [("DVE", dve_xn)]
            yield ("ln", c)

            for k in range(KT):
                par = k % 2
                A("scalar", lambda s, k=k, par=par, t0=t0: s.activation(
                    out=xsq[0][:], in_=xn1[:, k * TC:(k + 1) * TC], func=AF.Square),
                  inc="ACT", wait=[("DVE", dve_xn)] + list(war[("xsq", 0)]))
                actv = V("ACT")
                A("tensor", lambda t, k=k, par=par: t.matmul(
                    out=sps[:], lhsT=ones_c[:], rhs=xsq[0][:], start=(k == 0), stop=(k == KT - 1)),
                  inc="PE", wait=([("ACT", actv)] + list(war[("mps",)])) if k == 0 else [("ACT", actv)])
                war[("xsq", 0)] = [("PE", V("PE"))]
            pe_rms = V("PE")
            A("vector", lambda v: v.tensor_scalar_mul(out=var[:], in0=sps[:], scalar1=1.0 / 1024.0),
              inc="DVE", wait=[("PE", pe_rms)])
            dve_var2 = V("DVE")
            war[("mps",)] = [("DVE", dve_var2)]
            A("scalar", lambda s: s.activation(out=rs[:], in_=var[:], func=AF.Ln, bias=cstt[0:1, 2:3]),
              inc="ACT", wait=[("DVE", dve_var2)])
            A("scalar", lambda s: s.activation(out=rs[:], in_=rs[:], func=AF.Exp, scale=cstt[0:1, 1:2]), inc="ACT")
            act_rs2 = V("ACT")
            A("tensor", lambda t: t.matmul(out=pa[:], lhsT=ones_r[:], rhs=rs[:], start=True, stop=True),
              inc="PE", wait=[("ACT", act_rs2)] + list(war[("pa",)]))
            pe_bc2 = V("PE")
            for k in range(KT):
                A("vector", lambda v, k=k, t0=t0: v.scalar_tensor_tensor(
                    out=xnr[:, k * TC:(k + 1) * TC], in0=xn1[:, k * TC:(k + 1) * TC],
                    scalar=nws[:, k:k + 1], in1=pa[:], op0=ALU.mult, op1=ALU.mult),
                  inc="DVE", wait=([("PE", pe_bc2), ld0] + list(war[("xnrk", k)])))
            dve_xnr = V("DVE")
            war[("pa",)] = [("DVE", dve_xnr)]

            for gi in range(2):
                ps = pst[gi % 2]
                pkey = ("pa",) if gi % 2 == 0 else ("pb",)
                npart = 128 if gi == 0 else 64
                col0 = gi * 128
                for k in range(KT):
                    A("tensor", lambda t, k=k, col0=col0, ps=ps, npart=npart: t.matmul(
                        out=ps[:npart, :], lhsT=wxpb[:, k * 192 + col0:k * 192 + col0 + npart],
                        rhs=xnr[:, k * TC:(k + 1) * TC], start=(k == 0), stop=(k == KT - 1)),
                      inc="PE",
                      wait=(list(war[pkey]) + [("ACT", setup_act), ("DVE", dve_xnr)]) if k == 0 else [])
                pev = V("PE")
                if gi == 0:
                    A("scalar", lambda s, ps=ps: s.copy(out=dlo_t[:], in_=ps[0:64, :]),
                      inc="ACT", wait=[("PE", pev)] + list(war[("dlo",)]))
                    A("scalar", lambda s, ps=ps: s.copy(out=bs_t[:], in_=ps[64:128, :]), inc="ACT")
                else:
                    A("scalar", lambda s, ps=ps: s.copy(out=cs_t[:], in_=ps[0:64, :]),
                      inc="ACT", wait=[("PE", pev)] + list(war[("cs",)]))
                war[pkey] = [("ACT", V("ACT"))]
            act_proj = V("ACT")

            for j in range(KT):
                ps = pst[j % 2]
                pkey = ("pa",) if j % 2 == 0 else ("pb",)
                A("tensor", lambda t, j=j, ps=ps: t.matmul(
                    out=ps[:], lhsT=wdt1[:, j * 128:(j + 1) * 128], rhs=dlo_t[:],
                    start=True, stop=True),
                  inc="PE",
                  wait=[("ACT", act_proj), ("DVE", setup_dve)] + list(war[pkey]))
                pev = V("PE")
                A("vector", lambda v, j=j, ps=ps: v.tensor_scalar_add(
                    out=spv[:], in0=ps[:], scalar1=bdt1[:, j:j + 1]),
                  inc="DVE", wait=[("PE", pev)] + list(war[("spv",)]))
                dve_v = V("DVE")
                war[pkey] = [("DVE", dve_v)]
                A("scalar", lambda s: s.activation(out=sav[:], in_=spv[:], func=AF.Abs),
                  inc="ACT", wait=[("DVE", dve_v)] + list(war[("sav",)]))
                A("scalar", lambda s: s.activation(out=sav[:], in_=sav[:], func=AF.Exp, scale=-1.0), inc="ACT")
                A("scalar", lambda s: s.activation(out=sav[:], in_=sav[:], func=AF.Ln, bias=1.0), inc="ACT")
                act_sp = V("ACT")
                A("vector", lambda v, j=j: v.scalar_tensor_tensor(
                    out=delta1[:, j * TC:(j + 1) * TC], in0=spv[:], scalar=0.0, in1=sav[:],
                    op0=ALU.max, op1=ALU.add),
                  inc="DVE", wait=[("ACT", act_sp)] + list(war[("delta", j)]))
                war[("spv",)] = [("ACT", act_sp)]
                war[("sav",)] = [("DVE", V("DVE"))]
                A("vector", lambda v, j=j: v.tensor_mul(
                    out=dxn[:, j * TC:(j + 1) * TC], in0=delta1[:, j * TC:(j + 1) * TC],
                    in1=xnr[:, j * TC:(j + 1) * TC]),
                  inc="DVE")
            dve_front = V("DVE")

            A("gpsimd", lambda g: g.memset(yacc[:], 0.0), inc="POOL", wait=war[("yacc",)])
            pool_ymem = V("POOL")

            act_cr = 0
            for sg in range(NSG):
                if sg in (6, 14, 22):
                    yield ("sg", c, (sg + 2) // 8)
                for sl in range(SG):
                    s_ = sg * SG + sl
                    A("tensor", lambda t, s_=s_, sl=sl: t.matmul(
                        out=brp[:, sl * TC:(sl + 1) * TC],
                        lhsT=eyes[:, s_:s_ + 1].broadcast_to([64, 128]),
                        rhs=bs_t[:], start=True, stop=True),
                      inc="PE",
                      wait=([("ACT", act_proj)] + list(war[("brp",)])) if sl == 0 else [])
                pe_b = V("PE")
                A("scalar", lambda s: s.copy(out=brep[:], in_=brp[:]),
                  inc="ACT", wait=[("PE", pe_b)] + list(war[("brep",)]))
                act_b = V("ACT")
                war[("brp",)] = [("ACT", act_b)]

                for j in range(KT):
                    par = 0
                    hpar = j % 2
                    for sl in range(SG):
                        s_ = sg * SG + sl
                        A("scalar", lambda s, j=j, par=par, sl=sl, s_=s_: s.activation(
                            out=dab[par][:, sl * TC:(sl + 1) * TC], in_=delta1[:, j * TC:(j + 1) * TC],
                            func=AF.Exp, scale=a1c[:, j * 64 + s_:j * 64 + s_ + 1]),
                          inc="ACT" if sl == SG - 1 else None,
                          wait=(list(war[("dab", par)]) + [("DVE", dve_front), ("DVE", setup_dve)]) if sl == 0 else [])
                    act_da = V("ACT")
                    A("vector", lambda v, par=par: v.memset(dab[par][:, 0::64], 0.0),
                      inc="DVE", wait=[("ACT", act_da)])
                    dve_rst = V("DVE")
                    A("vector", lambda v, j=j, par=par: v.tensor_mul(
                        out=ub[par][:].rearrange("p (s t) -> p s t", s=SG),
                        in0=dxn[:, j * TC:(j + 1) * TC].unsqueeze(1).broadcast_to([128, SG, TC]),
                        in1=brep[:].rearrange("p (s t) -> p s t", s=SG)),
                      inc="DVE", wait=[("ACT", act_b)])
                    A("vector", lambda v, par=par, hpar=hpar: v.tensor_tensor_scan(
                        out=hb[hpar][:], data0=dab[par][:], data1=ub[par][:],
                        initial=0.0, op0=ALU.mult, op1=ALU.add),
                      inc="DVE",
                      wait=[("DVE", dve_rst)] + list(war[("hb", hpar)]))
                    dve_scan = V("DVE")
                    war[("dab", par)] = [("DVE", dve_scan)]
                    if j == 0:
                        for sl in range(SG):
                            s_ = sg * SG + sl
                            A("tensor", lambda t, s_=s_, sl=sl: t.matmul(
                                out=brp[:, sl * TC:(sl + 1) * TC],
                                lhsT=eyes[:, s_:s_ + 1].broadcast_to([64, 128]),
                                rhs=cs_t[:], start=True, stop=True),
                              inc="PE",
                              wait=([("ACT", act_b)] + list(war[("brp",)])) if sl == 0 else [])
                        pe_cr = V("PE")
                        A("scalar", lambda s: s.copy(out=crep[:], in_=brp[:]),
                          inc="ACT", wait=[("PE", pe_cr)] + list(war[("crep",)]))
                        act_cr = V("ACT")
                        war[("brp",)] = [("ACT", act_cr)]
                    A("vector", lambda v, par=par, hpar=hpar: v.tensor_mul(
                        out=wbf[par][:], in0=hb[hpar][:], in1=crep[:]),
                      inc="DVE", wait=[("ACT", act_cr)] + list(war[("wbf", par)]))
                    dve_w = V("DVE")
                    war[("hb", hpar)] = [("DVE", dve_w)]
                    A("gpsimd", lambda g, par=par: g.tensor_add(
                        out=ysg[par][:], in0=wbf[par][:, 0:TC], in1=wbf[par][:, TC:2 * TC]),
                      inc="POOL", wait=[("DVE", dve_w)])
                    A("gpsimd", lambda g, j=j, par=par: g.tensor_add(
                        out=yacc[:, j * TC:(j + 1) * TC], in0=yacc[:, j * TC:(j + 1) * TC],
                        in1=ysg[par][:]),
                      inc="POOL", wait=[("POOL", pool_ymem)])
                    pool_acc = V("POOL")
                    war[("wbf", par)] = [("POOL", pool_acc)]
                    war[("yacck", j)] = [("POOL", pool_acc)]
                war[("brep",)] = [("DVE", V("DVE"))]
                war[("crep",)] = [("DVE", V("DVE"))]
            war[("dlo",)] = [("PE", V("PE"))]
            war[("cs",)] = [("PE", V("PE"))]
            for j in range(KT):
                war[("delta", j)] = [("ACT", V("ACT")), ("DVE", V("DVE"))]

            obb = ob[c % 2]
            for j in range(KT):
                A("vector", lambda v, j=j, obb=obb: v.scalar_tensor_tensor(
                    out=obb[:, j * TC:(j + 1) * TC], in0=xn1[:, j * TC:(j + 1) * TC],
                    scalar=d1s[:, j:j + 1], in1=yacc[:, j * TC:(j + 1) * TC],
                    op0=ALU.mult, op1=ALU.add),
                  inc="DVE", wait=list(war[("yacck", j)]) + list(war[("ob", c % 2)]))
            dve_ob = V("DVE")
            war[("yacc",)] = [("DVE", dve_ob)]
            obdve[c] = dve_ob
            for j in range(KT):
                war[("xnrk", j)] = [("PE", V("PE")), ("DVE", V("DVE"))]
            for k in range(KT):
                war[("xnk", k)] = [(PF + "O_xn1", V(PF + "O_xn1")), ("ACT", V("ACT"))]
            yield ("mid", c)
            if c > 0:
                emit_back(c - 1)
                yield ("back", c - 1)
        emit_back(NCH1 - 1)
        yield ("back", NCH1 - 1)


def stage_D1(d, x1_by_core):
    dtw = np.asarray(d["L1_attn_dt_proj_w"])
    opT = np.ascontiguousarray(np.asarray(d["L1_attn_out_proj_w"]).T)
    wop1r = np.ascontiguousarray(opT.reshape(8, 128, 8, 128).transpose(2, 1, 0, 3).reshape(8, 128, 1024))
    eye = np.eye(64).astype(BF)
    cstv = np.ascontiguousarray(np.broadcast_to(
        np.array([1e-5, -0.5, 1e-6, 1e-8], np.float32)[None, :], (128, 4)))
    base = {
        "nwc": _col(np.asarray(d["L1_attn_norm_w"])),
        "w_xp": np.ascontiguousarray(np.asarray(d["L1_attn_x_proj_w"]).T),
        "w_dt1": np.ascontiguousarray(dtw.T),
        "b_dt1": _col(np.asarray(d["L1_attn_dt_proj_b"])),
        "a1in": np.ascontiguousarray(np.asarray(d["L1_attn_A_log"]).reshape(8, 128, 64).transpose(1, 0, 2).reshape(128, 512)),
        "d1c": _col(np.asarray(d["L1_attn_D"])),
        "w_op1r": wop1r,
        "eye64": eye,
        "cst": cstv,
    }
    return [dict(base, x1in=np.ascontiguousarray(x1_by_core[c])) for c in range(8)]


def stage_D2(d, x1_by_core, s1_by_core, xn1_by_core):
    w1 = np.asarray(d["L1_ffn_W1"])
    w2 = np.asarray(d["L1_ffn_W2"])
    w1r = np.ascontiguousarray(w1.reshape(8, 8, 128, 32, 128).transpose(0, 3, 2, 1, 4).reshape(8, 32, 128, 1024))
    w2r = np.ascontiguousarray(w2.reshape(8, 32, 128, 8, 128).transpose(0, 3, 2, 1, 4).reshape(8, 8, 128, 4096))
    b1cv = np.ascontiguousarray(np.asarray(d["L1_ffn_b1"]).reshape(8, 32, 128).transpose(2, 0, 1).reshape(128, 256))
    b2cv = np.ascontiguousarray(np.asarray(d["L1_ffn_b2"]).reshape(8, 8, 128).transpose(2, 0, 1).reshape(128, 64))
    wgtv = np.ascontiguousarray(np.asarray(d["L1_ffn_Wg"]).reshape(8, 128, 8).transpose(1, 0, 2).reshape(128, 64))
    bgrv = np.ascontiguousarray(np.broadcast_to(np.asarray(d["L1_ffn_bg"])[None, :], (128, 8)))
    e128 = np.eye(128).astype(BF)
    e8 = np.eye(8).astype(BF)
    maps = []
    for c in range(8):
        m = {
            "xn_in": np.ascontiguousarray(xn1_by_core[c]),
            "x_own": np.ascontiguousarray(x1_by_core[c]),
            "s_in": np.ascontiguousarray(s1_by_core[c]),
            "bgr": bgrv,
            "wgt": wgtv, "w1r": w1r, "w2r": w2r, "b1c": b1cv, "b2c": b2cv,
            "eye128": e128, "eye8": e8,
        }
        maps.append(m)
    return maps


def build_fused(phases="all"):
    nc = bass.Bass("TRN2", target_bir_lowering=False, debug=False)
    E = lambda nm, sh, dt: nc.dram_tensor(nm, sh, dt, kind="ExternalInput").ap()
    xb = E("xb", [1024, 4096], F32)
    x_own = E("x_own", [1024, 2048], F32)
    w_ip = E("w_ip", [1024, 2240], F32)
    w_conv = E("w_conv", [128, 32], F32)
    b_conv = E("b_conv", [128, 8], F32)
    w_dt = E("w_dt", [64, 1024], F32)
    b_dt = E("b_dt", [128, 8], F32)
    a_log = E("a_log", [128, 64], F32)
    d_prm = E("d_prm", [128, 8], F32)
    w_op = E("w_op", [1024, 1024], F32)
    eye64 = E("eye64", [64, 64], BF16)
    cst = E("cst", [128, 4], F32)
    eye128 = E("eye128", [128, 128], BF16)
    eye8 = E("eye8", [8, 8], BF16)
    wgt0 = E("wgt0", [128, 64], F32)
    w1r0 = E("w1r0", [8, 32, 128, 1024], F32)
    w2r0 = E("w2r0", [8, 8, 128, 4096], F32)
    b1c0 = E("b1c0", [128, 256], F32)
    b2c0 = E("b2c0", [128, 64], F32)
    nwc = E("nwc", [128, 8], F32)
    w_xp = E("w_xp", [1024, 192], F32)
    w_dt1 = E("w_dt1", [64, 1024], F32)
    b_dt1 = E("b_dt1", [128, 8], F32)
    a1in = E("a1in", [128, 512], F32)
    d1c = E("d1c", [128, 8], F32)
    w_op1r = E("w_op1r", [8, 128, 1024], F32)
    wgt1 = E("wgt1", [128, 64], F32)
    w1r1 = E("w1r1", [8, 32, 128, 1024], F32)
    w2r1 = E("w2r1", [8, 8, 128, 4096], F32)
    b1c1 = E("b1c1", [128, 256], F32)
    b2c1 = E("b2c1", [128, 64], F32)
    bgr = E("bgr", [128, 8], F32)
    out = nc.dram_tensor("out", [1024, 2048], F32, kind="ExternalOutput").ap()
    cc_in = nc.dram_tensor("cc_in", [2048, 2048], F32).ap()
    cc_out = nc.dram_tensor("cc_out", [1024, 2048], F32).ap()
    xn_dr = nc.dram_tensor("xn_dr", [1024, 4096], BF16).ap()
    x1_dr = nc.dram_tensor("x1_dr", [1024, 2048], F32).ap()
    s1_dr = nc.dram_tensor("s1_dr", [1024, 2048], BF16).ap()
    xn1_dr = nc.dram_tensor("xn1_dr", [1024, 2048], BF16).ap()

    p = Prog()
    with contextlib.ExitStack() as es0:
        ioB = {"pref": "B_", "xb": xb, "w_ip": w_ip, "w_conv": w_conv, "b_conv": b_conv,
               "w_dt": w_dt, "b_dt": b_dt, "a_log": a_log, "d_prm": d_prm, "w_op": w_op,
               "eye64": eye64, "cst": cst, "xn_out": xn_dr,
               "spar_st": lambda dcol, t0: cc_in[
                   (t0 // 2048) * 1024 + dcol * 128:(t0 // 2048) * 1024 + (dcol + 1) * 128,
                   (t0 % 2048):(t0 % 2048) + TC]}
        _emit_B(nc, p, ioB)
        out_after_B = [("B_O_sp", p.v("B_O_sp"))]
        p.barrier()
        p.add("gpsimd", lambda g: g.collective_compute(
            "ReduceScatter", ALU.add, replica_groups=[[0, 1], [2, 3], [4, 5], [6, 7]],
            ins=[cc_in], outs=[cc_out]),
            inc=("CC", 1), wait=out_after_B)
        xnc_dr = nc.dram_tensor("xnc_dr", [1024, 2048], BF16).ap()
        w1bd0 = nc.dram_tensor("w1bd0", [8, 32, 128, 1024], BF16).ap()
        w2bd0 = nc.dram_tensor("w2bd0", [8, 8, 128, 4096], BF16).ap()
        ioC = {"pref": "C_", "lay": 0, "s_mode": "single_f32", "nst": 2,
               "ln_x": x_own, "ln_cst": cst,
               "xn_st": lambda k, t0: xnc_dr[k * 128:(k + 1) * 128, t0:t0 + 512],
               "xn_ld": lambda k, tm0: xnc_dr[k * 128:(k + 1) * 128, tm0:tm0 + TCM],
               "x_ld": lambda j, tm0: x_own[j * 128:(j + 1) * 128, tm0:tm0 + TCM],
               "s_ld": lambda j, tm0: cc_out[j * 128:(j + 1) * 128, tm0:tm0 + TCM],
               "s_wait": [("CC", 1)],
               "out_st": lambda j, tm0: x1_dr[j * 128:(j + 1) * 128, tm0:tm0 + TCM],
               "wgt": wgt0, "w1r": w1r0, "w2r": w2r0, "b1c": b1c0, "b2c": b2c0,
               "w1b_dr": w1bd0, "w2b_dr": w2bd0,
               "eye128": eye128, "eye8": eye8}
        if phases in ("p1", "p1x"):
            ioC["out_st"] = lambda j, tm0: out[j * 128:(j + 1) * 128, tm0:tm0 + TCM]
        if phases == "p1x":
            xn_dbg = nc.dram_tensor("xn_dbg", [1024, 2048], BF16, kind="ExternalInput").ap()
            ioC.pop("ln_x"); ioC.pop("ln_cst"); ioC.pop("xn_st")
            ioC["xn_ld"] = lambda k, tm0: xn_dbg[k * 128:(k + 1) * 128, tm0:tm0 + TCM]
        _emit_tail(nc, p, ioC)
        if phases == "p1x":
            finish(p, es0, nc)
            return nc
        if phases == "p1":
            finish(p, es0, nc)
            return nc
        p.barrier()
        ioD1 = {"pref": "E_", "x1in": x1_dr, "nwc": nwc, "w_xp": w_xp, "w_dt1": w_dt1,
                "b_dt1": b_dt1, "a1in": a1in, "d1c": d1c, "w_op1r": w_op1r,
                "eye64": eye64, "cst": cst, "s1o": s1_dr, "xn1o": xn1_dr}
        w1bd1 = nc.dram_tensor("w1bd1", [8, 32, 128, 1024], BF16).ap()
        w2bd1 = nc.dram_tensor("w2bd1", [8, 8, 128, 4096], BF16).ap()
        ioD2 = {"pref": "F_", "lay": 1, "s_mode": "single_bf16",
                "xn_ld": lambda k, tm0: xn1_dr[k * 128:(k + 1) * 128, tm0:tm0 + TCM],
                "x_ld": lambda j, tm0: x1_dr[j * 128:(j + 1) * 128, tm0:tm0 + TCM],
                "s_ld": lambda j, tm0: s1_dr[j * 128:(j + 1) * 128, tm0:tm0 + TCM],
                "out_st": lambda j, tm0: out[j * 128:(j + 1) * 128, tm0:tm0 + TCM],
                "bgr": bgr, "wgt": wgt1, "w1r": w1r1, "w2r": w2r1, "b1c": b1c1,
                "b2c": b2c1, "w1b_dr": w1bd1, "w2b_dr": w2bd1,
                "eye128": eye128, "eye8": eye8}
        d1g = _emit_D1_gen(nc, p, ioD1)
        d2g = _emit_tail_gen(nc, p, ioD2)
        r = next(d1g)
        assert r == "setup"
        r = next(d2g)
        assert r == "setup"
        r = next(d1g)
        assert r == ("ln", 0), r
        ioD2["xn_wait"] = [("E_O_xn1", p.v("E_O_xn1"))]
        r = next(d2g)
        assert r == ("gating", 0), r
        for w in range(1, 4):
            r = next(d1g)
            assert r == ("sg", 0, w), r
        r = next(d1g)
        assert r == ("mid", 0), r
        for c in range(4):
            if c + 1 < 4:
                r = next(d1g)
                assert r == ("ln", c + 1), r
                for w in range(1, 4):
                    r = next(d1g)
                    assert r == ("sg", c + 1, w), r
                    for e in (2 * (w - 1), 2 * w - 1):
                        r = next(d2g)
                        assert r == ("block", c, e), r
                r = next(d1g)
                assert r == ("mid", c + 1), r
                for e in (6, 7):
                    r = next(d2g)
                    assert r == ("block", c, e), r
            else:
                for e in range(8):
                    r = next(d2g)
                    assert r == ("block", c, e), r
            r = next(d1g)
            assert r == ("back", c), r
            ioD2["s_wait"] = [("E_O_sp0", p.v("E_O_sp0")), ("E_O_sp1", p.v("E_O_sp1"))]
            r = next(d2g)
            assert r == ("combine", c), r
            if c + 1 < 4:
                ioD2["xn_wait"] = [("E_O_xn1", p.v("E_O_xn1"))]
                r = next(d2g)
                assert r == ("gating", c + 1), r
        for r in d2g:
            pass
        for r in d1g:
            pass
        finish(p, es0, nc)
    return nc


def stage_fused(d):
    mapsB = stage_B(d)
    x = np.asarray(d["L0_attn_x"])
    w1 = np.asarray(d["L0_ffn_W1"]); w2 = np.asarray(d["L0_ffn_W2"])
    w1r0 = np.ascontiguousarray(w1.reshape(8, 8, 128, 32, 128).transpose(0, 3, 2, 1, 4).reshape(8, 32, 128, 1024))
    w2r0 = np.ascontiguousarray(w2.reshape(8, 32, 128, 8, 128).transpose(0, 3, 2, 1, 4).reshape(8, 8, 128, 4096))
    b1c0 = np.ascontiguousarray(np.asarray(d["L0_ffn_b1"]).reshape(8, 32, 128).transpose(2, 0, 1).reshape(128, 256))
    b2c0 = np.ascontiguousarray(np.asarray(d["L0_ffn_b2"]).reshape(8, 8, 128).transpose(2, 0, 1).reshape(128, 64))
    wgt0 = np.ascontiguousarray(np.asarray(d["L0_ffn_W_gate"]).reshape(8, 128, 8).transpose(1, 0, 2).reshape(128, 64))
    w11 = np.asarray(d["L1_ffn_W1"]); w21 = np.asarray(d["L1_ffn_W2"])
    w1r1 = np.ascontiguousarray(w11.reshape(8, 8, 128, 32, 128).transpose(0, 3, 2, 1, 4).reshape(8, 32, 128, 1024))
    w2r1 = np.ascontiguousarray(w21.reshape(8, 32, 128, 8, 128).transpose(0, 3, 2, 1, 4).reshape(8, 8, 128, 4096))
    b1c1 = np.ascontiguousarray(np.asarray(d["L1_ffn_b1"]).reshape(8, 32, 128).transpose(2, 0, 1).reshape(128, 256))
    b2c1 = np.ascontiguousarray(np.asarray(d["L1_ffn_b2"]).reshape(8, 8, 128).transpose(2, 0, 1).reshape(128, 64))
    wgt1 = np.ascontiguousarray(np.asarray(d["L1_ffn_Wg"]).reshape(8, 128, 8).transpose(1, 0, 2).reshape(128, 64))
    bgrv = np.ascontiguousarray(np.broadcast_to(np.asarray(d["L1_ffn_bg"])[None, :], (128, 8)))
    dtw1 = np.asarray(d["L1_attn_dt_proj_w"])
    opT1 = np.ascontiguousarray(np.asarray(d["L1_attn_out_proj_w"]).T)
    wop1r = np.ascontiguousarray(opT1.reshape(8, 128, 8, 128).transpose(2, 1, 0, 3).reshape(8, 128, 1024))
    cstv = np.ascontiguousarray(np.broadcast_to(
        np.array([1e-5, -0.5, 1e-6, 1e-8], np.float32)[None, :], (128, 4)))
    e128 = np.eye(128).astype(BF)
    e8 = np.eye(8).astype(BF)
    shared = {
        "eye128": e128, "eye8": e8,
        "wgt0": wgt0, "w1r0": w1r0, "w2r0": w2r0, "b1c0": b1c0, "b2c0": b2c0,
        "nwc": _col(np.asarray(d["L1_attn_norm_w"])),
        "w_xp": np.ascontiguousarray(np.asarray(d["L1_attn_x_proj_w"]).T),
        "w_dt1": np.ascontiguousarray(dtw1.T),
        "b_dt1": _col(np.asarray(d["L1_attn_dt_proj_b"])),
        "a1in": np.ascontiguousarray(np.asarray(d["L1_attn_A_log"]).reshape(8, 128, 64).transpose(1, 0, 2).reshape(128, 512)),
        "d1c": _col(np.asarray(d["L1_attn_D"])),
        "w_op1r": wop1r,
        "wgt1": wgt1, "w1r1": w1r1, "w2r1": w2r1, "b1c1": b1c1, "b2c1": b2c1,
        "bgr": bgrv,
    }
    maps = []
    for c in range(8):
        b, ih = c // 2, c % 2
        m = dict(mapsB[c])
        m["cst"] = cstv
        m["x_own"] = np.ascontiguousarray(x[b].T[:, ih * 2048:(ih + 1) * 2048])
        m.update(shared)
        maps.append(m)
    return maps


def kernel(**inputs):
    d = {k: np.asarray(v) for k, v in inputs.items()}
    nc = build_fused()
    res = run_bass_kernel_spmd(nc, stage_fused(d), core_ids=list(range(8)))
    out = np.empty((4, 4096, 1024), np.float32)
    for c in range(8):
        b, ih = c // 2, c % 2
        out[b, ih * 2048:(ih + 1) * 2048, :] = np.asarray(res.results[c]["out"]).T
    return out


# version 13
# speedup vs baseline: 1.0231x; 1.0231x over previous
import contextlib
from collections import defaultdict
import numpy as np
import ml_dtypes
import concourse.bass as bass
import concourse.mybir as mybir
from concourse.bass_utils import run_bass_kernel_spmd

F32 = mybir.dt.float32
BF16 = mybir.dt.bfloat16
AF = mybir.ActivationFunctionType
ALU = mybir.AluOpType
BF = ml_dtypes.bfloat16


class Prog:

    def __init__(self):
        self.ops = []
        self.cnt = defaultdict(int)

    def add(self, eng, fn, inc=None, wait=()):
        if inc is not None and isinstance(inc, str):
            inc = (inc, 1)
        wait = tuple((s, v) for (s, v) in wait if v > 0)
        self.ops.append((eng, fn, inc, wait))
        if inc:
            self.cnt[inc[0]] += inc[1]
            return self.cnt[inc[0]]
        return None

    def v(self, sem):
        return self.cnt[sem]

    def barrier(self):
        self.ops.append(("BARRIER", dict(self.cnt), None, ()))

    def emit(self, block, sems):
        def mk(eng_name):
            def run(eng):
                last = defaultdict(int)
                pend = None
                for e, fn, inc, waits in self.ops:
                    if e == "BARRIER":
                        pend = fn
                        continue
                    if e != eng_name:
                        continue
                    if pend is not None:
                        waits = list(waits) + [(s, v) for s, v in pend.items()]
                        pend = None
                    for s, val in waits:
                        if val > last[s]:
                            eng.wait_ge(sems[s], val)
                            last[s] = val
                    instr = fn(eng)
                    if inc is not None:
                        assert instr is not None, f"op on {e} returned None but has inc {inc}"
                        instr.then_inc(sems[inc[0]], inc[1])
            return run
        block.sync(mk("sync"))
        block.scalar(mk("scalar"))
        block.vector(mk("vector"))
        block.gpsimd(mk("gpsimd"))
        block.tensor(mk("tensor"))


def finish(p, es, nc):
    names = sorted(set(p.cnt.keys()) | {"PE", "ACT", "DVE", "POOL"})
    final = [(n, p.cnt[n]) for n in names if ("O_" in n or n == "OUT") and p.cnt[n] > 0]
    sems = {n: es.enter_context(nc.semaphore(f"sem{i}")) for i, n in enumerate(names)}
    def fin(g):
        ins = None
        for n, v in final:
            ins = g.wait_ge(sems[n], v)
        return ins
    p.add("gpsimd", fin)
    block = es.enter_context(nc.Block())
    p.emit(block, sems)


TC = 512
NCH = 4096 // TC
KT = 8
IT = 8
SG = 2
NSG = 64 // SG


def build_B():
    nc = bass.Bass("TRN2", target_bir_lowering=False, debug=False)
    io = {}
    for nm, sh, dt in [("xb", [1024, 4096], F32), ("w_ip", [1024, 2240], F32),
                       ("w_conv", [128, 32], F32), ("b_conv", [128, 8], F32),
                       ("w_dt", [64, 1024], F32), ("b_dt", [128, 8], F32),
                       ("a_log", [128, 64], F32), ("d_prm", [128, 8], F32),
                       ("w_op", [1024, 1024], F32), ("eye64", [64, 64], BF16),
                       ("cst", [128, 4], F32)]:
        io[nm] = nc.dram_tensor(nm, sh, dt, kind="ExternalInput").ap()
    s_par = nc.dram_tensor("s_par", [1024, 4096], F32, kind="ExternalOutput").ap()
    io["xn_out"] = nc.dram_tensor("xn_out", [1024, 4096], BF16, kind="ExternalOutput").ap()
    io["spar_st"] = lambda dcol, t0: s_par[dcol * 128:(dcol + 1) * 128, t0:t0 + TC]
    p = Prog()
    with contextlib.ExitStack() as es0:
        _emit_B(nc, p, io)
        finish(p, es0, nc)
    return nc


def _emit_B(nc, p, io):
    for _ in _emit_B_gen(nc, p, io):
        pass


def _emit_B_gen(nc, p, io):
    xb, w_ip, w_conv, b_conv = io["xb"], io["w_ip"], io["w_conv"], io["b_conv"]
    w_dt, b_dt, a_log, d_prm = io["w_dt"], io["b_dt"], io["a_log"], io["d_prm"]
    w_op, eye64, cst, xn_out = io["w_op"], io["eye64"], io["cst"], io["xn_out"]
    spar_st = io["spar_st"]
    PF = io.get("pref", "")
    with contextlib.ExitStack() as es:
        _pf = io.get("pref", "")
        T = lambda name, shape, dt: es.enter_context(nc.sbuf_tensor(_pf + name, shape, dt))
        PT = lambda name, shape, dt: es.enter_context(nc.psum_tensor(_pf + name, shape, dt))
        wip = T("wip", [128, KT * 2240], BF16)
        wop = T("wop", [128, KT * 1024], BF16)
        wdt = T("wdtb", [64, 1024], BF16)
        wcv = T("wcv", [128, 32], F32)
        bcv = T("bcv", [128, 8], F32)
        bdt = T("bdtb", [128, 8], F32)
        dpr = T("dpr", [128, 8], F32)
        arep = T("arep", [128, 64], F32)
        eyes = T("eyes", [64, 64], BF16)
        ones_c = T("ones_c", [128, 1], F32)
        ones_r = T("ones_r", [1, 128], F32)
        wst = T("wst", [128, 1120], F32)
        cstt = T("cstt", [128, 4], F32)
        wdst = T("wdst", [64, 512], F32)
        xfs = [T(f"xf{i}", [128, TC], F32) for i in range(2)]
        xsq = [T(f"xsq{i}", [128, TC], F32) for i in range(2)]
        nm = T("nm", [1, TC], F32)
        m2 = T("m2", [1, TC], F32)
        var = T("var", [1, TC], F32)
        rs = T("rs", [1, TC], F32)
        nmr = T("nmr", [1, TC], F32)
        xnt = [T(f"xnt{i}", [128, TC], F32) for i in range(1)]
        xn = T("xn", [128, KT * TC], BF16)
        xin = T("xin", [128, IT * (TC + 3)], BF16)
        zb = T("zb", [128, IT * TC], BF16)
        bcb = T("bcb", [64, TC], BF16)
        ccb = T("ccb", [64, TC], BF16)
        drw = T("drw", [64, TC], BF16)
        spv = [T(f"spv{i}", [128, TC], F32) for i in range(1)]
        sav = [T(f"sav{i}", [128, TC], F32) for i in range(1)]
        delta = T("delta", [128, IT * TC], BF16)
        xc = T("xc", [128, IT * TC], BF16)
        cacc = [T(f"cacc{i}", [128, TC], F32) for i in range(1)]
        dxc = T("dxc", [128, IT * TC], BF16)
        brep = T("brep", [128, SG * TC], BF16)
        crep = T("crep", [128, SG * TC], BF16)
        dab = [T(f"dab{i}", [128, SG * (TC + 1)], BF16) for i in range(1)]
        ub = [T(f"ub{i}", [128, SG * (TC + 1)], BF16) for i in range(1)]
        hb = [T(f"hb{i}", [128, SG * (TC + 1)], BF16) for i in range(2)]
        wbf = [T(f"wbf{i}", [128, SG * TC], BF16) for i in range(1)]
        ysg = [T(f"ysg{i}", [128, TC], F32) for i in range(1)]
        yacc = T("yacc", [128, IT * TC], F32)
        hstate = T("hstate", [128, IT * 64], BF16)
        szb = [T(f"szb{i}", [128, TC], BF16) for i in range(1)]
        y1f = [T(f"y1f{i}", [128, TC], F32) for i in range(1)]
        ypp = T("ypp", [128, IT * TC], BF16)
        spo = [T(f"spo{i}", [128, TC], F32) for i in range(1)]
        pa = PT("pa", [128, TC], F32)
        pb = PT("pb", [128, TC], F32)
        brp = PT("brp", [128, SG * TC], F32)
        mps = pa[0:1, 0:TC]
        sps = pb[0:1, 0:TC]
        pst = [pa, pb]

        A = p.add
        V = p.v

        A("vector", lambda v: v.memset(ones_c[:], 1.0), inc="DVE")
        A("vector", lambda v: v.memset(ones_r[:], 1.0), inc="DVE")
        for k in range(IT):
            A("vector", lambda v, k=k: v.memset(xin[:, k * (TC + 3):k * (TC + 3) + 3], 0.0), inc="DVE")
        A("sync", lambda s: s.dma_start(out=eyes[:], in_=eye64), inc=(PF + "L_stp", 16))
        A("sync", lambda s: s.dma_start(out=cstt[:], in_=cst), inc=(PF + "L_stp", 16))
        A("sync", lambda s: s.dma_start(out=arep[:], in_=a_log), inc=(PF + "L_stp", 16))
        A("sync", lambda s: s.dma_start(out=bcv[:], in_=b_conv), inc=(PF + "L_stp", 16))
        A("sync", lambda s: s.dma_start(out=bdt[:], in_=b_dt), inc=(PF + "L_stp", 16))
        A("sync", lambda s: s.dma_start(out=dpr[:], in_=d_prm), inc=(PF + "L_stp", 16))
        A("sync", lambda s: s.dma_start(out=wcv[:], in_=w_conv), inc=(PF + "L_stp", 16))
        ld0v = V(PF + "L_stp")
        ld0 = (PF + "L_stp", ld0v)
        A("scalar", lambda s: s.activation(out=arep[:], in_=arep[:], func=AF.Exp),
          inc="ACT", wait=[ld0])
        A("vector", lambda v: v.tensor_scalar_mul(out=arep[:], in0=arep[:], scalar1=-1.0),
          inc="DVE", wait=[("ACT", V("ACT"))])
        for h in range(2):
            w_cast = V("DVE")
            A("sync", lambda s, h=h: s.dma_start(out=wdst[:], in_=w_dt[:, h * 512:(h + 1) * 512]),
              inc=(PF + "L_wdst", 16), wait=[("DVE", w_cast)])
            A("vector", lambda v, h=h: v.tensor_copy(out=wdt[:, h * 512:(h + 1) * 512], in_=wdst[:]),
              inc="DVE", wait=[(PF + "L_wdst", V(PF + "L_wdst"))])
        for k in range(KT):
            for h in range(2):
                c0, c1 = h * 1120, min(2240, (h + 1) * 1120)
                w_cast = V("ACT")
                A("sync", lambda s, k=k, c0=c0, c1=c1: s.dma_start(
                    out=wst[:, :c1 - c0], in_=w_ip[k * 128:(k + 1) * 128, c0:c1]),
                  inc=(PF + "L_wst", 16), wait=[("ACT", w_cast)])
                A("scalar", lambda s, k=k, c0=c0, c1=c1: s.copy(
                    out=wip[:, k * 2240 + c0:k * 2240 + c1], in_=wst[:, :c1 - c0]),
                  inc="ACT", wait=[(PF + "L_wst", V(PF + "L_wst"))])
        for k in range(KT):
            w_cast = V("ACT")
            A("sync", lambda s, k=k: s.dma_start(out=wst[:, :1024], in_=w_op[k * 128:(k + 1) * 128, :]),
              inc=(PF + "L_wst", 16), wait=[("ACT", w_cast)])
            A("scalar", lambda s, k=k: s.copy(out=wop[:, k * 1024:(k + 1) * 1024], in_=wst[:, :1024]),
              inc="ACT", wait=[(PF + "L_wst", V(PF + "L_wst"))])
        A("vector", lambda v: v.memset(dab[0][:], 0.0), inc="DVE")
        A("gpsimd", lambda g: g.memset(hstate[:], 0.0), inc="POOL")
        pool_hst0 = V("POOL")
        setup_act = V("ACT")
        setup_dve = V("DVE")

        war = defaultdict(tuple)
        for j in range(IT):
            war[("hst", j)] = [("POOL", pool_hst0)]
        yield "setup"

        for c in range(NCH):
            t0 = c * TC
            for k in range(KT):
                par = k % 2
                A("sync", lambda s, k=k, par=par, t0=t0: s.dma_start(
                    out=xfs[par][:], in_=xb[k * 128:(k + 1) * 128, t0:t0 + TC]),
                  inc=(PF + f"L_xf{par}", 16), wait=war[("xf", par)])
                ldw = (PF + f"L_xf{par}", V(PF + f"L_xf{par}"))
                A("scalar", lambda s, par=par: s.activation(out=xsq[par][:], in_=xfs[par][:], func=AF.Square),
                  inc="ACT", wait=[ldw] + list(war[("xsq", par)]))
                actv = V("ACT")
                A("tensor", lambda t, k=k, par=par: t.matmul(
                    out=mps[:], lhsT=ones_c[:], rhs=xfs[par][:], start=(k == 0), stop=(k == KT - 1)),
                  wait=([ldw] + list(war[("mps",)]) + list(war[("pa",)]) + list(war[("pb",)])) if k == 0 else [ldw])
                A("tensor", lambda t, k=k, par=par: t.matmul(
                    out=sps[:], lhsT=ones_c[:], rhs=xsq[par][:], start=(k == 0), stop=(k == KT - 1)),
                  inc="PE",
                  wait=[("ACT", actv)])
                war[("xf", par)] = [("PE", V("PE"))]
                war[("xsq", par)] = [("PE", V("PE"))]
            pe_stats = V("PE")
            A("vector", lambda v: v.tensor_scalar_mul(out=nm[:], in0=mps[:], scalar1=-1.0 / 1024.0),
              inc="DVE", wait=[("PE", pe_stats)])
            A("vector", lambda v: v.tensor_mul(out=m2[:], in0=nm[:], in1=nm[:]), inc="DVE")
            A("vector", lambda v: v.scalar_tensor_tensor(
                out=var[:], in0=sps[:], scalar=1.0 / 1024.0, in1=m2[:], op0=ALU.mult, op1=ALU.subtract),
              inc="DVE", wait=[("PE", pe_stats)])
            dve_var = V("DVE")
            war[("mps",)] = [("DVE", dve_var)]
            A("scalar", lambda s: s.activation(out=rs[:], in_=var[:], func=AF.Ln, bias=cstt[0:1, 0:1]),
              inc="ACT", wait=[("DVE", dve_var)])
            A("scalar", lambda s: s.activation(out=rs[:], in_=rs[:], func=AF.Exp, scale=cstt[0:1, 1:2]), inc="ACT")
            act_rs = V("ACT")
            A("vector", lambda v: v.tensor_mul(out=nmr[:], in0=nm[:], in1=rs[:]),
              inc="DVE", wait=[("ACT", act_rs)])
            dve_nmr = V("DVE")
            A("tensor", lambda t: t.matmul(out=pa[:], lhsT=ones_r[:], rhs=rs[:], start=True, stop=True),
              wait=[("ACT", act_rs)] + list(war[("pa",)]))
            A("tensor", lambda t: t.matmul(out=pb[:], lhsT=ones_r[:], rhs=nmr[:], start=True, stop=True),
              inc="PE", wait=[("DVE", dve_nmr)] + list(war[("pb",)]))
            pe_bc = V("PE")
            for k in range(KT):
                par = k % 2
                A("sync", lambda s, k=k, par=par, t0=t0: s.dma_start(
                    out=xfs[par][:], in_=xb[k * 128:(k + 1) * 128, t0:t0 + TC]),
                  inc=(PF + f"L_xf{par}", 16), wait=war[("xf", par)])
                ldw = (PF + f"L_xf{par}", V(PF + f"L_xf{par}"))
                A("vector", lambda v, par=par: v.tensor_mul(out=xnt[0][:], in0=xfs[par][:], in1=pa[:]),
                  inc="DVE", wait=[ldw, ("PE", pe_bc)])
                A("vector", lambda v, k=k, par=par: v.tensor_add(
                    out=xn[:, k * TC:(k + 1) * TC], in0=xnt[0][:], in1=pb[:]),
                  inc="DVE", wait=war[("xnk", k)])
                dv = V("DVE")
                war[("xf", par)] = [("DVE", dv)]
                A("gpsimd", lambda g, k=k, t0=t0: g.dma_start(
                    out=xn_out[k * 128:(k + 1) * 128, t0:t0 + TC], in_=xn[:, k * TC:(k + 1) * TC]),
                  inc=(PF + "O_xn", 16), wait=[("DVE", dv)])
            oxn_end = V(PF + "O_xn")
            for k in range(KT):
                war[("xnk", k)] = [(PF + "O_xn", oxn_end)]
            dve_xn = V("DVE")
            war[("pa",)] = [("DVE", dve_xn)]
            war[("pb",)] = [("DVE", dve_xn)]

            groups = [("xin", j) for j in range(IT)] + [("z", j) for j in range(IT)] + [("bc", 0), ("draw", 0)]
            for gi, (gname, j) in enumerate(groups):
                ps = pst[gi % 2]
                pkey = ("pa",) if gi % 2 == 0 else ("pb",)
                col0 = {"xin": j * 128, "z": 1024 + j * 128, "bc": 2048, "draw": 2176}[gname]
                npart = 64 if gname == "draw" else 128
                for k in range(KT):
                    A("tensor", lambda t, k=k, col0=col0, ps=ps, npart=npart: t.matmul(
                        out=ps[:npart, :], lhsT=wip[:, k * 2240 + col0:k * 2240 + col0 + npart],
                        rhs=xn[:, k * TC:(k + 1) * TC], start=(k == 0), stop=(k == KT - 1)),
                      inc="PE",
                      wait=(list(war[pkey]) + [("ACT", setup_act), ("DVE", dve_xn)]) if k == 0 else [])
                pev = V("PE")
                if gname == "xin":
                    A("scalar", lambda s, j=j, ps=ps: s.copy(
                        out=xin[:, j * (TC + 3) + 3:(j + 1) * (TC + 3)], in_=ps[:]),
                      inc="ACT", wait=[("PE", pev)] + list(war[("xin", j)]))
                elif gname == "z":
                    A("scalar", lambda s, j=j, ps=ps: s.copy(out=zb[:, j * TC:(j + 1) * TC], in_=ps[:]),
                      inc="ACT", wait=[("PE", pev)] + list(war[("z", j)]))
                elif gname == "bc":
                    A("scalar", lambda s, ps=ps: s.copy(out=bcb[:], in_=ps[0:64, :]),
                      inc="ACT", wait=[("PE", pev)] + list(war[("bcb",)]))
                    A("scalar", lambda s, ps=ps: s.copy(out=ccb[:], in_=ps[64:128, :]),
                      inc="ACT")
                else:
                    A("scalar", lambda s, ps=ps: s.copy(out=drw[:], in_=ps[:64, :]),
                      inc="ACT", wait=[("PE", pev)] + list(war[("drw",)]))
                war[pkey] = [("ACT", V("ACT"))]
            act_front = V("ACT")

            for j in range(IT):
                par = 0
                ps = pst[j % 2]
                pkey = ("pa",) if j % 2 == 0 else ("pb",)
                A("tensor", lambda t, j=j, ps=ps: t.matmul(
                    out=ps[:], lhsT=wdt[:, j * 128:(j + 1) * 128], rhs=drw[:],
                    start=True, stop=True),
                  inc="PE",
                  wait=[("ACT", act_front), ("DVE", setup_dve)] + list(war[pkey]))
                pev = V("PE")
                A("vector", lambda v, j=j, par=par, ps=ps: v.tensor_scalar_add(
                    out=spv[par][:], in0=ps[:], scalar1=bdt[:, j:j + 1]),
                  inc="DVE", wait=[("PE", pev)] + list(war[("spv", par)]))
                dve_v = V("DVE")
                war[pkey] = [("DVE", dve_v)]
                A("scalar", lambda s, par=par: s.activation(out=sav[par][:], in_=spv[par][:], func=AF.Abs),
                  inc="ACT", wait=[("DVE", dve_v)] + list(war[("sav", par)]))
                A("scalar", lambda s, par=par: s.activation(out=sav[par][:], in_=sav[par][:], func=AF.Exp, scale=-1.0),
                  inc="ACT")
                A("scalar", lambda s, par=par: s.activation(out=sav[par][:], in_=sav[par][:], func=AF.Ln, bias=1.0),
                  inc="ACT")
                act_sp = V("ACT")
                A("vector", lambda v, j=j, par=par: v.scalar_tensor_tensor(
                    out=delta[:, j * TC:(j + 1) * TC], in0=spv[par][:], scalar=0.0, in1=sav[par][:],
                    op0=ALU.max, op1=ALU.add),
                  inc="DVE", wait=[("ACT", act_sp)] + list(war[("delta", j)]))
                war[("spv", par)] = [("ACT", act_sp)]
                war[("sav", par)] = [("DVE", V("DVE"))]

            for j in range(IT):
                par = 0
                x0 = j * (TC + 3)
                A("vector", lambda v, j=j, par=par, x0=x0: v.tensor_scalar_mul(
                    out=cacc[par][:], in0=xin[:, x0:x0 + TC], scalar1=wcv[:, j * 4:j * 4 + 1]),
                  inc="DVE", wait=[("ACT", act_front)] + list(war[("cacc", par)]))
                for kk in range(1, 4):
                    A("vector", lambda v, par=par, x0=x0, kk=kk, j=j: v.scalar_tensor_tensor(
                        out=cacc[par][:], in0=xin[:, x0 + kk:x0 + kk + TC],
                        scalar=wcv[:, j * 4 + kk:j * 4 + kk + 1], in1=cacc[par][:],
                        op0=ALU.mult, op1=ALU.add),
                      inc="DVE")
                dve_c = V("DVE")
                A("scalar", lambda s, j=j, par=par: s.activation(
                    out=xc[:, j * TC:(j + 1) * TC], in_=cacc[par][:], func=AF.Silu,
                    bias=bcv[:, j:j + 1]),
                  inc="ACT", wait=[("DVE", dve_c)] + list(war[("xck", j)]))
                act_xc = V("ACT")
                war[("cacc", par)] = [("ACT", act_xc)]
                A("vector", lambda v, x0=x0: v.tensor_copy(
                    out=xin[:, x0:x0 + 3], in_=xin[:, x0 + TC:x0 + TC + 3]),
                  inc="DVE")
                war[("xin", j)] = [("DVE", V("DVE"))]
                A("vector", lambda v, j=j: v.tensor_mul(
                    out=dxc[:, j * TC:(j + 1) * TC], in0=delta[:, j * TC:(j + 1) * TC],
                    in1=xc[:, j * TC:(j + 1) * TC]),
                  inc="DVE", wait=[("ACT", act_xc)])
            dve_front = V("DVE")

            A("gpsimd", lambda g: g.memset(yacc[:], 0.0), inc="POOL", wait=war[("yacc",)])
            pool_ymem = V("POOL")

            act_cr = 0
            for sg in range(NSG):
                for sl in range(SG):
                    s_ = sg * SG + sl
                    A("tensor", lambda t, s_=s_, sl=sl: t.matmul(
                        out=brp[:, sl * TC:(sl + 1) * TC],
                        lhsT=eyes[:, s_:s_ + 1].broadcast_to([64, 128]),
                        rhs=bcb[:], start=True, stop=True),
                      inc="PE",
                      wait=([("ACT", act_front)] + list(war[("brp",)])) if sl == 0 else [])
                pe_b = V("PE")
                A("scalar", lambda s: s.copy(out=brep[:], in_=brp[:]),
                  inc="ACT", wait=[("PE", pe_b)] + list(war[("brep",)]))
                act_b = V("ACT")
                war[("brp",)] = [("ACT", act_b)]

                for j in range(IT):
                    par = 0
                    hpar = j % 2
                    SEG = TC + 1
                    for sl in range(SG):
                        s_ = sg * SG + sl
                        A("scalar", lambda s, j=j, par=par, sl=sl, s_=s_: s.activation(
                            out=dab[par][:, sl * SEG + 1:(sl + 1) * SEG], in_=delta[:, j * TC:(j + 1) * TC],
                            func=AF.Exp, scale=arep[:, s_:s_ + 1]),
                          inc="ACT" if sl == SG - 1 else None,
                          wait=(list(war[("dab", par)]) + [("DVE", dve_front), ("DVE", setup_dve)]) if sl == 0 else [])
                    act_da = V("ACT")
                    A("vector", lambda v, j=j, par=par: v.tensor_mul(
                        out=ub[par][:].rearrange("p (s t) -> p s t", s=SG)[:, :, 1:],
                        in0=dxc[:, j * TC:(j + 1) * TC].unsqueeze(1).broadcast_to([128, SG, TC]),
                        in1=brep[:].rearrange("p (s t) -> p s t", s=SG)),
                      inc="DVE", wait=[("ACT", act_b)])
                    A("scalar", lambda s, j=j, par=par, sg=sg: s.copy(
                        out=ub[par][:, 0::SEG],
                        in_=hstate[:, j * 64 + sg * SG:j * 64 + (sg + 1) * SG]),
                      inc="ACT", wait=list(war[("hst", j)]) + [("DVE", V("DVE"))])
                    act_inj = V("ACT")
                    A("vector", lambda v, par=par, hpar=hpar: v.tensor_tensor_scan(
                        out=hb[hpar][:], data0=dab[par][:], data1=ub[par][:],
                        initial=0.0, op0=ALU.mult, op1=ALU.add),
                      inc="DVE",
                      wait=[("ACT", act_inj)] + list(war[("hb", hpar)]))
                    dve_scan = V("DVE")
                    war[("dab", par)] = [("DVE", dve_scan)]
                    A("gpsimd", lambda g, j=j, hpar=hpar, sg=sg: g.tensor_copy(
                        out=hstate[:, j * 64 + sg * SG:j * 64 + (sg + 1) * SG],
                        in_=hb[hpar][:, TC::SEG]),
                      inc="POOL", wait=[("DVE", dve_scan)])
                    war[("hst", j)] = [("POOL", V("POOL"))]
                    if j == 0:
                        for sl in range(SG):
                            s_ = sg * SG + sl
                            A("tensor", lambda t, s_=s_, sl=sl: t.matmul(
                                out=brp[:, sl * TC:(sl + 1) * TC],
                                lhsT=eyes[:, s_:s_ + 1].broadcast_to([64, 128]),
                                rhs=ccb[:], start=True, stop=True),
                              inc="PE",
                              wait=([("ACT", act_b)] + list(war[("brp",)])) if sl == 0 else [])
                        pe_cr = V("PE")
                        A("scalar", lambda s: s.copy(out=crep[:], in_=brp[:]),
                          inc="ACT", wait=[("PE", pe_cr)] + list(war[("crep",)]))
                        act_cr = V("ACT")
                        war[("brp",)] = [("ACT", act_cr)]
                    A("vector", lambda v, par=par, hpar=hpar: v.tensor_mul(
                        out=wbf[par][:].rearrange("p (s t) -> p s t", s=SG),
                        in0=hb[hpar][:].rearrange("p (s t) -> p s t", s=SG)[:, :, 1:],
                        in1=crep[:].rearrange("p (s t) -> p s t", s=SG)),
                      inc="DVE", wait=[("ACT", act_cr)] + list(war[("wbf", par)]))
                    dve_w = V("DVE")
                    war[("hb", hpar)] = [("POOL", p.cnt["POOL"]), ("DVE", dve_w)]
                    A("gpsimd", lambda g, par=par: g.tensor_add(
                        out=ysg[par][:], in0=wbf[par][:, 0:TC], in1=wbf[par][:, TC:2 * TC]),
                      inc="POOL", wait=[("DVE", dve_w)])
                    A("gpsimd", lambda g, j=j, par=par: g.tensor_add(
                        out=yacc[:, j * TC:(j + 1) * TC], in0=yacc[:, j * TC:(j + 1) * TC],
                        in1=ysg[par][:]),
                      inc="POOL", wait=[("POOL", pool_ymem)])
                    pool_acc = V("POOL")
                    war[("wbf", par)] = [("POOL", pool_acc)]
                    war[("yacck", j)] = [("POOL", pool_acc)]
                war[("brep",)] = [("DVE", V("DVE"))]
                war[("crep",)] = [("DVE", V("DVE"))]
            war[("bcb",)] = [("PE", V("PE"))]
            war[("drw",)] = [("PE", V("PE"))]
            for j in range(IT):
                war[("delta", j)] = [("ACT", V("ACT")), ("DVE", V("DVE"))]

            for j in range(IT):
                par = 0
                A("scalar", lambda s, j=j, par=par: s.activation(
                    out=szb[par][:], in_=zb[:, j * TC:(j + 1) * TC], func=AF.Silu),
                  inc="ACT", wait=list(war[("szb", par)]))
                act_sz = V("ACT")
                A("vector", lambda v, j=j, par=par: v.scalar_tensor_tensor(
                    out=y1f[par][:], in0=xc[:, j * TC:(j + 1) * TC], scalar=dpr[:, j:j + 1],
                    in1=yacc[:, j * TC:(j + 1) * TC], op0=ALU.mult, op1=ALU.add),
                  inc="DVE", wait=list(war[("yacck", j)]) + list(war[("y1", par)]))
                A("vector", lambda v, j=j, par=par: v.tensor_mul(
                    out=ypp[:, j * TC:(j + 1) * TC], in0=y1f[par][:], in1=szb[par][:]),
                  inc="DVE", wait=[("ACT", act_sz)] + list(war[("yppk", j)]))
                dve_ypp = V("DVE")
                war[("y1", par)] = [("DVE", dve_ypp)]
                war[("szb", par)] = [("DVE", dve_ypp)]
                war[("z", j)] = [("ACT", act_sz)]
                war[("xck", j)] = [("DVE", dve_ypp)]
                war[("yacc",)] = [("DVE", dve_ypp)]
            dve_yppall = V("DVE")
            for dcol in range(8):
                ps = pst[dcol % 2]
                pkey = ("pa",) if dcol % 2 == 0 else ("pb",)
                par = 0
                for k in range(IT):
                    A("tensor", lambda t, dcol=dcol, k=k, ps=ps: t.matmul(
                        out=ps[:], lhsT=wop[:, k * 1024 + dcol * 128:k * 1024 + dcol * 128 + 128],
                        rhs=ypp[:, k * TC:(k + 1) * TC], start=(k == 0), stop=(k == IT - 1)),
                      inc="PE",
                      wait=([("DVE", dve_yppall), ("ACT", setup_act)] + list(war[pkey])) if k == 0 else [])
                pev = V("PE")
                A("scalar", lambda s, par=par, ps=ps: s.copy(out=spo[par][:], in_=ps[:]),
                  inc="ACT", wait=[("PE", pev)] + list(war[("spo", par)]))
                actv = V("ACT")
                war[pkey] = [("ACT", actv)]
                A("gpsimd", lambda g, dcol=dcol, par=par, t0=t0: g.dma_start(
                    out=spar_st(dcol, t0), in_=spo[par][:]),
                  inc=(PF + "O_sp", 16), wait=[("ACT", actv)])
                war[("spo", par)] = [(PF + "O_sp", V(PF + "O_sp"))]
            pe_end = V("PE")
            for j in range(IT):
                war[("yppk", j)] = [("PE", pe_end)]
            for k in range(KT):
                war[("xnk", k)] = list(war[("xnk", k)]) + [("PE", pe_end)]
            war[("spo", 0)] = list(war[("spo", 0)])
            yield ("chunk", c)


def _col(a):
    return np.ascontiguousarray(np.asarray(a).reshape(8, 128).T)


def stage_B(d):
    x = np.asarray(d["L0_attn_x"])
    wipT = np.ascontiguousarray(np.asarray(d["L0_attn_in_proj_w"]).T)
    opT = np.ascontiguousarray(np.asarray(d["L0_attn_out_proj_w"]).T)
    dtw = np.asarray(d["L0_attn_dt_proj_w"])
    eye = np.eye(64).astype(BF)
    maps = []
    for c in range(8):
        b, ih = c // 2, c % 2
        sl = slice(ih * 1024, (ih + 1) * 1024)
        m = {
            "xb": np.ascontiguousarray(x[b].T),
            "w_ip": np.ascontiguousarray(np.concatenate(
                [wipT[:, ih * 1024:(ih + 1) * 1024],
                 wipT[:, 2048 + ih * 1024:2048 + (ih + 1) * 1024],
                 wipT[:, 4096:4288]], axis=1)),
            "w_conv": np.ascontiguousarray(
                np.asarray(d["L0_attn_conv_w"])[sl, 0, :].reshape(8, 128, 4).transpose(1, 0, 2).reshape(128, 32)),
            "b_conv": _col(d["L0_attn_conv_b"][sl]),
            "w_dt": np.ascontiguousarray(dtw[sl].T),
            "b_dt": _col(d["L0_attn_dt_proj_b"][sl]),
            "a_log": np.ascontiguousarray(np.broadcast_to(
                np.asarray(d["L0_attn_A_log"])[None, :], (128, 64))),
            "d_prm": _col(d["L0_attn_D_param"][sl]),
            "w_op": np.ascontiguousarray(opT[sl, :]),
            "eye64": eye,
            "cst": np.ascontiguousarray(np.broadcast_to(
                np.array([1e-5, -0.5, 0.0, 0.0], np.float32)[None, :], (128, 4))),
        }
        maps.append(m)
    return maps


TCM = 512
NTCM = 2048 // TCM
NTT = 16


def build_C():
    return build_tail(0)


def build_D2():
    return build_tail(1)


def build_tail(lay=0):
    nc = bass.Bass("TRN2", target_bir_lowering=False, debug=False)
    io = {"lay": lay, "s_mode": "dual" if lay == 0 else "single_bf16"}
    xn_in = nc.dram_tensor("xn_in", [1024, 2048], BF16, kind="ExternalInput").ap()
    x_own = nc.dram_tensor("x_own", [1024, 2048], F32, kind="ExternalInput").ap()
    if lay == 0:
        s_a = nc.dram_tensor("s_a", [1024, 2048], F32, kind="ExternalInput").ap()
        s_b = nc.dram_tensor("s_b", [1024, 2048], F32, kind="ExternalInput").ap()
        io["sa_ld"] = lambda j, tm0: s_a[j * 128:(j + 1) * 128, tm0:tm0 + TCM]
        io["sb_ld"] = lambda j, tm0: s_b[j * 128:(j + 1) * 128, tm0:tm0 + TCM]
    else:
        s_in = nc.dram_tensor("s_in", [1024, 2048], BF16, kind="ExternalInput").ap()
        io["bgr"] = nc.dram_tensor("bgr", [128, 8], F32, kind="ExternalInput").ap()
        io["s_ld"] = lambda j, tm0: s_in[j * 128:(j + 1) * 128, tm0:tm0 + TCM]
    for nm, sh, dt in [("wgt", [128, 64], F32), ("w1r", [8, 32, 128, 1024], F32),
                       ("w2r", [8, 8, 128, 4096], F32), ("b1c", [128, 256], F32),
                       ("b2c", [128, 64], F32), ("eye128", [128, 128], BF16),
                       ("eye8", [8, 8], BF16)]:
        io[nm] = nc.dram_tensor(nm, sh, dt, kind="ExternalInput").ap()
    x1 = nc.dram_tensor("x1", [1024, 2048], F32, kind="ExternalOutput").ap()
    io["xn_ld"] = lambda k, tm0: xn_in[k * 128:(k + 1) * 128, tm0:tm0 + TCM]
    io["x_ld"] = lambda j, tm0: x_own[j * 128:(j + 1) * 128, tm0:tm0 + TCM]
    io["out_st"] = lambda j, tm0: x1[j * 128:(j + 1) * 128, tm0:tm0 + TCM]
    io["w1b_dr"] = nc.dram_tensor("w1b_dr", [8, 32, 128, 1024], BF16).ap()
    io["w2b_dr"] = nc.dram_tensor("w2b_dr", [8, 8, 128, 4096], BF16).ap()
    p = Prog()
    with contextlib.ExitStack() as es0:
        _emit_tail(nc, p, io)
        finish(p, es0, nc)
    return nc


def _emit_tail(nc, p, io):
    for _ in _emit_tail_gen(nc, p, io):
        pass


def _emit_tail_gen(nc, p, io):
    lay = io["lay"]
    s_mode = io["s_mode"]
    wgt, b1c, b2c = io["wgt"], io["b1c"], io["b2c"]
    w1r, w2r = io["w1r"], io["w2r"]
    w1bd, w2bd = io["w1b_dr"], io["w2b_dr"]
    eye128, eye8 = io["eye128"], io["eye8"]
    xn_ld, x_ld, out_st = io.get("xn_ld"), io["x_ld"], io["out_st"]
    PF = io.get("pref", "")
    with contextlib.ExitStack() as es:
        _pf = io.get("pref", "")
        T = lambda name, shape, dt: es.enter_context(nc.sbuf_tensor(_pf + name, shape, dt))
        PT = lambda name, shape, dt: es.enter_context(nc.psum_tensor(_pf + name, shape, dt))
        wgb = T("wgb", [128, 64], BF16)
        wgf = T("wgf", [128, 64], F32)
        b1s = T("b1s", [128, 256], F32)
        b2s = T("b2s", [128, 64], F32)
        id128 = T("id128", [128, 128], BF16)
        id8 = T("id8", [8, 8], BF16)
        xns = T("xns", [128, 8 * 512], BF16)
        gsb = T("gsb", [128, 32], F32)
        mm1a = T("mm1a", [128, 4], F32)
        mm2a = T("mm2a", [128, 4], F32)
        mk1a = T("mk1a", [128, 32], F32)
        gg2a = T("gg2a", [128, 32], F32)
        sela = T("sela", [128, 32], F32)
        ee1a = T("ee1a", [128, 32], F32)
        eesa = T("eesa", [128, 32], F32)
        ssma = T("ssma", [128, 4], F32)
        rcpa = T("rcpa", [128, 4], F32)
        msk_tm = T("msk_tm", [128, 128], BF16)
        mask_fm = T("mask_fm", [8, 2048], BF16)
        mrep = T("mrep", [128, 512], BF16)
        if lay == 1:
            bgs = T("bgs", [128, 8], F32)
            sss = T("sss", [128, 512], BF16)
        w1st = [T(f"w1st{i}", [128, 1024], F32) for i in range(io.get("nst", 1))]
        w1b = [T(f"w1b{i}", [128, 1024], BF16) for i in range(2)]
        w2st = [T(f"w2st{i}", [128, 2048], F32) for i in range(1)]
        w2b = [T(f"w2b{i}", [128, 2048], BF16) for i in range(2)]
        hsb = T("hsb", [128, 32 * 512], BF16)
        ytmp = [T(f"ytmp{i}", [128, 512], F32) for i in range(2)]
        tmp2 = T("tmp2", [128, 512], BF16)
        yacc = T("yaccC", [128, 8 * 512], F32 if lay == 0 else BF16)
        gbf = [T(f"gbf{i}", [128, 512], BF16) for i in range(1)]
        xsl = [T(f"xsl{i}", [128, 512], F32) for i in range(1)]
        sas = [T(f"sas{i}", [128, 512], F32) for i in range(1)]
        sbs = [T(f"sbs{i}", [128, 512], F32) for i in range(1)]
        x1s = [T(f"x1s{i}", [128, 512], F32) for i in range(1)]
        ptr = PT("ptr", [8, 128], BF16)
        pc = [PT(f"pc{i}", [128, 512], F32) for i in range(2)]
        pm = PT("pm", [128, 512], F32)
        gpsA = pm[:, 0:32]

        A = p.add
        V = p.v
        war = defaultdict(tuple)

        A("sync", lambda s: s.dma_start(out=id128[:], in_=eye128), inc=(PF + "L_stp", 16))
        A("sync", lambda s: s.dma_start(out=id8[:], in_=eye8), inc=(PF + "L_stp", 16))
        A("sync", lambda s: s.dma_start(out=wgf[:], in_=wgt), inc=(PF + "L_stp", 16))
        A("sync", lambda s: s.dma_start(out=b1s[:], in_=b1c), inc=(PF + "L_stp", 16))
        A("sync", lambda s: s.dma_start(out=b2s[:], in_=b2c), inc=(PF + "L_stp", 16))
        if lay == 1:
            A("sync", lambda s: s.dma_start(out=bgs[:], in_=io["bgr"]), inc=(PF + "L_stp", 16))
        ld0 = (PF + "L_stp", V(PF + "L_stp"))
        A("vector", lambda v: v.tensor_copy(out=wgb[:], in_=wgf[:]), inc="DVE", wait=[ld0])
        setup_dve = V("DVE")

        ln_out = 0
        if io.get("ln_x") is not None:
            lnx = io["ln_x"]
            lnc = io["ln_cst"]
            xn_st = io["xn_st"]
            xfs2 = [ytmp[0][:], ytmp[1][:]]
            xsq2 = [xsl[0][:], sas[0][:]]
            nmT = yacc[0:1, 0:512]
            m2T = yacc[0:1, 512:1024]
            varT = yacc[0:1, 1024:1536]
            rsT = yacc[0:1, 1536:2048]
            nmrT = yacc[0:1, 2048:2560]
            xntT = x1s[0][:]
            xobT = gbf[0][:]
            onescT = T("onescT", [128, 1], F32)
            onesrT = T("onesrT", [1, 128], F32)
            csttT = T("csttT", [128, 4], F32)
            mpsT = pc[0][0:1, 0:512]
            spsT = pc[1][0:1, 0:512]
            A("vector", lambda v: v.memset(onescT[:], 1.0), inc="DVE")
            A("vector", lambda v: v.memset(onesrT[:], 1.0), inc="DVE")
            A("sync", lambda s: s.dma_start(out=csttT[:], in_=lnc), inc=(PF + "L_stp", 16))
            setup_dve = V("DVE")
            ldc = (PF + "L_stp", V(PF + "L_stp"))
            for c in range(4):
                t0 = c * 512
                for k in range(8):
                    par = k % 2
                    A("sync", lambda s, k=k, par=par, t0=t0: s.dma_start(
                        out=xfs2[par], in_=lnx[k * 128:(k + 1) * 128, t0:t0 + 512]),
                      inc=(PF + f"L_lnx{par}", 16), wait=war[("xfT", par)])
                    ldw = (PF + f"L_lnx{par}", V(PF + f"L_lnx{par}"))
                    A("scalar", lambda s, par=par: s.activation(out=xsq2[par], in_=xfs2[par], func=AF.Square),
                      inc="ACT", wait=[ldw] + list(war[("xsqT", par)]))
                    actv = V("ACT")
                    A("tensor", lambda t, k=k, par=par: t.matmul(
                        out=mpsT, lhsT=onescT[:], rhs=xfs2[par], start=(k == 0), stop=(k == 7)),
                      inc="PE",
                      wait=([ldw, ("DVE", setup_dve)] + list(war[("mpsT",)])) if k == 0 else [ldw])
                    A("tensor", lambda t, k=k, par=par: t.matmul(
                        out=spsT, lhsT=onescT[:], rhs=xsq2[par], start=(k == 0), stop=(k == 7)),
                      inc="PE", wait=[("ACT", actv)])
                    war[("xfT", par)] = [("PE", V("PE"))]
                    war[("xsqT", par)] = [("PE", V("PE"))]
                pe_stats = V("PE")
                A("vector", lambda v: v.tensor_scalar_mul(out=nmT, in0=mpsT, scalar1=-1.0 / 1024.0),
                  inc="DVE", wait=[("PE", pe_stats)])
                A("vector", lambda v: v.tensor_mul(out=m2T, in0=nmT, in1=nmT), inc="DVE")
                A("vector", lambda v: v.scalar_tensor_tensor(
                    out=varT, in0=spsT, scalar=1.0 / 1024.0, in1=m2T, op0=ALU.mult, op1=ALU.subtract),
                  inc="DVE", wait=[("PE", pe_stats)])
                dve_var = V("DVE")
                war[("mpsT",)] = [("DVE", dve_var)]
                A("scalar", lambda s: s.activation(out=rsT, in_=varT, func=AF.Ln, bias=csttT[0:1, 0:1]),
                  inc="ACT", wait=[("DVE", dve_var), ldc])
                A("scalar", lambda s: s.activation(out=rsT, in_=rsT, func=AF.Exp, scale=csttT[0:1, 1:2]),
                  inc="ACT")
                act_rs = V("ACT")
                A("vector", lambda v: v.tensor_mul(out=nmrT, in0=nmT, in1=rsT),
                  inc="DVE", wait=[("ACT", act_rs)])
                dve_nmr = V("DVE")
                A("tensor", lambda t: t.matmul(out=pm[:], lhsT=onesrT[:], rhs=rsT, start=True, stop=True),
                  inc="PE", wait=[("ACT", act_rs)] + list(war[("pmLN",)]))
                pe_b1 = V("PE")
                A("tensor", lambda t: t.matmul(out=pc[0][:], lhsT=onesrT[:], rhs=nmrT, start=True, stop=True),
                  inc="PE", wait=[("DVE", dve_nmr), ("DVE", V("DVE"))])
                pe_bc = V("PE")
                for k in range(8):
                    par = k % 2
                    A("sync", lambda s, k=k, par=par, t0=t0: s.dma_start(
                        out=xfs2[par], in_=lnx[k * 128:(k + 1) * 128, t0:t0 + 512]),
                      inc=(PF + f"L_lnx{par}", 16), wait=war[("xfT", par)])
                    ldw = (PF + f"L_lnx{par}", V(PF + f"L_lnx{par}"))
                    A("vector", lambda v, par=par: v.tensor_mul(out=xntT, in0=xfs2[par], in1=pm[:]),
                      inc="DVE", wait=[ldw, ("PE", pe_bc)])
                    A("vector", lambda v, par=par: v.tensor_add(out=xobT, in0=xntT, in1=pc[0][:]),
                      inc="DVE", wait=war[("xobT",)])
                    dv = V("DVE")
                    war[("xfT", par)] = [("DVE", dv)]
                    A("gpsimd", lambda g, k=k, t0=t0: g.dma_start(out=xn_st(k, t0), in_=xobT),
                      inc=(PF + "O_gbf", 16), wait=[("DVE", dv)])
                    war[("xobT",)] = [(PF + "O_gbf", V(PF + "O_gbf"))]
                war[("pmLN",)] = [("DVE", V("DVE"))]
                war[("pm",)] = [("DVE", V("DVE"))]
                war[("pc", 0)] = [("DVE", V("DVE"))]
                war[("mpsT",)] = [("DVE", V("DVE"))]
            ln_out = V(PF + "O_gbf")
            setup_dve = V("DVE")
            war[("ytmp", 0)] = [("PE", V("PE")), ("DVE", setup_dve), ("ACT", V("ACT"))]
            war[("ytmp", 1)] = list(war[("ytmp", 0)])
            war[("xsl", 0)] = list(war[("ytmp", 0)])
            war[("sas", 0)] = list(war[("ytmp", 0)])
            war[("x1s", 0)] = [("DVE", setup_dve)]
            war[("gbf", 0)] = [(PF + "O_gbf", ln_out)]
            for j in range(8):
                war[("yacck", j)] = [("PE", V("PE"))]
        yield "setup"

        g3 = lambda t_: t_.rearrange("p (t e) -> p t e", e=8)
        bc3 = lambda t_: t_.unsqueeze(2).broadcast_to([128, 4, 8])
        first_e_pe = {}
        for tcm in range(4):
            tm0 = tcm * 512
            xnw = list(war[("xns",)])
            if io.get("ln_x") is not None:
                xnw += [(PF + "O_gbf", ln_out)]
            else:
                xnw += list(io.get("xn_wait", ()))
            for k in range(8):
                A("sync", lambda s, k=k, tm0=tm0: s.dma_start(
                    out=xns[:, k * 512:(k + 1) * 512], in_=xn_ld(k, tm0)),
                  inc=(PF + "L_xns", 16), wait=xnw if k == 0 else [])
            ld_xn = (PF + "L_xns", V(PF + "L_xns"))
            for tt in range(4):
                for k in range(8):
                    A("tensor", lambda t, tt=tt, k=k: t.matmul(
                        out=gpsA[:, tt * 8:(tt + 1) * 8],
                        lhsT=xns[:, k * 512 + tt * 128:k * 512 + tt * 128 + 128],
                        rhs=wgb[:, k * 8:(k + 1) * 8], start=(k == 0), stop=(k == 7)),
                      inc="PE",
                      wait=([("DVE", setup_dve), ld_xn] + list(war[("gpsA",)])) if (k == 0 and tt == 0) else [])
            pe_gate = V("PE")
            A("scalar", lambda s: s.copy(out=gsb[:], in_=gpsA[:]), inc="ACT",
              wait=[("PE", pe_gate)] + list(war[("gsb",)]))
            act_gsb = V("ACT")
            if lay == 0:
                A("vector", lambda v: v.tensor_reduce(out=mm1a[:].unsqueeze(2), in_=g3(gsb[:]),
                                                      axis=mybir.AxisListType.X, op=ALU.max),
                  inc="DVE", wait=[("ACT", act_gsb)])
                A("scalar", lambda s: s.activation(out=ee1a[:], in_=gsb[:], func=AF.Exp),
                  inc="ACT")
                A("vector", lambda v: v.tensor_tensor(out=g3(mk1a[:]), in0=g3(gsb[:]), in1=bc3(mm1a[:]),
                                                      op=ALU.is_ge), inc="DVE", wait=[("DVE", V("DVE"))])
                A("vector", lambda v: v.scalar_tensor_tensor(out=gg2a[:], in0=mk1a[:], scalar=-30000.0,
                                                             in1=gsb[:], op0=ALU.mult, op1=ALU.add),
                  inc="DVE")
                A("vector", lambda v: v.tensor_reduce(out=mm2a[:].unsqueeze(2), in_=g3(gg2a[:]),
                                                      axis=mybir.AxisListType.X, op=ALU.max), inc="DVE",
                  wait=[("DVE", V("DVE"))])
                A("vector", lambda v: v.tensor_tensor(out=g3(sela[:]), in0=g3(gsb[:]), in1=bc3(mm2a[:]),
                                                      op=ALU.is_ge), inc="DVE", wait=[("DVE", V("DVE"))])
                A("vector", lambda v: v.tensor_mul(out=eesa[:], in0=ee1a[:], in1=sela[:]),
                  inc="DVE", wait=[("ACT", V("ACT"))])
                A("vector", lambda v: v.tensor_reduce(out=ssma[:].unsqueeze(2), in_=g3(eesa[:]),
                                                      axis=mybir.AxisListType.X, op=ALU.add), inc="DVE",
                  wait=[("DVE", V("DVE"))])
                A("vector", lambda v: v.reciprocal(out=rcpa[:], in_=ssma[:]), inc="DVE",
                  wait=[("DVE", V("DVE"))])
                A("vector", lambda v, tcm=tcm: v.scalar_tensor_tensor(
                    out=g3(msk_tm[:, tcm * 32:(tcm + 1) * 32]), in0=g3(eesa[:]),
                    scalar=0.125, in1=bc3(rcpa[:]),
                    op0=ALU.mult, op1=ALU.mult), inc="DVE",
                  wait=[("DVE", V("DVE"))])
            else:
                A("vector", lambda v: v.tensor_tensor(
                    out=g3(gsb[:]), in0=g3(gsb[:]),
                    in1=bgs[:].unsqueeze(1).broadcast_to([128, 4, 8]), op=ALU.add),
                  inc="DVE", wait=[("ACT", act_gsb), ld0])
                dve_lg = V("DVE")
                A("scalar", lambda s: s.activation(out=ee1a[:], in_=gsb[:], func=AF.Exp),
                  inc="ACT", wait=[("DVE", dve_lg)])
                A("vector", lambda v: v.tensor_reduce(out=ssma[:].unsqueeze(2), in_=g3(ee1a[:]),
                                                      axis=mybir.AxisListType.X, op=ALU.add), inc="DVE",
                  wait=[("ACT", V("ACT"))])
                A("vector", lambda v: v.reciprocal(out=rcpa[:], in_=ssma[:]), inc="DVE",
                  wait=[("DVE", V("DVE"))])
                A("vector", lambda v: v.tensor_tensor(out=g3(gsb[:]), in0=g3(ee1a[:]), in1=bc3(rcpa[:]),
                                                      op=ALU.mult), inc="DVE", wait=[("DVE", V("DVE"))])
                A("vector", lambda v: v.tensor_reduce(out=mm1a[:].unsqueeze(2), in_=g3(gsb[:]),
                                                      axis=mybir.AxisListType.X, op=ALU.max), inc="DVE",
                  wait=[("DVE", V("DVE"))])
                A("vector", lambda v: v.tensor_tensor(out=g3(mk1a[:]), in0=g3(gsb[:]), in1=bc3(mm1a[:]),
                                                      op=ALU.is_ge), inc="DVE", wait=[("DVE", V("DVE"))])
                A("vector", lambda v: v.scalar_tensor_tensor(out=gg2a[:], in0=mk1a[:], scalar=-30000.0,
                                                             in1=gsb[:], op0=ALU.mult, op1=ALU.add),
                  inc="DVE")
                A("vector", lambda v: v.tensor_reduce(out=mm2a[:].unsqueeze(2), in_=g3(gg2a[:]),
                                                      axis=mybir.AxisListType.X, op=ALU.max), inc="DVE",
                  wait=[("DVE", V("DVE"))])
                A("vector", lambda v: v.tensor_tensor(out=g3(sela[:]), in0=g3(gsb[:]), in1=bc3(mm2a[:]),
                                                      op=ALU.is_ge), inc="DVE", wait=[("DVE", V("DVE"))])
                A("vector", lambda v: v.tensor_mul(out=eesa[:], in0=gsb[:], in1=sela[:]), inc="DVE")
                A("vector", lambda v: v.tensor_reduce(out=ssma[:].unsqueeze(2), in_=g3(eesa[:]),
                                                      axis=mybir.AxisListType.X, op=ALU.add), inc="DVE",
                  wait=[("DVE", V("DVE"))])
                A("vector", lambda v: v.tensor_scalar_add(out=ssma[:], in0=ssma[:], scalar1=1e-8),
                  inc="DVE", wait=[("DVE", V("DVE"))])
                A("vector", lambda v: v.reciprocal(out=rcpa[:], in_=ssma[:]), inc="DVE",
                  wait=[("DVE", V("DVE"))])
                A("vector", lambda v, tcm=tcm: v.tensor_tensor(
                    out=g3(msk_tm[:, tcm * 32:(tcm + 1) * 32]), in0=g3(eesa[:]),
                    in1=bc3(rcpa[:]), op=ALU.mult), inc="DVE",
                  wait=[("DVE", V("DVE"))])
            dve_msk = V("DVE")
            war[("gsb",)] = [("DVE", dve_msk)]
            for tt in range(4):
                A("tensor", lambda t, tt=tt, tcm=tcm: t.transpose(
                    out=ptr[:], in_=msk_tm[:, tcm * 32 + tt * 8:tcm * 32 + (tt + 1) * 8], identity=id128[:]),
                  inc="PE", wait=[("DVE", dve_msk), ld0] + list(war[("ptr",)]))
                pev = V("PE")
                A("scalar", lambda s, tt=tt, tm0=tm0: s.copy(
                    out=mask_fm[:, tm0 + tt * 128:tm0 + (tt + 1) * 128], in_=ptr[:]),
                  inc="ACT", wait=[("PE", pev)])
                war[("ptr",)] = [("ACT", V("ACT"))]
            act_mask = V("ACT")
            war[("gpsA",)] = [("ACT", act_mask)]
            yield ("gating", tcm)

            for e in range(8):
                A("tensor", lambda t, e=e, tm0=tm0: t.matmul(
                    out=pm[:], lhsT=id8[:, e:e + 1].broadcast_to([8, 128]),
                    rhs=mask_fm[:, tm0:tm0 + 512], start=True, stop=True),
                  inc="PE",
                  wait=[("ACT", act_mask)] + list(war[("pm",)]))
                pev = V("PE")
                A("scalar", lambda s: s.copy(out=mrep[:], in_=pm[:]),
                  inc="ACT", wait=[("PE", pev)] + list(war[("mrep",)]))
                act_mr = V("ACT")
                war[("pm",)] = [("ACT", act_mr)]
                for hcol in range(32):
                    par = hcol % 2
                    wp = hcol % 2
                    pp = pc[par]
                    if tcm == 0:
                        A("sync", lambda s, e=e, hcol=hcol, wp=wp: s.dma_start(
                            out=w1st[wp % len(w1st)][:], in_=w1r[e, hcol]),
                          inc=(PF + f"L_w1s{wp}", 16), wait=war[("w1st", wp % len(w1st))])
                        ldv = (PF + f"L_w1s{wp}", V(PF + f"L_w1s{wp}"))
                        A("scalar", lambda s2, wp=wp: s2.copy(out=w1b[wp][:], in_=w1st[wp % len(w1st)][:]),
                          inc="ACT", wait=[ldv] + list(war[("w1b", wp)]))
                        dve_w1 = V("ACT")
                        war[("w1st", wp % len(w1st))] = [("ACT", dve_w1)]
                        A("gpsimd", lambda g, e=e, hcol=hcol, wp=wp: g.dma_start(
                            out=w1bd[e, hcol], in_=w1b[wp][:]),
                          inc=(PF + f"O_w1b{wp}", 16), wait=[("ACT", dve_w1)])
                        wb_out = V(PF + f"O_w1b{wp}")
                        ld_w1 = [("ACT", dve_w1)]
                        war[("w1b", wp)] = [(PF + f"O_w1b{wp}", wb_out)]
                    else:
                        A("sync", lambda s, e=e, hcol=hcol, wp=wp: s.dma_start(
                            out=w1b[wp][:], in_=w1bd[e, hcol]),
                          inc=(PF + f"L_w1b{wp}", 16), wait=list(war[("w1b", wp)]) + list(war[("wb_done",)]))
                        ld_w1 = [(PF + f"L_w1b{wp}", V(PF + f"L_w1b{wp}"))]
                    for k in range(8):
                        A("tensor", lambda t, k=k, wp=wp, tm0=tm0, pp=pp: t.matmul(
                            out=pp[:], lhsT=w1b[wp][:, k * 128:(k + 1) * 128],
                            rhs=xns[:, k * 512:(k + 1) * 512],
                            start=(k == 0), stop=(k == 7)),
                          inc="PE",
                          wait=(ld_w1 + list(war[("pc", par)])) if k == 0 else [])
                    pev = V("PE")
                    war[("w1b", wp)] = list(war[("w1b", wp)]) + [("PE", pev)] if tcm == 0 else [("PE", pev)]
                    A("scalar", lambda s, e=e, hcol=hcol, pp=pp: s.activation(
                        out=hsb[:, hcol * 512:(hcol + 1) * 512], in_=pp[:],
                        func=AF.Relu if lay == 0 else AF.Gelu_apprx_tanh,
                        bias=b1s[:, e * 32 + hcol:e * 32 + hcol + 1]),
                      inc="ACT", wait=[("PE", pev), ld0] + list(war[("hsbk", hcol)]))
                    war[("pc", par)] = [("ACT", V("ACT"))]
                act_h = V("ACT")
                for dcol in range(8):
                    par = dcol % 2
                    pp = pc[par]
                    for hh in range(2):
                        sp = hh
                        if tcm == 0:
                            A("sync", lambda s, e=e, dcol=dcol, hh=hh, sp=sp: s.dma_start(
                                out=w2st[0][:], in_=w2r[e, dcol][:, hh * 2048:(hh + 1) * 2048]),
                              inc=(PF + f"L_w2s{sp}", 16), wait=war[("w2st", 0)])
                            ldv = (PF + f"L_w2s{sp}", V(PF + f"L_w2s{sp}"))
                            A("scalar", lambda s2, sp=sp: s2.copy(out=w2b[sp][:], in_=w2st[0][:]),
                              inc="ACT", wait=[ldv] + list(war[("w2b", sp)]))
                            dve_w2 = V("ACT")
                            war[("w2st", 0)] = [("ACT", dve_w2)]
                            A("gpsimd", lambda g, e=e, dcol=dcol, hh=hh, sp=sp: g.dma_start(
                                out=w2bd[e, dcol][:, hh * 2048:(hh + 1) * 2048], in_=w2b[sp][:]),
                              inc=(PF + f"O_w2b{sp}", 16), wait=[("ACT", dve_w2)])
                            war[("w2b", sp)] = [(PF + f"O_w2b{sp}", V(PF + f"O_w2b{sp}"))]
                            ld_w2 = [("ACT", dve_w2)]
                        else:
                            A("sync", lambda s, e=e, dcol=dcol, hh=hh, sp=sp: s.dma_start(
                                out=w2b[sp][:], in_=w2bd[e, dcol][:, hh * 2048:(hh + 1) * 2048]),
                              inc=(PF + f"L_w2b{sp}", 16), wait=list(war[("w2b", sp)]) + list(war[("wb_done",)]))
                            ld_w2 = [(PF + f"L_w2b{sp}", V(PF + f"L_w2b{sp}"))]
                        for kh in range(16):
                            A("tensor", lambda t, kh=kh, hh=hh, sp=sp, pp=pp: t.matmul(
                                out=pp[:], lhsT=w2b[sp][:, kh * 128:(kh + 1) * 128],
                                rhs=hsb[:, (hh * 16 + kh) * 512:(hh * 16 + kh + 1) * 512],
                                start=(hh == 0 and kh == 0), stop=(hh == 1 and kh == 15)),
                              inc="PE",
                              wait=(ld_w2 + ([("ACT", act_h)] + list(war[("pc", par)]) if hh == 0 else [])) if kh == 0 else [])
                        pev = V("PE")
                        war[("w2b", sp)] = list(war[("w2b", sp)]) + [("PE", pev)] if tcm == 0 else [("PE", pev)]
                    A("scalar", lambda s, e=e, dcol=dcol, par=par, pp=pp: s.activation(
                        out=ytmp[par][:], in_=pp[:], func=AF.Identity,
                        bias=b2s[:, e * 8 + dcol:e * 8 + dcol + 1]),
                      inc="ACT", wait=[("PE", pev), ld0] + list(war[("ytmp", par)]))
                    war[("pc", par)] = [("ACT", V("ACT"))]
                    act_yt = V("ACT")
                    if e == 0:
                        A("gpsimd", lambda v, dcol=dcol, par=par: v.tensor_mul(
                            out=yacc[:, dcol * 512:(dcol + 1) * 512], in0=ytmp[par][:], in1=mrep[:]),
                          inc="POOL", wait=[("ACT", act_yt), ("ACT", act_mr)] + list(war[("yacck", dcol)]))
                    else:
                        A("gpsimd", lambda v, dcol=dcol, par=par: v.tensor_mul(
                            out=tmp2[:], in0=ytmp[par][:], in1=mrep[:]),
                          inc="POOL", wait=[("ACT", act_yt), ("ACT", act_mr)])
                        A("gpsimd", lambda v, dcol=dcol: v.tensor_add(
                            out=yacc[:, dcol * 512:(dcol + 1) * 512],
                            in0=yacc[:, dcol * 512:(dcol + 1) * 512], in1=tmp2[:]),
                          inc="POOL")
                    war[("ytmp", par)] = [("POOL", V("POOL"))]
                war[("mrep",)] = [("POOL", V("POOL"))]
                yield ("block", tcm, e)
            if tcm == 0:
                war[("wb_done",)] = [(PF + "O_w1b0", V(PF + "O_w1b0")), (PF + "O_w1b1", V(PF + "O_w1b1")),
                                     (PF + "O_w2b0", V(PF + "O_w2b0")), (PF + "O_w2b1", V(PF + "O_w2b1"))]
            for hcol in range(32):
                war[("hsbk", hcol)] = [("PE", V("PE"))]
            for j in range(8):
                par = 0
                A("scalar", lambda s, j=j, par=par: s.activation(
                    out=gbf[par][:], in_=yacc[:, j * 512:(j + 1) * 512], func=AF.Silu),
                  inc="ACT", wait=[("POOL", V("POOL"))] + list(war[("gbf", par)]))
                act_g = V("ACT")
                A("sync", lambda s, j=j, par=par, tm0=tm0: s.dma_start(
                    out=xsl[par][:], in_=x_ld(j, tm0)),
                  inc=(PF + "L_xsl", 16), wait=war[("xsl", par)])
                ld_xsl = (PF + "L_xsl", V(PF + "L_xsl"))
                if s_mode == "dual":
                    A("sync", lambda s, j=j, par=par, tm0=tm0: s.dma_start(
                        out=sas[par][:], in_=io["sa_ld"](j, tm0)),
                      inc=(PF + "L_sas", 16), wait=war[("sas", par)])
                    A("sync", lambda s, j=j, par=par, tm0=tm0: s.dma_start(
                        out=sbs[par][:], in_=io["sb_ld"](j, tm0)),
                      inc=(PF + "L_sbs", 16), wait=war[("sbs", par)])
                    ldv = [(PF + "L_sas", V(PF + "L_sas")), (PF + "L_sbs", V(PF + "L_sbs"))]
                    A("gpsimd", lambda v, par=par: v.tensor_add(out=sas[par][:], in0=sas[par][:], in1=sbs[par][:]),
                      inc="POOL", wait=ldv)
                elif s_mode == "single_f32":
                    A("sync", lambda s, j=j, par=par, tm0=tm0: s.dma_start(
                        out=sas[par][:], in_=io["s_ld"](j, tm0)),
                      inc=(PF + "L_sas", 16), wait=tuple(war[("sas", par)]) + tuple(io.get("s_wait", ())))
                    ldv = [(PF + "L_sas", V(PF + "L_sas"))]
                else:
                    A("sync", lambda s, j=j, tm0=tm0: s.dma_start(
                        out=sss[:], in_=io["s_ld"](j, tm0)),
                      inc=(PF + "L_sss", 16), wait=list(war[("sas", par)]) + list(io.get("s_wait", ())))
                    ldv = [(PF + "L_sss", V(PF + "L_sss"))]
                    A("gpsimd", lambda v, par=par: v.tensor_copy(out=sas[par][:], in_=sss[:]),
                      inc="POOL", wait=ldv)
                A("gpsimd", lambda v, par=par: v.tensor_mul(out=sas[par][:], in0=sas[par][:], in1=gbf[par][:]),
                  inc="POOL", wait=[("ACT", act_g)] + ldv)
                A("gpsimd", lambda v, par=par: v.tensor_add(out=x1s[par][:], in0=xsl[par][:], in1=sas[par][:]),
                  inc="POOL", wait=[ld_xsl] + list(war[("x1s", par)]))
                dve_x1 = V("POOL")
                war[("sbs", par)] = [("POOL", dve_x1)]
                war[("sas", par)] = [("POOL", dve_x1)]
                war[("xsl", par)] = [("POOL", dve_x1)]
                war[("gbf", par)] = [("POOL", dve_x1)]
                A("gpsimd", lambda g, j=j, par=par, tm0=tm0: g.dma_start(
                    out=out_st(j, tm0), in_=x1s[par][:]),
                  inc=(PF + "O_x1s", 16), wait=[("POOL", dve_x1)])
                war[("x1s", par)] = [(PF + "O_x1s", V(PF + "O_x1s"))]
                war[("yacck", j)] = [("ACT", act_g)]
            war[("xns",)] = [("PE", V("PE"))]
            yield ("combine", tcm)


def stage_C(d, res_b):
    x = np.asarray(d["L0_attn_x"])
    w1 = np.asarray(d["L0_ffn_W1"])
    w2 = np.asarray(d["L0_ffn_W2"])
    w1r = np.ascontiguousarray(w1.reshape(8, 8, 128, 32, 128).transpose(0, 3, 2, 1, 4).reshape(8, 32, 128, 1024))
    w2r = np.ascontiguousarray(w2.reshape(8, 32, 128, 8, 128).transpose(0, 3, 2, 1, 4).reshape(8, 8, 128, 4096))
    b1cv = np.ascontiguousarray(np.asarray(d["L0_ffn_b1"]).reshape(8, 32, 128).transpose(2, 0, 1).reshape(128, 256))
    b2cv = np.ascontiguousarray(np.asarray(d["L0_ffn_b2"]).reshape(8, 8, 128).transpose(2, 0, 1).reshape(128, 64))
    wgtv = np.ascontiguousarray(np.asarray(d["L0_ffn_W_gate"]).reshape(8, 128, 8).transpose(1, 0, 2).reshape(128, 64))
    e128 = np.eye(128).astype(BF)
    e8 = np.eye(8).astype(BF)
    maps = []
    for c in range(8):
        b, ih = c // 2, c % 2
        tsl = slice(ih * 2048, (ih + 1) * 2048)
        partner = 2 * b + (1 - ih)
        m = {
            "xn_in": np.ascontiguousarray(res_b[c]["xn_out"][:, tsl]),
            "x_own": np.ascontiguousarray(x[b].T[:, tsl]),
            "s_a": np.ascontiguousarray(res_b[c]["s_par"][:, tsl]),
            "s_b": np.ascontiguousarray(res_b[partner]["s_par"][:, tsl]),
            "wgt": wgtv, "w1r": w1r, "w2r": w2r, "b1c": b1cv, "b2c": b2cv,
            "eye128": e128, "eye8": e8,
        }
        maps.append(m)
    return maps


def build_D1():
    nc = bass.Bass("TRN2", target_bir_lowering=False, debug=False)
    io = {}
    for nm, sh, dt in [("x1in", [1024, 2048], F32), ("nwc", [128, 8], F32),
                       ("w_xp", [1024, 192], F32), ("w_dt1", [64, 1024], F32),
                       ("b_dt1", [128, 8], F32), ("a1in", [128, 512], F32),
                       ("d1c", [128, 8], F32), ("w_op1r", [8, 128, 1024], F32),
                       ("eye64", [64, 64], BF16), ("cst", [128, 4], F32)]:
        io[nm] = nc.dram_tensor(nm, sh, dt, kind="ExternalInput").ap()
    io["s1o"] = nc.dram_tensor("s1o", [1024, 2048], BF16, kind="ExternalOutput").ap()
    io["xn1o"] = nc.dram_tensor("xn1o", [1024, 2048], BF16, kind="ExternalOutput").ap()
    p = Prog()
    with contextlib.ExitStack() as es0:
        _emit_D1(nc, p, io)
        finish(p, es0, nc)
    return nc


def _emit_D1(nc, p, io):
    for _ in _emit_D1_gen(nc, p, io):
        pass


def _emit_D1_gen(nc, p, io):
    x1in, nwc, w_xp, w_dt1 = io["x1in"], io["nwc"], io["w_xp"], io["w_dt1"]
    b_dt1, a1in, d1c, w_op1r = io["b_dt1"], io["a1in"], io["d1c"], io["w_op1r"]
    eye64, cst, s1o, xn1o = io["eye64"], io["cst"], io["s1o"], io["xn1o"]
    NCH1 = 2048 // TC
    PF = io.get("pref", "")
    with contextlib.ExitStack() as es:
        _pf = io.get("pref", "")
        T = lambda name, shape, dt: es.enter_context(nc.sbuf_tensor(_pf + name, shape, dt))
        PT = lambda name, shape, dt: es.enter_context(nc.psum_tensor(_pf + name, shape, dt))
        nws = T("nws", [128, 8], F32)
        wxpb = T("wxpb", [128, 8 * 192], BF16)
        wdt1 = T("wdt1b", [64, 1024], BF16)
        bdt1 = T("bdt1", [128, 8], F32)
        a1c = T("a1c", [128, 512], F32)
        d1s = T("d1s", [128, 8], F32)
        eyes = T("eyes", [64, 64], BF16)
        cstt = T("cstt", [128, 4], F32)
        ones_c = T("ones_c", [128, 1], F32)
        ones_r = T("ones_r", [1, 128], F32)
        wst = T("wstD", [128, 1024], F32)
        wopb = [T(f"wopb{i}", [128, 1024], BF16) for i in range(1)]
        xfs = [T(f"xfD{i}", [128, TC], F32) for i in range(2)]
        xsq = [T(f"xsqD{i}", [128, TC], F32) for i in range(1)]
        nm = T("nmD", [1, TC], F32)
        m2 = T("m2D", [1, TC], F32)
        var = T("varD", [1, TC], F32)
        rs = T("rsD", [1, TC], F32)
        nmr = T("nmrD", [1, TC], F32)
        xnt = T("xntD", [128, TC], F32)
        xn1 = T("xn1", [128, KT * TC], BF16)
        xnr = T("xnr", [128, KT * TC], BF16)
        dlo_t = T("dlo_t", [64, TC], BF16)
        bs_t = T("bs_t", [64, TC], BF16)
        cs_t = T("cs_t", [64, TC], BF16)
        spv = T("spvD", [128, TC], F32)
        sav = T("savD", [128, TC], F32)
        delta1 = T("delta1", [128, KT * TC], BF16)
        dxn = T("dxn", [128, KT * TC], BF16)
        brep = T("brepD", [128, SG * TC], BF16)
        crep = T("crepD", [128, SG * TC], BF16)
        dab = [T(f"dabD{i}", [128, SG * TC], BF16) for i in range(1)]
        ub = [T(f"ubD{i}", [128, SG * TC], BF16) for i in range(1)]
        hb = [T(f"hbD{i}", [128, SG * TC], BF16) for i in range(2)]
        wbf = [T(f"wbfD{i}", [128, SG * TC], BF16) for i in range(1)]
        ysg = [T(f"ysgD{i}", [128, TC], F32) for i in range(1)]
        yacc = T("yaccD", [128, KT * TC], BF16)
        ob = [T(f"ob{i}", [128, KT * TC], BF16) for i in range(2)]
        spo = [T(f"spoD{i}", [128, TC], BF16) for i in range(2)]
        pa = PT("paD", [128, TC], F32)
        pb = PT("pbD", [128, TC], F32)
        brp = PT("brpD", [128, SG * TC], F32)
        mps = pa[0:1, 0:TC]
        sps = pb[0:1, 0:TC]
        pst = [pa, pb]

        A = p.add
        V = p.v
        war = defaultdict(tuple)

        A("vector", lambda v: v.memset(ones_c[:], 1.0), inc="DVE")
        A("vector", lambda v: v.memset(ones_r[:], 1.0), inc="DVE")
        A("sync", lambda s: s.dma_start(out=eyes[:], in_=eye64), inc=(PF + "L_stp", 16))
        A("sync", lambda s: s.dma_start(out=cstt[:], in_=cst), inc=(PF + "L_stp", 16))
        A("sync", lambda s: s.dma_start(out=nws[:], in_=nwc), inc=(PF + "L_stp", 16))
        A("sync", lambda s: s.dma_start(out=bdt1[:], in_=b_dt1), inc=(PF + "L_stp", 16))
        A("sync", lambda s: s.dma_start(out=d1s[:], in_=d1c), inc=(PF + "L_stp", 16))
        A("sync", lambda s: s.dma_start(out=a1c[:], in_=a1in), inc=(PF + "L_stp", 16))
        ld0v = V(PF + "L_stp")
        ld0 = (PF + "L_stp", ld0v)
        A("scalar", lambda s: s.activation(out=a1c[:], in_=a1c[:], func=AF.Exp),
          inc="ACT", wait=[ld0])
        A("vector", lambda v: v.tensor_scalar_mul(out=a1c[:], in0=a1c[:], scalar1=-1.0),
          inc="DVE", wait=[("ACT", V("ACT"))])
        for k in range(KT):
            w_cast = V("ACT")
            A("sync", lambda s, k=k: s.dma_start(out=wst[:, :192], in_=w_xp[k * 128:(k + 1) * 128, :]),
              inc=(PF + "L_wst", 16), wait=[("ACT", w_cast)])
            A("scalar", lambda s, k=k: s.copy(out=wxpb[:, k * 192:(k + 1) * 192], in_=wst[:, :192]),
              inc="ACT", wait=[(PF + "L_wst", V(PF + "L_wst"))])
        for h in range(2):
            w_cast = V("ACT")
            A("sync", lambda s, h=h: s.dma_start(out=wst[:64, :512], in_=w_dt1[:, h * 512:(h + 1) * 512]),
              inc=(PF + "L_wst", 16), wait=[("ACT", w_cast)])
            A("scalar", lambda s, h=h: s.copy(out=wdt1[:, h * 512:(h + 1) * 512], in_=wst[:64, :512]),
              inc="ACT", wait=[(PF + "L_wst", V(PF + "L_wst"))])
        setup_act = V("ACT")
        setup_dve = V("DVE")
        yield "setup"

        obdve = {}

        def emit_back(c):
            t0 = c * TC
            obb = ob[c % 2]
            for dcol in range(8):
                ps = pst[dcol % 2]
                pkey = ("pa",) if dcol % 2 == 0 else ("pb",)
                par = dcol % 2
                wp = 0
                wld = V("ACT")
                A("sync", lambda s, dcol=dcol, wp=wp: s.dma_start(
                    out=wst[:], in_=w_op1r[dcol]),
                  inc=(PF + "L_wst", 16), wait=[("ACT", wld)] + list(war[("wopst",)]))
                ldv = (PF + "L_wst", V(PF + "L_wst"))
                A("scalar", lambda s, wp=wp: s.copy(out=wopb[wp][:], in_=wst[:]),
                  inc="ACT", wait=[ldv] + list(war[("wopb",)]))
                act_wc = V("ACT")
                war[("wopst",)] = [("ACT", act_wc)]
                for k in range(KT):
                    A("tensor", lambda t, k=k, wp=wp, ps=ps, obb=obb: t.matmul(
                        out=ps[:], lhsT=wopb[wp][:, k * 128:(k + 1) * 128],
                        rhs=obb[:, k * TC:(k + 1) * TC], start=(k == 0), stop=(k == KT - 1)),
                      inc="PE",
                      wait=([("DVE", obdve[c]), ("ACT", act_wc)] + list(war[pkey])) if k == 0 else [])
                pev = V("PE")
                war[("wopb",)] = [("PE", pev)]
                A("scalar", lambda s, par=par, ps=ps: s.copy(out=spo[par][:], in_=ps[:]),
                  inc="ACT", wait=[("PE", pev)] + list(war[("spo", par)]))
                actv = V("ACT")
                war[pkey] = [("ACT", actv)]
                A("gpsimd", lambda g, dcol=dcol, par=par, t0=t0: g.dma_start(
                    out=io["s1o"][dcol * 128:(dcol + 1) * 128, t0:t0 + TC], in_=spo[par][:]),
                  inc=(PF + f"O_sp{par}", 16), wait=[("ACT", actv)])
                war[("spo", par)] = [(PF + f"O_sp{par}", V(PF + f"O_sp{par}"))]
            pe_end = V("PE")
            war[("ob", c % 2)] = [("PE", pe_end)]

        for c in range(NCH1):
            t0 = c * TC
            for k in range(KT):
                par = k % 2
                A("sync", lambda s, k=k, par=par, t0=t0: s.dma_start(
                    out=xfs[par][:], in_=x1in[k * 128:(k + 1) * 128, t0:t0 + TC]),
                  inc=(PF + f"L_xf{par}", 16), wait=war[("xf", par)])
                ldw = (PF + f"L_xf{par}", V(PF + f"L_xf{par}"))
                A("scalar", lambda s, par=par: s.activation(out=xsq[0][:], in_=xfs[par][:], func=AF.Square),
                  inc="ACT", wait=[ldw] + list(war[("xsq", 0)]))
                actv = V("ACT")
                A("tensor", lambda t, k=k, par=par: t.matmul(
                    out=mps[:], lhsT=ones_c[:], rhs=xfs[par][:], start=(k == 0), stop=(k == KT - 1)),
                  inc="PE",
                  wait=([ldw, ("DVE", setup_dve)] + list(war[("mps",)]) + list(war[("pa",)]) + list(war[("pb",)])) if k == 0 else [ldw])
                A("tensor", lambda t, k=k, par=par: t.matmul(
                    out=sps[:], lhsT=ones_c[:], rhs=xsq[0][:], start=(k == 0), stop=(k == KT - 1)),
                  inc="PE", wait=[("ACT", actv)])
                war[("xf", par)] = [("PE", V("PE"))]
                war[("xsq", 0)] = [("PE", V("PE"))]
            pe_stats = V("PE")
            A("vector", lambda v: v.tensor_scalar_mul(out=nm[:], in0=mps[:], scalar1=-1.0 / 1024.0),
              inc="DVE", wait=[("PE", pe_stats)])
            A("vector", lambda v: v.tensor_mul(out=m2[:], in0=nm[:], in1=nm[:]), inc="DVE")
            A("vector", lambda v: v.scalar_tensor_tensor(
                out=var[:], in0=sps[:], scalar=1.0 / 1024.0, in1=m2[:], op0=ALU.mult, op1=ALU.subtract),
              inc="DVE", wait=[("PE", pe_stats)])
            dve_var = V("DVE")
            war[("mps",)] = [("DVE", dve_var)]
            A("scalar", lambda s: s.activation(out=rs[:], in_=var[:], func=AF.Ln, bias=cstt[0:1, 0:1]),
              inc="ACT", wait=[("DVE", dve_var)])
            A("scalar", lambda s: s.activation(out=rs[:], in_=rs[:], func=AF.Exp, scale=cstt[0:1, 1:2]), inc="ACT")
            act_rs = V("ACT")
            A("vector", lambda v: v.tensor_mul(out=nmr[:], in0=nm[:], in1=rs[:]),
              inc="DVE", wait=[("ACT", act_rs)])
            dve_nmr = V("DVE")
            A("tensor", lambda t: t.matmul(out=pa[:], lhsT=ones_r[:], rhs=rs[:], start=True, stop=True),
              inc="PE", wait=[("ACT", act_rs)] + list(war[("pa",)]))
            A("tensor", lambda t: t.matmul(out=pb[:], lhsT=ones_r[:], rhs=nmr[:], start=True, stop=True),
              inc="PE", wait=[("DVE", dve_nmr)] + list(war[("pb",)]))
            pe_bc = V("PE")
            for k in range(KT):
                par = k % 2
                A("sync", lambda s, k=k, par=par, t0=t0: s.dma_start(
                    out=xfs[par][:], in_=x1in[k * 128:(k + 1) * 128, t0:t0 + TC]),
                  inc=(PF + f"L_xf{par}", 16), wait=war[("xf", par)])
                ldw = (PF + f"L_xf{par}", V(PF + f"L_xf{par}"))
                A("vector", lambda v, par=par: v.tensor_mul(out=xnt[:], in0=xfs[par][:], in1=pa[:]),
                  inc="DVE", wait=[ldw, ("PE", pe_bc)])
                A("vector", lambda v, k=k, par=par, t0=t0: v.tensor_add(
                    out=xn1[:, k * TC:(k + 1) * TC], in0=xnt[:], in1=pb[:]),
                  inc="DVE", wait=war[("xnk", k)])
                dv = V("DVE")
                war[("xf", par)] = [("DVE", dv)]
                A("gpsimd", lambda g, k=k, t0=t0: g.dma_start(
                    out=io["xn1o"][k * 128:(k + 1) * 128, t0:t0 + TC], in_=xn1[:, k * TC:(k + 1) * TC]),
                  inc=(PF + "O_xn1", 16), wait=[("DVE", dv)])
            oxn_end = V(PF + "O_xn1")
            for k in range(KT):
                war[("xnk", k)] = [(PF + "O_xn1", oxn_end)]
            dve_xn = V("DVE")
            war[("pa",)] = [("DVE", dve_xn)]
            war[("pb",)] = [("DVE", dve_xn)]
            yield ("ln", c)

            for k in range(KT):
                par = k % 2
                A("scalar", lambda s, k=k, par=par, t0=t0: s.activation(
                    out=xsq[0][:], in_=xn1[:, k * TC:(k + 1) * TC], func=AF.Square),
                  inc="ACT", wait=[("DVE", dve_xn)] + list(war[("xsq", 0)]))
                actv = V("ACT")
                A("tensor", lambda t, k=k, par=par: t.matmul(
                    out=sps[:], lhsT=ones_c[:], rhs=xsq[0][:], start=(k == 0), stop=(k == KT - 1)),
                  inc="PE", wait=([("ACT", actv)] + list(war[("mps",)])) if k == 0 else [("ACT", actv)])
                war[("xsq", 0)] = [("PE", V("PE"))]
            pe_rms = V("PE")
            A("vector", lambda v: v.tensor_scalar_mul(out=var[:], in0=sps[:], scalar1=1.0 / 1024.0),
              inc="DVE", wait=[("PE", pe_rms)])
            dve_var2 = V("DVE")
            war[("mps",)] = [("DVE", dve_var2)]
            A("scalar", lambda s: s.activation(out=rs[:], in_=var[:], func=AF.Ln, bias=cstt[0:1, 2:3]),
              inc="ACT", wait=[("DVE", dve_var2)])
            A("scalar", lambda s: s.activation(out=rs[:], in_=rs[:], func=AF.Exp, scale=cstt[0:1, 1:2]), inc="ACT")
            act_rs2 = V("ACT")
            A("tensor", lambda t: t.matmul(out=pa[:], lhsT=ones_r[:], rhs=rs[:], start=True, stop=True),
              inc="PE", wait=[("ACT", act_rs2)] + list(war[("pa",)]))
            pe_bc2 = V("PE")
            for k in range(KT):
                A("vector", lambda v, k=k, t0=t0: v.scalar_tensor_tensor(
                    out=xnr[:, k * TC:(k + 1) * TC], in0=xn1[:, k * TC:(k + 1) * TC],
                    scalar=nws[:, k:k + 1], in1=pa[:], op0=ALU.mult, op1=ALU.mult),
                  inc="DVE", wait=([("PE", pe_bc2), ld0] + list(war[("xnrk", k)])))
            dve_xnr = V("DVE")
            war[("pa",)] = [("DVE", dve_xnr)]

            for gi in range(2):
                ps = pst[gi % 2]
                pkey = ("pa",) if gi % 2 == 0 else ("pb",)
                npart = 128 if gi == 0 else 64
                col0 = gi * 128
                for k in range(KT):
                    A("tensor", lambda t, k=k, col0=col0, ps=ps, npart=npart: t.matmul(
                        out=ps[:npart, :], lhsT=wxpb[:, k * 192 + col0:k * 192 + col0 + npart],
                        rhs=xnr[:, k * TC:(k + 1) * TC], start=(k == 0), stop=(k == KT - 1)),
                      inc="PE",
                      wait=(list(war[pkey]) + [("ACT", setup_act), ("DVE", dve_xnr)]) if k == 0 else [])
                pev = V("PE")
                if gi == 0:
                    A("scalar", lambda s, ps=ps: s.copy(out=dlo_t[:], in_=ps[0:64, :]),
                      inc="ACT", wait=[("PE", pev)] + list(war[("dlo",)]))
                    A("scalar", lambda s, ps=ps: s.copy(out=bs_t[:], in_=ps[64:128, :]), inc="ACT")
                else:
                    A("scalar", lambda s, ps=ps: s.copy(out=cs_t[:], in_=ps[0:64, :]),
                      inc="ACT", wait=[("PE", pev)] + list(war[("cs",)]))
                war[pkey] = [("ACT", V("ACT"))]
            act_proj = V("ACT")

            for j in range(KT):
                ps = pst[j % 2]
                pkey = ("pa",) if j % 2 == 0 else ("pb",)
                A("tensor", lambda t, j=j, ps=ps: t.matmul(
                    out=ps[:], lhsT=wdt1[:, j * 128:(j + 1) * 128], rhs=dlo_t[:],
                    start=True, stop=True),
                  inc="PE",
                  wait=[("ACT", act_proj), ("DVE", setup_dve)] + list(war[pkey]))
                pev = V("PE")
                A("vector", lambda v, j=j, ps=ps: v.tensor_scalar_add(
                    out=spv[:], in0=ps[:], scalar1=bdt1[:, j:j + 1]),
                  inc="DVE", wait=[("PE", pev)] + list(war[("spv",)]))
                dve_v = V("DVE")
                war[pkey] = [("DVE", dve_v)]
                A("scalar", lambda s: s.activation(out=sav[:], in_=spv[:], func=AF.Abs),
                  inc="ACT", wait=[("DVE", dve_v)] + list(war[("sav",)]))
                A("scalar", lambda s: s.activation(out=sav[:], in_=sav[:], func=AF.Exp, scale=-1.0), inc="ACT")
                A("scalar", lambda s: s.activation(out=sav[:], in_=sav[:], func=AF.Ln, bias=1.0), inc="ACT")
                act_sp = V("ACT")
                A("vector", lambda v, j=j: v.scalar_tensor_tensor(
                    out=delta1[:, j * TC:(j + 1) * TC], in0=spv[:], scalar=0.0, in1=sav[:],
                    op0=ALU.max, op1=ALU.add),
                  inc="DVE", wait=[("ACT", act_sp)] + list(war[("delta", j)]))
                war[("spv",)] = [("ACT", act_sp)]
                war[("sav",)] = [("DVE", V("DVE"))]
                A("vector", lambda v, j=j: v.tensor_mul(
                    out=dxn[:, j * TC:(j + 1) * TC], in0=delta1[:, j * TC:(j + 1) * TC],
                    in1=xnr[:, j * TC:(j + 1) * TC]),
                  inc="DVE")
            dve_front = V("DVE")

            A("gpsimd", lambda g: g.memset(yacc[:], 0.0), inc="POOL", wait=war[("yacc",)])
            pool_ymem = V("POOL")

            act_cr = 0
            for sg in range(NSG):
                if sg in (6, 14, 22):
                    yield ("sg", c, (sg + 2) // 8)
                for sl in range(SG):
                    s_ = sg * SG + sl
                    A("tensor", lambda t, s_=s_, sl=sl: t.matmul(
                        out=brp[:, sl * TC:(sl + 1) * TC],
                        lhsT=eyes[:, s_:s_ + 1].broadcast_to([64, 128]),
                        rhs=bs_t[:], start=True, stop=True),
                      inc="PE",
                      wait=([("ACT", act_proj)] + list(war[("brp",)])) if sl == 0 else [])
                pe_b = V("PE")
                A("scalar", lambda s: s.copy(out=brep[:], in_=brp[:]),
                  inc="ACT", wait=[("PE", pe_b)] + list(war[("brep",)]))
                act_b = V("ACT")
                war[("brp",)] = [("ACT", act_b)]

                for j in range(KT):
                    par = 0
                    hpar = j % 2
                    for sl in range(SG):
                        s_ = sg * SG + sl
                        A("scalar", lambda s, j=j, par=par, sl=sl, s_=s_: s.activation(
                            out=dab[par][:, sl * TC:(sl + 1) * TC], in_=delta1[:, j * TC:(j + 1) * TC],
                            func=AF.Exp, scale=a1c[:, j * 64 + s_:j * 64 + s_ + 1]),
                          inc="ACT" if sl == SG - 1 else None,
                          wait=(list(war[("dab", par)]) + [("DVE", dve_front), ("DVE", setup_dve)]) if sl == 0 else [])
                    act_da = V("ACT")
                    A("vector", lambda v, par=par: v.memset(dab[par][:, 0::64], 0.0),
                      inc="DVE", wait=[("ACT", act_da)])
                    dve_rst = V("DVE")
                    A("vector", lambda v, j=j, par=par: v.tensor_mul(
                        out=ub[par][:].rearrange("p (s t) -> p s t", s=SG),
                        in0=dxn[:, j * TC:(j + 1) * TC].unsqueeze(1).broadcast_to([128, SG, TC]),
                        in1=brep[:].rearrange("p (s t) -> p s t", s=SG)),
                      inc="DVE", wait=[("ACT", act_b)])
                    A("vector", lambda v, par=par, hpar=hpar: v.tensor_tensor_scan(
                        out=hb[hpar][:], data0=dab[par][:], data1=ub[par][:],
                        initial=0.0, op0=ALU.mult, op1=ALU.add),
                      inc="DVE",
                      wait=[("DVE", dve_rst)] + list(war[("hb", hpar)]))
                    dve_scan = V("DVE")
                    war[("dab", par)] = [("DVE", dve_scan)]
                    if j == 0:
                        for sl in range(SG):
                            s_ = sg * SG + sl
                            A("tensor", lambda t, s_=s_, sl=sl: t.matmul(
                                out=brp[:, sl * TC:(sl + 1) * TC],
                                lhsT=eyes[:, s_:s_ + 1].broadcast_to([64, 128]),
                                rhs=cs_t[:], start=True, stop=True),
                              inc="PE",
                              wait=([("ACT", act_b)] + list(war[("brp",)])) if sl == 0 else [])
                        pe_cr = V("PE")
                        A("scalar", lambda s: s.copy(out=crep[:], in_=brp[:]),
                          inc="ACT", wait=[("PE", pe_cr)] + list(war[("crep",)]))
                        act_cr = V("ACT")
                        war[("brp",)] = [("ACT", act_cr)]
                    A("vector", lambda v, par=par, hpar=hpar: v.tensor_mul(
                        out=wbf[par][:], in0=hb[hpar][:], in1=crep[:]),
                      inc="DVE", wait=[("ACT", act_cr)] + list(war[("wbf", par)]))
                    dve_w = V("DVE")
                    war[("hb", hpar)] = [("DVE", dve_w)]
                    A("gpsimd", lambda g, par=par: g.tensor_add(
                        out=ysg[par][:], in0=wbf[par][:, 0:TC], in1=wbf[par][:, TC:2 * TC]),
                      inc="POOL", wait=[("DVE", dve_w)])
                    A("gpsimd", lambda g, j=j, par=par: g.tensor_add(
                        out=yacc[:, j * TC:(j + 1) * TC], in0=yacc[:, j * TC:(j + 1) * TC],
                        in1=ysg[par][:]),
                      inc="POOL", wait=[("POOL", pool_ymem)])
                    pool_acc = V("POOL")
                    war[("wbf", par)] = [("POOL", pool_acc)]
                    war[("yacck", j)] = [("POOL", pool_acc)]
                war[("brep",)] = [("DVE", V("DVE"))]
                war[("crep",)] = [("DVE", V("DVE"))]
            war[("dlo",)] = [("PE", V("PE"))]
            war[("cs",)] = [("PE", V("PE"))]
            for j in range(KT):
                war[("delta", j)] = [("ACT", V("ACT")), ("DVE", V("DVE"))]

            obb = ob[c % 2]
            for j in range(KT):
                A("vector", lambda v, j=j, obb=obb: v.scalar_tensor_tensor(
                    out=obb[:, j * TC:(j + 1) * TC], in0=xn1[:, j * TC:(j + 1) * TC],
                    scalar=d1s[:, j:j + 1], in1=yacc[:, j * TC:(j + 1) * TC],
                    op0=ALU.mult, op1=ALU.add),
                  inc="DVE", wait=list(war[("yacck", j)]) + list(war[("ob", c % 2)]))
            dve_ob = V("DVE")
            war[("yacc",)] = [("DVE", dve_ob)]
            obdve[c] = dve_ob
            for j in range(KT):
                war[("xnrk", j)] = [("PE", V("PE")), ("DVE", V("DVE"))]
            for k in range(KT):
                war[("xnk", k)] = [(PF + "O_xn1", V(PF + "O_xn1")), ("ACT", V("ACT"))]
            yield ("mid", c)
            if c > 0:
                emit_back(c - 1)
                yield ("back", c - 1)
        emit_back(NCH1 - 1)
        yield ("back", NCH1 - 1)


def stage_D1(d, x1_by_core):
    dtw = np.asarray(d["L1_attn_dt_proj_w"])
    opT = np.ascontiguousarray(np.asarray(d["L1_attn_out_proj_w"]).T)
    wop1r = np.ascontiguousarray(opT.reshape(8, 128, 8, 128).transpose(2, 1, 0, 3).reshape(8, 128, 1024))
    eye = np.eye(64).astype(BF)
    cstv = np.ascontiguousarray(np.broadcast_to(
        np.array([1e-5, -0.5, 1e-6, 1e-8], np.float32)[None, :], (128, 4)))
    base = {
        "nwc": _col(np.asarray(d["L1_attn_norm_w"])),
        "w_xp": np.ascontiguousarray(np.asarray(d["L1_attn_x_proj_w"]).T),
        "w_dt1": np.ascontiguousarray(dtw.T),
        "b_dt1": _col(np.asarray(d["L1_attn_dt_proj_b"])),
        "a1in": np.ascontiguousarray(np.asarray(d["L1_attn_A_log"]).reshape(8, 128, 64).transpose(1, 0, 2).reshape(128, 512)),
        "d1c": _col(np.asarray(d["L1_attn_D"])),
        "w_op1r": wop1r,
        "eye64": eye,
        "cst": cstv,
    }
    return [dict(base, x1in=np.ascontiguousarray(x1_by_core[c])) for c in range(8)]


def stage_D2(d, x1_by_core, s1_by_core, xn1_by_core):
    w1 = np.asarray(d["L1_ffn_W1"])
    w2 = np.asarray(d["L1_ffn_W2"])
    w1r = np.ascontiguousarray(w1.reshape(8, 8, 128, 32, 128).transpose(0, 3, 2, 1, 4).reshape(8, 32, 128, 1024))
    w2r = np.ascontiguousarray(w2.reshape(8, 32, 128, 8, 128).transpose(0, 3, 2, 1, 4).reshape(8, 8, 128, 4096))
    b1cv = np.ascontiguousarray(np.asarray(d["L1_ffn_b1"]).reshape(8, 32, 128).transpose(2, 0, 1).reshape(128, 256))
    b2cv = np.ascontiguousarray(np.asarray(d["L1_ffn_b2"]).reshape(8, 8, 128).transpose(2, 0, 1).reshape(128, 64))
    wgtv = np.ascontiguousarray(np.asarray(d["L1_ffn_Wg"]).reshape(8, 128, 8).transpose(1, 0, 2).reshape(128, 64))
    bgrv = np.ascontiguousarray(np.broadcast_to(np.asarray(d["L1_ffn_bg"])[None, :], (128, 8)))
    e128 = np.eye(128).astype(BF)
    e8 = np.eye(8).astype(BF)
    maps = []
    for c in range(8):
        m = {
            "xn_in": np.ascontiguousarray(xn1_by_core[c]),
            "x_own": np.ascontiguousarray(x1_by_core[c]),
            "s_in": np.ascontiguousarray(s1_by_core[c]),
            "bgr": bgrv,
            "wgt": wgtv, "w1r": w1r, "w2r": w2r, "b1c": b1cv, "b2c": b2cv,
            "eye128": e128, "eye8": e8,
        }
        maps.append(m)
    return maps


def build_fused(phases="all"):
    nc = bass.Bass("TRN2", target_bir_lowering=False, debug=False)
    E = lambda nm, sh, dt: nc.dram_tensor(nm, sh, dt, kind="ExternalInput").ap()
    xb = E("xb", [1024, 4096], F32)
    x_own = E("x_own", [1024, 2048], F32)
    w_ip = E("w_ip", [1024, 2240], F32)
    w_conv = E("w_conv", [128, 32], F32)
    b_conv = E("b_conv", [128, 8], F32)
    w_dt = E("w_dt", [64, 1024], F32)
    b_dt = E("b_dt", [128, 8], F32)
    a_log = E("a_log", [128, 64], F32)
    d_prm = E("d_prm", [128, 8], F32)
    w_op = E("w_op", [1024, 1024], F32)
    eye64 = E("eye64", [64, 64], BF16)
    cst = E("cst", [128, 4], F32)
    eye128 = E("eye128", [128, 128], BF16)
    eye8 = E("eye8", [8, 8], BF16)
    wgt0 = E("wgt0", [128, 64], F32)
    w1r0 = E("w1r0", [8, 32, 128, 1024], F32)
    w2r0 = E("w2r0", [8, 8, 128, 4096], F32)
    b1c0 = E("b1c0", [128, 256], F32)
    b2c0 = E("b2c0", [128, 64], F32)
    nwc = E("nwc", [128, 8], F32)
    w_xp = E("w_xp", [1024, 192], F32)
    w_dt1 = E("w_dt1", [64, 1024], F32)
    b_dt1 = E("b_dt1", [128, 8], F32)
    a1in = E("a1in", [128, 512], F32)
    d1c = E("d1c", [128, 8], F32)
    w_op1r = E("w_op1r", [8, 128, 1024], F32)
    wgt1 = E("wgt1", [128, 64], F32)
    w1r1 = E("w1r1", [8, 32, 128, 1024], F32)
    w2r1 = E("w2r1", [8, 8, 128, 4096], F32)
    b1c1 = E("b1c1", [128, 256], F32)
    b2c1 = E("b2c1", [128, 64], F32)
    bgr = E("bgr", [128, 8], F32)
    out = nc.dram_tensor("out", [1024, 2048], F32, kind="ExternalOutput").ap()
    cc_in = nc.dram_tensor("cc_in", [2048, 2048], F32).ap()
    cc_out = nc.dram_tensor("cc_out", [1024, 2048], F32).ap()
    xn_dr = nc.dram_tensor("xn_dr", [1024, 4096], BF16).ap()
    x1_dr = nc.dram_tensor("x1_dr", [1024, 2048], F32).ap()
    s1_dr = nc.dram_tensor("s1_dr", [1024, 2048], BF16).ap()
    xn1_dr = nc.dram_tensor("xn1_dr", [1024, 2048], BF16).ap()

    p = Prog()
    with contextlib.ExitStack() as es0:
        ioB = {"pref": "B_", "xb": xb, "w_ip": w_ip, "w_conv": w_conv, "b_conv": b_conv,
               "w_dt": w_dt, "b_dt": b_dt, "a_log": a_log, "d_prm": d_prm, "w_op": w_op,
               "eye64": eye64, "cst": cst, "xn_out": xn_dr,
               "spar_st": lambda dcol, t0: cc_in[
                   (t0 // 2048) * 1024 + dcol * 128:(t0 // 2048) * 1024 + (dcol + 1) * 128,
                   (t0 % 2048):(t0 % 2048) + TC]}
        _emit_B(nc, p, ioB)
        out_after_B = [("B_O_sp", p.v("B_O_sp"))]
        p.barrier()
        p.add("gpsimd", lambda g: g.collective_compute(
            "ReduceScatter", ALU.add, replica_groups=[[0, 1], [2, 3], [4, 5], [6, 7]],
            ins=[cc_in], outs=[cc_out]),
            inc=("CC", 1), wait=out_after_B)
        xnc_dr = nc.dram_tensor("xnc_dr", [1024, 2048], BF16).ap()
        w1bd0 = nc.dram_tensor("w1bd0", [8, 32, 128, 1024], BF16).ap()
        w2bd0 = nc.dram_tensor("w2bd0", [8, 8, 128, 4096], BF16).ap()
        ioC = {"pref": "C_", "lay": 0, "s_mode": "single_f32", "nst": 2,
               "ln_x": x_own, "ln_cst": cst,
               "xn_st": lambda k, t0: xnc_dr[k * 128:(k + 1) * 128, t0:t0 + 512],
               "xn_ld": lambda k, tm0: xnc_dr[k * 128:(k + 1) * 128, tm0:tm0 + TCM],
               "x_ld": lambda j, tm0: x_own[j * 128:(j + 1) * 128, tm0:tm0 + TCM],
               "s_ld": lambda j, tm0: cc_out[j * 128:(j + 1) * 128, tm0:tm0 + TCM],
               "s_wait": [("CC", 1)],
               "out_st": lambda j, tm0: x1_dr[j * 128:(j + 1) * 128, tm0:tm0 + TCM],
               "wgt": wgt0, "w1r": w1r0, "w2r": w2r0, "b1c": b1c0, "b2c": b2c0,
               "w1b_dr": w1bd0, "w2b_dr": w2bd0,
               "eye128": eye128, "eye8": eye8}
        if phases in ("p1", "p1x"):
            ioC["out_st"] = lambda j, tm0: out[j * 128:(j + 1) * 128, tm0:tm0 + TCM]
        if phases == "p1x":
            xn_dbg = nc.dram_tensor("xn_dbg", [1024, 2048], BF16, kind="ExternalInput").ap()
            ioC.pop("ln_x"); ioC.pop("ln_cst"); ioC.pop("xn_st")
            ioC["xn_ld"] = lambda k, tm0: xn_dbg[k * 128:(k + 1) * 128, tm0:tm0 + TCM]
        _emit_tail(nc, p, ioC)
        if phases == "p1x":
            finish(p, es0, nc)
            return nc
        if phases == "p1":
            finish(p, es0, nc)
            return nc
        p.barrier()
        ioD1 = {"pref": "E_", "x1in": x1_dr, "nwc": nwc, "w_xp": w_xp, "w_dt1": w_dt1,
                "b_dt1": b_dt1, "a1in": a1in, "d1c": d1c, "w_op1r": w_op1r,
                "eye64": eye64, "cst": cst, "s1o": s1_dr, "xn1o": xn1_dr}
        w1bd1 = nc.dram_tensor("w1bd1", [8, 32, 128, 1024], BF16).ap()
        w2bd1 = nc.dram_tensor("w2bd1", [8, 8, 128, 4096], BF16).ap()
        ioD2 = {"pref": "F_", "lay": 1, "s_mode": "single_bf16",
                "xn_ld": lambda k, tm0: xn1_dr[k * 128:(k + 1) * 128, tm0:tm0 + TCM],
                "x_ld": lambda j, tm0: x1_dr[j * 128:(j + 1) * 128, tm0:tm0 + TCM],
                "s_ld": lambda j, tm0: s1_dr[j * 128:(j + 1) * 128, tm0:tm0 + TCM],
                "out_st": lambda j, tm0: out[j * 128:(j + 1) * 128, tm0:tm0 + TCM],
                "bgr": bgr, "wgt": wgt1, "w1r": w1r1, "w2r": w2r1, "b1c": b1c1,
                "b2c": b2c1, "w1b_dr": w1bd1, "w2b_dr": w2bd1,
                "eye128": eye128, "eye8": eye8}
        d1g = _emit_D1_gen(nc, p, ioD1)
        d2g = _emit_tail_gen(nc, p, ioD2)
        r = next(d1g)
        assert r == "setup"
        r = next(d2g)
        assert r == "setup"
        r = next(d1g)
        assert r == ("ln", 0), r
        ioD2["xn_wait"] = [("E_O_xn1", p.v("E_O_xn1"))]
        r = next(d2g)
        assert r == ("gating", 0), r
        for w in range(1, 4):
            r = next(d1g)
            assert r == ("sg", 0, w), r
        r = next(d1g)
        assert r == ("mid", 0), r
        for c in range(4):
            if c + 1 < 4:
                r = next(d1g)
                assert r == ("ln", c + 1), r
                for w in range(1, 4):
                    r = next(d1g)
                    assert r == ("sg", c + 1, w), r
                    for e in (2 * (w - 1), 2 * w - 1):
                        r = next(d2g)
                        assert r == ("block", c, e), r
                r = next(d1g)
                assert r == ("mid", c + 1), r
                for e in (6, 7):
                    r = next(d2g)
                    assert r == ("block", c, e), r
            else:
                for e in range(8):
                    r = next(d2g)
                    assert r == ("block", c, e), r
            r = next(d1g)
            assert r == ("back", c), r
            ioD2["s_wait"] = [("E_O_sp0", p.v("E_O_sp0")), ("E_O_sp1", p.v("E_O_sp1"))]
            r = next(d2g)
            assert r == ("combine", c), r
            if c + 1 < 4:
                ioD2["xn_wait"] = [("E_O_xn1", p.v("E_O_xn1"))]
                r = next(d2g)
                assert r == ("gating", c + 1), r
        for r in d2g:
            pass
        for r in d1g:
            pass
        finish(p, es0, nc)
    return nc


def stage_fused(d):
    mapsB = stage_B(d)
    x = np.asarray(d["L0_attn_x"])
    w1 = np.asarray(d["L0_ffn_W1"]); w2 = np.asarray(d["L0_ffn_W2"])
    w1r0 = np.ascontiguousarray(w1.reshape(8, 8, 128, 32, 128).transpose(0, 3, 2, 1, 4).reshape(8, 32, 128, 1024))
    w2r0 = np.ascontiguousarray(w2.reshape(8, 32, 128, 8, 128).transpose(0, 3, 2, 1, 4).reshape(8, 8, 128, 4096))
    b1c0 = np.ascontiguousarray(np.asarray(d["L0_ffn_b1"]).reshape(8, 32, 128).transpose(2, 0, 1).reshape(128, 256))
    b2c0 = np.ascontiguousarray(np.asarray(d["L0_ffn_b2"]).reshape(8, 8, 128).transpose(2, 0, 1).reshape(128, 64))
    wgt0 = np.ascontiguousarray(np.asarray(d["L0_ffn_W_gate"]).reshape(8, 128, 8).transpose(1, 0, 2).reshape(128, 64))
    w11 = np.asarray(d["L1_ffn_W1"]); w21 = np.asarray(d["L1_ffn_W2"])
    w1r1 = np.ascontiguousarray(w11.reshape(8, 8, 128, 32, 128).transpose(0, 3, 2, 1, 4).reshape(8, 32, 128, 1024))
    w2r1 = np.ascontiguousarray(w21.reshape(8, 32, 128, 8, 128).transpose(0, 3, 2, 1, 4).reshape(8, 8, 128, 4096))
    b1c1 = np.ascontiguousarray(np.asarray(d["L1_ffn_b1"]).reshape(8, 32, 128).transpose(2, 0, 1).reshape(128, 256))
    b2c1 = np.ascontiguousarray(np.asarray(d["L1_ffn_b2"]).reshape(8, 8, 128).transpose(2, 0, 1).reshape(128, 64))
    wgt1 = np.ascontiguousarray(np.asarray(d["L1_ffn_Wg"]).reshape(8, 128, 8).transpose(1, 0, 2).reshape(128, 64))
    bgrv = np.ascontiguousarray(np.broadcast_to(np.asarray(d["L1_ffn_bg"])[None, :], (128, 8)))
    dtw1 = np.asarray(d["L1_attn_dt_proj_w"])
    opT1 = np.ascontiguousarray(np.asarray(d["L1_attn_out_proj_w"]).T)
    wop1r = np.ascontiguousarray(opT1.reshape(8, 128, 8, 128).transpose(2, 1, 0, 3).reshape(8, 128, 1024))
    cstv = np.ascontiguousarray(np.broadcast_to(
        np.array([1e-5, -0.5, 1e-6, 1e-8], np.float32)[None, :], (128, 4)))
    e128 = np.eye(128).astype(BF)
    e8 = np.eye(8).astype(BF)
    shared = {
        "eye128": e128, "eye8": e8,
        "wgt0": wgt0, "w1r0": w1r0, "w2r0": w2r0, "b1c0": b1c0, "b2c0": b2c0,
        "nwc": _col(np.asarray(d["L1_attn_norm_w"])),
        "w_xp": np.ascontiguousarray(np.asarray(d["L1_attn_x_proj_w"]).T),
        "w_dt1": np.ascontiguousarray(dtw1.T),
        "b_dt1": _col(np.asarray(d["L1_attn_dt_proj_b"])),
        "a1in": np.ascontiguousarray(np.asarray(d["L1_attn_A_log"]).reshape(8, 128, 64).transpose(1, 0, 2).reshape(128, 512)),
        "d1c": _col(np.asarray(d["L1_attn_D"])),
        "w_op1r": wop1r,
        "wgt1": wgt1, "w1r1": w1r1, "w2r1": w2r1, "b1c1": b1c1, "b2c1": b2c1,
        "bgr": bgrv,
    }
    maps = []
    for c in range(8):
        b, ih = c // 2, c % 2
        m = dict(mapsB[c])
        m["cst"] = cstv
        m["x_own"] = np.ascontiguousarray(x[b].T[:, ih * 2048:(ih + 1) * 2048])
        m.update(shared)
        maps.append(m)
    return maps


def kernel(**inputs):
    d = {k: np.asarray(v) for k, v in inputs.items()}
    nc = build_fused()
    res = run_bass_kernel_spmd(nc, stage_fused(d), core_ids=list(range(8)))
    out = np.empty((4, 4096, 1024), np.float32)
    for c in range(8):
        b, ih = c // 2, c % 2
        out[b, ih * 2048:(ih + 1) * 2048, :] = np.asarray(res.results[c]["out"]).T
    return out


# version 14
# speedup vs baseline: 1.0911x; 1.0665x over previous
import contextlib
from collections import defaultdict
import numpy as np
import ml_dtypes
import concourse.bass as bass
import concourse.mybir as mybir
from concourse.bass_utils import run_bass_kernel_spmd

F32 = mybir.dt.float32
BF16 = mybir.dt.bfloat16
AF = mybir.ActivationFunctionType
ALU = mybir.AluOpType
BF = ml_dtypes.bfloat16


class Prog:

    def __init__(self):
        self.ops = []
        self.cnt = defaultdict(int)

    def add(self, eng, fn, inc=None, wait=()):
        if inc is not None and isinstance(inc, str):
            inc = (inc, 1)
        wait = tuple((s, v) for (s, v) in wait if v > 0)
        self.ops.append((eng, fn, inc, wait))
        if inc:
            self.cnt[inc[0]] += inc[1]
            return self.cnt[inc[0]]
        return None

    def v(self, sem):
        return self.cnt[sem]

    def barrier(self):
        self.ops.append(("BARRIER", dict(self.cnt), None, ()))

    def emit(self, block, sems):
        def mk(eng_name):
            def run(eng):
                last = defaultdict(int)
                pend = None
                for e, fn, inc, waits in self.ops:
                    if e == "BARRIER":
                        pend = fn
                        continue
                    if e != eng_name:
                        continue
                    if pend is not None:
                        waits = list(waits) + [(s, v) for s, v in pend.items()]
                        pend = None
                    for s, val in waits:
                        if val > last[s]:
                            eng.wait_ge(sems[s], val)
                            last[s] = val
                    instr = fn(eng)
                    if inc is not None:
                        assert instr is not None, f"op on {e} returned None but has inc {inc}"
                        instr.then_inc(sems[inc[0]], inc[1])
            return run
        block.sync(mk("sync"))
        block.scalar(mk("scalar"))
        block.vector(mk("vector"))
        block.gpsimd(mk("gpsimd"))
        block.tensor(mk("tensor"))


def finish(p, es, nc):
    names = sorted(set(p.cnt.keys()) | {"PE", "ACT", "DVE", "POOL"})
    final = [(n, p.cnt[n]) for n in names if ("O_" in n or n == "OUT") and p.cnt[n] > 0]
    sems = {n: es.enter_context(nc.semaphore(f"sem{i}")) for i, n in enumerate(names)}
    def fin(g):
        ins = None
        for n, v in final:
            ins = g.wait_ge(sems[n], v)
        return ins
    p.add("gpsimd", fin)
    block = es.enter_context(nc.Block())
    p.emit(block, sems)


TC = 512
NCH = 4096 // TC
KT = 8
IT = 8
SG = 4
NSG = 64 // SG


def build_B():
    nc = bass.Bass("TRN2", target_bir_lowering=False, debug=False)
    io = {}
    for nm, sh, dt in [("xb", [1024, 4096], F32), ("w_ip", [1024, 2240], F32),
                       ("w_conv", [128, 32], F32), ("b_conv", [128, 8], F32),
                       ("w_dt", [64, 1024], F32), ("b_dt", [128, 8], F32),
                       ("a_log", [128, 64], F32), ("d_prm", [128, 8], F32),
                       ("w_op", [1024, 1024], F32), ("eye64", [64, 64], BF16),
                       ("cst", [128, 4], F32)]:
        io[nm] = nc.dram_tensor(nm, sh, dt, kind="ExternalInput").ap()
    s_par = nc.dram_tensor("s_par", [1024, 4096], F32, kind="ExternalOutput").ap()
    io["xn_out"] = nc.dram_tensor("xn_out", [1024, 4096], BF16, kind="ExternalOutput").ap()
    io["spar_st"] = lambda dcol, t0: s_par[dcol * 128:(dcol + 1) * 128, t0:t0 + TC]
    p = Prog()
    with contextlib.ExitStack() as es0:
        _emit_B(nc, p, io)
        finish(p, es0, nc)
    return nc


def _emit_B(nc, p, io):
    for _ in _emit_B_gen(nc, p, io):
        pass


def _emit_B_gen(nc, p, io):
    xb, w_ip, w_conv, b_conv = io["xb"], io["w_ip"], io["w_conv"], io["b_conv"]
    w_dt, b_dt, a_log, d_prm = io["w_dt"], io["b_dt"], io["a_log"], io["d_prm"]
    w_op, eye64, cst, xn_out = io["w_op"], io["eye64"], io["cst"], io["xn_out"]
    spar_st = io["spar_st"]
    PF = io.get("pref", "")
    with contextlib.ExitStack() as es:
        _pf = io.get("pref", "")
        T = lambda name, shape, dt: es.enter_context(nc.sbuf_tensor(_pf + name, shape, dt))
        PT = lambda name, shape, dt: es.enter_context(nc.psum_tensor(_pf + name, shape, dt))
        wip = T("wip", [128, KT * 2240], BF16)
        wop = T("wop", [128, KT * 1024], BF16)
        wdt = T("wdtb", [64, 1024], BF16)
        wcv = T("wcv", [128, 32], F32)
        bcv = T("bcv", [128, 8], F32)
        bdt = T("bdtb", [128, 8], F32)
        dpr = T("dpr", [128, 8], F32)
        arep = T("arep", [128, 64], F32)
        eyes = T("eyes", [64, 64], BF16)
        ones_c = T("ones_c", [128, 1], F32)
        ones_r = T("ones_r", [1, 128], F32)
        wst = T("wst", [128, 1120], F32)
        cstt = T("cstt", [128, 4], F32)
        wdst = T("wdst", [64, 512], F32)
        xfs = [T(f"xf{i}", [128, TC], F32) for i in range(2)]
        xsq = [T(f"xsq{i}", [128, TC], F32) for i in range(2)]
        nm = T("nm", [1, TC], F32)
        m2 = T("m2", [1, TC], F32)
        var = T("var", [1, TC], F32)
        rs = T("rs", [1, TC], F32)
        nmr = T("nmr", [1, TC], F32)
        xnt = [T(f"xnt{i}", [128, TC], F32) for i in range(1)]
        xn = T("xn", [128, KT * TC], BF16)
        xin = T("xin", [128, IT * (TC + 3)], BF16)
        zb = T("zb", [128, IT * TC], BF16)
        bcb = T("bcb", [64, TC], BF16)
        ccb = T("ccb", [64, TC], BF16)
        drw = T("drw", [64, TC], BF16)
        spv = [T(f"spv{i}", [128, TC], F32) for i in range(1)]
        sav = [T(f"sav{i}", [128, TC], F32) for i in range(1)]
        delta = T("delta", [128, IT * TC], BF16)
        xc = T("xc", [128, IT * TC], BF16)
        cacc = [T(f"cacc{i}", [128, TC], F32) for i in range(1)]
        dxc = T("dxc", [128, IT * TC], BF16)
        brep = T("brep", [128, SG * TC], BF16)
        crep = T("crep", [128, SG * TC], BF16)
        dab = [T(f"dab{i}", [128, SG * TC], F32) for i in range(1)]
        ub = [T(f"ub{i}", [128, SG * TC], BF16) for i in range(1)]
        hb = [T(f"hb{i}", [128, SG * TC], BF16) for i in range(2)]
        wbf = [T(f"wbf{i}", [128, SG * TC], BF16) for i in range(1)]
        t1p = [T(f"t1p{i}", [128, TC], BF16) for i in range(1)]
        t2p = [T(f"t2p{i}", [128, TC], BF16) for i in range(1)]
        ysg = [T(f"ysg{i}", [128, TC], F32) for i in range(1)]
        yacc = T("yacc", [128, IT * TC], F32)
        hstate = T("hstate", [128, IT * 64], F32)
        szb = [T(f"szb{i}", [128, TC], BF16) for i in range(1)]
        y1f = [T(f"y1f{i}", [128, TC], F32) for i in range(1)]
        ypp = T("ypp", [128, IT * TC], BF16)
        spo = [T(f"spo{i}", [128, TC], F32) for i in range(1)]
        pa = PT("pa", [128, TC], F32)
        pb = PT("pb", [128, TC], F32)
        brp = PT("brp", [128, SG * TC], F32)
        mps = PT("mps", [1, TC], F32)
        sps = PT("sps", [1, TC], F32)
        pst = [pa, pb]

        A = p.add
        V = p.v

        A("vector", lambda v: v.memset(ones_c[:], 1.0), inc="DVE")
        A("vector", lambda v: v.memset(ones_r[:], 1.0), inc="DVE")
        A("vector", lambda v: v.memset(hstate[:], 0.0), inc="DVE")
        for k in range(IT):
            A("vector", lambda v, k=k: v.memset(xin[:, k * (TC + 3):k * (TC + 3) + 3], 0.0), inc="DVE")
        A("sync", lambda s: s.dma_start(out=eyes[:], in_=eye64), inc=(PF + "L_stp", 16))
        A("sync", lambda s: s.dma_start(out=cstt[:], in_=cst), inc=(PF + "L_stp", 16))
        A("sync", lambda s: s.dma_start(out=arep[:], in_=a_log), inc=(PF + "L_stp", 16))
        A("sync", lambda s: s.dma_start(out=bcv[:], in_=b_conv), inc=(PF + "L_stp", 16))
        A("sync", lambda s: s.dma_start(out=bdt[:], in_=b_dt), inc=(PF + "L_stp", 16))
        A("sync", lambda s: s.dma_start(out=dpr[:], in_=d_prm), inc=(PF + "L_stp", 16))
        A("sync", lambda s: s.dma_start(out=wcv[:], in_=w_conv), inc=(PF + "L_stp", 16))
        ld0v = V(PF + "L_stp")
        ld0 = (PF + "L_stp", ld0v)
        A("scalar", lambda s: s.activation(out=arep[:], in_=arep[:], func=AF.Exp),
          inc="ACT", wait=[ld0])
        A("vector", lambda v: v.tensor_scalar_mul(out=arep[:], in0=arep[:], scalar1=-1.0),
          inc="DVE", wait=[("ACT", V("ACT"))])
        for h in range(2):
            w_cast = V("DVE")
            A("sync", lambda s, h=h: s.dma_start(out=wdst[:], in_=w_dt[:, h * 512:(h + 1) * 512]),
              inc=(PF + "L_wdst", 16), wait=[("DVE", w_cast)])
            A("vector", lambda v, h=h: v.tensor_copy(out=wdt[:, h * 512:(h + 1) * 512], in_=wdst[:]),
              inc="DVE", wait=[(PF + "L_wdst", V(PF + "L_wdst"))])
        for k in range(KT):
            for h in range(2):
                c0, c1 = h * 1120, min(2240, (h + 1) * 1120)
                w_cast = V("ACT")
                A("sync", lambda s, k=k, c0=c0, c1=c1: s.dma_start(
                    out=wst[:, :c1 - c0], in_=w_ip[k * 128:(k + 1) * 128, c0:c1]),
                  inc=(PF + "L_wst", 16), wait=[("ACT", w_cast)])
                A("scalar", lambda s, k=k, c0=c0, c1=c1: s.copy(
                    out=wip[:, k * 2240 + c0:k * 2240 + c1], in_=wst[:, :c1 - c0]),
                  inc="ACT", wait=[(PF + "L_wst", V(PF + "L_wst"))])
        for k in range(KT):
            w_cast = V("ACT")
            A("sync", lambda s, k=k: s.dma_start(out=wst[:, :1024], in_=w_op[k * 128:(k + 1) * 128, :]),
              inc=(PF + "L_wst", 16), wait=[("ACT", w_cast)])
            A("scalar", lambda s, k=k: s.copy(out=wop[:, k * 1024:(k + 1) * 1024], in_=wst[:, :1024]),
              inc="ACT", wait=[(PF + "L_wst", V(PF + "L_wst"))])
        setup_act = V("ACT")
        setup_dve = V("DVE")

        war = defaultdict(tuple)
        yield "setup"

        for c in range(NCH):
            t0 = c * TC
            for k in range(KT):
                par = k % 2
                A("sync", lambda s, k=k, par=par, t0=t0: s.dma_start(
                    out=xfs[par][:], in_=xb[k * 128:(k + 1) * 128, t0:t0 + TC]),
                  inc=(PF + f"L_xf{par}", 16), wait=war[("xf", par)])
                ldw = (PF + f"L_xf{par}", V(PF + f"L_xf{par}"))
                A("scalar", lambda s, par=par: s.activation(out=xsq[par][:], in_=xfs[par][:], func=AF.Square),
                  inc="ACT", wait=[ldw] + list(war[("xsq", par)]))
                actv = V("ACT")
                A("tensor", lambda t, k=k, par=par: t.matmul(
                    out=mps[:], lhsT=ones_c[:], rhs=xfs[par][:], start=(k == 0), stop=(k == KT - 1)),
                  wait=([ldw] + list(war[("mps",)])) if k == 0 else [ldw])
                A("tensor", lambda t, k=k, par=par: t.matmul(
                    out=sps[:], lhsT=ones_c[:], rhs=xsq[par][:], start=(k == 0), stop=(k == KT - 1)),
                  inc="PE",
                  wait=[("ACT", actv)])
                war[("xf", par)] = [("PE", V("PE"))]
                war[("xsq", par)] = [("PE", V("PE"))]
            pe_stats = V("PE")
            A("vector", lambda v: v.tensor_scalar_mul(out=nm[:], in0=mps[:], scalar1=-1.0 / 1024.0),
              inc="DVE", wait=[("PE", pe_stats)])
            A("vector", lambda v: v.tensor_mul(out=m2[:], in0=nm[:], in1=nm[:]), inc="DVE")
            A("vector", lambda v: v.scalar_tensor_tensor(
                out=var[:], in0=sps[:], scalar=1.0 / 1024.0, in1=m2[:], op0=ALU.mult, op1=ALU.subtract),
              inc="DVE", wait=[("PE", pe_stats)])
            dve_var = V("DVE")
            war[("mps",)] = [("DVE", dve_var)]
            A("scalar", lambda s: s.activation(out=rs[:], in_=var[:], func=AF.Ln, bias=cstt[0:1, 0:1]),
              inc="ACT", wait=[("DVE", dve_var)])
            A("scalar", lambda s: s.activation(out=rs[:], in_=rs[:], func=AF.Exp, scale=cstt[0:1, 1:2]), inc="ACT")
            act_rs = V("ACT")
            A("vector", lambda v: v.tensor_mul(out=nmr[:], in0=nm[:], in1=rs[:]),
              inc="DVE", wait=[("ACT", act_rs)])
            dve_nmr = V("DVE")
            A("tensor", lambda t: t.matmul(out=pa[:], lhsT=ones_r[:], rhs=rs[:], start=True, stop=True),
              wait=[("ACT", act_rs)] + list(war[("pa",)]))
            A("tensor", lambda t: t.matmul(out=pb[:], lhsT=ones_r[:], rhs=nmr[:], start=True, stop=True),
              inc="PE", wait=[("DVE", dve_nmr)] + list(war[("pb",)]))
            pe_bc = V("PE")
            for k in range(KT):
                par = k % 2
                A("sync", lambda s, k=k, par=par, t0=t0: s.dma_start(
                    out=xfs[par][:], in_=xb[k * 128:(k + 1) * 128, t0:t0 + TC]),
                  inc=(PF + f"L_xf{par}", 16), wait=war[("xf", par)])
                ldw = (PF + f"L_xf{par}", V(PF + f"L_xf{par}"))
                A("vector", lambda v, par=par: v.tensor_mul(out=xnt[0][:], in0=xfs[par][:], in1=pa[:]),
                  inc="DVE", wait=[ldw, ("PE", pe_bc)])
                A("vector", lambda v, k=k, par=par: v.tensor_add(
                    out=xn[:, k * TC:(k + 1) * TC], in0=xnt[0][:], in1=pb[:]),
                  inc="DVE", wait=war[("xnk", k)])
                dv = V("DVE")
                war[("xf", par)] = [("DVE", dv)]
                A("gpsimd", lambda g, k=k, t0=t0: g.dma_start(
                    out=xn_out[k * 128:(k + 1) * 128, t0:t0 + TC], in_=xn[:, k * TC:(k + 1) * TC]),
                  inc=(PF + "O_xn", 16), wait=[("DVE", dv)])
            oxn_end = V(PF + "O_xn")
            for k in range(KT):
                war[("xnk", k)] = [(PF + "O_xn", oxn_end)]
            dve_xn = V("DVE")
            war[("pa",)] = [("DVE", dve_xn)]
            war[("pb",)] = [("DVE", dve_xn)]

            groups = [("xin", j) for j in range(IT)] + [("z", j) for j in range(IT)] + [("bc", 0), ("draw", 0)]
            for gi, (gname, j) in enumerate(groups):
                ps = pst[gi % 2]
                pkey = ("pa",) if gi % 2 == 0 else ("pb",)
                col0 = {"xin": j * 128, "z": 1024 + j * 128, "bc": 2048, "draw": 2176}[gname]
                npart = 64 if gname == "draw" else 128
                for k in range(KT):
                    A("tensor", lambda t, k=k, col0=col0, ps=ps, npart=npart: t.matmul(
                        out=ps[:npart, :], lhsT=wip[:, k * 2240 + col0:k * 2240 + col0 + npart],
                        rhs=xn[:, k * TC:(k + 1) * TC], start=(k == 0), stop=(k == KT - 1)),
                      inc="PE",
                      wait=(list(war[pkey]) + [("ACT", setup_act), ("DVE", dve_xn)]) if k == 0 else [])
                pev = V("PE")
                if gname == "xin":
                    A("scalar", lambda s, j=j, ps=ps: s.copy(
                        out=xin[:, j * (TC + 3) + 3:(j + 1) * (TC + 3)], in_=ps[:]),
                      inc="ACT", wait=[("PE", pev)] + list(war[("xin", j)]))
                elif gname == "z":
                    A("scalar", lambda s, j=j, ps=ps: s.copy(out=zb[:, j * TC:(j + 1) * TC], in_=ps[:]),
                      inc="ACT", wait=[("PE", pev)] + list(war[("z", j)]))
                elif gname == "bc":
                    A("scalar", lambda s, ps=ps: s.copy(out=bcb[:], in_=ps[0:64, :]),
                      inc="ACT", wait=[("PE", pev)] + list(war[("bcb",)]))
                    A("scalar", lambda s, ps=ps: s.copy(out=ccb[:], in_=ps[64:128, :]),
                      inc="ACT")
                else:
                    A("scalar", lambda s, ps=ps: s.copy(out=drw[:], in_=ps[:64, :]),
                      inc="ACT", wait=[("PE", pev)] + list(war[("drw",)]))
                war[pkey] = [("ACT", V("ACT"))]
            act_front = V("ACT")

            for j in range(IT):
                par = 0
                ps = pst[j % 2]
                pkey = ("pa",) if j % 2 == 0 else ("pb",)
                A("tensor", lambda t, j=j, ps=ps: t.matmul(
                    out=ps[:], lhsT=wdt[:, j * 128:(j + 1) * 128], rhs=drw[:],
                    start=True, stop=True),
                  inc="PE",
                  wait=[("ACT", act_front), ("DVE", setup_dve)] + list(war[pkey]))
                pev = V("PE")
                A("vector", lambda v, j=j, par=par, ps=ps: v.tensor_scalar_add(
                    out=spv[par][:], in0=ps[:], scalar1=bdt[:, j:j + 1]),
                  inc="DVE", wait=[("PE", pev)] + list(war[("spv", par)]))
                dve_v = V("DVE")
                war[pkey] = [("DVE", dve_v)]
                A("scalar", lambda s, par=par: s.activation(out=sav[par][:], in_=spv[par][:], func=AF.Abs),
                  inc="ACT", wait=[("DVE", dve_v)] + list(war[("sav", par)]))
                A("scalar", lambda s, par=par: s.activation(out=sav[par][:], in_=sav[par][:], func=AF.Exp, scale=-1.0),
                  inc="ACT")
                A("scalar", lambda s, par=par: s.activation(out=sav[par][:], in_=sav[par][:], func=AF.Ln, bias=1.0),
                  inc="ACT")
                act_sp = V("ACT")
                A("vector", lambda v, j=j, par=par: v.scalar_tensor_tensor(
                    out=delta[:, j * TC:(j + 1) * TC], in0=spv[par][:], scalar=0.0, in1=sav[par][:],
                    op0=ALU.max, op1=ALU.add),
                  inc="DVE", wait=[("ACT", act_sp)] + list(war[("delta", j)]))
                war[("spv", par)] = [("ACT", act_sp)]
                war[("sav", par)] = [("DVE", V("DVE"))]

            for j in range(IT):
                par = 0
                x0 = j * (TC + 3)
                A("vector", lambda v, j=j, par=par, x0=x0: v.tensor_scalar_mul(
                    out=cacc[par][:], in0=xin[:, x0:x0 + TC], scalar1=wcv[:, j * 4:j * 4 + 1]),
                  inc="DVE", wait=[("ACT", act_front)] + list(war[("cacc", par)]))
                for kk in range(1, 4):
                    A("vector", lambda v, par=par, x0=x0, kk=kk, j=j: v.scalar_tensor_tensor(
                        out=cacc[par][:], in0=xin[:, x0 + kk:x0 + kk + TC],
                        scalar=wcv[:, j * 4 + kk:j * 4 + kk + 1], in1=cacc[par][:],
                        op0=ALU.mult, op1=ALU.add),
                      inc="DVE")
                dve_c = V("DVE")
                A("scalar", lambda s, j=j, par=par: s.activation(
                    out=xc[:, j * TC:(j + 1) * TC], in_=cacc[par][:], func=AF.Silu,
                    bias=bcv[:, j:j + 1]),
                  inc="ACT", wait=[("DVE", dve_c)] + list(war[("xck", j)]))
                act_xc = V("ACT")
                war[("cacc", par)] = [("ACT", act_xc)]
                A("vector", lambda v, x0=x0: v.tensor_copy(
                    out=xin[:, x0:x0 + 3], in_=xin[:, x0 + TC:x0 + TC + 3]),
                  inc="DVE")
                war[("xin", j)] = [("DVE", V("DVE"))]
                A("vector", lambda v, j=j: v.tensor_mul(
                    out=dxc[:, j * TC:(j + 1) * TC], in0=delta[:, j * TC:(j + 1) * TC],
                    in1=xc[:, j * TC:(j + 1) * TC]),
                  inc="DVE", wait=[("ACT", act_xc)])
            dve_front = V("DVE")

            A("gpsimd", lambda g: g.memset(yacc[:], 0.0), inc="POOL", wait=war[("yacc",)])
            pool_ymem = V("POOL")

            act_cr = 0
            for sg in range(NSG):
                for sl in range(SG):
                    s_ = sg * SG + sl
                    A("tensor", lambda t, s_=s_, sl=sl: t.matmul(
                        out=brp[:, sl * TC:(sl + 1) * TC],
                        lhsT=eyes[:, s_:s_ + 1].broadcast_to([64, 128]),
                        rhs=bcb[:], start=True, stop=True),
                      inc="PE",
                      wait=([("ACT", act_front)] + list(war[("brp",)])) if sl == 0 else [])
                pe_b = V("PE")
                A("scalar", lambda s: s.copy(out=brep[:], in_=brp[:]),
                  inc="ACT", wait=[("PE", pe_b)] + list(war[("brep",)]))
                act_b = V("ACT")
                war[("brp",)] = [("ACT", act_b)]

                for j in range(IT):
                    par = 0
                    hpar = j % 2
                    for sl in range(SG):
                        s_ = sg * SG + sl
                        A("scalar", lambda s, j=j, par=par, sl=sl, s_=s_: s.activation(
                            out=dab[par][:, sl * TC:(sl + 1) * TC], in_=delta[:, j * TC:(j + 1) * TC],
                            func=AF.Exp, scale=arep[:, s_:s_ + 1]),
                          inc="ACT" if sl == SG - 1 else None,
                          wait=(list(war[("dab", par)]) + [("DVE", dve_front), ("DVE", setup_dve)]) if sl == 0 else [])
                    act_da = V("ACT")
                    A("vector", lambda v, j=j, par=par: v.tensor_mul(
                        out=ub[par][:].rearrange("p (s t) -> p s t", s=SG),
                        in0=dxc[:, j * TC:(j + 1) * TC].unsqueeze(1).broadcast_to([128, SG, TC]),
                        in1=brep[:].rearrange("p (s t) -> p s t", s=SG)),
                      inc="DVE", wait=[("ACT", act_b)])
                    for sl in range(SG):
                        s_ = sg * SG + sl
                        init = 0.0 if c == 0 else hstate[:, j * 64 + s_:j * 64 + s_ + 1]
                        A("vector", lambda v, par=par, hpar=hpar, sl=sl, init=init: v.tensor_tensor_scan(
                            out=hb[hpar][:, sl * TC:(sl + 1) * TC],
                            data0=dab[par][:, sl * TC:(sl + 1) * TC],
                            data1=ub[par][:, sl * TC:(sl + 1) * TC],
                            initial=init, op0=ALU.mult, op1=ALU.add),
                          inc="DVE" if sl == SG - 1 else None,
                          wait=([("ACT", act_da)] + list(war[("hb", hpar)]) + list(war[("hst", j)])) if sl == 0 else [])
                    dve_scan = V("DVE")
                    war[("dab", par)] = [("DVE", dve_scan)]
                    A("gpsimd", lambda g, j=j, hpar=hpar, sg=sg: g.tensor_copy(
                        out=hstate[:, j * 64 + sg * SG:j * 64 + (sg + 1) * SG],
                        in_=hb[hpar][:, TC - 1::TC]),
                      inc="POOL", wait=[("DVE", dve_scan)])
                    war[("hst", j)] = [("POOL", V("POOL"))]
                    if j == 0:
                        for sl in range(SG):
                            s_ = sg * SG + sl
                            A("tensor", lambda t, s_=s_, sl=sl: t.matmul(
                                out=brp[:, sl * TC:(sl + 1) * TC],
                                lhsT=eyes[:, s_:s_ + 1].broadcast_to([64, 128]),
                                rhs=ccb[:], start=True, stop=True),
                              inc="PE",
                              wait=([("ACT", act_b)] + list(war[("brp",)])) if sl == 0 else [])
                        pe_cr = V("PE")
                        A("scalar", lambda s: s.copy(out=crep[:], in_=brp[:]),
                          inc="ACT", wait=[("PE", pe_cr)] + list(war[("crep",)]))
                        act_cr = V("ACT")
                        war[("brp",)] = [("ACT", act_cr)]
                    A("vector", lambda v, par=par, hpar=hpar: v.tensor_mul(
                        out=wbf[par][:], in0=hb[hpar][:], in1=crep[:]),
                      inc="DVE", wait=[("ACT", act_cr)] + list(war[("wbf", par)]))
                    dve_w = V("DVE")
                    war[("hb", hpar)] = [("POOL", p.cnt["POOL"]), ("DVE", dve_w)]
                    A("gpsimd", lambda g, par=par: g.tensor_add(
                        out=t1p[par][:], in0=wbf[par][:, 0:TC], in1=wbf[par][:, TC:2 * TC]),
                      inc="POOL", wait=[("DVE", dve_w)])
                    A("gpsimd", lambda g, par=par: g.tensor_add(
                        out=t2p[par][:], in0=wbf[par][:, 2 * TC:3 * TC], in1=wbf[par][:, 3 * TC:4 * TC]),
                      inc="POOL")
                    A("gpsimd", lambda g, par=par: g.tensor_add(
                        out=ysg[par][:], in0=t1p[par][:], in1=t2p[par][:]),
                      inc="POOL")
                    A("gpsimd", lambda g, j=j, par=par: g.tensor_add(
                        out=yacc[:, j * TC:(j + 1) * TC], in0=yacc[:, j * TC:(j + 1) * TC],
                        in1=ysg[par][:]),
                      inc="POOL", wait=[("POOL", pool_ymem)])
                    pool_acc = V("POOL")
                    war[("wbf", par)] = [("POOL", pool_acc)]
                    war[("yacck", j)] = [("POOL", pool_acc)]
                war[("brep",)] = [("DVE", V("DVE"))]
                war[("crep",)] = [("DVE", V("DVE"))]
            war[("bcb",)] = [("PE", V("PE"))]
            war[("drw",)] = [("PE", V("PE"))]
            for j in range(IT):
                war[("delta", j)] = [("ACT", V("ACT")), ("DVE", V("DVE"))]

            for j in range(IT):
                par = 0
                A("scalar", lambda s, j=j, par=par: s.activation(
                    out=szb[par][:], in_=zb[:, j * TC:(j + 1) * TC], func=AF.Silu),
                  inc="ACT", wait=list(war[("szb", par)]))
                act_sz = V("ACT")
                A("vector", lambda v, j=j, par=par: v.scalar_tensor_tensor(
                    out=y1f[par][:], in0=xc[:, j * TC:(j + 1) * TC], scalar=dpr[:, j:j + 1],
                    in1=yacc[:, j * TC:(j + 1) * TC], op0=ALU.mult, op1=ALU.add),
                  inc="DVE", wait=list(war[("yacck", j)]) + list(war[("y1", par)]))
                A("vector", lambda v, j=j, par=par: v.tensor_mul(
                    out=ypp[:, j * TC:(j + 1) * TC], in0=y1f[par][:], in1=szb[par][:]),
                  inc="DVE", wait=[("ACT", act_sz)] + list(war[("yppk", j)]))
                dve_ypp = V("DVE")
                war[("y1", par)] = [("DVE", dve_ypp)]
                war[("szb", par)] = [("DVE", dve_ypp)]
                war[("z", j)] = [("ACT", act_sz)]
                war[("xck", j)] = [("DVE", dve_ypp)]
                war[("yacc",)] = [("DVE", dve_ypp)]
            dve_yppall = V("DVE")
            for dcol in range(8):
                ps = pst[dcol % 2]
                pkey = ("pa",) if dcol % 2 == 0 else ("pb",)
                par = 0
                for k in range(IT):
                    A("tensor", lambda t, dcol=dcol, k=k, ps=ps: t.matmul(
                        out=ps[:], lhsT=wop[:, k * 1024 + dcol * 128:k * 1024 + dcol * 128 + 128],
                        rhs=ypp[:, k * TC:(k + 1) * TC], start=(k == 0), stop=(k == IT - 1)),
                      inc="PE",
                      wait=([("DVE", dve_yppall), ("ACT", setup_act)] + list(war[pkey])) if k == 0 else [])
                pev = V("PE")
                A("scalar", lambda s, par=par, ps=ps: s.copy(out=spo[par][:], in_=ps[:]),
                  inc="ACT", wait=[("PE", pev)] + list(war[("spo", par)]))
                actv = V("ACT")
                war[pkey] = [("ACT", actv)]
                A("gpsimd", lambda g, dcol=dcol, par=par, t0=t0: g.dma_start(
                    out=spar_st(dcol, t0), in_=spo[par][:]),
                  inc=(PF + "O_sp", 16), wait=[("ACT", actv)])
                war[("spo", par)] = [(PF + "O_sp", V(PF + "O_sp"))]
            pe_end = V("PE")
            for j in range(IT):
                war[("yppk", j)] = [("PE", pe_end)]
            for k in range(KT):
                war[("xnk", k)] = list(war[("xnk", k)]) + [("PE", pe_end)]
            war[("spo", 0)] = list(war[("spo", 0)])
            yield ("chunk", c)


def _col(a):
    return np.ascontiguousarray(np.asarray(a).reshape(8, 128).T)


def stage_B(d):
    x = np.asarray(d["L0_attn_x"])
    wipT = np.ascontiguousarray(np.asarray(d["L0_attn_in_proj_w"]).T)
    opT = np.ascontiguousarray(np.asarray(d["L0_attn_out_proj_w"]).T)
    dtw = np.asarray(d["L0_attn_dt_proj_w"])
    eye = np.eye(64).astype(BF)
    maps = []
    for c in range(8):
        b, ih = c // 2, c % 2
        sl = slice(ih * 1024, (ih + 1) * 1024)
        m = {
            "xb": np.ascontiguousarray(x[b].T),
            "w_ip": np.ascontiguousarray(np.concatenate(
                [wipT[:, ih * 1024:(ih + 1) * 1024],
                 wipT[:, 2048 + ih * 1024:2048 + (ih + 1) * 1024],
                 wipT[:, 4096:4288]], axis=1)),
            "w_conv": np.ascontiguousarray(
                np.asarray(d["L0_attn_conv_w"])[sl, 0, :].reshape(8, 128, 4).transpose(1, 0, 2).reshape(128, 32)),
            "b_conv": _col(d["L0_attn_conv_b"][sl]),
            "w_dt": np.ascontiguousarray(dtw[sl].T),
            "b_dt": _col(d["L0_attn_dt_proj_b"][sl]),
            "a_log": np.ascontiguousarray(np.broadcast_to(
                np.asarray(d["L0_attn_A_log"])[None, :], (128, 64))),
            "d_prm": _col(d["L0_attn_D_param"][sl]),
            "w_op": np.ascontiguousarray(opT[sl, :]),
            "eye64": eye,
            "cst": np.ascontiguousarray(np.broadcast_to(
                np.array([1e-5, -0.5, 0.0, 0.0], np.float32)[None, :], (128, 4))),
        }
        maps.append(m)
    return maps


TCM = 512
NTCM = 2048 // TCM
NTT = 16


def build_C():
    return build_tail(0)


def build_D2():
    return build_tail(1)


def build_tail(lay=0):
    nc = bass.Bass("TRN2", target_bir_lowering=False, debug=False)
    io = {"lay": lay, "s_mode": "dual" if lay == 0 else "single_bf16"}
    xn_in = nc.dram_tensor("xn_in", [1024, 2048], BF16, kind="ExternalInput").ap()
    x_own = nc.dram_tensor("x_own", [1024, 2048], F32, kind="ExternalInput").ap()
    if lay == 0:
        s_a = nc.dram_tensor("s_a", [1024, 2048], F32, kind="ExternalInput").ap()
        s_b = nc.dram_tensor("s_b", [1024, 2048], F32, kind="ExternalInput").ap()
        io["sa_ld"] = lambda j, tm0: s_a[j * 128:(j + 1) * 128, tm0:tm0 + TCM]
        io["sb_ld"] = lambda j, tm0: s_b[j * 128:(j + 1) * 128, tm0:tm0 + TCM]
    else:
        s_in = nc.dram_tensor("s_in", [1024, 2048], BF16, kind="ExternalInput").ap()
        io["bgr"] = nc.dram_tensor("bgr", [128, 8], F32, kind="ExternalInput").ap()
        io["s_ld"] = lambda j, tm0: s_in[j * 128:(j + 1) * 128, tm0:tm0 + TCM]
    for nm, sh, dt in [("wgt", [128, 64], F32), ("w1r", [8, 32, 128, 1024], F32),
                       ("w2r", [8, 8, 128, 4096], F32), ("b1c", [128, 256], F32),
                       ("b2c", [128, 64], F32), ("eye128", [128, 128], BF16),
                       ("eye8", [8, 8], BF16)]:
        io[nm] = nc.dram_tensor(nm, sh, dt, kind="ExternalInput").ap()
    x1 = nc.dram_tensor("x1", [1024, 2048], F32, kind="ExternalOutput").ap()
    io["xn_ld"] = lambda k, tm0: xn_in[k * 128:(k + 1) * 128, tm0:tm0 + TCM]
    io["x_ld"] = lambda j, tm0: x_own[j * 128:(j + 1) * 128, tm0:tm0 + TCM]
    io["out_st"] = lambda j, tm0: x1[j * 128:(j + 1) * 128, tm0:tm0 + TCM]
    io["w1b_dr"] = nc.dram_tensor("w1b_dr", [8, 32, 128, 1024], BF16).ap()
    io["w2b_dr"] = nc.dram_tensor("w2b_dr", [8, 8, 128, 4096], BF16).ap()
    p = Prog()
    with contextlib.ExitStack() as es0:
        _emit_tail(nc, p, io)
        finish(p, es0, nc)
    return nc


def _emit_tail(nc, p, io):
    for _ in _emit_tail_gen(nc, p, io):
        pass


def _emit_tail_gen(nc, p, io):
    lay = io["lay"]
    s_mode = io["s_mode"]
    wgt, b1c, b2c = io["wgt"], io["b1c"], io["b2c"]
    w1r, w2r = io["w1r"], io["w2r"]
    w1bd, w2bd = io["w1b_dr"], io["w2b_dr"]
    eye128, eye8 = io["eye128"], io["eye8"]
    xn_ld, x_ld, out_st = io.get("xn_ld"), io["x_ld"], io["out_st"]
    PF = io.get("pref", "")
    with contextlib.ExitStack() as es:
        _pf = io.get("pref", "")
        T = lambda name, shape, dt: es.enter_context(nc.sbuf_tensor(_pf + name, shape, dt))
        PT = lambda name, shape, dt: es.enter_context(nc.psum_tensor(_pf + name, shape, dt))
        wgb = T("wgb", [128, 64], BF16)
        wgf = T("wgf", [128, 64], F32)
        b1s = T("b1s", [128, 256], F32)
        b2s = T("b2s", [128, 64], F32)
        id128 = T("id128", [128, 128], BF16)
        id8 = T("id8", [8, 8], BF16)
        xns = T("xns", [128, 8 * 512], BF16)
        gsb = T("gsb", [128, 32], F32)
        mm1a = T("mm1a", [128, 4], F32)
        mm2a = T("mm2a", [128, 4], F32)
        mk1a = T("mk1a", [128, 32], F32)
        gg2a = T("gg2a", [128, 32], F32)
        sela = T("sela", [128, 32], F32)
        ee1a = T("ee1a", [128, 32], F32)
        eesa = T("eesa", [128, 32], F32)
        ssma = T("ssma", [128, 4], F32)
        rcpa = T("rcpa", [128, 4], F32)
        msk_tm = T("msk_tm", [128, 128], BF16)
        mask_fm = T("mask_fm", [8, 2048], BF16)
        mrep = T("mrep", [128, 512], BF16)
        if lay == 1:
            bgs = T("bgs", [128, 8], F32)
            sss = T("sss", [128, 512], BF16)
        w1st = [T(f"w1st{i}", [128, 1024], F32) for i in range(2)]
        w1b = [T(f"w1b{i}", [128, 1024], BF16) for i in range(2)]
        w2st = [T(f"w2st{i}", [128, 2048], F32) for i in range(2)]
        w2b = [T(f"w2b{i}", [128, 2048], BF16) for i in range(2)]
        hsb = T("hsb", [128, 32 * 512], BF16)
        ytmp = [T(f"ytmp{i}", [128, 512], F32) for i in range(2)]
        yacc = T("yaccC", [128, 8 * 512], F32)
        gbf = [T(f"gbf{i}", [128, 512], BF16) for i in range(1)]
        xsl = [T(f"xsl{i}", [128, 512], F32) for i in range(1)]
        sas = [T(f"sas{i}", [128, 512], F32) for i in range(1)]
        sbs = [T(f"sbs{i}", [128, 512], F32) for i in range(1)]
        x1s = [T(f"x1s{i}", [128, 512], F32) for i in range(1)]
        gpsA = PT("gpsA", [128, 32], F32)
        ptr = PT("ptr", [8, 128], BF16)
        pc = [PT(f"pc{i}", [128, 512], F32) for i in range(2)]
        pm = PT("pm", [128, 512], F32)

        A = p.add
        V = p.v
        war = defaultdict(tuple)

        A("sync", lambda s: s.dma_start(out=id128[:], in_=eye128), inc=(PF + "L_stp", 16))
        A("sync", lambda s: s.dma_start(out=id8[:], in_=eye8), inc=(PF + "L_stp", 16))
        A("sync", lambda s: s.dma_start(out=wgf[:], in_=wgt), inc=(PF + "L_stp", 16))
        A("sync", lambda s: s.dma_start(out=b1s[:], in_=b1c), inc=(PF + "L_stp", 16))
        A("sync", lambda s: s.dma_start(out=b2s[:], in_=b2c), inc=(PF + "L_stp", 16))
        if lay == 1:
            A("sync", lambda s: s.dma_start(out=bgs[:], in_=io["bgr"]), inc=(PF + "L_stp", 16))
        ld0 = (PF + "L_stp", V(PF + "L_stp"))
        A("vector", lambda v: v.tensor_copy(out=wgb[:], in_=wgf[:]), inc="DVE", wait=[ld0])
        setup_dve = V("DVE")

        ln_out = 0
        if io.get("ln_x") is not None:
            lnx = io["ln_x"]
            lnc = io["ln_cst"]
            xn_st = io["xn_st"]
            xfs2 = [ytmp[0][:], ytmp[1][:]]
            xsq2 = [xsl[0][:], sas[0][:]]
            nmT = yacc[0:1, 0:512]
            m2T = yacc[0:1, 512:1024]
            varT = yacc[0:1, 1024:1536]
            rsT = yacc[0:1, 1536:2048]
            nmrT = yacc[0:1, 2048:2560]
            xntT = x1s[0][:]
            xobT = gbf[0][:]
            onescT = T("onescT", [128, 1], F32)
            onesrT = T("onesrT", [1, 128], F32)
            csttT = T("csttT", [128, 4], F32)
            mpsT = pc[0][0:1, 0:512]
            spsT = pc[1][0:1, 0:512]
            A("vector", lambda v: v.memset(onescT[:], 1.0), inc="DVE")
            A("vector", lambda v: v.memset(onesrT[:], 1.0), inc="DVE")
            A("sync", lambda s: s.dma_start(out=csttT[:], in_=lnc), inc=(PF + "L_stp", 16))
            setup_dve = V("DVE")
            ldc = (PF + "L_stp", V(PF + "L_stp"))
            for c in range(4):
                t0 = c * 512
                for k in range(8):
                    par = k % 2
                    A("sync", lambda s, k=k, par=par, t0=t0: s.dma_start(
                        out=xfs2[par], in_=lnx[k * 128:(k + 1) * 128, t0:t0 + 512]),
                      inc=(PF + f"L_lnx{par}", 16), wait=war[("xfT", par)])
                    ldw = (PF + f"L_lnx{par}", V(PF + f"L_lnx{par}"))
                    A("scalar", lambda s, par=par: s.activation(out=xsq2[par], in_=xfs2[par], func=AF.Square),
                      inc="ACT", wait=[ldw] + list(war[("xsqT", par)]))
                    actv = V("ACT")
                    A("tensor", lambda t, k=k, par=par: t.matmul(
                        out=mpsT, lhsT=onescT[:], rhs=xfs2[par], start=(k == 0), stop=(k == 7)),
                      inc="PE",
                      wait=([ldw, ("DVE", setup_dve)] + list(war[("mpsT",)])) if k == 0 else [ldw])
                    A("tensor", lambda t, k=k, par=par: t.matmul(
                        out=spsT, lhsT=onescT[:], rhs=xsq2[par], start=(k == 0), stop=(k == 7)),
                      inc="PE", wait=[("ACT", actv)])
                    war[("xfT", par)] = [("PE", V("PE"))]
                    war[("xsqT", par)] = [("PE", V("PE"))]
                pe_stats = V("PE")
                A("vector", lambda v: v.tensor_scalar_mul(out=nmT, in0=mpsT, scalar1=-1.0 / 1024.0),
                  inc="DVE", wait=[("PE", pe_stats)])
                A("vector", lambda v: v.tensor_mul(out=m2T, in0=nmT, in1=nmT), inc="DVE")
                A("vector", lambda v: v.scalar_tensor_tensor(
                    out=varT, in0=spsT, scalar=1.0 / 1024.0, in1=m2T, op0=ALU.mult, op1=ALU.subtract),
                  inc="DVE", wait=[("PE", pe_stats)])
                dve_var = V("DVE")
                war[("mpsT",)] = [("DVE", dve_var)]
                A("scalar", lambda s: s.activation(out=rsT, in_=varT, func=AF.Ln, bias=csttT[0:1, 0:1]),
                  inc="ACT", wait=[("DVE", dve_var), ldc])
                A("scalar", lambda s: s.activation(out=rsT, in_=rsT, func=AF.Exp, scale=csttT[0:1, 1:2]),
                  inc="ACT")
                act_rs = V("ACT")
                A("vector", lambda v: v.tensor_mul(out=nmrT, in0=nmT, in1=rsT),
                  inc="DVE", wait=[("ACT", act_rs)])
                dve_nmr = V("DVE")
                A("tensor", lambda t: t.matmul(out=pm[:], lhsT=onesrT[:], rhs=rsT, start=True, stop=True),
                  inc="PE", wait=[("ACT", act_rs)] + list(war[("pmLN",)]))
                pe_b1 = V("PE")
                A("tensor", lambda t: t.matmul(out=pc[0][:], lhsT=onesrT[:], rhs=nmrT, start=True, stop=True),
                  inc="PE", wait=[("DVE", dve_nmr), ("DVE", V("DVE"))])
                pe_bc = V("PE")
                for k in range(8):
                    par = k % 2
                    A("sync", lambda s, k=k, par=par, t0=t0: s.dma_start(
                        out=xfs2[par], in_=lnx[k * 128:(k + 1) * 128, t0:t0 + 512]),
                      inc=(PF + f"L_lnx{par}", 16), wait=war[("xfT", par)])
                    ldw = (PF + f"L_lnx{par}", V(PF + f"L_lnx{par}"))
                    A("vector", lambda v, par=par: v.tensor_mul(out=xntT, in0=xfs2[par], in1=pm[:]),
                      inc="DVE", wait=[ldw, ("PE", pe_bc)])
                    A("vector", lambda v, par=par: v.tensor_add(out=xobT, in0=xntT, in1=pc[0][:]),
                      inc="DVE", wait=war[("xobT",)])
                    dv = V("DVE")
                    war[("xfT", par)] = [("DVE", dv)]
                    A("gpsimd", lambda g, k=k, t0=t0: g.dma_start(out=xn_st(k, t0), in_=xobT),
                      inc=(PF + "O_gbf", 16), wait=[("DVE", dv)])
                    war[("xobT",)] = [(PF + "O_gbf", V(PF + "O_gbf"))]
                war[("pmLN",)] = [("DVE", V("DVE"))]
                war[("pm",)] = [("DVE", V("DVE"))]
                war[("pc", 0)] = [("DVE", V("DVE"))]
                war[("mpsT",)] = [("DVE", V("DVE"))]
            ln_out = V(PF + "O_gbf")
            setup_dve = V("DVE")
            war[("ytmp", 0)] = [("PE", V("PE")), ("DVE", setup_dve), ("ACT", V("ACT"))]
            war[("ytmp", 1)] = list(war[("ytmp", 0)])
            war[("xsl", 0)] = list(war[("ytmp", 0)])
            war[("sas", 0)] = list(war[("ytmp", 0)])
            war[("x1s", 0)] = [("DVE", setup_dve)]
            war[("gbf", 0)] = [(PF + "O_gbf", ln_out)]
            for j in range(8):
                war[("yacck", j)] = [("PE", V("PE"))]
        yield "setup"

        g3 = lambda t_: t_.rearrange("p (t e) -> p t e", e=8)
        bc3 = lambda t_: t_.unsqueeze(2).broadcast_to([128, 4, 8])
        first_e_pe = {}
        for tcm in range(4):
            tm0 = tcm * 512
            xnw = list(war[("xns",)])
            if io.get("ln_x") is not None:
                xnw += [(PF + "O_gbf", ln_out)]
            else:
                xnw += list(io.get("xn_wait", ()))
            for k in range(8):
                A("sync", lambda s, k=k, tm0=tm0: s.dma_start(
                    out=xns[:, k * 512:(k + 1) * 512], in_=xn_ld(k, tm0)),
                  inc=(PF + "L_xns", 16), wait=xnw if k == 0 else [])
            ld_xn = (PF + "L_xns", V(PF + "L_xns"))
            for tt in range(4):
                for k in range(8):
                    A("tensor", lambda t, tt=tt, k=k: t.matmul(
                        out=gpsA[:, tt * 8:(tt + 1) * 8],
                        lhsT=xns[:, k * 512 + tt * 128:k * 512 + tt * 128 + 128],
                        rhs=wgb[:, k * 8:(k + 1) * 8], start=(k == 0), stop=(k == 7)),
                      inc="PE",
                      wait=([("DVE", setup_dve), ld_xn] + list(war[("gpsA",)])) if (k == 0 and tt == 0) else [])
            pe_gate = V("PE")
            A("scalar", lambda s: s.copy(out=gsb[:], in_=gpsA[:]), inc="ACT",
              wait=[("PE", pe_gate)] + list(war[("gsb",)]))
            act_gsb = V("ACT")
            if lay == 0:
                A("vector", lambda v: v.tensor_reduce(out=mm1a[:].unsqueeze(2), in_=g3(gsb[:]),
                                                      axis=mybir.AxisListType.X, op=ALU.max),
                  inc="DVE", wait=[("ACT", act_gsb)])
                A("scalar", lambda s: s.activation(out=ee1a[:], in_=gsb[:], func=AF.Exp),
                  inc="ACT")
                A("vector", lambda v: v.tensor_tensor(out=g3(mk1a[:]), in0=g3(gsb[:]), in1=bc3(mm1a[:]),
                                                      op=ALU.is_ge), inc="DVE", wait=[("DVE", V("DVE"))])
                A("vector", lambda v: v.scalar_tensor_tensor(out=gg2a[:], in0=mk1a[:], scalar=-30000.0,
                                                             in1=gsb[:], op0=ALU.mult, op1=ALU.add),
                  inc="DVE")
                A("vector", lambda v: v.tensor_reduce(out=mm2a[:].unsqueeze(2), in_=g3(gg2a[:]),
                                                      axis=mybir.AxisListType.X, op=ALU.max), inc="DVE",
                  wait=[("DVE", V("DVE"))])
                A("vector", lambda v: v.tensor_tensor(out=g3(sela[:]), in0=g3(gsb[:]), in1=bc3(mm2a[:]),
                                                      op=ALU.is_ge), inc="DVE", wait=[("DVE", V("DVE"))])
                A("vector", lambda v: v.tensor_mul(out=eesa[:], in0=ee1a[:], in1=sela[:]),
                  inc="DVE", wait=[("ACT", V("ACT"))])
                A("vector", lambda v: v.tensor_reduce(out=ssma[:].unsqueeze(2), in_=g3(eesa[:]),
                                                      axis=mybir.AxisListType.X, op=ALU.add), inc="DVE",
                  wait=[("DVE", V("DVE"))])
                A("vector", lambda v: v.reciprocal(out=rcpa[:], in_=ssma[:]), inc="DVE",
                  wait=[("DVE", V("DVE"))])
                A("vector", lambda v, tcm=tcm: v.scalar_tensor_tensor(
                    out=g3(msk_tm[:, tcm * 32:(tcm + 1) * 32]), in0=g3(eesa[:]),
                    scalar=0.125, in1=bc3(rcpa[:]),
                    op0=ALU.mult, op1=ALU.mult), inc="DVE",
                  wait=[("DVE", V("DVE"))])
            else:
                A("vector", lambda v: v.tensor_tensor(
                    out=g3(gsb[:]), in0=g3(gsb[:]),
                    in1=bgs[:].unsqueeze(1).broadcast_to([128, 4, 8]), op=ALU.add),
                  inc="DVE", wait=[("ACT", act_gsb), ld0])
                dve_lg = V("DVE")
                A("scalar", lambda s: s.activation(out=ee1a[:], in_=gsb[:], func=AF.Exp),
                  inc="ACT", wait=[("DVE", dve_lg)])
                A("vector", lambda v: v.tensor_reduce(out=ssma[:].unsqueeze(2), in_=g3(ee1a[:]),
                                                      axis=mybir.AxisListType.X, op=ALU.add), inc="DVE",
                  wait=[("ACT", V("ACT"))])
                A("vector", lambda v: v.reciprocal(out=rcpa[:], in_=ssma[:]), inc="DVE",
                  wait=[("DVE", V("DVE"))])
                A("vector", lambda v: v.tensor_tensor(out=g3(gsb[:]), in0=g3(ee1a[:]), in1=bc3(rcpa[:]),
                                                      op=ALU.mult), inc="DVE", wait=[("DVE", V("DVE"))])
                A("vector", lambda v: v.tensor_reduce(out=mm1a[:].unsqueeze(2), in_=g3(gsb[:]),
                                                      axis=mybir.AxisListType.X, op=ALU.max), inc="DVE",
                  wait=[("DVE", V("DVE"))])
                A("vector", lambda v: v.tensor_tensor(out=g3(mk1a[:]), in0=g3(gsb[:]), in1=bc3(mm1a[:]),
                                                      op=ALU.is_ge), inc="DVE", wait=[("DVE", V("DVE"))])
                A("vector", lambda v: v.scalar_tensor_tensor(out=gg2a[:], in0=mk1a[:], scalar=-30000.0,
                                                             in1=gsb[:], op0=ALU.mult, op1=ALU.add),
                  inc="DVE")
                A("vector", lambda v: v.tensor_reduce(out=mm2a[:].unsqueeze(2), in_=g3(gg2a[:]),
                                                      axis=mybir.AxisListType.X, op=ALU.max), inc="DVE",
                  wait=[("DVE", V("DVE"))])
                A("vector", lambda v: v.tensor_tensor(out=g3(sela[:]), in0=g3(gsb[:]), in1=bc3(mm2a[:]),
                                                      op=ALU.is_ge), inc="DVE", wait=[("DVE", V("DVE"))])
                A("vector", lambda v: v.tensor_mul(out=eesa[:], in0=gsb[:], in1=sela[:]), inc="DVE")
                A("vector", lambda v: v.tensor_reduce(out=ssma[:].unsqueeze(2), in_=g3(eesa[:]),
                                                      axis=mybir.AxisListType.X, op=ALU.add), inc="DVE",
                  wait=[("DVE", V("DVE"))])
                A("vector", lambda v: v.tensor_scalar_add(out=ssma[:], in0=ssma[:], scalar1=1e-8),
                  inc="DVE", wait=[("DVE", V("DVE"))])
                A("vector", lambda v: v.reciprocal(out=rcpa[:], in_=ssma[:]), inc="DVE",
                  wait=[("DVE", V("DVE"))])
                A("vector", lambda v, tcm=tcm: v.tensor_tensor(
                    out=g3(msk_tm[:, tcm * 32:(tcm + 1) * 32]), in0=g3(eesa[:]),
                    in1=bc3(rcpa[:]), op=ALU.mult), inc="DVE",
                  wait=[("DVE", V("DVE"))])
            dve_msk = V("DVE")
            war[("gsb",)] = [("DVE", dve_msk)]
            for tt in range(4):
                A("tensor", lambda t, tt=tt, tcm=tcm: t.transpose(
                    out=ptr[:], in_=msk_tm[:, tcm * 32 + tt * 8:tcm * 32 + (tt + 1) * 8], identity=id128[:]),
                  inc="PE", wait=[("DVE", dve_msk), ld0] + list(war[("ptr",)]))
                pev = V("PE")
                A("scalar", lambda s, tt=tt, tm0=tm0: s.copy(
                    out=mask_fm[:, tm0 + tt * 128:tm0 + (tt + 1) * 128], in_=ptr[:]),
                  inc="ACT", wait=[("PE", pev)])
                war[("ptr",)] = [("ACT", V("ACT"))]
            act_mask = V("ACT")
            war[("gpsA",)] = [("ACT", act_mask)]
            yield ("gating", tcm)

            for e in range(8):
                A("tensor", lambda t, e=e, tm0=tm0: t.matmul(
                    out=pm[:], lhsT=id8[:, e:e + 1].broadcast_to([8, 128]),
                    rhs=mask_fm[:, tm0:tm0 + 512], start=True, stop=True),
                  inc="PE",
                  wait=[("ACT", act_mask)] + list(war[("pm",)]))
                pev = V("PE")
                A("scalar", lambda s: s.copy(out=mrep[:], in_=pm[:]),
                  inc="ACT", wait=[("PE", pev)] + list(war[("mrep",)]))
                act_mr = V("ACT")
                war[("pm",)] = [("ACT", act_mr)]
                for hcol in range(32):
                    par = hcol % 2
                    wp = hcol % 2
                    pp = pc[par]
                    if tcm == 0:
                        A("sync", lambda s, e=e, hcol=hcol, wp=wp: s.dma_start(
                            out=w1st[wp][:], in_=w1r[e, hcol]),
                          inc=(PF + f"L_w1s{wp}", 16), wait=war[("w1st", wp)])
                        ldv = (PF + f"L_w1s{wp}", V(PF + f"L_w1s{wp}"))
                        A("vector", lambda v, wp=wp: v.tensor_copy(out=w1b[wp][:], in_=w1st[wp][:]),
                          inc="DVE", wait=[ldv] + list(war[("w1b", wp)]))
                        dve_w1 = V("DVE")
                        war[("w1st", wp)] = [("DVE", dve_w1)]
                        A("gpsimd", lambda g, e=e, hcol=hcol, wp=wp: g.dma_start(
                            out=w1bd[e, hcol], in_=w1b[wp][:]),
                          inc=(PF + f"O_w1b{wp}", 16), wait=[("DVE", dve_w1)])
                        wb_out = V(PF + f"O_w1b{wp}")
                        ld_w1 = [("DVE", dve_w1)]
                        war[("w1b", wp)] = [(PF + f"O_w1b{wp}", wb_out)]
                    else:
                        A("sync", lambda s, e=e, hcol=hcol, wp=wp: s.dma_start(
                            out=w1b[wp][:], in_=w1bd[e, hcol]),
                          inc=(PF + f"L_w1b{wp}", 16), wait=list(war[("w1b", wp)]) + list(war[("wb_done",)]))
                        ld_w1 = [(PF + f"L_w1b{wp}", V(PF + f"L_w1b{wp}"))]
                    for k in range(8):
                        A("tensor", lambda t, k=k, wp=wp, tm0=tm0, pp=pp: t.matmul(
                            out=pp[:], lhsT=w1b[wp][:, k * 128:(k + 1) * 128],
                            rhs=xns[:, k * 512:(k + 1) * 512],
                            start=(k == 0), stop=(k == 7)),
                          inc="PE",
                          wait=(ld_w1 + list(war[("pc", par)])) if k == 0 else [])
                    pev = V("PE")
                    war[("w1b", wp)] = list(war[("w1b", wp)]) + [("PE", pev)] if tcm == 0 else [("PE", pev)]
                    A("scalar", lambda s, e=e, hcol=hcol, pp=pp: s.activation(
                        out=hsb[:, hcol * 512:(hcol + 1) * 512], in_=pp[:],
                        func=AF.Relu if lay == 0 else AF.Gelu_apprx_tanh,
                        bias=b1s[:, e * 32 + hcol:e * 32 + hcol + 1]),
                      inc="ACT", wait=[("PE", pev), ld0] + list(war[("hsbk", hcol)]))
                    war[("pc", par)] = [("ACT", V("ACT"))]
                act_h = V("ACT")
                for dcol in range(8):
                    par = dcol % 2
                    pp = pc[par]
                    for hh in range(2):
                        sp = hh
                        if tcm == 0:
                            A("sync", lambda s, e=e, dcol=dcol, hh=hh, sp=sp: s.dma_start(
                                out=w2st[sp][:], in_=w2r[e, dcol][:, hh * 2048:(hh + 1) * 2048]),
                              inc=(PF + f"L_w2s{sp}", 16), wait=war[("w2st", sp)])
                            ldv = (PF + f"L_w2s{sp}", V(PF + f"L_w2s{sp}"))
                            A("vector", lambda v, sp=sp: v.tensor_copy(out=w2b[sp][:], in_=w2st[sp][:]),
                              inc="DVE", wait=[ldv] + list(war[("w2b", sp)]))
                            dve_w2 = V("DVE")
                            war[("w2st", sp)] = [("DVE", dve_w2)]
                            A("gpsimd", lambda g, e=e, dcol=dcol, hh=hh, sp=sp: g.dma_start(
                                out=w2bd[e, dcol][:, hh * 2048:(hh + 1) * 2048], in_=w2b[sp][:]),
                              inc=(PF + f"O_w2b{sp}", 16), wait=[("DVE", dve_w2)])
                            war[("w2b", sp)] = [(PF + f"O_w2b{sp}", V(PF + f"O_w2b{sp}"))]
                            ld_w2 = [("DVE", dve_w2)]
                        else:
                            A("sync", lambda s, e=e, dcol=dcol, hh=hh, sp=sp: s.dma_start(
                                out=w2b[sp][:], in_=w2bd[e, dcol][:, hh * 2048:(hh + 1) * 2048]),
                              inc=(PF + f"L_w2b{sp}", 16), wait=list(war[("w2b", sp)]) + list(war[("wb_done",)]))
                            ld_w2 = [(PF + f"L_w2b{sp}", V(PF + f"L_w2b{sp}"))]
                        for kh in range(16):
                            A("tensor", lambda t, kh=kh, hh=hh, sp=sp, pp=pp: t.matmul(
                                out=pp[:], lhsT=w2b[sp][:, kh * 128:(kh + 1) * 128],
                                rhs=hsb[:, (hh * 16 + kh) * 512:(hh * 16 + kh + 1) * 512],
                                start=(hh == 0 and kh == 0), stop=(hh == 1 and kh == 15)),
                              inc="PE",
                              wait=(ld_w2 + ([("ACT", act_h)] + list(war[("pc", par)]) if hh == 0 else [])) if kh == 0 else [])
                        pev = V("PE")
                        war[("w2b", sp)] = list(war[("w2b", sp)]) + [("PE", pev)] if tcm == 0 else [("PE", pev)]
                    A("vector", lambda v, e=e, dcol=dcol, par=par, pp=pp: v.scalar_tensor_tensor(
                        out=ytmp[par][:], in0=pp[:], scalar=b2s[:, e * 8 + dcol:e * 8 + dcol + 1],
                        in1=mrep[:], op0=ALU.add, op1=ALU.mult),
                      inc="DVE", wait=[("PE", pev), ("ACT", act_mr)] + list(war[("ytmp", par)]))
                    war[("pc", par)] = [("DVE", V("DVE"))]
                    if e == 0:
                        A("vector", lambda v, dcol=dcol, par=par: v.tensor_copy(
                            out=yacc[:, dcol * 512:(dcol + 1) * 512], in_=ytmp[par][:]),
                          inc="DVE", wait=war[("yacck", dcol)])
                    else:
                        A("vector", lambda v, dcol=dcol, par=par: v.tensor_add(
                            out=yacc[:, dcol * 512:(dcol + 1) * 512],
                            in0=yacc[:, dcol * 512:(dcol + 1) * 512], in1=ytmp[par][:]),
                          inc="DVE")
                    war[("ytmp", par)] = [("DVE", V("DVE"))]
                war[("mrep",)] = [("DVE", V("DVE"))]
                yield ("block", tcm, e)
            if tcm == 0:
                war[("wb_done",)] = [(PF + "O_w1b0", V(PF + "O_w1b0")), (PF + "O_w1b1", V(PF + "O_w1b1")),
                                     (PF + "O_w2b0", V(PF + "O_w2b0")), (PF + "O_w2b1", V(PF + "O_w2b1"))]
            for hcol in range(32):
                war[("hsbk", hcol)] = [("PE", V("PE"))]
            for j in range(8):
                par = 0
                A("scalar", lambda s, j=j, par=par: s.activation(
                    out=gbf[par][:], in_=yacc[:, j * 512:(j + 1) * 512], func=AF.Silu),
                  inc="ACT", wait=[("DVE", V("DVE"))] + list(war[("gbf", par)]))
                act_g = V("ACT")
                A("sync", lambda s, j=j, par=par, tm0=tm0: s.dma_start(
                    out=xsl[par][:], in_=x_ld(j, tm0)),
                  inc=(PF + "L_xsl", 16), wait=war[("xsl", par)])
                ld_xsl = (PF + "L_xsl", V(PF + "L_xsl"))
                if s_mode == "dual":
                    A("sync", lambda s, j=j, par=par, tm0=tm0: s.dma_start(
                        out=sas[par][:], in_=io["sa_ld"](j, tm0)),
                      inc=(PF + "L_sas", 16), wait=war[("sas", par)])
                    A("sync", lambda s, j=j, par=par, tm0=tm0: s.dma_start(
                        out=sbs[par][:], in_=io["sb_ld"](j, tm0)),
                      inc=(PF + "L_sbs", 16), wait=war[("sbs", par)])
                    ldv = [(PF + "L_sas", V(PF + "L_sas")), (PF + "L_sbs", V(PF + "L_sbs"))]
                    A("vector", lambda v, par=par: v.tensor_add(out=sas[par][:], in0=sas[par][:], in1=sbs[par][:]),
                      inc="DVE", wait=ldv)
                elif s_mode == "single_f32":
                    A("sync", lambda s, j=j, par=par, tm0=tm0: s.dma_start(
                        out=sas[par][:], in_=io["s_ld"](j, tm0)),
                      inc=(PF + "L_sas", 16), wait=tuple(war[("sas", par)]) + tuple(io.get("s_wait", ())))
                    ldv = [(PF + "L_sas", V(PF + "L_sas"))]
                else:
                    A("sync", lambda s, j=j, tm0=tm0: s.dma_start(
                        out=sss[:], in_=io["s_ld"](j, tm0)),
                      inc=(PF + "L_sss", 16), wait=list(war[("sas", par)]) + list(io.get("s_wait", ())))
                    ldv = [(PF + "L_sss", V(PF + "L_sss"))]
                    A("vector", lambda v, par=par: v.tensor_copy(out=sas[par][:], in_=sss[:]),
                      inc="DVE", wait=ldv)
                A("vector", lambda v, par=par: v.tensor_mul(out=sas[par][:], in0=sas[par][:], in1=gbf[par][:]),
                  inc="DVE", wait=[("ACT", act_g)] + ldv)
                A("vector", lambda v, par=par: v.tensor_add(out=x1s[par][:], in0=xsl[par][:], in1=sas[par][:]),
                  inc="DVE", wait=[ld_xsl] + list(war[("x1s", par)]))
                dve_x1 = V("DVE")
                war[("sbs", par)] = [("DVE", dve_x1)]
                war[("sas", par)] = [("DVE", dve_x1)]
                war[("xsl", par)] = [("DVE", dve_x1)]
                war[("gbf", par)] = [("DVE", dve_x1)]
                A("gpsimd", lambda g, j=j, par=par, tm0=tm0: g.dma_start(
                    out=out_st(j, tm0), in_=x1s[par][:]),
                  inc=(PF + "O_x1s", 16), wait=[("DVE", dve_x1)])
                war[("x1s", par)] = [(PF + "O_x1s", V(PF + "O_x1s"))]
                war[("yacck", j)] = [("ACT", act_g)]
            war[("xns",)] = [("PE", V("PE"))]
            yield ("combine", tcm)


def stage_C(d, res_b):
    x = np.asarray(d["L0_attn_x"])
    w1 = np.asarray(d["L0_ffn_W1"])
    w2 = np.asarray(d["L0_ffn_W2"])
    w1r = np.ascontiguousarray(w1.reshape(8, 8, 128, 32, 128).transpose(0, 3, 2, 1, 4).reshape(8, 32, 128, 1024))
    w2r = np.ascontiguousarray(w2.reshape(8, 32, 128, 8, 128).transpose(0, 3, 2, 1, 4).reshape(8, 8, 128, 4096))
    b1cv = np.ascontiguousarray(np.asarray(d["L0_ffn_b1"]).reshape(8, 32, 128).transpose(2, 0, 1).reshape(128, 256))
    b2cv = np.ascontiguousarray(np.asarray(d["L0_ffn_b2"]).reshape(8, 8, 128).transpose(2, 0, 1).reshape(128, 64))
    wgtv = np.ascontiguousarray(np.asarray(d["L0_ffn_W_gate"]).reshape(8, 128, 8).transpose(1, 0, 2).reshape(128, 64))
    e128 = np.eye(128).astype(BF)
    e8 = np.eye(8).astype(BF)
    maps = []
    for c in range(8):
        b, ih = c // 2, c % 2
        tsl = slice(ih * 2048, (ih + 1) * 2048)
        partner = 2 * b + (1 - ih)
        m = {
            "xn_in": np.ascontiguousarray(res_b[c]["xn_out"][:, tsl]),
            "x_own": np.ascontiguousarray(x[b].T[:, tsl]),
            "s_a": np.ascontiguousarray(res_b[c]["s_par"][:, tsl]),
            "s_b": np.ascontiguousarray(res_b[partner]["s_par"][:, tsl]),
            "wgt": wgtv, "w1r": w1r, "w2r": w2r, "b1c": b1cv, "b2c": b2cv,
            "eye128": e128, "eye8": e8,
        }
        maps.append(m)
    return maps


def build_D1():
    nc = bass.Bass("TRN2", target_bir_lowering=False, debug=False)
    io = {}
    for nm, sh, dt in [("x1in", [1024, 2048], F32), ("nwc", [128, 8], F32),
                       ("w_xp", [1024, 192], F32), ("w_dt1", [64, 1024], F32),
                       ("b_dt1", [128, 8], F32), ("a1in", [128, 512], F32),
                       ("d1c", [128, 8], F32), ("w_op1r", [8, 128, 1024], F32),
                       ("eye64", [64, 64], BF16), ("cst", [128, 4], F32)]:
        io[nm] = nc.dram_tensor(nm, sh, dt, kind="ExternalInput").ap()
    io["s1o"] = nc.dram_tensor("s1o", [1024, 2048], BF16, kind="ExternalOutput").ap()
    io["xn1o"] = nc.dram_tensor("xn1o", [1024, 2048], BF16, kind="ExternalOutput").ap()
    p = Prog()
    with contextlib.ExitStack() as es0:
        _emit_D1(nc, p, io)
        finish(p, es0, nc)
    return nc


def _emit_D1(nc, p, io):
    for _ in _emit_D1_gen(nc, p, io):
        pass


def _emit_D1_gen(nc, p, io):
    x1in, nwc, w_xp, w_dt1 = io["x1in"], io["nwc"], io["w_xp"], io["w_dt1"]
    b_dt1, a1in, d1c, w_op1r = io["b_dt1"], io["a1in"], io["d1c"], io["w_op1r"]
    eye64, cst, s1o, xn1o = io["eye64"], io["cst"], io["s1o"], io["xn1o"]
    NCH1 = 2048 // TC
    PF = io.get("pref", "")
    with contextlib.ExitStack() as es:
        _pf = io.get("pref", "")
        T = lambda name, shape, dt: es.enter_context(nc.sbuf_tensor(_pf + name, shape, dt))
        PT = lambda name, shape, dt: es.enter_context(nc.psum_tensor(_pf + name, shape, dt))
        nws = T("nws", [128, 8], F32)
        wxpb = T("wxpb", [128, 8 * 192], BF16)
        wdt1 = T("wdt1b", [64, 1024], BF16)
        bdt1 = T("bdt1", [128, 8], F32)
        a1c = T("a1c", [128, 512], F32)
        d1s = T("d1s", [128, 8], F32)
        eyes = T("eyes", [64, 64], BF16)
        cstt = T("cstt", [128, 4], F32)
        ones_c = T("ones_c", [128, 1], F32)
        ones_r = T("ones_r", [1, 128], F32)
        wst = T("wstD", [128, 1024], F32)
        wopb = [T(f"wopb{i}", [128, 1024], BF16) for i in range(1)]
        xfs = [T(f"xfD{i}", [128, TC], F32) for i in range(2)]
        xsq = [T(f"xsqD{i}", [128, TC], F32) for i in range(2)]
        nm = T("nmD", [1, TC], F32)
        m2 = T("m2D", [1, TC], F32)
        var = T("varD", [1, TC], F32)
        rs = T("rsD", [1, TC], F32)
        nmr = T("nmrD", [1, TC], F32)
        xnt = T("xntD", [128, TC], F32)
        xn1 = T("xn1", [128, KT * 2048], BF16)
        xnr = T("xnr", [128, KT * TC], BF16)
        dlo_t = T("dlo_t", [64, TC], BF16)
        bs_t = T("bs_t", [64, TC], BF16)
        cs_t = T("cs_t", [64, TC], BF16)
        spv = T("spvD", [128, TC], F32)
        sav = T("savD", [128, TC], F32)
        delta1 = T("delta1", [128, KT * TC], BF16)
        dxn = T("dxn", [128, KT * TC], BF16)
        brep = T("brepD", [128, SG * TC], BF16)
        crep = T("crepD", [128, SG * TC], BF16)
        dab = [T(f"dabD{i}", [128, SG * TC], F32) for i in range(1)]
        ub = [T(f"ubD{i}", [128, SG * TC], BF16) for i in range(1)]
        hb = [T(f"hbD{i}", [128, SG * TC], BF16) for i in range(2)]
        wbf = [T(f"wbfD{i}", [128, SG * TC], BF16) for i in range(1)]
        t1p = [T(f"t1pD{i}", [128, TC], BF16) for i in range(1)]
        t2p = [T(f"t2pD{i}", [128, TC], BF16) for i in range(1)]
        ysg = [T(f"ysgD{i}", [128, TC], F32) for i in range(1)]
        yacc = T("yaccD", [128, KT * TC], F32)
        ob = T("ob", [128, KT * TC], BF16)
        spo = [T(f"spoD{i}", [128, TC], BF16) for i in range(2)]
        pa = PT("paD", [128, TC], F32)
        pb = PT("pbD", [128, TC], F32)
        brp = PT("brpD", [128, SG * TC], F32)
        mps = PT("mpsD", [1, TC], F32)
        sps = PT("spsD", [1, TC], F32)
        pst = [pa, pb]

        A = p.add
        V = p.v
        war = defaultdict(tuple)

        A("vector", lambda v: v.memset(ones_c[:], 1.0), inc="DVE")
        A("vector", lambda v: v.memset(ones_r[:], 1.0), inc="DVE")
        A("sync", lambda s: s.dma_start(out=eyes[:], in_=eye64), inc=(PF + "L_stp", 16))
        A("sync", lambda s: s.dma_start(out=cstt[:], in_=cst), inc=(PF + "L_stp", 16))
        A("sync", lambda s: s.dma_start(out=nws[:], in_=nwc), inc=(PF + "L_stp", 16))
        A("sync", lambda s: s.dma_start(out=bdt1[:], in_=b_dt1), inc=(PF + "L_stp", 16))
        A("sync", lambda s: s.dma_start(out=d1s[:], in_=d1c), inc=(PF + "L_stp", 16))
        A("sync", lambda s: s.dma_start(out=a1c[:], in_=a1in), inc=(PF + "L_stp", 16))
        ld0v = V(PF + "L_stp")
        ld0 = (PF + "L_stp", ld0v)
        A("scalar", lambda s: s.activation(out=a1c[:], in_=a1c[:], func=AF.Exp),
          inc="ACT", wait=[ld0])
        A("vector", lambda v: v.tensor_scalar_mul(out=a1c[:], in0=a1c[:], scalar1=-1.0),
          inc="DVE", wait=[("ACT", V("ACT"))])
        for k in range(KT):
            w_cast = V("ACT")
            A("sync", lambda s, k=k: s.dma_start(out=wst[:, :192], in_=w_xp[k * 128:(k + 1) * 128, :]),
              inc=(PF + "L_wst", 16), wait=[("ACT", w_cast)])
            A("scalar", lambda s, k=k: s.copy(out=wxpb[:, k * 192:(k + 1) * 192], in_=wst[:, :192]),
              inc="ACT", wait=[(PF + "L_wst", V(PF + "L_wst"))])
        for h in range(2):
            w_cast = V("ACT")
            A("sync", lambda s, h=h: s.dma_start(out=wst[:64, :512], in_=w_dt1[:, h * 512:(h + 1) * 512]),
              inc=(PF + "L_wst", 16), wait=[("ACT", w_cast)])
            A("scalar", lambda s, h=h: s.copy(out=wdt1[:, h * 512:(h + 1) * 512], in_=wst[:64, :512]),
              inc="ACT", wait=[(PF + "L_wst", V(PF + "L_wst"))])
        setup_act = V("ACT")
        setup_dve = V("DVE")
        yield "setup"

        for c in range(NCH1):
            t0 = c * TC
            for k in range(KT):
                par = k % 2
                A("sync", lambda s, k=k, par=par, t0=t0: s.dma_start(
                    out=xfs[par][:], in_=x1in[k * 128:(k + 1) * 128, t0:t0 + TC]),
                  inc=(PF + f"L_xf{par}", 16), wait=war[("xf", par)])
                ldw = (PF + f"L_xf{par}", V(PF + f"L_xf{par}"))
                A("scalar", lambda s, par=par: s.activation(out=xsq[par][:], in_=xfs[par][:], func=AF.Square),
                  inc="ACT", wait=[ldw] + list(war[("xsq", par)]))
                actv = V("ACT")
                A("tensor", lambda t, k=k, par=par: t.matmul(
                    out=mps[:], lhsT=ones_c[:], rhs=xfs[par][:], start=(k == 0), stop=(k == KT - 1)),
                  inc="PE",
                  wait=([ldw, ("DVE", setup_dve)] + list(war[("mps",)])) if k == 0 else [ldw])
                A("tensor", lambda t, k=k, par=par: t.matmul(
                    out=sps[:], lhsT=ones_c[:], rhs=xsq[par][:], start=(k == 0), stop=(k == KT - 1)),
                  inc="PE", wait=[("ACT", actv)])
                war[("xf", par)] = [("PE", V("PE"))]
                war[("xsq", par)] = [("PE", V("PE"))]
            pe_stats = V("PE")
            A("vector", lambda v: v.tensor_scalar_mul(out=nm[:], in0=mps[:], scalar1=-1.0 / 1024.0),
              inc="DVE", wait=[("PE", pe_stats)])
            A("vector", lambda v: v.tensor_mul(out=m2[:], in0=nm[:], in1=nm[:]), inc="DVE")
            A("vector", lambda v: v.scalar_tensor_tensor(
                out=var[:], in0=sps[:], scalar=1.0 / 1024.0, in1=m2[:], op0=ALU.mult, op1=ALU.subtract),
              inc="DVE", wait=[("PE", pe_stats)])
            dve_var = V("DVE")
            war[("mps",)] = [("DVE", dve_var)]
            A("scalar", lambda s: s.activation(out=rs[:], in_=var[:], func=AF.Ln, bias=cstt[0:1, 0:1]),
              inc="ACT", wait=[("DVE", dve_var)])
            A("scalar", lambda s: s.activation(out=rs[:], in_=rs[:], func=AF.Exp, scale=cstt[0:1, 1:2]), inc="ACT")
            act_rs = V("ACT")
            A("vector", lambda v: v.tensor_mul(out=nmr[:], in0=nm[:], in1=rs[:]),
              inc="DVE", wait=[("ACT", act_rs)])
            dve_nmr = V("DVE")
            A("tensor", lambda t: t.matmul(out=pa[:], lhsT=ones_r[:], rhs=rs[:], start=True, stop=True),
              inc="PE", wait=[("ACT", act_rs)] + list(war[("pa",)]))
            A("tensor", lambda t: t.matmul(out=pb[:], lhsT=ones_r[:], rhs=nmr[:], start=True, stop=True),
              inc="PE", wait=[("DVE", dve_nmr)] + list(war[("pb",)]))
            pe_bc = V("PE")
            for k in range(KT):
                par = k % 2
                A("sync", lambda s, k=k, par=par, t0=t0: s.dma_start(
                    out=xfs[par][:], in_=x1in[k * 128:(k + 1) * 128, t0:t0 + TC]),
                  inc=(PF + f"L_xf{par}", 16), wait=war[("xf", par)])
                ldw = (PF + f"L_xf{par}", V(PF + f"L_xf{par}"))
                A("vector", lambda v, par=par: v.tensor_mul(out=xnt[:], in0=xfs[par][:], in1=pa[:]),
                  inc="DVE", wait=[ldw, ("PE", pe_bc)])
                A("vector", lambda v, k=k, par=par, t0=t0: v.tensor_add(
                    out=xn1[:, k * 2048 + t0:k * 2048 + t0 + TC], in0=xnt[:], in1=pb[:]),
                  inc="DVE", wait=war[("xnk", k)])
                dv = V("DVE")
                war[("xf", par)] = [("DVE", dv)]
                A("gpsimd", lambda g, k=k, t0=t0: g.dma_start(
                    out=io["xn1o"][k * 128:(k + 1) * 128, t0:t0 + TC], in_=xn1[:, k * 2048 + t0:k * 2048 + t0 + TC]),
                  inc=(PF + "O_xn1", 16), wait=[("DVE", dv)])
            oxn_end = V(PF + "O_xn1")
            for k in range(KT):
                war[("xnk", k)] = [(PF + "O_xn1", oxn_end)]
            dve_xn = V("DVE")
            war[("pa",)] = [("DVE", dve_xn)]
            war[("pb",)] = [("DVE", dve_xn)]

            for k in range(KT):
                par = k % 2
                A("scalar", lambda s, k=k, par=par, t0=t0: s.activation(
                    out=xsq[par][:], in_=xn1[:, k * 2048 + t0:k * 2048 + t0 + TC], func=AF.Square),
                  inc="ACT", wait=[("DVE", dve_xn)] + list(war[("xsq", par)]))
                actv = V("ACT")
                A("tensor", lambda t, k=k, par=par: t.matmul(
                    out=sps[:], lhsT=ones_c[:], rhs=xsq[par][:], start=(k == 0), stop=(k == KT - 1)),
                  inc="PE", wait=([("ACT", actv)] + list(war[("mps",)])) if k == 0 else [("ACT", actv)])
                war[("xsq", par)] = [("PE", V("PE"))]
            pe_rms = V("PE")
            A("vector", lambda v: v.tensor_scalar_mul(out=var[:], in0=sps[:], scalar1=1.0 / 1024.0),
              inc="DVE", wait=[("PE", pe_rms)])
            dve_var2 = V("DVE")
            war[("mps",)] = [("DVE", dve_var2)]
            A("scalar", lambda s: s.activation(out=rs[:], in_=var[:], func=AF.Ln, bias=cstt[0:1, 2:3]),
              inc="ACT", wait=[("DVE", dve_var2)])
            A("scalar", lambda s: s.activation(out=rs[:], in_=rs[:], func=AF.Exp, scale=cstt[0:1, 1:2]), inc="ACT")
            act_rs2 = V("ACT")
            A("tensor", lambda t: t.matmul(out=pa[:], lhsT=ones_r[:], rhs=rs[:], start=True, stop=True),
              inc="PE", wait=[("ACT", act_rs2)] + list(war[("pa",)]))
            pe_bc2 = V("PE")
            for k in range(KT):
                A("vector", lambda v, k=k, t0=t0: v.scalar_tensor_tensor(
                    out=xnr[:, k * TC:(k + 1) * TC], in0=xn1[:, k * 2048 + t0:k * 2048 + t0 + TC],
                    scalar=nws[:, k:k + 1], in1=pa[:], op0=ALU.mult, op1=ALU.mult),
                  inc="DVE", wait=([("PE", pe_bc2), ld0] + list(war[("xnrk", k)])))
            dve_xnr = V("DVE")
            war[("pa",)] = [("DVE", dve_xnr)]

            for gi in range(2):
                ps = pst[gi % 2]
                pkey = ("pa",) if gi % 2 == 0 else ("pb",)
                npart = 128 if gi == 0 else 64
                col0 = gi * 128
                for k in range(KT):
                    A("tensor", lambda t, k=k, col0=col0, ps=ps, npart=npart: t.matmul(
                        out=ps[:npart, :], lhsT=wxpb[:, k * 192 + col0:k * 192 + col0 + npart],
                        rhs=xnr[:, k * TC:(k + 1) * TC], start=(k == 0), stop=(k == KT - 1)),
                      inc="PE",
                      wait=(list(war[pkey]) + [("ACT", setup_act), ("DVE", dve_xnr)]) if k == 0 else [])
                pev = V("PE")
                if gi == 0:
                    A("scalar", lambda s, ps=ps: s.copy(out=dlo_t[:], in_=ps[0:64, :]),
                      inc="ACT", wait=[("PE", pev)] + list(war[("dlo",)]))
                    A("scalar", lambda s, ps=ps: s.copy(out=bs_t[:], in_=ps[64:128, :]), inc="ACT")
                else:
                    A("scalar", lambda s, ps=ps: s.copy(out=cs_t[:], in_=ps[0:64, :]),
                      inc="ACT", wait=[("PE", pev)] + list(war[("cs",)]))
                war[pkey] = [("ACT", V("ACT"))]
            act_proj = V("ACT")

            for j in range(KT):
                ps = pst[j % 2]
                pkey = ("pa",) if j % 2 == 0 else ("pb",)
                A("tensor", lambda t, j=j, ps=ps: t.matmul(
                    out=ps[:], lhsT=wdt1[:, j * 128:(j + 1) * 128], rhs=dlo_t[:],
                    start=True, stop=True),
                  inc="PE",
                  wait=[("ACT", act_proj), ("DVE", setup_dve)] + list(war[pkey]))
                pev = V("PE")
                A("vector", lambda v, j=j, ps=ps: v.tensor_scalar_add(
                    out=spv[:], in0=ps[:], scalar1=bdt1[:, j:j + 1]),
                  inc="DVE", wait=[("PE", pev)] + list(war[("spv",)]))
                dve_v = V("DVE")
                war[pkey] = [("DVE", dve_v)]
                A("scalar", lambda s: s.activation(out=sav[:], in_=spv[:], func=AF.Abs),
                  inc="ACT", wait=[("DVE", dve_v)] + list(war[("sav",)]))
                A("scalar", lambda s: s.activation(out=sav[:], in_=sav[:], func=AF.Exp, scale=-1.0), inc="ACT")
                A("scalar", lambda s: s.activation(out=sav[:], in_=sav[:], func=AF.Ln, bias=1.0), inc="ACT")
                act_sp = V("ACT")
                A("vector", lambda v, j=j: v.scalar_tensor_tensor(
                    out=delta1[:, j * TC:(j + 1) * TC], in0=spv[:], scalar=0.0, in1=sav[:],
                    op0=ALU.max, op1=ALU.add),
                  inc="DVE", wait=[("ACT", act_sp)] + list(war[("delta", j)]))
                war[("spv",)] = [("ACT", act_sp)]
                war[("sav",)] = [("DVE", V("DVE"))]
                A("vector", lambda v, j=j: v.tensor_mul(
                    out=dxn[:, j * TC:(j + 1) * TC], in0=delta1[:, j * TC:(j + 1) * TC],
                    in1=xnr[:, j * TC:(j + 1) * TC]),
                  inc="DVE")
            dve_front = V("DVE")

            A("gpsimd", lambda g: g.memset(yacc[:], 0.0), inc="POOL", wait=war[("yacc",)])
            pool_ymem = V("POOL")

            act_cr = 0
            for sg in range(NSG):
                for sl in range(SG):
                    s_ = sg * SG + sl
                    A("tensor", lambda t, s_=s_, sl=sl: t.matmul(
                        out=brp[:, sl * TC:(sl + 1) * TC],
                        lhsT=eyes[:, s_:s_ + 1].broadcast_to([64, 128]),
                        rhs=bs_t[:], start=True, stop=True),
                      inc="PE",
                      wait=([("ACT", act_proj)] + list(war[("brp",)])) if sl == 0 else [])
                pe_b = V("PE")
                A("scalar", lambda s: s.copy(out=brep[:], in_=brp[:]),
                  inc="ACT", wait=[("PE", pe_b)] + list(war[("brep",)]))
                act_b = V("ACT")
                war[("brp",)] = [("ACT", act_b)]

                for j in range(KT):
                    par = 0
                    hpar = j % 2
                    for sl in range(SG):
                        s_ = sg * SG + sl
                        A("scalar", lambda s, j=j, par=par, sl=sl, s_=s_: s.activation(
                            out=dab[par][:, sl * TC:(sl + 1) * TC], in_=delta1[:, j * TC:(j + 1) * TC],
                            func=AF.Exp, scale=a1c[:, j * 64 + s_:j * 64 + s_ + 1]),
                          inc="ACT" if sl == SG - 1 else None,
                          wait=(list(war[("dab", par)]) + [("DVE", dve_front), ("DVE", setup_dve)]) if sl == 0 else [])
                    act_da = V("ACT")
                    A("vector", lambda v, par=par: v.memset(dab[par][:, 0::64], 0.0),
                      inc="DVE", wait=[("ACT", act_da)])
                    dve_rst = V("DVE")
                    A("vector", lambda v, j=j, par=par: v.tensor_mul(
                        out=ub[par][:].rearrange("p (s t) -> p s t", s=SG),
                        in0=dxn[:, j * TC:(j + 1) * TC].unsqueeze(1).broadcast_to([128, SG, TC]),
                        in1=brep[:].rearrange("p (s t) -> p s t", s=SG)),
                      inc="DVE", wait=[("ACT", act_b)])
                    A("vector", lambda v, par=par, hpar=hpar: v.tensor_tensor_scan(
                        out=hb[hpar][:], data0=dab[par][:], data1=ub[par][:],
                        initial=0.0, op0=ALU.mult, op1=ALU.add),
                      inc="DVE",
                      wait=[("DVE", dve_rst)] + list(war[("hb", hpar)]))
                    dve_scan = V("DVE")
                    war[("dab", par)] = [("DVE", dve_scan)]
                    if j == 0:
                        for sl in range(SG):
                            s_ = sg * SG + sl
                            A("tensor", lambda t, s_=s_, sl=sl: t.matmul(
                                out=brp[:, sl * TC:(sl + 1) * TC],
                                lhsT=eyes[:, s_:s_ + 1].broadcast_to([64, 128]),
                                rhs=cs_t[:], start=True, stop=True),
                              inc="PE",
                              wait=([("ACT", act_b)] + list(war[("brp",)])) if sl == 0 else [])
                        pe_cr = V("PE")
                        A("scalar", lambda s: s.copy(out=crep[:], in_=brp[:]),
                          inc="ACT", wait=[("PE", pe_cr)] + list(war[("crep",)]))
                        act_cr = V("ACT")
                        war[("brp",)] = [("ACT", act_cr)]
                    A("vector", lambda v, par=par, hpar=hpar: v.tensor_mul(
                        out=wbf[par][:], in0=hb[hpar][:], in1=crep[:]),
                      inc="DVE", wait=[("ACT", act_cr)] + list(war[("wbf", par)]))
                    dve_w = V("DVE")
                    war[("hb", hpar)] = [("DVE", dve_w)]
                    A("gpsimd", lambda g, par=par: g.tensor_add(
                        out=t1p[par][:], in0=wbf[par][:, 0:TC], in1=wbf[par][:, TC:2 * TC]),
                      inc="POOL", wait=[("DVE", dve_w)])
                    A("gpsimd", lambda g, par=par: g.tensor_add(
                        out=t2p[par][:], in0=wbf[par][:, 2 * TC:3 * TC], in1=wbf[par][:, 3 * TC:4 * TC]),
                      inc="POOL")
                    A("gpsimd", lambda g, par=par: g.tensor_add(
                        out=ysg[par][:], in0=t1p[par][:], in1=t2p[par][:]),
                      inc="POOL")
                    A("gpsimd", lambda g, j=j, par=par: g.tensor_add(
                        out=yacc[:, j * TC:(j + 1) * TC], in0=yacc[:, j * TC:(j + 1) * TC],
                        in1=ysg[par][:]),
                      inc="POOL", wait=[("POOL", pool_ymem)])
                    pool_acc = V("POOL")
                    war[("wbf", par)] = [("POOL", pool_acc)]
                    war[("yacck", j)] = [("POOL", pool_acc)]
                war[("brep",)] = [("DVE", V("DVE"))]
                war[("crep",)] = [("DVE", V("DVE"))]
            war[("dlo",)] = [("PE", V("PE"))]
            war[("cs",)] = [("PE", V("PE"))]
            for j in range(KT):
                war[("delta", j)] = [("ACT", V("ACT")), ("DVE", V("DVE"))]

            for j in range(KT):
                A("vector", lambda v, j=j, t0=t0: v.scalar_tensor_tensor(
                    out=ob[:, j * TC:(j + 1) * TC], in0=xn1[:, j * 2048 + t0:j * 2048 + t0 + TC],
                    scalar=d1s[:, j:j + 1], in1=yacc[:, j * TC:(j + 1) * TC],
                    op0=ALU.mult, op1=ALU.add),
                  inc="DVE", wait=list(war[("yacck", j)]) + list(war[("obk", j)]))
            dve_ob = V("DVE")
            war[("yacc",)] = [("DVE", dve_ob)]
            for dcol in range(8):
                ps = pst[dcol % 2]
                pkey = ("pa",) if dcol % 2 == 0 else ("pb",)
                par = dcol % 2
                wp = 0
                wld = V("ACT")
                A("sync", lambda s, dcol=dcol, wp=wp: s.dma_start(
                    out=wst[:], in_=w_op1r[dcol]),
                  inc=(PF + "L_wst", 16), wait=[("ACT", wld)] + list(war[("wopst",)]))
                ldv = (PF + "L_wst", V(PF + "L_wst"))
                A("scalar", lambda s, wp=wp: s.copy(out=wopb[wp][:], in_=wst[:]),
                  inc="ACT", wait=[ldv] + list(war[("wopb",)]))
                act_wc = V("ACT")
                war[("wopst",)] = [("ACT", act_wc)]
                for k in range(KT):
                    A("tensor", lambda t, k=k, wp=wp, ps=ps: t.matmul(
                        out=ps[:], lhsT=wopb[wp][:, k * 128:(k + 1) * 128],
                        rhs=ob[:, k * TC:(k + 1) * TC], start=(k == 0), stop=(k == KT - 1)),
                      inc="PE",
                      wait=([("DVE", dve_ob), ("ACT", act_wc)] + list(war[pkey])) if k == 0 else [])
                pev = V("PE")
                war[("wopb",)] = [("PE", pev)]
                A("scalar", lambda s, par=par, ps=ps: s.copy(out=spo[par][:], in_=ps[:]),
                  inc="ACT", wait=[("PE", pev)] + list(war[("spo", par)]))
                actv = V("ACT")
                war[pkey] = [("ACT", actv)]
                A("gpsimd", lambda g, dcol=dcol, par=par, t0=t0: g.dma_start(
                    out=io["s1o"][dcol * 128:(dcol + 1) * 128, t0:t0 + TC], in_=spo[par][:]),
                  inc=(PF + f"O_sp{par}", 16), wait=[("ACT", actv)])
                war[("spo", par)] = [(PF + f"O_sp{par}", V(PF + f"O_sp{par}"))]
            pe_end = V("PE")
            for j in range(KT):
                war[("obk", j)] = [("PE", pe_end)]
                war[("xnrk", j)] = [("PE", pe_end), ("DVE", V("DVE"))]
            for k in range(KT):
                war[("xnk", k)] = [("PE", pe_end), ("ACT", V("ACT"))]
            yield ("chunk", c)


def stage_D1(d, x1_by_core):
    dtw = np.asarray(d["L1_attn_dt_proj_w"])
    opT = np.ascontiguousarray(np.asarray(d["L1_attn_out_proj_w"]).T)
    wop1r = np.ascontiguousarray(opT.reshape(8, 128, 8, 128).transpose(2, 1, 0, 3).reshape(8, 128, 1024))
    eye = np.eye(64).astype(BF)
    cstv = np.ascontiguousarray(np.broadcast_to(
        np.array([1e-5, -0.5, 1e-6, 1e-8], np.float32)[None, :], (128, 4)))
    base = {
        "nwc": _col(np.asarray(d["L1_attn_norm_w"])),
        "w_xp": np.ascontiguousarray(np.asarray(d["L1_attn_x_proj_w"]).T),
        "w_dt1": np.ascontiguousarray(dtw.T),
        "b_dt1": _col(np.asarray(d["L1_attn_dt_proj_b"])),
        "a1in": np.ascontiguousarray(np.asarray(d["L1_attn_A_log"]).reshape(8, 128, 64).transpose(1, 0, 2).reshape(128, 512)),
        "d1c": _col(np.asarray(d["L1_attn_D"])),
        "w_op1r": wop1r,
        "eye64": eye,
        "cst": cstv,
    }
    return [dict(base, x1in=np.ascontiguousarray(x1_by_core[c])) for c in range(8)]


def stage_D2(d, x1_by_core, s1_by_core, xn1_by_core):
    w1 = np.asarray(d["L1_ffn_W1"])
    w2 = np.asarray(d["L1_ffn_W2"])
    w1r = np.ascontiguousarray(w1.reshape(8, 8, 128, 32, 128).transpose(0, 3, 2, 1, 4).reshape(8, 32, 128, 1024))
    w2r = np.ascontiguousarray(w2.reshape(8, 32, 128, 8, 128).transpose(0, 3, 2, 1, 4).reshape(8, 8, 128, 4096))
    b1cv = np.ascontiguousarray(np.asarray(d["L1_ffn_b1"]).reshape(8, 32, 128).transpose(2, 0, 1).reshape(128, 256))
    b2cv = np.ascontiguousarray(np.asarray(d["L1_ffn_b2"]).reshape(8, 8, 128).transpose(2, 0, 1).reshape(128, 64))
    wgtv = np.ascontiguousarray(np.asarray(d["L1_ffn_Wg"]).reshape(8, 128, 8).transpose(1, 0, 2).reshape(128, 64))
    bgrv = np.ascontiguousarray(np.broadcast_to(np.asarray(d["L1_ffn_bg"])[None, :], (128, 8)))
    e128 = np.eye(128).astype(BF)
    e8 = np.eye(8).astype(BF)
    maps = []
    for c in range(8):
        m = {
            "xn_in": np.ascontiguousarray(xn1_by_core[c]),
            "x_own": np.ascontiguousarray(x1_by_core[c]),
            "s_in": np.ascontiguousarray(s1_by_core[c]),
            "bgr": bgrv,
            "wgt": wgtv, "w1r": w1r, "w2r": w2r, "b1c": b1cv, "b2c": b2cv,
            "eye128": e128, "eye8": e8,
        }
        maps.append(m)
    return maps


def build_fused(phases="all"):
    nc = bass.Bass("TRN2", target_bir_lowering=False, debug=False)
    E = lambda nm, sh, dt: nc.dram_tensor(nm, sh, dt, kind="ExternalInput").ap()
    xb = E("xb", [1024, 4096], F32)
    x_own = E("x_own", [1024, 2048], F32)
    w_ip = E("w_ip", [1024, 2240], F32)
    w_conv = E("w_conv", [128, 32], F32)
    b_conv = E("b_conv", [128, 8], F32)
    w_dt = E("w_dt", [64, 1024], F32)
    b_dt = E("b_dt", [128, 8], F32)
    a_log = E("a_log", [128, 64], F32)
    d_prm = E("d_prm", [128, 8], F32)
    w_op = E("w_op", [1024, 1024], F32)
    eye64 = E("eye64", [64, 64], BF16)
    cst = E("cst", [128, 4], F32)
    eye128 = E("eye128", [128, 128], BF16)
    eye8 = E("eye8", [8, 8], BF16)
    wgt0 = E("wgt0", [128, 64], F32)
    w1r0 = E("w1r0", [8, 32, 128, 1024], F32)
    w2r0 = E("w2r0", [8, 8, 128, 4096], F32)
    b1c0 = E("b1c0", [128, 256], F32)
    b2c0 = E("b2c0", [128, 64], F32)
    nwc = E("nwc", [128, 8], F32)
    w_xp = E("w_xp", [1024, 192], F32)
    w_dt1 = E("w_dt1", [64, 1024], F32)
    b_dt1 = E("b_dt1", [128, 8], F32)
    a1in = E("a1in", [128, 512], F32)
    d1c = E("d1c", [128, 8], F32)
    w_op1r = E("w_op1r", [8, 128, 1024], F32)
    wgt1 = E("wgt1", [128, 64], F32)
    w1r1 = E("w1r1", [8, 32, 128, 1024], F32)
    w2r1 = E("w2r1", [8, 8, 128, 4096], F32)
    b1c1 = E("b1c1", [128, 256], F32)
    b2c1 = E("b2c1", [128, 64], F32)
    bgr = E("bgr", [128, 8], F32)
    out = nc.dram_tensor("out", [1024, 2048], F32, kind="ExternalOutput").ap()
    cc_in = nc.dram_tensor("cc_in", [2048, 2048], F32).ap()
    cc_out = nc.dram_tensor("cc_out", [1024, 2048], F32).ap()
    xn_dr = nc.dram_tensor("xn_dr", [1024, 4096], BF16).ap()
    x1_dr = nc.dram_tensor("x1_dr", [1024, 2048], F32).ap()
    s1_dr = nc.dram_tensor("s1_dr", [1024, 2048], BF16).ap()
    xn1_dr = nc.dram_tensor("xn1_dr", [1024, 2048], BF16).ap()

    p = Prog()
    with contextlib.ExitStack() as es0:
        ioB = {"pref": "B_", "xb": xb, "w_ip": w_ip, "w_conv": w_conv, "b_conv": b_conv,
               "w_dt": w_dt, "b_dt": b_dt, "a_log": a_log, "d_prm": d_prm, "w_op": w_op,
               "eye64": eye64, "cst": cst, "xn_out": xn_dr,
               "spar_st": lambda dcol, t0: cc_in[
                   (t0 // 2048) * 1024 + dcol * 128:(t0 // 2048) * 1024 + (dcol + 1) * 128,
                   (t0 % 2048):(t0 % 2048) + TC]}
        _emit_B(nc, p, ioB)
        out_after_B = [("B_O_sp", p.v("B_O_sp"))]
        p.barrier()
        p.add("gpsimd", lambda g: g.collective_compute(
            "ReduceScatter", ALU.add, replica_groups=[[0, 1], [2, 3], [4, 5], [6, 7]],
            ins=[cc_in], outs=[cc_out]),
            inc=("CC", 1), wait=out_after_B)
        xnc_dr = nc.dram_tensor("xnc_dr", [1024, 2048], BF16).ap()
        w1bd0 = nc.dram_tensor("w1bd0", [8, 32, 128, 1024], BF16).ap()
        w2bd0 = nc.dram_tensor("w2bd0", [8, 8, 128, 4096], BF16).ap()
        ioC = {"pref": "C_", "lay": 0, "s_mode": "single_f32",
               "ln_x": x_own, "ln_cst": cst,
               "xn_st": lambda k, t0: xnc_dr[k * 128:(k + 1) * 128, t0:t0 + 512],
               "xn_ld": lambda k, tm0: xnc_dr[k * 128:(k + 1) * 128, tm0:tm0 + TCM],
               "x_ld": lambda j, tm0: x_own[j * 128:(j + 1) * 128, tm0:tm0 + TCM],
               "s_ld": lambda j, tm0: cc_out[j * 128:(j + 1) * 128, tm0:tm0 + TCM],
               "s_wait": [("CC", 1)],
               "out_st": lambda j, tm0: x1_dr[j * 128:(j + 1) * 128, tm0:tm0 + TCM],
               "wgt": wgt0, "w1r": w1r0, "w2r": w2r0, "b1c": b1c0, "b2c": b2c0,
               "w1b_dr": w1bd0, "w2b_dr": w2bd0,
               "eye128": eye128, "eye8": eye8}
        if phases in ("p1", "p1x"):
            ioC["out_st"] = lambda j, tm0: out[j * 128:(j + 1) * 128, tm0:tm0 + TCM]
        if phases == "p1x":
            xn_dbg = nc.dram_tensor("xn_dbg", [1024, 2048], BF16, kind="ExternalInput").ap()
            ioC.pop("ln_x"); ioC.pop("ln_cst"); ioC.pop("xn_st")
            ioC["xn_ld"] = lambda k, tm0: xn_dbg[k * 128:(k + 1) * 128, tm0:tm0 + TCM]
        _emit_tail(nc, p, ioC)
        if phases == "p1x":
            finish(p, es0, nc)
            return nc
        if phases == "p1":
            finish(p, es0, nc)
            return nc
        p.barrier()
        ioD1 = {"pref": "E_", "x1in": x1_dr, "nwc": nwc, "w_xp": w_xp, "w_dt1": w_dt1,
                "b_dt1": b_dt1, "a1in": a1in, "d1c": d1c, "w_op1r": w_op1r,
                "eye64": eye64, "cst": cst, "s1o": s1_dr, "xn1o": xn1_dr}
        _emit_D1(nc, p, ioD1)
        p.barrier()
        w1bd1 = nc.dram_tensor("w1bd1", [8, 32, 128, 1024], BF16).ap()
        w2bd1 = nc.dram_tensor("w2bd1", [8, 8, 128, 4096], BF16).ap()
        ioD2 = {"pref": "F_", "lay": 1, "s_mode": "single_bf16",
                "xn_ld": lambda k, tm0: xn1_dr[k * 128:(k + 1) * 128, tm0:tm0 + TCM],
                "x_ld": lambda j, tm0: x1_dr[j * 128:(j + 1) * 128, tm0:tm0 + TCM],
                "s_ld": lambda j, tm0: s1_dr[j * 128:(j + 1) * 128, tm0:tm0 + TCM],
                "out_st": lambda j, tm0: out[j * 128:(j + 1) * 128, tm0:tm0 + TCM],
                "bgr": bgr, "wgt": wgt1, "w1r": w1r1, "w2r": w2r1, "b1c": b1c1,
                "b2c": b2c1, "w1b_dr": w1bd1, "w2b_dr": w2bd1,
                "eye128": eye128, "eye8": eye8}
        _emit_tail(nc, p, ioD2)
        finish(p, es0, nc)
    return nc


def stage_fused(d):
    mapsB = stage_B(d)
    x = np.asarray(d["L0_attn_x"])
    w1 = np.asarray(d["L0_ffn_W1"]); w2 = np.asarray(d["L0_ffn_W2"])
    w1r0 = np.ascontiguousarray(w1.reshape(8, 8, 128, 32, 128).transpose(0, 3, 2, 1, 4).reshape(8, 32, 128, 1024))
    w2r0 = np.ascontiguousarray(w2.reshape(8, 32, 128, 8, 128).transpose(0, 3, 2, 1, 4).reshape(8, 8, 128, 4096))
    b1c0 = np.ascontiguousarray(np.asarray(d["L0_ffn_b1"]).reshape(8, 32, 128).transpose(2, 0, 1).reshape(128, 256))
    b2c0 = np.ascontiguousarray(np.asarray(d["L0_ffn_b2"]).reshape(8, 8, 128).transpose(2, 0, 1).reshape(128, 64))
    wgt0 = np.ascontiguousarray(np.asarray(d["L0_ffn_W_gate"]).reshape(8, 128, 8).transpose(1, 0, 2).reshape(128, 64))
    w11 = np.asarray(d["L1_ffn_W1"]); w21 = np.asarray(d["L1_ffn_W2"])
    w1r1 = np.ascontiguousarray(w11.reshape(8, 8, 128, 32, 128).transpose(0, 3, 2, 1, 4).reshape(8, 32, 128, 1024))
    w2r1 = np.ascontiguousarray(w21.reshape(8, 32, 128, 8, 128).transpose(0, 3, 2, 1, 4).reshape(8, 8, 128, 4096))
    b1c1 = np.ascontiguousarray(np.asarray(d["L1_ffn_b1"]).reshape(8, 32, 128).transpose(2, 0, 1).reshape(128, 256))
    b2c1 = np.ascontiguousarray(np.asarray(d["L1_ffn_b2"]).reshape(8, 8, 128).transpose(2, 0, 1).reshape(128, 64))
    wgt1 = np.ascontiguousarray(np.asarray(d["L1_ffn_Wg"]).reshape(8, 128, 8).transpose(1, 0, 2).reshape(128, 64))
    bgrv = np.ascontiguousarray(np.broadcast_to(np.asarray(d["L1_ffn_bg"])[None, :], (128, 8)))
    dtw1 = np.asarray(d["L1_attn_dt_proj_w"])
    opT1 = np.ascontiguousarray(np.asarray(d["L1_attn_out_proj_w"]).T)
    wop1r = np.ascontiguousarray(opT1.reshape(8, 128, 8, 128).transpose(2, 1, 0, 3).reshape(8, 128, 1024))
    cstv = np.ascontiguousarray(np.broadcast_to(
        np.array([1e-5, -0.5, 1e-6, 1e-8], np.float32)[None, :], (128, 4)))
    e128 = np.eye(128).astype(BF)
    e8 = np.eye(8).astype(BF)
    shared = {
        "eye128": e128, "eye8": e8,
        "wgt0": wgt0, "w1r0": w1r0, "w2r0": w2r0, "b1c0": b1c0, "b2c0": b2c0,
        "nwc": _col(np.asarray(d["L1_attn_norm_w"])),
        "w_xp": np.ascontiguousarray(np.asarray(d["L1_attn_x_proj_w"]).T),
        "w_dt1": np.ascontiguousarray(dtw1.T),
        "b_dt1": _col(np.asarray(d["L1_attn_dt_proj_b"])),
        "a1in": np.ascontiguousarray(np.asarray(d["L1_attn_A_log"]).reshape(8, 128, 64).transpose(1, 0, 2).reshape(128, 512)),
        "d1c": _col(np.asarray(d["L1_attn_D"])),
        "w_op1r": wop1r,
        "wgt1": wgt1, "w1r1": w1r1, "w2r1": w2r1, "b1c1": b1c1, "b2c1": b2c1,
        "bgr": bgrv,
    }
    maps = []
    for c in range(8):
        b, ih = c // 2, c % 2
        m = dict(mapsB[c])
        m["cst"] = cstv
        m["x_own"] = np.ascontiguousarray(x[b].T[:, ih * 2048:(ih + 1) * 2048])
        m.update(shared)
        maps.append(m)
    return maps


def kernel(**inputs):
    d = {k: np.asarray(v) for k, v in inputs.items()}
    nc = build_fused()
    res = run_bass_kernel_spmd(nc, stage_fused(d), core_ids=list(range(8)))
    out = np.empty((4, 4096, 1024), np.float32)
    for c in range(8):
        b, ih = c // 2, c % 2
        out[b, ih * 2048:(ih + 1) * 2048, :] = np.asarray(res.results[c]["out"]).T
    return out


# version 15 (final)
# speedup vs baseline: 1.1963x; 1.0964x over previous
import contextlib
from collections import defaultdict
import numpy as np
import ml_dtypes
import concourse.bass as bass
import concourse.mybir as mybir
from concourse.bass_utils import run_bass_kernel_spmd

F32 = mybir.dt.float32
BF16 = mybir.dt.bfloat16
AF = mybir.ActivationFunctionType
ALU = mybir.AluOpType
BF = ml_dtypes.bfloat16


class Prog:

    def __init__(self):
        self.ops = []
        self.cnt = defaultdict(int)

    def add(self, eng, fn, inc=None, wait=()):
        if inc is not None and isinstance(inc, str):
            inc = (inc, 1)
        wait = tuple((s, v) for (s, v) in wait if v > 0)
        self.ops.append((eng, fn, inc, wait))
        if inc:
            self.cnt[inc[0]] += inc[1]
            return self.cnt[inc[0]]
        return None

    def v(self, sem):
        return self.cnt[sem]

    def barrier(self):
        self.ops.append(("BARRIER", dict(self.cnt), None, ()))

    def emit(self, block, sems):
        def mk(eng_name):
            def run(eng):
                last = defaultdict(int)
                pend = None
                for e, fn, inc, waits in self.ops:
                    if e == "BARRIER":
                        pend = fn
                        continue
                    if e != eng_name:
                        continue
                    if pend is not None:
                        waits = list(waits) + [(s, v) for s, v in pend.items()]
                        pend = None
                    for s, val in waits:
                        if val > last[s]:
                            eng.wait_ge(sems[s], val)
                            last[s] = val
                    instr = fn(eng)
                    if inc is not None:
                        assert instr is not None, f"op on {e} returned None but has inc {inc}"
                        instr.then_inc(sems[inc[0]], inc[1])
            return run
        block.sync(mk("sync"))
        block.scalar(mk("scalar"))
        block.vector(mk("vector"))
        block.gpsimd(mk("gpsimd"))
        block.tensor(mk("tensor"))


def finish(p, es, nc):
    names = sorted(set(p.cnt.keys()) | {"PE", "ACT", "DVE", "POOL"})
    final = [(n, p.cnt[n]) for n in names if ("O_" in n or n == "OUT") and p.cnt[n] > 0]
    sems = {n: es.enter_context(nc.semaphore(f"sem{i}")) for i, n in enumerate(names)}
    def fin(g):
        ins = None
        for n, v in final:
            ins = g.wait_ge(sems[n], v)
        return ins
    p.add("gpsimd", fin)
    block = es.enter_context(nc.Block())
    p.emit(block, sems)


TC = 512
NCH = 4096 // TC
KT = 8
IT = 8
SG = 4
NSG = 64 // SG


def build_B():
    nc = bass.Bass("TRN2", target_bir_lowering=False, debug=False)
    io = {}
    for nm, sh, dt in [("xb", [1024, 4096], F32), ("w_ip", [1024, 2240], F32),
                       ("w_conv", [128, 32], F32), ("b_conv", [128, 8], F32),
                       ("w_dt", [64, 1024], F32), ("b_dt", [128, 8], F32),
                       ("a_log", [128, 64], F32), ("d_prm", [128, 8], F32),
                       ("w_op", [1024, 1024], F32), ("eye64", [64, 64], BF16),
                       ("cst", [128, 4], F32)]:
        io[nm] = nc.dram_tensor(nm, sh, dt, kind="ExternalInput").ap()
    s_par = nc.dram_tensor("s_par", [1024, 4096], F32, kind="ExternalOutput").ap()
    io["xn_out"] = nc.dram_tensor("xn_out", [1024, 4096], BF16, kind="ExternalOutput").ap()
    io["spar_st"] = lambda dcol, t0: s_par[dcol * 128:(dcol + 1) * 128, t0:t0 + TC]
    p = Prog()
    with contextlib.ExitStack() as es0:
        _emit_B(nc, p, io)
        finish(p, es0, nc)
    return nc


def _emit_B(nc, p, io):
    for _ in _emit_B_gen(nc, p, io):
        pass


def _emit_B_gen(nc, p, io):
    xb, w_ip, w_conv, b_conv = io["xb"], io["w_ip"], io["w_conv"], io["b_conv"]
    w_dt, b_dt, a_log, d_prm = io["w_dt"], io["b_dt"], io["a_log"], io["d_prm"]
    w_op, eye64, cst, xn_out = io["w_op"], io["eye64"], io["cst"], io["xn_out"]
    spar_st = io["spar_st"]
    PF = io.get("pref", "")
    with contextlib.ExitStack() as es:
        _pf = io.get("pref", "")
        T = lambda name, shape, dt: es.enter_context(nc.sbuf_tensor(_pf + name, shape, dt))
        PT = lambda name, shape, dt: es.enter_context(nc.psum_tensor(_pf + name, shape, dt))
        wip = T("wip", [128, KT * 2240], BF16)
        wop = T("wop", [128, KT * 1024], BF16)
        wdt = T("wdtb", [64, 1024], BF16)
        wcv = T("wcv", [128, 32], F32)
        bcv = T("bcv", [128, 8], F32)
        bdt = T("bdtb", [128, 8], F32)
        dpr = T("dpr", [128, 8], F32)
        arep = T("arep", [128, 64], F32)
        eyes = T("eyes", [64, 64], BF16)
        ones_c = T("ones_c", [128, 1], F32)
        ones_r = T("ones_r", [1, 128], F32)
        wst = T("wst", [128, 1120], F32)
        cstt = T("cstt", [128, 4], F32)
        wdst = T("wdst", [64, 512], F32)
        xfs = [T(f"xf{i}", [128, TC], F32) for i in range(2)]
        xsq = [T(f"xsq{i}", [128, TC], F32) for i in range(2)]
        nm = T("nm", [1, TC], F32)
        m2 = T("m2", [1, TC], F32)
        var = T("var", [1, TC], F32)
        rs = T("rs", [1, TC], F32)
        nmr = T("nmr", [1, TC], F32)
        xnt = [T(f"xnt{i}", [128, TC], F32) for i in range(1)]
        xn = T("xn", [128, KT * TC], BF16)
        xin = T("xin", [128, IT * (TC + 3)], BF16)
        zb = T("zb", [128, IT * TC], BF16)
        bcb = T("bcb", [64, TC], BF16)
        ccb = T("ccb", [64, TC], BF16)
        drw = T("drw", [64, TC], BF16)
        spv = [T(f"spv{i}", [128, TC], F32) for i in range(1)]
        sav = [T(f"sav{i}", [128, TC], F32) for i in range(1)]
        delta = T("delta", [128, IT * TC], BF16)
        xc = T("xc", [128, IT * TC], BF16)
        cacc = [T(f"cacc{i}", [128, TC], F32) for i in range(1)]
        dxc = T("dxc", [128, IT * TC], BF16)
        brep = T("brep", [128, SG * TC], BF16)
        crep = T("crep", [128, SG * TC], BF16)
        dab = [T(f"dab{i}", [128, SG * TC], F32) for i in range(1)]
        ub = [T(f"ub{i}", [128, SG * TC], BF16) for i in range(1)]
        hb = [T(f"hb{i}", [128, SG * TC], BF16) for i in range(2)]
        wbf = [T(f"wbf{i}", [128, SG * TC], BF16) for i in range(1)]
        t1p = [T(f"t1p{i}", [128, TC], BF16) for i in range(1)]
        t2p = [T(f"t2p{i}", [128, TC], BF16) for i in range(1)]
        ysg = [T(f"ysg{i}", [128, TC], F32) for i in range(1)]
        yacc = T("yacc", [128, IT * TC], F32)
        hstate = T("hstate", [128, IT * 64], F32)
        szb = [T(f"szb{i}", [128, TC], BF16) for i in range(1)]
        y1f = [T(f"y1f{i}", [128, TC], F32) for i in range(1)]
        ypp = T("ypp", [128, IT * TC], BF16)
        spo = [T(f"spo{i}", [128, TC], F32) for i in range(1)]
        pa = PT("pa", [128, TC], F32)
        pb = PT("pb", [128, TC], F32)
        brp = PT("brp", [128, SG * TC], F32)
        mps = PT("mps", [1, TC], F32)
        sps = PT("sps", [1, TC], F32)
        pst = [pa, pb]

        A = p.add
        V = p.v

        A("vector", lambda v: v.memset(ones_c[:], 1.0), inc="DVE")
        A("vector", lambda v: v.memset(ones_r[:], 1.0), inc="DVE")
        A("vector", lambda v: v.memset(hstate[:], 0.0), inc="DVE")
        for k in range(IT):
            A("vector", lambda v, k=k: v.memset(xin[:, k * (TC + 3):k * (TC + 3) + 3], 0.0), inc="DVE")
        A("sync", lambda s: s.dma_start(out=eyes[:], in_=eye64), inc=(PF + "L_stp", 16))
        A("sync", lambda s: s.dma_start(out=cstt[:], in_=cst), inc=(PF + "L_stp", 16))
        A("sync", lambda s: s.dma_start(out=arep[:], in_=a_log), inc=(PF + "L_stp", 16))
        A("sync", lambda s: s.dma_start(out=bcv[:], in_=b_conv), inc=(PF + "L_stp", 16))
        A("sync", lambda s: s.dma_start(out=bdt[:], in_=b_dt), inc=(PF + "L_stp", 16))
        A("sync", lambda s: s.dma_start(out=dpr[:], in_=d_prm), inc=(PF + "L_stp", 16))
        A("sync", lambda s: s.dma_start(out=wcv[:], in_=w_conv), inc=(PF + "L_stp", 16))
        ld0v = V(PF + "L_stp")
        ld0 = (PF + "L_stp", ld0v)
        A("scalar", lambda s: s.activation(out=arep[:], in_=arep[:], func=AF.Exp),
          inc="ACT", wait=[ld0])
        A("vector", lambda v: v.tensor_scalar_mul(out=arep[:], in0=arep[:], scalar1=-1.0),
          inc="DVE", wait=[("ACT", V("ACT"))])
        for h in range(2):
            w_cast = V("DVE")
            A("sync", lambda s, h=h: s.dma_start(out=wdst[:], in_=w_dt[:, h * 512:(h + 1) * 512]),
              inc=(PF + "L_wdst", 16), wait=[("DVE", w_cast)])
            A("vector", lambda v, h=h: v.tensor_copy(out=wdt[:, h * 512:(h + 1) * 512], in_=wdst[:]),
              inc="DVE", wait=[(PF + "L_wdst", V(PF + "L_wdst"))])
        for k in range(KT):
            for h in range(2):
                c0, c1 = h * 1120, min(2240, (h + 1) * 1120)
                w_cast = V("ACT")
                A("sync", lambda s, k=k, c0=c0, c1=c1: s.dma_start(
                    out=wst[:, :c1 - c0], in_=w_ip[k * 128:(k + 1) * 128, c0:c1]),
                  inc=(PF + "L_wst", 16), wait=[("ACT", w_cast)])
                A("scalar", lambda s, k=k, c0=c0, c1=c1: s.copy(
                    out=wip[:, k * 2240 + c0:k * 2240 + c1], in_=wst[:, :c1 - c0]),
                  inc="ACT", wait=[(PF + "L_wst", V(PF + "L_wst"))])
        for k in range(KT):
            w_cast = V("ACT")
            A("sync", lambda s, k=k: s.dma_start(out=wst[:, :1024], in_=w_op[k * 128:(k + 1) * 128, :]),
              inc=(PF + "L_wst", 16), wait=[("ACT", w_cast)])
            A("scalar", lambda s, k=k: s.copy(out=wop[:, k * 1024:(k + 1) * 1024], in_=wst[:, :1024]),
              inc="ACT", wait=[(PF + "L_wst", V(PF + "L_wst"))])
        setup_act = V("ACT")
        setup_dve = V("DVE")

        war = defaultdict(tuple)
        yield "setup"

        for c in range(NCH):
            t0 = c * TC
            for k in range(KT):
                par = k % 2
                A("sync", lambda s, k=k, par=par, t0=t0: s.dma_start(
                    out=xfs[par][:], in_=xb[k * 128:(k + 1) * 128, t0:t0 + TC]),
                  inc=(PF + f"L_xf{par}", 16), wait=war[("xf", par)])
                ldw = (PF + f"L_xf{par}", V(PF + f"L_xf{par}"))
                A("scalar", lambda s, par=par: s.activation(out=xsq[par][:], in_=xfs[par][:], func=AF.Square),
                  inc="ACT", wait=[ldw] + list(war[("xsq", par)]))
                actv = V("ACT")
                A("tensor", lambda t, k=k, par=par: t.matmul(
                    out=mps[:], lhsT=ones_c[:], rhs=xfs[par][:], start=(k == 0), stop=(k == KT - 1)),
                  wait=([ldw] + list(war[("mps",)])) if k == 0 else [ldw])
                A("tensor", lambda t, k=k, par=par: t.matmul(
                    out=sps[:], lhsT=ones_c[:], rhs=xsq[par][:], start=(k == 0), stop=(k == KT - 1)),
                  inc="PE",
                  wait=[("ACT", actv)])
                war[("xf", par)] = [("PE", V("PE"))]
                war[("xsq", par)] = [("PE", V("PE"))]
            pe_stats = V("PE")
            A("vector", lambda v: v.tensor_scalar_mul(out=nm[:], in0=mps[:], scalar1=-1.0 / 1024.0),
              inc="DVE", wait=[("PE", pe_stats)])
            A("vector", lambda v: v.tensor_mul(out=m2[:], in0=nm[:], in1=nm[:]), inc="DVE")
            A("vector", lambda v: v.scalar_tensor_tensor(
                out=var[:], in0=sps[:], scalar=1.0 / 1024.0, in1=m2[:], op0=ALU.mult, op1=ALU.subtract),
              inc="DVE", wait=[("PE", pe_stats)])
            dve_var = V("DVE")
            war[("mps",)] = [("DVE", dve_var)]
            A("scalar", lambda s: s.activation(out=rs[:], in_=var[:], func=AF.Ln, bias=cstt[0:1, 0:1]),
              inc="ACT", wait=[("DVE", dve_var)])
            A("scalar", lambda s: s.activation(out=rs[:], in_=rs[:], func=AF.Exp, scale=cstt[0:1, 1:2]), inc="ACT")
            act_rs = V("ACT")
            A("vector", lambda v: v.tensor_mul(out=nmr[:], in0=nm[:], in1=rs[:]),
              inc="DVE", wait=[("ACT", act_rs)])
            dve_nmr = V("DVE")
            A("tensor", lambda t: t.matmul(out=pa[:], lhsT=ones_r[:], rhs=rs[:], start=True, stop=True),
              wait=[("ACT", act_rs)] + list(war[("pa",)]))
            A("tensor", lambda t: t.matmul(out=pb[:], lhsT=ones_r[:], rhs=nmr[:], start=True, stop=True),
              inc="PE", wait=[("DVE", dve_nmr)] + list(war[("pb",)]))
            pe_bc = V("PE")
            for k in range(KT):
                par = k % 2
                A("sync", lambda s, k=k, par=par, t0=t0: s.dma_start(
                    out=xfs[par][:], in_=xb[k * 128:(k + 1) * 128, t0:t0 + TC]),
                  inc=(PF + f"L_xf{par}", 16), wait=war[("xf", par)])
                ldw = (PF + f"L_xf{par}", V(PF + f"L_xf{par}"))
                A("vector", lambda v, par=par: v.tensor_mul(out=xnt[0][:], in0=xfs[par][:], in1=pa[:]),
                  inc="DVE", wait=[ldw, ("PE", pe_bc)])
                A("vector", lambda v, k=k, par=par: v.tensor_add(
                    out=xn[:, k * TC:(k + 1) * TC], in0=xnt[0][:], in1=pb[:]),
                  inc="DVE", wait=war[("xnk", k)])
                dv = V("DVE")
                war[("xf", par)] = [("DVE", dv)]
                A("gpsimd", lambda g, k=k, t0=t0: g.dma_start(
                    out=xn_out[k * 128:(k + 1) * 128, t0:t0 + TC], in_=xn[:, k * TC:(k + 1) * TC]),
                  inc=(PF + "O_xn", 16), wait=[("DVE", dv)])
            oxn_end = V(PF + "O_xn")
            for k in range(KT):
                war[("xnk", k)] = [(PF + "O_xn", oxn_end)]
            dve_xn = V("DVE")
            war[("pa",)] = [("DVE", dve_xn)]
            war[("pb",)] = [("DVE", dve_xn)]

            groups = [("xin", j) for j in range(IT)] + [("z", j) for j in range(IT)] + [("bc", 0), ("draw", 0)]
            for gi, (gname, j) in enumerate(groups):
                ps = pst[gi % 2]
                pkey = ("pa",) if gi % 2 == 0 else ("pb",)
                col0 = {"xin": j * 128, "z": 1024 + j * 128, "bc": 2048, "draw": 2176}[gname]
                npart = 64 if gname == "draw" else 128
                for k in range(KT):
                    A("tensor", lambda t, k=k, col0=col0, ps=ps, npart=npart: t.matmul(
                        out=ps[:npart, :], lhsT=wip[:, k * 2240 + col0:k * 2240 + col0 + npart],
                        rhs=xn[:, k * TC:(k + 1) * TC], start=(k == 0), stop=(k == KT - 1)),
                      inc="PE",
                      wait=(list(war[pkey]) + [("ACT", setup_act), ("DVE", dve_xn)]) if k == 0 else [])
                pev = V("PE")
                if gname == "xin":
                    A("scalar", lambda s, j=j, ps=ps: s.copy(
                        out=xin[:, j * (TC + 3) + 3:(j + 1) * (TC + 3)], in_=ps[:]),
                      inc="ACT", wait=[("PE", pev)] + list(war[("xin", j)]))
                elif gname == "z":
                    A("scalar", lambda s, j=j, ps=ps: s.copy(out=zb[:, j * TC:(j + 1) * TC], in_=ps[:]),
                      inc="ACT", wait=[("PE", pev)] + list(war[("z", j)]))
                elif gname == "bc":
                    A("scalar", lambda s, ps=ps: s.copy(out=bcb[:], in_=ps[0:64, :]),
                      inc="ACT", wait=[("PE", pev)] + list(war[("bcb",)]))
                    A("scalar", lambda s, ps=ps: s.copy(out=ccb[:], in_=ps[64:128, :]),
                      inc="ACT")
                else:
                    A("scalar", lambda s, ps=ps: s.copy(out=drw[:], in_=ps[:64, :]),
                      inc="ACT", wait=[("PE", pev)] + list(war[("drw",)]))
                war[pkey] = [("ACT", V("ACT"))]
            act_front = V("ACT")

            for j in range(IT):
                par = 0
                ps = pst[j % 2]
                pkey = ("pa",) if j % 2 == 0 else ("pb",)
                A("tensor", lambda t, j=j, ps=ps: t.matmul(
                    out=ps[:], lhsT=wdt[:, j * 128:(j + 1) * 128], rhs=drw[:],
                    start=True, stop=True),
                  inc="PE",
                  wait=[("ACT", act_front), ("DVE", setup_dve)] + list(war[pkey]))
                pev = V("PE")
                A("vector", lambda v, j=j, par=par, ps=ps: v.tensor_scalar_add(
                    out=spv[par][:], in0=ps[:], scalar1=bdt[:, j:j + 1]),
                  inc="DVE", wait=[("PE", pev)] + list(war[("spv", par)]))
                dve_v = V("DVE")
                war[pkey] = [("DVE", dve_v)]
                A("scalar", lambda s, par=par: s.activation(out=sav[par][:], in_=spv[par][:], func=AF.Abs),
                  inc="ACT", wait=[("DVE", dve_v)] + list(war[("sav", par)]))
                A("scalar", lambda s, par=par: s.activation(out=sav[par][:], in_=sav[par][:], func=AF.Exp, scale=-1.0),
                  inc="ACT")
                A("scalar", lambda s, par=par: s.activation(out=sav[par][:], in_=sav[par][:], func=AF.Ln, bias=1.0),
                  inc="ACT")
                act_sp = V("ACT")
                A("vector", lambda v, j=j, par=par: v.scalar_tensor_tensor(
                    out=delta[:, j * TC:(j + 1) * TC], in0=spv[par][:], scalar=0.0, in1=sav[par][:],
                    op0=ALU.max, op1=ALU.add),
                  inc="DVE", wait=[("ACT", act_sp)] + list(war[("delta", j)]))
                war[("spv", par)] = [("ACT", act_sp)]
                war[("sav", par)] = [("DVE", V("DVE"))]

            for j in range(IT):
                par = 0
                x0 = j * (TC + 3)
                A("vector", lambda v, j=j, par=par, x0=x0: v.tensor_scalar_mul(
                    out=cacc[par][:], in0=xin[:, x0:x0 + TC], scalar1=wcv[:, j * 4:j * 4 + 1]),
                  inc="DVE", wait=[("ACT", act_front)] + list(war[("cacc", par)]))
                for kk in range(1, 4):
                    A("vector", lambda v, par=par, x0=x0, kk=kk, j=j: v.scalar_tensor_tensor(
                        out=cacc[par][:], in0=xin[:, x0 + kk:x0 + kk + TC],
                        scalar=wcv[:, j * 4 + kk:j * 4 + kk + 1], in1=cacc[par][:],
                        op0=ALU.mult, op1=ALU.add),
                      inc="DVE")
                dve_c = V("DVE")
                A("scalar", lambda s, j=j, par=par: s.activation(
                    out=xc[:, j * TC:(j + 1) * TC], in_=cacc[par][:], func=AF.Silu,
                    bias=bcv[:, j:j + 1]),
                  inc="ACT", wait=[("DVE", dve_c)] + list(war[("xck", j)]))
                act_xc = V("ACT")
                war[("cacc", par)] = [("ACT", act_xc)]
                A("vector", lambda v, x0=x0: v.tensor_copy(
                    out=xin[:, x0:x0 + 3], in_=xin[:, x0 + TC:x0 + TC + 3]),
                  inc="DVE")
                war[("xin", j)] = [("DVE", V("DVE"))]
                A("vector", lambda v, j=j: v.tensor_mul(
                    out=dxc[:, j * TC:(j + 1) * TC], in0=delta[:, j * TC:(j + 1) * TC],
                    in1=xc[:, j * TC:(j + 1) * TC]),
                  inc="DVE", wait=[("ACT", act_xc)])
            dve_front = V("DVE")

            A("gpsimd", lambda g: g.memset(yacc[:], 0.0), inc="POOL", wait=war[("yacc",)])
            pool_ymem = V("POOL")

            act_cr = 0
            for sg in range(NSG):
                for sl in range(SG):
                    s_ = sg * SG + sl
                    A("tensor", lambda t, s_=s_, sl=sl: t.matmul(
                        out=brp[:, sl * TC:(sl + 1) * TC],
                        lhsT=eyes[:, s_:s_ + 1].broadcast_to([64, 128]),
                        rhs=bcb[:], start=True, stop=True),
                      inc="PE",
                      wait=([("ACT", act_front)] + list(war[("brp",)])) if sl == 0 else [])
                pe_b = V("PE")
                A("scalar", lambda s: s.copy(out=brep[:], in_=brp[:]),
                  inc="ACT", wait=[("PE", pe_b)] + list(war[("brep",)]))
                act_b = V("ACT")
                war[("brp",)] = [("ACT", act_b)]

                for j in range(IT):
                    par = 0
                    hpar = j % 2
                    for sl in range(SG):
                        s_ = sg * SG + sl
                        A("scalar", lambda s, j=j, par=par, sl=sl, s_=s_: s.activation(
                            out=dab[par][:, sl * TC:(sl + 1) * TC], in_=delta[:, j * TC:(j + 1) * TC],
                            func=AF.Exp, scale=arep[:, s_:s_ + 1]),
                          inc="ACT" if sl == SG - 1 else None,
                          wait=(list(war[("dab", par)]) + [("DVE", dve_front), ("DVE", setup_dve)]) if sl == 0 else [])
                    act_da = V("ACT")
                    A("vector", lambda v, j=j, par=par: v.tensor_mul(
                        out=ub[par][:].rearrange("p (s t) -> p s t", s=SG),
                        in0=dxc[:, j * TC:(j + 1) * TC].unsqueeze(1).broadcast_to([128, SG, TC]),
                        in1=brep[:].rearrange("p (s t) -> p s t", s=SG)),
                      inc="DVE", wait=[("ACT", act_b)])
                    for sl in range(SG):
                        s_ = sg * SG + sl
                        init = 0.0 if c == 0 else hstate[:, j * 64 + s_:j * 64 + s_ + 1]
                        A("vector", lambda v, par=par, hpar=hpar, sl=sl, init=init: v.tensor_tensor_scan(
                            out=hb[hpar][:, sl * TC:(sl + 1) * TC],
                            data0=dab[par][:, sl * TC:(sl + 1) * TC],
                            data1=ub[par][:, sl * TC:(sl + 1) * TC],
                            initial=init, op0=ALU.mult, op1=ALU.add),
                          inc="DVE" if sl == SG - 1 else None,
                          wait=([("ACT", act_da)] + list(war[("hb", hpar)]) + list(war[("hst", j)])) if sl == 0 else [])
                    dve_scan = V("DVE")
                    war[("dab", par)] = [("DVE", dve_scan)]
                    A("gpsimd", lambda g, j=j, hpar=hpar, sg=sg: g.tensor_copy(
                        out=hstate[:, j * 64 + sg * SG:j * 64 + (sg + 1) * SG],
                        in_=hb[hpar][:, TC - 1::TC]),
                      inc="POOL", wait=[("DVE", dve_scan)])
                    war[("hst", j)] = [("POOL", V("POOL"))]
                    if j == 0:
                        for sl in range(SG):
                            s_ = sg * SG + sl
                            A("tensor", lambda t, s_=s_, sl=sl: t.matmul(
                                out=brp[:, sl * TC:(sl + 1) * TC],
                                lhsT=eyes[:, s_:s_ + 1].broadcast_to([64, 128]),
                                rhs=ccb[:], start=True, stop=True),
                              inc="PE",
                              wait=([("ACT", act_b)] + list(war[("brp",)])) if sl == 0 else [])
                        pe_cr = V("PE")
                        A("scalar", lambda s: s.copy(out=crep[:], in_=brp[:]),
                          inc="ACT", wait=[("PE", pe_cr)] + list(war[("crep",)]))
                        act_cr = V("ACT")
                        war[("brp",)] = [("ACT", act_cr)]
                    A("vector", lambda v, par=par, hpar=hpar: v.tensor_mul(
                        out=wbf[par][:], in0=hb[hpar][:], in1=crep[:]),
                      inc="DVE", wait=[("ACT", act_cr)] + list(war[("wbf", par)]))
                    dve_w = V("DVE")
                    war[("hb", hpar)] = [("POOL", p.cnt["POOL"]), ("DVE", dve_w)]
                    A("gpsimd", lambda g, par=par: g.tensor_add(
                        out=t1p[par][:], in0=wbf[par][:, 0:TC], in1=wbf[par][:, TC:2 * TC]),
                      inc="POOL", wait=[("DVE", dve_w)])
                    A("gpsimd", lambda g, par=par: g.tensor_add(
                        out=t2p[par][:], in0=wbf[par][:, 2 * TC:3 * TC], in1=wbf[par][:, 3 * TC:4 * TC]),
                      inc="POOL")
                    A("gpsimd", lambda g, par=par: g.tensor_add(
                        out=ysg[par][:], in0=t1p[par][:], in1=t2p[par][:]),
                      inc="POOL")
                    A("gpsimd", lambda g, j=j, par=par: g.tensor_add(
                        out=yacc[:, j * TC:(j + 1) * TC], in0=yacc[:, j * TC:(j + 1) * TC],
                        in1=ysg[par][:]),
                      inc="POOL", wait=[("POOL", pool_ymem)])
                    pool_acc = V("POOL")
                    war[("wbf", par)] = [("POOL", pool_acc)]
                    war[("yacck", j)] = [("POOL", pool_acc)]
                war[("brep",)] = [("DVE", V("DVE"))]
                war[("crep",)] = [("DVE", V("DVE"))]
            war[("bcb",)] = [("PE", V("PE"))]
            war[("drw",)] = [("PE", V("PE"))]
            for j in range(IT):
                war[("delta", j)] = [("ACT", V("ACT")), ("DVE", V("DVE"))]

            for j in range(IT):
                par = 0
                A("scalar", lambda s, j=j, par=par: s.activation(
                    out=szb[par][:], in_=zb[:, j * TC:(j + 1) * TC], func=AF.Silu),
                  inc="ACT", wait=list(war[("szb", par)]))
                act_sz = V("ACT")
                A("vector", lambda v, j=j, par=par: v.scalar_tensor_tensor(
                    out=y1f[par][:], in0=xc[:, j * TC:(j + 1) * TC], scalar=dpr[:, j:j + 1],
                    in1=yacc[:, j * TC:(j + 1) * TC], op0=ALU.mult, op1=ALU.add),
                  inc="DVE", wait=list(war[("yacck", j)]) + list(war[("y1", par)]))
                A("vector", lambda v, j=j, par=par: v.tensor_mul(
                    out=ypp[:, j * TC:(j + 1) * TC], in0=y1f[par][:], in1=szb[par][:]),
                  inc="DVE", wait=[("ACT", act_sz)] + list(war[("yppk", j)]))
                dve_ypp = V("DVE")
                war[("y1", par)] = [("DVE", dve_ypp)]
                war[("szb", par)] = [("DVE", dve_ypp)]
                war[("z", j)] = [("ACT", act_sz)]
                war[("xck", j)] = [("DVE", dve_ypp)]
                war[("yacc",)] = [("DVE", dve_ypp)]
            dve_yppall = V("DVE")
            for dcol in range(8):
                ps = pst[dcol % 2]
                pkey = ("pa",) if dcol % 2 == 0 else ("pb",)
                par = 0
                for k in range(IT):
                    A("tensor", lambda t, dcol=dcol, k=k, ps=ps: t.matmul(
                        out=ps[:], lhsT=wop[:, k * 1024 + dcol * 128:k * 1024 + dcol * 128 + 128],
                        rhs=ypp[:, k * TC:(k + 1) * TC], start=(k == 0), stop=(k == IT - 1)),
                      inc="PE",
                      wait=([("DVE", dve_yppall), ("ACT", setup_act)] + list(war[pkey])) if k == 0 else [])
                pev = V("PE")
                A("scalar", lambda s, par=par, ps=ps: s.copy(out=spo[par][:], in_=ps[:]),
                  inc="ACT", wait=[("PE", pev)] + list(war[("spo", par)]))
                actv = V("ACT")
                war[pkey] = [("ACT", actv)]
                A("gpsimd", lambda g, dcol=dcol, par=par, t0=t0: g.dma_start(
                    out=spar_st(dcol, t0), in_=spo[par][:]),
                  inc=(PF + "O_sp", 16), wait=[("ACT", actv)])
                war[("spo", par)] = [(PF + "O_sp", V(PF + "O_sp"))]
            pe_end = V("PE")
            for j in range(IT):
                war[("yppk", j)] = [("PE", pe_end)]
            for k in range(KT):
                war[("xnk", k)] = list(war[("xnk", k)]) + [("PE", pe_end)]
            war[("spo", 0)] = list(war[("spo", 0)])
            yield ("chunk", c)


def _col(a):
    return np.ascontiguousarray(np.asarray(a).reshape(8, 128).T)


def stage_B(d):
    x = np.asarray(d["L0_attn_x"])
    wipT = np.ascontiguousarray(np.asarray(d["L0_attn_in_proj_w"]).T)
    opT = np.ascontiguousarray(np.asarray(d["L0_attn_out_proj_w"]).T)
    dtw = np.asarray(d["L0_attn_dt_proj_w"])
    eye = np.eye(64).astype(BF)
    maps = []
    for c in range(8):
        b, ih = c // 2, c % 2
        sl = slice(ih * 1024, (ih + 1) * 1024)
        m = {
            "xb": np.ascontiguousarray(x[b].T),
            "w_ip": np.ascontiguousarray(np.concatenate(
                [wipT[:, ih * 1024:(ih + 1) * 1024],
                 wipT[:, 2048 + ih * 1024:2048 + (ih + 1) * 1024],
                 wipT[:, 4096:4288]], axis=1)),
            "w_conv": np.ascontiguousarray(
                np.asarray(d["L0_attn_conv_w"])[sl, 0, :].reshape(8, 128, 4).transpose(1, 0, 2).reshape(128, 32)),
            "b_conv": _col(d["L0_attn_conv_b"][sl]),
            "w_dt": np.ascontiguousarray(dtw[sl].T),
            "b_dt": _col(d["L0_attn_dt_proj_b"][sl]),
            "a_log": np.ascontiguousarray(np.broadcast_to(
                np.asarray(d["L0_attn_A_log"])[None, :], (128, 64))),
            "d_prm": _col(d["L0_attn_D_param"][sl]),
            "w_op": np.ascontiguousarray(opT[sl, :]),
            "eye64": eye,
            "cst": np.ascontiguousarray(np.broadcast_to(
                np.array([1e-5, -0.5, 0.0, 0.0], np.float32)[None, :], (128, 4))),
        }
        maps.append(m)
    return maps


TCM = 512
NTCM = 2048 // TCM
NTT = 16


def build_C():
    return build_tail(0)


def build_D2():
    return build_tail(1)


def build_tail(lay=0):
    nc = bass.Bass("TRN2", target_bir_lowering=False, debug=False)
    io = {"lay": lay, "s_mode": "dual" if lay == 0 else "single_bf16"}
    xn_in = nc.dram_tensor("xn_in", [1024, 2048], BF16, kind="ExternalInput").ap()
    x_own = nc.dram_tensor("x_own", [1024, 2048], F32, kind="ExternalInput").ap()
    if lay == 0:
        s_a = nc.dram_tensor("s_a", [1024, 2048], F32, kind="ExternalInput").ap()
        s_b = nc.dram_tensor("s_b", [1024, 2048], F32, kind="ExternalInput").ap()
        io["sa_ld"] = lambda j, tm0: s_a[j * 128:(j + 1) * 128, tm0:tm0 + TCM]
        io["sb_ld"] = lambda j, tm0: s_b[j * 128:(j + 1) * 128, tm0:tm0 + TCM]
    else:
        s_in = nc.dram_tensor("s_in", [1024, 2048], BF16, kind="ExternalInput").ap()
        io["bgr"] = nc.dram_tensor("bgr", [128, 8], F32, kind="ExternalInput").ap()
        io["s_ld"] = lambda j, tm0: s_in[j * 128:(j + 1) * 128, tm0:tm0 + TCM]
    for nm, sh, dt in [("wgt", [128, 64], F32), ("w1r", [8, 32, 128, 1024], F32),
                       ("w2r", [8, 8, 128, 4096], F32), ("b1c", [128, 256], F32),
                       ("b2c", [128, 64], F32), ("eye128", [128, 128], BF16),
                       ("eye8", [8, 8], BF16)]:
        io[nm] = nc.dram_tensor(nm, sh, dt, kind="ExternalInput").ap()
    x1 = nc.dram_tensor("x1", [1024, 2048], F32, kind="ExternalOutput").ap()
    io["xn_ld"] = lambda k, tm0: xn_in[k * 128:(k + 1) * 128, tm0:tm0 + TCM]
    io["x_ld"] = lambda j, tm0: x_own[j * 128:(j + 1) * 128, tm0:tm0 + TCM]
    io["out_st"] = lambda j, tm0: x1[j * 128:(j + 1) * 128, tm0:tm0 + TCM]
    io["w1b_dr"] = nc.dram_tensor("w1b_dr", [8, 32, 128, 1024], BF16).ap()
    io["w2b_dr"] = nc.dram_tensor("w2b_dr", [8, 8, 128, 4096], BF16).ap()
    p = Prog()
    with contextlib.ExitStack() as es0:
        _emit_tail(nc, p, io)
        finish(p, es0, nc)
    return nc


def _emit_tail(nc, p, io):
    for _ in _emit_tail_gen(nc, p, io):
        pass


def _emit_tail_gen(nc, p, io):
    lay = io["lay"]
    s_mode = io["s_mode"]
    wgt, b1c, b2c = io["wgt"], io["b1c"], io["b2c"]
    w1r, w2r = io["w1r"], io["w2r"]
    w1bd, w2bd = io["w1b_dr"], io["w2b_dr"]
    eye128, eye8 = io["eye128"], io["eye8"]
    xn_ld, x_ld, out_st = io.get("xn_ld"), io["x_ld"], io["out_st"]
    PF = io.get("pref", "")
    with contextlib.ExitStack() as es:
        _pf = io.get("pref", "")
        T = lambda name, shape, dt: es.enter_context(nc.sbuf_tensor(_pf + name, shape, dt))
        PT = lambda name, shape, dt: es.enter_context(nc.psum_tensor(_pf + name, shape, dt))
        wgb = T("wgb", [128, 64], BF16)
        wgf = T("wgf", [128, 64], F32)
        b1s = T("b1s", [128, 256], F32)
        b2s = T("b2s", [128, 64], F32)
        id128 = T("id128", [128, 128], BF16)
        id8 = T("id8", [8, 8], BF16)
        xns2 = [T(f"xns{i}", [128, 8 * 512], BF16) for i in range(2)]
        gsb = T("gsb", [128, 32], F32)
        mm1a = T("mm1a", [128, 4], F32)
        mm2a = T("mm2a", [128, 4], F32)
        mk1a = T("mk1a", [128, 32], F32)
        gg2a = T("gg2a", [128, 32], F32)
        sela = T("sela", [128, 32], F32)
        ee1a = T("ee1a", [128, 32], F32)
        eesa = T("eesa", [128, 32], F32)
        ssma = T("ssma", [128, 4], F32)
        rcpa = T("rcpa", [128, 4], F32)
        msk_tm = T("msk_tm", [128, 128], BF16)
        mask_fm = T("mask_fm", [8, 2048], BF16)
        mrep2 = [T(f"mrep{i}", [128, 512], BF16) for i in range(2)]
        if lay == 1:
            bgs = T("bgs", [128, 8], F32)
            sss = T("sss", [128, 512], BF16)
        w1st = [T(f"w1st{i}", [128, 1024], F32) for i in range(2)]
        w1b = [T(f"w1b{i}", [128, 1024], BF16) for i in range(2)]
        w2st = [T(f"w2st{i}", [128, 2048], F32) for i in range(2)]
        w2b = [T(f"w2b{i}", [128, 2048], BF16) for i in range(2)]
        hsb2 = [T(f"hsb{i}", [128, 32 * 512], BF16) for i in range(2)]
        ytmp = [T(f"ytmp{i}", [128, 512], F32) for i in range(2)]
        yacc2 = [T(f"yaccC{i}", [128, 8 * 512], F32) for i in range(2)]
        gbf = [T(f"gbf{i}", [128, 512], BF16) for i in range(1)]
        xsl = [T(f"xsl{i}", [128, 512], F32) for i in range(1)]
        sas = [T(f"sas{i}", [128, 512], F32) for i in range(1)]
        sbs = [T(f"sbs{i}", [128, 512], F32) for i in range(1)]
        x1s = [T(f"x1s{i}", [128, 512], F32) for i in range(1)]
        gpsA = PT("gpsA", [128, 32], F32)
        ptr = PT("ptr", [8, 128], BF16)
        pc = [PT(f"pc{i}", [128, 512], F32) for i in range(4)]
        pm = PT("pm", [128, 512], F32)

        A = p.add
        V = p.v
        war = defaultdict(tuple)

        A("sync", lambda s: s.dma_start(out=id128[:], in_=eye128), inc=(PF + "L_stp", 16))
        A("sync", lambda s: s.dma_start(out=id8[:], in_=eye8), inc=(PF + "L_stp", 16))
        A("sync", lambda s: s.dma_start(out=wgf[:], in_=wgt), inc=(PF + "L_stp", 16))
        A("sync", lambda s: s.dma_start(out=b1s[:], in_=b1c), inc=(PF + "L_stp", 16))
        A("sync", lambda s: s.dma_start(out=b2s[:], in_=b2c), inc=(PF + "L_stp", 16))
        if lay == 1:
            A("sync", lambda s: s.dma_start(out=bgs[:], in_=io["bgr"]), inc=(PF + "L_stp", 16))
        ld0 = (PF + "L_stp", V(PF + "L_stp"))
        A("vector", lambda v: v.tensor_copy(out=wgb[:], in_=wgf[:]), inc="DVE", wait=[ld0])
        setup_dve = V("DVE")

        ln_out = 0
        if io.get("ln_x") is not None:
            lnx = io["ln_x"]
            lnc = io["ln_cst"]
            xn_st = io["xn_st"]
            xfs2 = [ytmp[0][:], ytmp[1][:]]
            xsq2 = [xsl[0][:], sas[0][:]]
            nmT = yacc2[0][0:1, 0:512]
            m2T = yacc2[0][0:1, 512:1024]
            varT = yacc2[0][0:1, 1024:1536]
            rsT = yacc2[0][0:1, 1536:2048]
            nmrT = yacc2[0][0:1, 2048:2560]
            xntT = x1s[0][:]
            xobT = gbf[0][:]
            onescT = T("onescT", [128, 1], F32)
            onesrT = T("onesrT", [1, 128], F32)
            csttT = T("csttT", [128, 4], F32)
            mpsT = pc[0][0:1, 0:512]
            spsT = pc[1][0:1, 0:512]
            A("vector", lambda v: v.memset(onescT[:], 1.0), inc="DVE")
            A("vector", lambda v: v.memset(onesrT[:], 1.0), inc="DVE")
            A("sync", lambda s: s.dma_start(out=csttT[:], in_=lnc), inc=(PF + "L_stp", 16))
            setup_dve = V("DVE")
            ldc = (PF + "L_stp", V(PF + "L_stp"))
            for c in range(4):
                t0 = c * 512
                for k in range(8):
                    par = k % 2
                    A("sync", lambda s, k=k, par=par, t0=t0: s.dma_start(
                        out=xfs2[par], in_=lnx[k * 128:(k + 1) * 128, t0:t0 + 512]),
                      inc=(PF + f"L_lnx{par}", 16), wait=war[("xfT", par)])
                    ldw = (PF + f"L_lnx{par}", V(PF + f"L_lnx{par}"))
                    A("scalar", lambda s, par=par: s.activation(out=xsq2[par], in_=xfs2[par], func=AF.Square),
                      inc="ACT", wait=[ldw] + list(war[("xsqT", par)]))
                    actv = V("ACT")
                    A("tensor", lambda t, k=k, par=par: t.matmul(
                        out=mpsT, lhsT=onescT[:], rhs=xfs2[par], start=(k == 0), stop=(k == 7)),
                      inc="PE",
                      wait=([ldw, ("DVE", setup_dve)] + list(war[("mpsT",)])) if k == 0 else [ldw])
                    A("tensor", lambda t, k=k, par=par: t.matmul(
                        out=spsT, lhsT=onescT[:], rhs=xsq2[par], start=(k == 0), stop=(k == 7)),
                      inc="PE", wait=[("ACT", actv)])
                    war[("xfT", par)] = [("PE", V("PE"))]
                    war[("xsqT", par)] = [("PE", V("PE"))]
                pe_stats = V("PE")
                A("vector", lambda v: v.tensor_scalar_mul(out=nmT, in0=mpsT, scalar1=-1.0 / 1024.0),
                  inc="DVE", wait=[("PE", pe_stats)])
                A("vector", lambda v: v.tensor_mul(out=m2T, in0=nmT, in1=nmT), inc="DVE")
                A("vector", lambda v: v.scalar_tensor_tensor(
                    out=varT, in0=spsT, scalar=1.0 / 1024.0, in1=m2T, op0=ALU.mult, op1=ALU.subtract),
                  inc="DVE", wait=[("PE", pe_stats)])
                dve_var = V("DVE")
                war[("mpsT",)] = [("DVE", dve_var)]
                A("scalar", lambda s: s.activation(out=rsT, in_=varT, func=AF.Ln, bias=csttT[0:1, 0:1]),
                  inc="ACT", wait=[("DVE", dve_var), ldc])
                A("scalar", lambda s: s.activation(out=rsT, in_=rsT, func=AF.Exp, scale=csttT[0:1, 1:2]),
                  inc="ACT")
                act_rs = V("ACT")
                A("vector", lambda v: v.tensor_mul(out=nmrT, in0=nmT, in1=rsT),
                  inc="DVE", wait=[("ACT", act_rs)])
                dve_nmr = V("DVE")
                A("tensor", lambda t: t.matmul(out=pm[:], lhsT=onesrT[:], rhs=rsT, start=True, stop=True),
                  inc="PE", wait=[("ACT", act_rs)] + list(war[("pmLN",)]))
                pe_b1 = V("PE")
                A("tensor", lambda t: t.matmul(out=pc[0][:], lhsT=onesrT[:], rhs=nmrT, start=True, stop=True),
                  inc="PE", wait=[("DVE", dve_nmr), ("DVE", V("DVE"))])
                pe_bc = V("PE")
                for k in range(8):
                    par = k % 2
                    A("sync", lambda s, k=k, par=par, t0=t0: s.dma_start(
                        out=xfs2[par], in_=lnx[k * 128:(k + 1) * 128, t0:t0 + 512]),
                      inc=(PF + f"L_lnx{par}", 16), wait=war[("xfT", par)])
                    ldw = (PF + f"L_lnx{par}", V(PF + f"L_lnx{par}"))
                    A("vector", lambda v, par=par: v.tensor_mul(out=xntT, in0=xfs2[par], in1=pm[:]),
                      inc="DVE", wait=[ldw, ("PE", pe_bc)])
                    A("vector", lambda v, par=par: v.tensor_add(out=xobT, in0=xntT, in1=pc[0][:]),
                      inc="DVE", wait=war[("xobT",)])
                    dv = V("DVE")
                    war[("xfT", par)] = [("DVE", dv)]
                    A("gpsimd", lambda g, k=k, t0=t0: g.dma_start(out=xn_st(k, t0), in_=xobT),
                      inc=(PF + "O_gbf", 16), wait=[("DVE", dv)])
                    war[("xobT",)] = [(PF + "O_gbf", V(PF + "O_gbf"))]
                war[("pmLN",)] = [("DVE", V("DVE"))]
                war[("pm",)] = [("DVE", V("DVE"))]
                war[("pc", 0)] = [("DVE", V("DVE"))]
                war[("pc", 1)] = [("DVE", V("DVE"))]
                war[("mpsT",)] = [("DVE", V("DVE"))]
            ln_out = V(PF + "O_gbf")
            setup_dve = V("DVE")
            war[("ytmp", 0)] = [("PE", V("PE")), ("DVE", setup_dve), ("ACT", V("ACT"))]
            war[("ytmp", 1)] = list(war[("ytmp", 0)])
            war[("xsl", 0)] = list(war[("ytmp", 0)])
            war[("sas", 0)] = list(war[("ytmp", 0)])
            war[("x1s", 0)] = [("DVE", setup_dve)]
            war[("gbf", 0)] = [(PF + "O_gbf", ln_out)]
            for t2 in range(2):
                for j in range(8):
                    war[("yacck", t2, j)] = [("PE", V("PE"))]
        yield "setup"

        g3 = lambda t_: t_.rearrange("p (t e) -> p t e", e=8)
        bc3 = lambda t_: t_.unsqueeze(2).broadcast_to([128, 4, 8])
        first_e_pe = {}
        for pass_ in range(2):
            act_masks = {}
            for t2 in range(2):
                tcm = pass_ * 2 + t2
                tm0 = tcm * 512
                xnw = list(war[("xns", t2)])
                if io.get("ln_x") is not None:
                    xnw += [(PF + "O_gbf", ln_out)]
                else:
                    xnw += list(io.get("xn_wait", ()))
                for k in range(8):
                    A("sync", lambda s, k=k, tm0=tm0, t2=t2: s.dma_start(
                        out=xns2[t2][:, k * 512:(k + 1) * 512], in_=xn_ld(k, tm0)),
                      inc=(PF + "L_xns", 16), wait=xnw if k == 0 else [])
                ld_xn = (PF + "L_xns", V(PF + "L_xns"))
                for tt in range(4):
                    for k in range(8):
                        A("tensor", lambda t, tt=tt, k=k, t2=t2: t.matmul(
                            out=gpsA[:, tt * 8:(tt + 1) * 8],
                            lhsT=xns2[t2][:, k * 512 + tt * 128:k * 512 + tt * 128 + 128],
                            rhs=wgb[:, k * 8:(k + 1) * 8], start=(k == 0), stop=(k == 7)),
                          inc="PE",
                          wait=([("DVE", setup_dve), ld_xn] + list(war[("gpsA",)])) if (k == 0 and tt == 0) else [])
                pe_gate = V("PE")
                A("scalar", lambda s: s.copy(out=gsb[:], in_=gpsA[:]), inc="ACT",
                  wait=[("PE", pe_gate)] + list(war[("gsb",)]))
                act_gsb = V("ACT")
                if lay == 0:
                    A("vector", lambda v: v.tensor_reduce(out=mm1a[:].unsqueeze(2), in_=g3(gsb[:]),
                                                          axis=mybir.AxisListType.X, op=ALU.max),
                      inc="DVE", wait=[("ACT", act_gsb)])
                    A("scalar", lambda s: s.activation(out=ee1a[:], in_=gsb[:], func=AF.Exp),
                      inc="ACT")
                    A("vector", lambda v: v.tensor_tensor(out=g3(mk1a[:]), in0=g3(gsb[:]), in1=bc3(mm1a[:]),
                                                          op=ALU.is_ge), inc="DVE", wait=[("DVE", V("DVE"))])
                    A("vector", lambda v: v.scalar_tensor_tensor(out=gg2a[:], in0=mk1a[:], scalar=-30000.0,
                                                                 in1=gsb[:], op0=ALU.mult, op1=ALU.add),
                      inc="DVE")
                    A("vector", lambda v: v.tensor_reduce(out=mm2a[:].unsqueeze(2), in_=g3(gg2a[:]),
                                                          axis=mybir.AxisListType.X, op=ALU.max), inc="DVE",
                      wait=[("DVE", V("DVE"))])
                    A("vector", lambda v: v.tensor_tensor(out=g3(sela[:]), in0=g3(gsb[:]), in1=bc3(mm2a[:]),
                                                          op=ALU.is_ge), inc="DVE", wait=[("DVE", V("DVE"))])
                    A("vector", lambda v: v.tensor_mul(out=eesa[:], in0=ee1a[:], in1=sela[:]),
                      inc="DVE", wait=[("ACT", V("ACT"))])
                    A("vector", lambda v: v.tensor_reduce(out=ssma[:].unsqueeze(2), in_=g3(eesa[:]),
                                                          axis=mybir.AxisListType.X, op=ALU.add), inc="DVE",
                      wait=[("DVE", V("DVE"))])
                    A("vector", lambda v: v.reciprocal(out=rcpa[:], in_=ssma[:]), inc="DVE",
                      wait=[("DVE", V("DVE"))])
                    A("vector", lambda v, tcm=tcm: v.scalar_tensor_tensor(
                        out=g3(msk_tm[:, tcm * 32:(tcm + 1) * 32]), in0=g3(eesa[:]),
                        scalar=0.125, in1=bc3(rcpa[:]),
                        op0=ALU.mult, op1=ALU.mult), inc="DVE",
                      wait=[("DVE", V("DVE"))])
                else:
                    A("vector", lambda v: v.tensor_tensor(
                        out=g3(gsb[:]), in0=g3(gsb[:]),
                        in1=bgs[:].unsqueeze(1).broadcast_to([128, 4, 8]), op=ALU.add),
                      inc="DVE", wait=[("ACT", act_gsb), ld0])
                    dve_lg = V("DVE")
                    A("scalar", lambda s: s.activation(out=ee1a[:], in_=gsb[:], func=AF.Exp),
                      inc="ACT", wait=[("DVE", dve_lg)])
                    A("vector", lambda v: v.tensor_reduce(out=ssma[:].unsqueeze(2), in_=g3(ee1a[:]),
                                                          axis=mybir.AxisListType.X, op=ALU.add), inc="DVE",
                      wait=[("ACT", V("ACT"))])
                    A("vector", lambda v: v.reciprocal(out=rcpa[:], in_=ssma[:]), inc="DVE",
                      wait=[("DVE", V("DVE"))])
                    A("vector", lambda v: v.tensor_tensor(out=g3(gsb[:]), in0=g3(ee1a[:]), in1=bc3(rcpa[:]),
                                                          op=ALU.mult), inc="DVE", wait=[("DVE", V("DVE"))])
                    A("vector", lambda v: v.tensor_reduce(out=mm1a[:].unsqueeze(2), in_=g3(gsb[:]),
                                                          axis=mybir.AxisListType.X, op=ALU.max), inc="DVE",
                      wait=[("DVE", V("DVE"))])
                    A("vector", lambda v: v.tensor_tensor(out=g3(mk1a[:]), in0=g3(gsb[:]), in1=bc3(mm1a[:]),
                                                          op=ALU.is_ge), inc="DVE", wait=[("DVE", V("DVE"))])
                    A("vector", lambda v: v.scalar_tensor_tensor(out=gg2a[:], in0=mk1a[:], scalar=-30000.0,
                                                                 in1=gsb[:], op0=ALU.mult, op1=ALU.add),
                      inc="DVE")
                    A("vector", lambda v: v.tensor_reduce(out=mm2a[:].unsqueeze(2), in_=g3(gg2a[:]),
                                                          axis=mybir.AxisListType.X, op=ALU.max), inc="DVE",
                      wait=[("DVE", V("DVE"))])
                    A("vector", lambda v: v.tensor_tensor(out=g3(sela[:]), in0=g3(gsb[:]), in1=bc3(mm2a[:]),
                                                          op=ALU.is_ge), inc="DVE", wait=[("DVE", V("DVE"))])
                    A("vector", lambda v: v.tensor_mul(out=eesa[:], in0=gsb[:], in1=sela[:]), inc="DVE")
                    A("vector", lambda v: v.tensor_reduce(out=ssma[:].unsqueeze(2), in_=g3(eesa[:]),
                                                          axis=mybir.AxisListType.X, op=ALU.add), inc="DVE",
                      wait=[("DVE", V("DVE"))])
                    A("vector", lambda v: v.tensor_scalar_add(out=ssma[:], in0=ssma[:], scalar1=1e-8),
                      inc="DVE", wait=[("DVE", V("DVE"))])
                    A("vector", lambda v: v.reciprocal(out=rcpa[:], in_=ssma[:]), inc="DVE",
                      wait=[("DVE", V("DVE"))])
                    A("vector", lambda v, tcm=tcm: v.tensor_tensor(
                        out=g3(msk_tm[:, tcm * 32:(tcm + 1) * 32]), in0=g3(eesa[:]),
                        in1=bc3(rcpa[:]), op=ALU.mult), inc="DVE",
                      wait=[("DVE", V("DVE"))])
                dve_msk = V("DVE")
                war[("gsb",)] = [("DVE", dve_msk)]
                for tt in range(4):
                    A("tensor", lambda t, tt=tt, tcm=tcm: t.transpose(
                        out=ptr[:], in_=msk_tm[:, tcm * 32 + tt * 8:tcm * 32 + (tt + 1) * 8], identity=id128[:]),
                      inc="PE", wait=[("DVE", dve_msk), ld0] + list(war[("ptr",)]))
                    pev = V("PE")
                    A("scalar", lambda s, tt=tt, tm0=tm0: s.copy(
                        out=mask_fm[:, tm0 + tt * 128:tm0 + (tt + 1) * 128], in_=ptr[:]),
                      inc="ACT", wait=[("PE", pev)])
                    war[("ptr",)] = [("ACT", V("ACT"))]
                act_mask = V("ACT")
                war[("gpsA",)] = [("ACT", act_mask)]
                yield ("gating", tcm)
                act_masks[t2] = act_mask
            for e in range(8):
                act_mrs = {}
                for t2 in range(2):
                    tcm = pass_ * 2 + t2
                    tm0 = tcm * 512
                    A("tensor", lambda t, e=e, tm0=tm0: t.matmul(
                        out=pm[:], lhsT=id8[:, e:e + 1].broadcast_to([8, 128]),
                        rhs=mask_fm[:, tm0:tm0 + 512], start=True, stop=True),
                      inc="PE",
                      wait=[("ACT", act_masks[t2])] + list(war[("pm",)]))
                    pev = V("PE")
                    A("scalar", lambda s, t2=t2: s.copy(out=mrep2[t2][:], in_=pm[:]),
                      inc="ACT", wait=[("PE", pev)] + list(war[("mrep", t2)]))
                    act_mrs[t2] = V("ACT")
                    war[("pm",)] = [("ACT", act_mrs[t2])]
                for hcol in range(32):
                    wp = hcol % 2
                    if pass_ == 0:
                        A("sync", lambda s, e=e, hcol=hcol, wp=wp: s.dma_start(
                            out=w1st[wp][:], in_=w1r[e, hcol]),
                          inc=(PF + f"L_w1s{wp}", 16), wait=war[("w1st", wp)])
                        ldv = (PF + f"L_w1s{wp}", V(PF + f"L_w1s{wp}"))
                        A("vector", lambda v, wp=wp: v.tensor_copy(out=w1b[wp][:], in_=w1st[wp][:]),
                          inc="DVE", wait=[ldv] + list(war[("w1b", wp)]))
                        dve_w1 = V("DVE")
                        war[("w1st", wp)] = [("DVE", dve_w1)]
                        A("gpsimd", lambda g, e=e, hcol=hcol, wp=wp: g.dma_start(
                            out=w1bd[e, hcol], in_=w1b[wp][:]),
                          inc=(PF + f"O_w1b{wp}", 16), wait=[("DVE", dve_w1)])
                        wb_out = V(PF + f"O_w1b{wp}")
                        ld_w1 = [("DVE", dve_w1)]
                        war[("w1b", wp)] = [(PF + f"O_w1b{wp}", wb_out)]
                    else:
                        A("sync", lambda s, e=e, hcol=hcol, wp=wp: s.dma_start(
                            out=w1b[wp][:], in_=w1bd[e, hcol]),
                          inc=(PF + f"L_w1b{wp}", 16), wait=list(war[("w1b", wp)]) + list(war[("wb_done",)]))
                        ld_w1 = [(PF + f"L_w1b{wp}", V(PF + f"L_w1b{wp}"))]
                    for t2 in range(2):
                        pp = pc[t2]
                        for k in range(8):
                            A("tensor", lambda t, k=k, wp=wp, t2=t2, pp=pp: t.matmul(
                                out=pp[:], lhsT=w1b[wp][:, k * 128:(k + 1) * 128],
                                rhs=xns2[t2][:, k * 512:(k + 1) * 512],
                                start=(k == 0), stop=(k == 7)),
                              inc="PE",
                              wait=((ld_w1 if t2 == 0 else []) + list(war[("pc", t2)]) + list(war[("pcy", 0, t2)])) if k == 0 else [])
                        pev = V("PE")
                        A("scalar", lambda s, e=e, hcol=hcol, t2=t2, pp=pp: s.activation(
                            out=hsb2[t2][:, hcol * 512:(hcol + 1) * 512], in_=pp[:],
                            func=AF.Relu if lay == 0 else AF.Gelu_apprx_tanh,
                            bias=b1s[:, e * 32 + hcol:e * 32 + hcol + 1]),
                          inc="ACT", wait=[("PE", pev), ld0] + list(war[("hsbk", t2, hcol)]))
                        war[("pc", t2)] = [("ACT", V("ACT"))]
                    war[("w1b", wp)] = list(war[("w1b", wp)]) + [("PE", V("PE"))] if pass_ == 0 else [("PE", V("PE"))]
                act_h = V("ACT")
                for dcol in range(8):
                    dpar = dcol % 2
                    for hh in range(2):
                        sp = hh
                        if pass_ == 0:
                            A("sync", lambda s, e=e, dcol=dcol, hh=hh, sp=sp: s.dma_start(
                                out=w2st[sp][:], in_=w2r[e, dcol][:, hh * 2048:(hh + 1) * 2048]),
                              inc=(PF + f"L_w2s{sp}", 16), wait=war[("w2st", sp)])
                            ldv = (PF + f"L_w2s{sp}", V(PF + f"L_w2s{sp}"))
                            A("vector", lambda v, sp=sp: v.tensor_copy(out=w2b[sp][:], in_=w2st[sp][:]),
                              inc="DVE", wait=[ldv] + list(war[("w2b", sp)]))
                            dve_w2 = V("DVE")
                            war[("w2st", sp)] = [("DVE", dve_w2)]
                            A("gpsimd", lambda g, e=e, dcol=dcol, hh=hh, sp=sp: g.dma_start(
                                out=w2bd[e, dcol][:, hh * 2048:(hh + 1) * 2048], in_=w2b[sp][:]),
                              inc=(PF + f"O_w2b{sp}", 16), wait=[("DVE", dve_w2)])
                            war[("w2b", sp)] = [(PF + f"O_w2b{sp}", V(PF + f"O_w2b{sp}"))]
                            ld_w2 = [("DVE", dve_w2)]
                        else:
                            A("sync", lambda s, e=e, dcol=dcol, hh=hh, sp=sp: s.dma_start(
                                out=w2b[sp][:], in_=w2bd[e, dcol][:, hh * 2048:(hh + 1) * 2048]),
                              inc=(PF + f"L_w2b{sp}", 16), wait=list(war[("w2b", sp)]) + list(war[("wb_done",)]))
                            ld_w2 = [(PF + f"L_w2b{sp}", V(PF + f"L_w2b{sp}"))]
                        for t2 in range(2):
                            pp = pc[2 * dpar + t2]
                            for kh in range(16):
                                A("tensor", lambda t, kh=kh, hh=hh, sp=sp, t2=t2, pp=pp: t.matmul(
                                    out=pp[:], lhsT=w2b[sp][:, kh * 128:(kh + 1) * 128],
                                    rhs=hsb2[t2][:, (hh * 16 + kh) * 512:(hh * 16 + kh + 1) * 512],
                                    start=(hh == 0 and kh == 0), stop=(hh == 1 and kh == 15)),
                                  inc="PE",
                                  wait=((ld_w2 if t2 == 0 else []) + (([("ACT", act_h)] + list(war[("pcy", dpar, t2)])) if hh == 0 else [])) if kh == 0 else [])
                        pev2 = V("PE")
                        war[("w2b", sp)] = list(war[("w2b", sp)]) + [("PE", pev2)] if pass_ == 0 else [("PE", pev2)]
                    pev = V("PE")
                    for t2 in range(2):
                        pp = pc[2 * dpar + t2]
                        A("vector", lambda v, e=e, dcol=dcol, t2=t2, pp=pp: v.scalar_tensor_tensor(
                            out=ytmp[t2][:], in0=pp[:], scalar=b2s[:, e * 8 + dcol:e * 8 + dcol + 1],
                            in1=mrep2[t2][:], op0=ALU.add, op1=ALU.mult),
                          inc="DVE", wait=[("PE", pev), ("ACT", act_mrs[t2])] + list(war[("ytmp", t2)]))
                        war[("pcy", dpar, t2)] = [("DVE", V("DVE"))]
                        if e == 0:
                            A("vector", lambda v, dcol=dcol, t2=t2: v.tensor_copy(
                                out=yacc2[t2][:, dcol * 512:(dcol + 1) * 512], in_=ytmp[t2][:]),
                              inc="DVE", wait=war[("yacck", t2, dcol)])
                        else:
                            A("vector", lambda v, dcol=dcol, t2=t2: v.tensor_add(
                                out=yacc2[t2][:, dcol * 512:(dcol + 1) * 512],
                                in0=yacc2[t2][:, dcol * 512:(dcol + 1) * 512], in1=ytmp[t2][:]),
                              inc="DVE")
                        war[("ytmp", t2)] = [("DVE", V("DVE"))]
                for t2 in range(2):
                    war[("mrep", t2)] = [("DVE", V("DVE"))]
                yield ("block", pass_ * 2, e)
            if pass_ == 0:
                war[("wb_done",)] = [(PF + "O_w1b0", V(PF + "O_w1b0")), (PF + "O_w1b1", V(PF + "O_w1b1")),
                                     (PF + "O_w2b0", V(PF + "O_w2b0")), (PF + "O_w2b1", V(PF + "O_w2b1"))]
            for t2 in range(2):
                for hcol in range(32):
                    war[("hsbk", t2, hcol)] = [("PE", V("PE"))]
            for t2 in range(2):
                tcm = pass_ * 2 + t2
                tm0 = tcm * 512
                for j in range(8):
                    par = 0
                    A("scalar", lambda s, j=j, par=par, t2=t2: s.activation(
                        out=gbf[par][:], in_=yacc2[t2][:, j * 512:(j + 1) * 512], func=AF.Silu),
                      inc="ACT", wait=[("DVE", V("DVE"))] + list(war[("gbf", par)]))
                    act_g = V("ACT")
                    A("sync", lambda s, j=j, par=par, tm0=tm0: s.dma_start(
                        out=xsl[par][:], in_=x_ld(j, tm0)),
                      inc=(PF + "L_xsl", 16), wait=war[("xsl", par)])
                    ld_xsl = (PF + "L_xsl", V(PF + "L_xsl"))
                    if s_mode == "dual":
                        A("sync", lambda s, j=j, par=par, tm0=tm0: s.dma_start(
                            out=sas[par][:], in_=io["sa_ld"](j, tm0)),
                          inc=(PF + "L_sas", 16), wait=war[("sas", par)])
                        A("sync", lambda s, j=j, par=par, tm0=tm0: s.dma_start(
                            out=sbs[par][:], in_=io["sb_ld"](j, tm0)),
                          inc=(PF + "L_sbs", 16), wait=war[("sbs", par)])
                        ldv = [(PF + "L_sas", V(PF + "L_sas")), (PF + "L_sbs", V(PF + "L_sbs"))]
                        A("vector", lambda v, par=par: v.tensor_add(out=sas[par][:], in0=sas[par][:], in1=sbs[par][:]),
                          inc="DVE", wait=ldv)
                    elif s_mode == "single_f32":
                        A("sync", lambda s, j=j, par=par, tm0=tm0: s.dma_start(
                            out=sas[par][:], in_=io["s_ld"](j, tm0)),
                          inc=(PF + "L_sas", 16), wait=tuple(war[("sas", par)]) + tuple(io.get("s_wait", ())))
                        ldv = [(PF + "L_sas", V(PF + "L_sas"))]
                    else:
                        A("sync", lambda s, j=j, tm0=tm0: s.dma_start(
                            out=sss[:], in_=io["s_ld"](j, tm0)),
                          inc=(PF + "L_sss", 16), wait=list(war[("sas", par)]) + list(io.get("s_wait", ())))
                        ldv = [(PF + "L_sss", V(PF + "L_sss"))]
                        A("vector", lambda v, par=par: v.tensor_copy(out=sas[par][:], in_=sss[:]),
                          inc="DVE", wait=ldv)
                    A("vector", lambda v, par=par: v.tensor_mul(out=sas[par][:], in0=sas[par][:], in1=gbf[par][:]),
                      inc="DVE", wait=[("ACT", act_g)] + ldv)
                    A("vector", lambda v, par=par: v.tensor_add(out=x1s[par][:], in0=xsl[par][:], in1=sas[par][:]),
                      inc="DVE", wait=[ld_xsl] + list(war[("x1s", par)]))
                    dve_x1 = V("DVE")
                    war[("sbs", par)] = [("DVE", dve_x1)]
                    war[("sas", par)] = [("DVE", dve_x1)]
                    war[("xsl", par)] = [("DVE", dve_x1)]
                    war[("gbf", par)] = [("DVE", dve_x1)]
                    A("gpsimd", lambda g, j=j, par=par, tm0=tm0: g.dma_start(
                        out=out_st(j, tm0), in_=x1s[par][:]),
                      inc=(PF + "O_x1s", 16), wait=[("DVE", dve_x1)])
                    war[("x1s", par)] = [(PF + "O_x1s", V(PF + "O_x1s"))]
                    war[("yacck", t2, j)] = [("ACT", act_g)]
                war[("xns", t2)] = [("PE", V("PE"))]
                yield ("combine", tcm)


def stage_C(d, res_b):
    x = np.asarray(d["L0_attn_x"])
    w1 = np.asarray(d["L0_ffn_W1"])
    w2 = np.asarray(d["L0_ffn_W2"])
    w1r = np.ascontiguousarray(w1.reshape(8, 8, 128, 32, 128).transpose(0, 3, 2, 1, 4).reshape(8, 32, 128, 1024))
    w2r = np.ascontiguousarray(w2.reshape(8, 32, 128, 8, 128).transpose(0, 3, 2, 1, 4).reshape(8, 8, 128, 4096))
    b1cv = np.ascontiguousarray(np.asarray(d["L0_ffn_b1"]).reshape(8, 32, 128).transpose(2, 0, 1).reshape(128, 256))
    b2cv = np.ascontiguousarray(np.asarray(d["L0_ffn_b2"]).reshape(8, 8, 128).transpose(2, 0, 1).reshape(128, 64))
    wgtv = np.ascontiguousarray(np.asarray(d["L0_ffn_W_gate"]).reshape(8, 128, 8).transpose(1, 0, 2).reshape(128, 64))
    e128 = np.eye(128).astype(BF)
    e8 = np.eye(8).astype(BF)
    maps = []
    for c in range(8):
        b, ih = c // 2, c % 2
        tsl = slice(ih * 2048, (ih + 1) * 2048)
        partner = 2 * b + (1 - ih)
        m = {
            "xn_in": np.ascontiguousarray(res_b[c]["xn_out"][:, tsl]),
            "x_own": np.ascontiguousarray(x[b].T[:, tsl]),
            "s_a": np.ascontiguousarray(res_b[c]["s_par"][:, tsl]),
            "s_b": np.ascontiguousarray(res_b[partner]["s_par"][:, tsl]),
            "wgt": wgtv, "w1r": w1r, "w2r": w2r, "b1c": b1cv, "b2c": b2cv,
            "eye128": e128, "eye8": e8,
        }
        maps.append(m)
    return maps


def build_D1():
    nc = bass.Bass("TRN2", target_bir_lowering=False, debug=False)
    io = {}
    for nm, sh, dt in [("x1in", [1024, 2048], F32), ("nwc", [128, 8], F32),
                       ("w_xp", [1024, 192], F32), ("w_dt1", [64, 1024], F32),
                       ("b_dt1", [128, 8], F32), ("a1in", [128, 512], F32),
                       ("d1c", [128, 8], F32), ("w_op1r", [8, 128, 1024], F32),
                       ("eye64", [64, 64], BF16), ("cst", [128, 4], F32)]:
        io[nm] = nc.dram_tensor(nm, sh, dt, kind="ExternalInput").ap()
    io["s1o"] = nc.dram_tensor("s1o", [1024, 2048], BF16, kind="ExternalOutput").ap()
    io["xn1o"] = nc.dram_tensor("xn1o", [1024, 2048], BF16, kind="ExternalOutput").ap()
    p = Prog()
    with contextlib.ExitStack() as es0:
        _emit_D1(nc, p, io)
        finish(p, es0, nc)
    return nc


def _emit_D1(nc, p, io):
    for _ in _emit_D1_gen(nc, p, io):
        pass


def _emit_D1_gen(nc, p, io):
    x1in, nwc, w_xp, w_dt1 = io["x1in"], io["nwc"], io["w_xp"], io["w_dt1"]
    b_dt1, a1in, d1c, w_op1r = io["b_dt1"], io["a1in"], io["d1c"], io["w_op1r"]
    eye64, cst, s1o, xn1o = io["eye64"], io["cst"], io["s1o"], io["xn1o"]
    NCH1 = 2048 // TC
    PF = io.get("pref", "")
    with contextlib.ExitStack() as es:
        _pf = io.get("pref", "")
        T = lambda name, shape, dt: es.enter_context(nc.sbuf_tensor(_pf + name, shape, dt))
        PT = lambda name, shape, dt: es.enter_context(nc.psum_tensor(_pf + name, shape, dt))
        nws = T("nws", [128, 8], F32)
        wxpb = T("wxpb", [128, 8 * 192], BF16)
        wdt1 = T("wdt1b", [64, 1024], BF16)
        bdt1 = T("bdt1", [128, 8], F32)
        a1c = T("a1c", [128, 512], F32)
        d1s = T("d1s", [128, 8], F32)
        eyes = T("eyes", [64, 64], BF16)
        cstt = T("cstt", [128, 4], F32)
        ones_c = T("ones_c", [128, 1], F32)
        ones_r = T("ones_r", [1, 128], F32)
        wst = T("wstD", [128, 1024], F32)
        wopb = [T(f"wopb{i}", [128, 1024], BF16) for i in range(1)]
        xfs = [T(f"xfD{i}", [128, TC], F32) for i in range(2)]
        xsq = [T(f"xsqD{i}", [128, TC], F32) for i in range(2)]
        nm = T("nmD", [1, TC], F32)
        m2 = T("m2D", [1, TC], F32)
        var = T("varD", [1, TC], F32)
        rs = T("rsD", [1, TC], F32)
        nmr = T("nmrD", [1, TC], F32)
        xnt = T("xntD", [128, TC], F32)
        xn1 = T("xn1", [128, KT * 2048], BF16)
        xnr = T("xnr", [128, KT * TC], BF16)
        dlo_t = T("dlo_t", [64, TC], BF16)
        bs_t = T("bs_t", [64, TC], BF16)
        cs_t = T("cs_t", [64, TC], BF16)
        spv = T("spvD", [128, TC], F32)
        sav = T("savD", [128, TC], F32)
        delta1 = T("delta1", [128, KT * TC], BF16)
        dxn = T("dxn", [128, KT * TC], BF16)
        brep = T("brepD", [128, SG * TC], BF16)
        crep = T("crepD", [128, SG * TC], BF16)
        dab = [T(f"dabD{i}", [128, SG * TC], F32) for i in range(1)]
        ub = [T(f"ubD{i}", [128, SG * TC], BF16) for i in range(1)]
        hb = [T(f"hbD{i}", [128, SG * TC], BF16) for i in range(2)]
        wbf = [T(f"wbfD{i}", [128, SG * TC], BF16) for i in range(1)]
        t1p = [T(f"t1pD{i}", [128, TC], BF16) for i in range(1)]
        t2p = [T(f"t2pD{i}", [128, TC], BF16) for i in range(1)]
        ysg = [T(f"ysgD{i}", [128, TC], F32) for i in range(1)]
        yacc = T("yaccD", [128, KT * TC], F32)
        ob = T("ob", [128, KT * TC], BF16)
        spo = [T(f"spoD{i}", [128, TC], BF16) for i in range(2)]
        pa = PT("paD", [128, TC], F32)
        pb = PT("pbD", [128, TC], F32)
        brp = PT("brpD", [128, SG * TC], F32)
        mps = PT("mpsD", [1, TC], F32)
        sps = PT("spsD", [1, TC], F32)
        pst = [pa, pb]

        A = p.add
        V = p.v
        war = defaultdict(tuple)

        A("vector", lambda v: v.memset(ones_c[:], 1.0), inc="DVE")
        A("vector", lambda v: v.memset(ones_r[:], 1.0), inc="DVE")
        A("sync", lambda s: s.dma_start(out=eyes[:], in_=eye64), inc=(PF + "L_stp", 16))
        A("sync", lambda s: s.dma_start(out=cstt[:], in_=cst), inc=(PF + "L_stp", 16))
        A("sync", lambda s: s.dma_start(out=nws[:], in_=nwc), inc=(PF + "L_stp", 16))
        A("sync", lambda s: s.dma_start(out=bdt1[:], in_=b_dt1), inc=(PF + "L_stp", 16))
        A("sync", lambda s: s.dma_start(out=d1s[:], in_=d1c), inc=(PF + "L_stp", 16))
        A("sync", lambda s: s.dma_start(out=a1c[:], in_=a1in), inc=(PF + "L_stp", 16))
        ld0v = V(PF + "L_stp")
        ld0 = (PF + "L_stp", ld0v)
        A("scalar", lambda s: s.activation(out=a1c[:], in_=a1c[:], func=AF.Exp),
          inc="ACT", wait=[ld0])
        A("vector", lambda v: v.tensor_scalar_mul(out=a1c[:], in0=a1c[:], scalar1=-1.0),
          inc="DVE", wait=[("ACT", V("ACT"))])
        for k in range(KT):
            w_cast = V("ACT")
            A("sync", lambda s, k=k: s.dma_start(out=wst[:, :192], in_=w_xp[k * 128:(k + 1) * 128, :]),
              inc=(PF + "L_wst", 16), wait=[("ACT", w_cast)])
            A("scalar", lambda s, k=k: s.copy(out=wxpb[:, k * 192:(k + 1) * 192], in_=wst[:, :192]),
              inc="ACT", wait=[(PF + "L_wst", V(PF + "L_wst"))])
        for h in range(2):
            w_cast = V("ACT")
            A("sync", lambda s, h=h: s.dma_start(out=wst[:64, :512], in_=w_dt1[:, h * 512:(h + 1) * 512]),
              inc=(PF + "L_wst", 16), wait=[("ACT", w_cast)])
            A("scalar", lambda s, h=h: s.copy(out=wdt1[:, h * 512:(h + 1) * 512], in_=wst[:64, :512]),
              inc="ACT", wait=[(PF + "L_wst", V(PF + "L_wst"))])
        setup_act = V("ACT")
        setup_dve = V("DVE")
        yield "setup"

        for c in range(NCH1):
            t0 = c * TC
            for k in range(KT):
                par = k % 2
                A("sync", lambda s, k=k, par=par, t0=t0: s.dma_start(
                    out=xfs[par][:], in_=x1in[k * 128:(k + 1) * 128, t0:t0 + TC]),
                  inc=(PF + f"L_xf{par}", 16), wait=war[("xf", par)])
                ldw = (PF + f"L_xf{par}", V(PF + f"L_xf{par}"))
                A("scalar", lambda s, par=par: s.activation(out=xsq[par][:], in_=xfs[par][:], func=AF.Square),
                  inc="ACT", wait=[ldw] + list(war[("xsq", par)]))
                actv = V("ACT")
                A("tensor", lambda t, k=k, par=par: t.matmul(
                    out=mps[:], lhsT=ones_c[:], rhs=xfs[par][:], start=(k == 0), stop=(k == KT - 1)),
                  inc="PE",
                  wait=([ldw, ("DVE", setup_dve)] + list(war[("mps",)])) if k == 0 else [ldw])
                A("tensor", lambda t, k=k, par=par: t.matmul(
                    out=sps[:], lhsT=ones_c[:], rhs=xsq[par][:], start=(k == 0), stop=(k == KT - 1)),
                  inc="PE", wait=[("ACT", actv)])
                war[("xf", par)] = [("PE", V("PE"))]
                war[("xsq", par)] = [("PE", V("PE"))]
            pe_stats = V("PE")
            A("vector", lambda v: v.tensor_scalar_mul(out=nm[:], in0=mps[:], scalar1=-1.0 / 1024.0),
              inc="DVE", wait=[("PE", pe_stats)])
            A("vector", lambda v: v.tensor_mul(out=m2[:], in0=nm[:], in1=nm[:]), inc="DVE")
            A("vector", lambda v: v.scalar_tensor_tensor(
                out=var[:], in0=sps[:], scalar=1.0 / 1024.0, in1=m2[:], op0=ALU.mult, op1=ALU.subtract),
              inc="DVE", wait=[("PE", pe_stats)])
            dve_var = V("DVE")
            war[("mps",)] = [("DVE", dve_var)]
            A("scalar", lambda s: s.activation(out=rs[:], in_=var[:], func=AF.Ln, bias=cstt[0:1, 0:1]),
              inc="ACT", wait=[("DVE", dve_var)])
            A("scalar", lambda s: s.activation(out=rs[:], in_=rs[:], func=AF.Exp, scale=cstt[0:1, 1:2]), inc="ACT")
            act_rs = V("ACT")
            A("vector", lambda v: v.tensor_mul(out=nmr[:], in0=nm[:], in1=rs[:]),
              inc="DVE", wait=[("ACT", act_rs)])
            dve_nmr = V("DVE")
            A("tensor", lambda t: t.matmul(out=pa[:], lhsT=ones_r[:], rhs=rs[:], start=True, stop=True),
              inc="PE", wait=[("ACT", act_rs)] + list(war[("pa",)]))
            A("tensor", lambda t: t.matmul(out=pb[:], lhsT=ones_r[:], rhs=nmr[:], start=True, stop=True),
              inc="PE", wait=[("DVE", dve_nmr)] + list(war[("pb",)]))
            pe_bc = V("PE")
            for k in range(KT):
                par = k % 2
                A("sync", lambda s, k=k, par=par, t0=t0: s.dma_start(
                    out=xfs[par][:], in_=x1in[k * 128:(k + 1) * 128, t0:t0 + TC]),
                  inc=(PF + f"L_xf{par}", 16), wait=war[("xf", par)])
                ldw = (PF + f"L_xf{par}", V(PF + f"L_xf{par}"))
                A("vector", lambda v, par=par: v.tensor_mul(out=xnt[:], in0=xfs[par][:], in1=pa[:]),
                  inc="DVE", wait=[ldw, ("PE", pe_bc)])
                A("vector", lambda v, k=k, par=par, t0=t0: v.tensor_add(
                    out=xn1[:, k * 2048 + t0:k * 2048 + t0 + TC], in0=xnt[:], in1=pb[:]),
                  inc="DVE", wait=war[("xnk", k)])
                dv = V("DVE")
                war[("xf", par)] = [("DVE", dv)]
                A("gpsimd", lambda g, k=k, t0=t0: g.dma_start(
                    out=io["xn1o"][k * 128:(k + 1) * 128, t0:t0 + TC], in_=xn1[:, k * 2048 + t0:k * 2048 + t0 + TC]),
                  inc=(PF + "O_xn1", 16), wait=[("DVE", dv)])
            oxn_end = V(PF + "O_xn1")
            for k in range(KT):
                war[("xnk", k)] = [(PF + "O_xn1", oxn_end)]
            dve_xn = V("DVE")
            war[("pa",)] = [("DVE", dve_xn)]
            war[("pb",)] = [("DVE", dve_xn)]

            for k in range(KT):
                par = k % 2
                A("scalar", lambda s, k=k, par=par, t0=t0: s.activation(
                    out=xsq[par][:], in_=xn1[:, k * 2048 + t0:k * 2048 + t0 + TC], func=AF.Square),
                  inc="ACT", wait=[("DVE", dve_xn)] + list(war[("xsq", par)]))
                actv = V("ACT")
                A("tensor", lambda t, k=k, par=par: t.matmul(
                    out=sps[:], lhsT=ones_c[:], rhs=xsq[par][:], start=(k == 0), stop=(k == KT - 1)),
                  inc="PE", wait=([("ACT", actv)] + list(war[("mps",)])) if k == 0 else [("ACT", actv)])
                war[("xsq", par)] = [("PE", V("PE"))]
            pe_rms = V("PE")
            A("vector", lambda v: v.tensor_scalar_mul(out=var[:], in0=sps[:], scalar1=1.0 / 1024.0),
              inc="DVE", wait=[("PE", pe_rms)])
            dve_var2 = V("DVE")
            war[("mps",)] = [("DVE", dve_var2)]
            A("scalar", lambda s: s.activation(out=rs[:], in_=var[:], func=AF.Ln, bias=cstt[0:1, 2:3]),
              inc="ACT", wait=[("DVE", dve_var2)])
            A("scalar", lambda s: s.activation(out=rs[:], in_=rs[:], func=AF.Exp, scale=cstt[0:1, 1:2]), inc="ACT")
            act_rs2 = V("ACT")
            A("tensor", lambda t: t.matmul(out=pa[:], lhsT=ones_r[:], rhs=rs[:], start=True, stop=True),
              inc="PE", wait=[("ACT", act_rs2)] + list(war[("pa",)]))
            pe_bc2 = V("PE")
            for k in range(KT):
                A("vector", lambda v, k=k, t0=t0: v.scalar_tensor_tensor(
                    out=xnr[:, k * TC:(k + 1) * TC], in0=xn1[:, k * 2048 + t0:k * 2048 + t0 + TC],
                    scalar=nws[:, k:k + 1], in1=pa[:], op0=ALU.mult, op1=ALU.mult),
                  inc="DVE", wait=([("PE", pe_bc2), ld0] + list(war[("xnrk", k)])))
            dve_xnr = V("DVE")
            war[("pa",)] = [("DVE", dve_xnr)]

            for gi in range(2):
                ps = pst[gi % 2]
                pkey = ("pa",) if gi % 2 == 0 else ("pb",)
                npart = 128 if gi == 0 else 64
                col0 = gi * 128
                for k in range(KT):
                    A("tensor", lambda t, k=k, col0=col0, ps=ps, npart=npart: t.matmul(
                        out=ps[:npart, :], lhsT=wxpb[:, k * 192 + col0:k * 192 + col0 + npart],
                        rhs=xnr[:, k * TC:(k + 1) * TC], start=(k == 0), stop=(k == KT - 1)),
                      inc="PE",
                      wait=(list(war[pkey]) + [("ACT", setup_act), ("DVE", dve_xnr)]) if k == 0 else [])
                pev = V("PE")
                if gi == 0:
                    A("scalar", lambda s, ps=ps: s.copy(out=dlo_t[:], in_=ps[0:64, :]),
                      inc="ACT", wait=[("PE", pev)] + list(war[("dlo",)]))
                    A("scalar", lambda s, ps=ps: s.copy(out=bs_t[:], in_=ps[64:128, :]), inc="ACT")
                else:
                    A("scalar", lambda s, ps=ps: s.copy(out=cs_t[:], in_=ps[0:64, :]),
                      inc="ACT", wait=[("PE", pev)] + list(war[("cs",)]))
                war[pkey] = [("ACT", V("ACT"))]
            act_proj = V("ACT")

            for j in range(KT):
                ps = pst[j % 2]
                pkey = ("pa",) if j % 2 == 0 else ("pb",)
                A("tensor", lambda t, j=j, ps=ps: t.matmul(
                    out=ps[:], lhsT=wdt1[:, j * 128:(j + 1) * 128], rhs=dlo_t[:],
                    start=True, stop=True),
                  inc="PE",
                  wait=[("ACT", act_proj), ("DVE", setup_dve)] + list(war[pkey]))
                pev = V("PE")
                A("vector", lambda v, j=j, ps=ps: v.tensor_scalar_add(
                    out=spv[:], in0=ps[:], scalar1=bdt1[:, j:j + 1]),
                  inc="DVE", wait=[("PE", pev)] + list(war[("spv",)]))
                dve_v = V("DVE")
                war[pkey] = [("DVE", dve_v)]
                A("scalar", lambda s: s.activation(out=sav[:], in_=spv[:], func=AF.Abs),
                  inc="ACT", wait=[("DVE", dve_v)] + list(war[("sav",)]))
                A("scalar", lambda s: s.activation(out=sav[:], in_=sav[:], func=AF.Exp, scale=-1.0), inc="ACT")
                A("scalar", lambda s: s.activation(out=sav[:], in_=sav[:], func=AF.Ln, bias=1.0), inc="ACT")
                act_sp = V("ACT")
                A("vector", lambda v, j=j: v.scalar_tensor_tensor(
                    out=delta1[:, j * TC:(j + 1) * TC], in0=spv[:], scalar=0.0, in1=sav[:],
                    op0=ALU.max, op1=ALU.add),
                  inc="DVE", wait=[("ACT", act_sp)] + list(war[("delta", j)]))
                war[("spv",)] = [("ACT", act_sp)]
                war[("sav",)] = [("DVE", V("DVE"))]
                A("vector", lambda v, j=j: v.tensor_mul(
                    out=dxn[:, j * TC:(j + 1) * TC], in0=delta1[:, j * TC:(j + 1) * TC],
                    in1=xnr[:, j * TC:(j + 1) * TC]),
                  inc="DVE")
            dve_front = V("DVE")

            A("gpsimd", lambda g: g.memset(yacc[:], 0.0), inc="POOL", wait=war[("yacc",)])
            pool_ymem = V("POOL")

            act_cr = 0
            for sg in range(NSG):
                for sl in range(SG):
                    s_ = sg * SG + sl
                    A("tensor", lambda t, s_=s_, sl=sl: t.matmul(
                        out=brp[:, sl * TC:(sl + 1) * TC],
                        lhsT=eyes[:, s_:s_ + 1].broadcast_to([64, 128]),
                        rhs=bs_t[:], start=True, stop=True),
                      inc="PE",
                      wait=([("ACT", act_proj)] + list(war[("brp",)])) if sl == 0 else [])
                pe_b = V("PE")
                A("scalar", lambda s: s.copy(out=brep[:], in_=brp[:]),
                  inc="ACT", wait=[("PE", pe_b)] + list(war[("brep",)]))
                act_b = V("ACT")
                war[("brp",)] = [("ACT", act_b)]

                for j in range(KT):
                    par = 0
                    hpar = j % 2
                    for sl in range(SG):
                        s_ = sg * SG + sl
                        A("scalar", lambda s, j=j, par=par, sl=sl, s_=s_: s.activation(
                            out=dab[par][:, sl * TC:(sl + 1) * TC], in_=delta1[:, j * TC:(j + 1) * TC],
                            func=AF.Exp, scale=a1c[:, j * 64 + s_:j * 64 + s_ + 1]),
                          inc="ACT" if sl == SG - 1 else None,
                          wait=(list(war[("dab", par)]) + [("DVE", dve_front), ("DVE", setup_dve)]) if sl == 0 else [])
                    act_da = V("ACT")
                    A("vector", lambda v, par=par: v.memset(dab[par][:, 0::64], 0.0),
                      inc="DVE", wait=[("ACT", act_da)])
                    dve_rst = V("DVE")
                    A("vector", lambda v, j=j, par=par: v.tensor_mul(
                        out=ub[par][:].rearrange("p (s t) -> p s t", s=SG),
                        in0=dxn[:, j * TC:(j + 1) * TC].unsqueeze(1).broadcast_to([128, SG, TC]),
                        in1=brep[:].rearrange("p (s t) -> p s t", s=SG)),
                      inc="DVE", wait=[("ACT", act_b)])
                    A("vector", lambda v, par=par, hpar=hpar: v.tensor_tensor_scan(
                        out=hb[hpar][:], data0=dab[par][:], data1=ub[par][:],
                        initial=0.0, op0=ALU.mult, op1=ALU.add),
                      inc="DVE",
                      wait=[("DVE", dve_rst)] + list(war[("hb", hpar)]))
                    dve_scan = V("DVE")
                    war[("dab", par)] = [("DVE", dve_scan)]
                    if j == 0:
                        for sl in range(SG):
                            s_ = sg * SG + sl
                            A("tensor", lambda t, s_=s_, sl=sl: t.matmul(
                                out=brp[:, sl * TC:(sl + 1) * TC],
                                lhsT=eyes[:, s_:s_ + 1].broadcast_to([64, 128]),
                                rhs=cs_t[:], start=True, stop=True),
                              inc="PE",
                              wait=([("ACT", act_b)] + list(war[("brp",)])) if sl == 0 else [])
                        pe_cr = V("PE")
                        A("scalar", lambda s: s.copy(out=crep[:], in_=brp[:]),
                          inc="ACT", wait=[("PE", pe_cr)] + list(war[("crep",)]))
                        act_cr = V("ACT")
                        war[("brp",)] = [("ACT", act_cr)]
                    A("vector", lambda v, par=par, hpar=hpar: v.tensor_mul(
                        out=wbf[par][:], in0=hb[hpar][:], in1=crep[:]),
                      inc="DVE", wait=[("ACT", act_cr)] + list(war[("wbf", par)]))
                    dve_w = V("DVE")
                    war[("hb", hpar)] = [("DVE", dve_w)]
                    A("gpsimd", lambda g, par=par: g.tensor_add(
                        out=t1p[par][:], in0=wbf[par][:, 0:TC], in1=wbf[par][:, TC:2 * TC]),
                      inc="POOL", wait=[("DVE", dve_w)])
                    A("gpsimd", lambda g, par=par: g.tensor_add(
                        out=t2p[par][:], in0=wbf[par][:, 2 * TC:3 * TC], in1=wbf[par][:, 3 * TC:4 * TC]),
                      inc="POOL")
                    A("gpsimd", lambda g, par=par: g.tensor_add(
                        out=ysg[par][:], in0=t1p[par][:], in1=t2p[par][:]),
                      inc="POOL")
                    A("gpsimd", lambda g, j=j, par=par: g.tensor_add(
                        out=yacc[:, j * TC:(j + 1) * TC], in0=yacc[:, j * TC:(j + 1) * TC],
                        in1=ysg[par][:]),
                      inc="POOL", wait=[("POOL", pool_ymem)])
                    pool_acc = V("POOL")
                    war[("wbf", par)] = [("POOL", pool_acc)]
                    war[("yacck", j)] = [("POOL", pool_acc)]
                war[("brep",)] = [("DVE", V("DVE"))]
                war[("crep",)] = [("DVE", V("DVE"))]
            war[("dlo",)] = [("PE", V("PE"))]
            war[("cs",)] = [("PE", V("PE"))]
            for j in range(KT):
                war[("delta", j)] = [("ACT", V("ACT")), ("DVE", V("DVE"))]

            for j in range(KT):
                A("vector", lambda v, j=j, t0=t0: v.scalar_tensor_tensor(
                    out=ob[:, j * TC:(j + 1) * TC], in0=xn1[:, j * 2048 + t0:j * 2048 + t0 + TC],
                    scalar=d1s[:, j:j + 1], in1=yacc[:, j * TC:(j + 1) * TC],
                    op0=ALU.mult, op1=ALU.add),
                  inc="DVE", wait=list(war[("yacck", j)]) + list(war[("obk", j)]))
            dve_ob = V("DVE")
            war[("yacc",)] = [("DVE", dve_ob)]
            for dcol in range(8):
                ps = pst[dcol % 2]
                pkey = ("pa",) if dcol % 2 == 0 else ("pb",)
                par = dcol % 2
                wp = 0
                wld = V("ACT")
                A("sync", lambda s, dcol=dcol, wp=wp: s.dma_start(
                    out=wst[:], in_=w_op1r[dcol]),
                  inc=(PF + "L_wst", 16), wait=[("ACT", wld)] + list(war[("wopst",)]))
                ldv = (PF + "L_wst", V(PF + "L_wst"))
                A("scalar", lambda s, wp=wp: s.copy(out=wopb[wp][:], in_=wst[:]),
                  inc="ACT", wait=[ldv] + list(war[("wopb",)]))
                act_wc = V("ACT")
                war[("wopst",)] = [("ACT", act_wc)]
                for k in range(KT):
                    A("tensor", lambda t, k=k, wp=wp, ps=ps: t.matmul(
                        out=ps[:], lhsT=wopb[wp][:, k * 128:(k + 1) * 128],
                        rhs=ob[:, k * TC:(k + 1) * TC], start=(k == 0), stop=(k == KT - 1)),
                      inc="PE",
                      wait=([("DVE", dve_ob), ("ACT", act_wc)] + list(war[pkey])) if k == 0 else [])
                pev = V("PE")
                war[("wopb",)] = [("PE", pev)]
                A("scalar", lambda s, par=par, ps=ps: s.copy(out=spo[par][:], in_=ps[:]),
                  inc="ACT", wait=[("PE", pev)] + list(war[("spo", par)]))
                actv = V("ACT")
                war[pkey] = [("ACT", actv)]
                A("gpsimd", lambda g, dcol=dcol, par=par, t0=t0: g.dma_start(
                    out=io["s1o"][dcol * 128:(dcol + 1) * 128, t0:t0 + TC], in_=spo[par][:]),
                  inc=(PF + f"O_sp{par}", 16), wait=[("ACT", actv)])
                war[("spo", par)] = [(PF + f"O_sp{par}", V(PF + f"O_sp{par}"))]
            pe_end = V("PE")
            for j in range(KT):
                war[("obk", j)] = [("PE", pe_end)]
                war[("xnrk", j)] = [("PE", pe_end), ("DVE", V("DVE"))]
            for k in range(KT):
                war[("xnk", k)] = [("PE", pe_end), ("ACT", V("ACT"))]
            yield ("chunk", c)


def stage_D1(d, x1_by_core):
    dtw = np.asarray(d["L1_attn_dt_proj_w"])
    opT = np.ascontiguousarray(np.asarray(d["L1_attn_out_proj_w"]).T)
    wop1r = np.ascontiguousarray(opT.reshape(8, 128, 8, 128).transpose(2, 1, 0, 3).reshape(8, 128, 1024))
    eye = np.eye(64).astype(BF)
    cstv = np.ascontiguousarray(np.broadcast_to(
        np.array([1e-5, -0.5, 1e-6, 1e-8], np.float32)[None, :], (128, 4)))
    base = {
        "nwc": _col(np.asarray(d["L1_attn_norm_w"])),
        "w_xp": np.ascontiguousarray(np.asarray(d["L1_attn_x_proj_w"]).T),
        "w_dt1": np.ascontiguousarray(dtw.T),
        "b_dt1": _col(np.asarray(d["L1_attn_dt_proj_b"])),
        "a1in": np.ascontiguousarray(np.asarray(d["L1_attn_A_log"]).reshape(8, 128, 64).transpose(1, 0, 2).reshape(128, 512)),
        "d1c": _col(np.asarray(d["L1_attn_D"])),
        "w_op1r": wop1r,
        "eye64": eye,
        "cst": cstv,
    }
    return [dict(base, x1in=np.ascontiguousarray(x1_by_core[c])) for c in range(8)]


def stage_D2(d, x1_by_core, s1_by_core, xn1_by_core):
    w1 = np.asarray(d["L1_ffn_W1"])
    w2 = np.asarray(d["L1_ffn_W2"])
    w1r = np.ascontiguousarray(w1.reshape(8, 8, 128, 32, 128).transpose(0, 3, 2, 1, 4).reshape(8, 32, 128, 1024))
    w2r = np.ascontiguousarray(w2.reshape(8, 32, 128, 8, 128).transpose(0, 3, 2, 1, 4).reshape(8, 8, 128, 4096))
    b1cv = np.ascontiguousarray(np.asarray(d["L1_ffn_b1"]).reshape(8, 32, 128).transpose(2, 0, 1).reshape(128, 256))
    b2cv = np.ascontiguousarray(np.asarray(d["L1_ffn_b2"]).reshape(8, 8, 128).transpose(2, 0, 1).reshape(128, 64))
    wgtv = np.ascontiguousarray(np.asarray(d["L1_ffn_Wg"]).reshape(8, 128, 8).transpose(1, 0, 2).reshape(128, 64))
    bgrv = np.ascontiguousarray(np.broadcast_to(np.asarray(d["L1_ffn_bg"])[None, :], (128, 8)))
    e128 = np.eye(128).astype(BF)
    e8 = np.eye(8).astype(BF)
    maps = []
    for c in range(8):
        m = {
            "xn_in": np.ascontiguousarray(xn1_by_core[c]),
            "x_own": np.ascontiguousarray(x1_by_core[c]),
            "s_in": np.ascontiguousarray(s1_by_core[c]),
            "bgr": bgrv,
            "wgt": wgtv, "w1r": w1r, "w2r": w2r, "b1c": b1cv, "b2c": b2cv,
            "eye128": e128, "eye8": e8,
        }
        maps.append(m)
    return maps


def build_fused(phases="all"):
    nc = bass.Bass("TRN2", target_bir_lowering=False, debug=False)
    E = lambda nm, sh, dt: nc.dram_tensor(nm, sh, dt, kind="ExternalInput").ap()
    xb = E("xb", [1024, 4096], F32)
    x_own = E("x_own", [1024, 2048], F32)
    w_ip = E("w_ip", [1024, 2240], F32)
    w_conv = E("w_conv", [128, 32], F32)
    b_conv = E("b_conv", [128, 8], F32)
    w_dt = E("w_dt", [64, 1024], F32)
    b_dt = E("b_dt", [128, 8], F32)
    a_log = E("a_log", [128, 64], F32)
    d_prm = E("d_prm", [128, 8], F32)
    w_op = E("w_op", [1024, 1024], F32)
    eye64 = E("eye64", [64, 64], BF16)
    cst = E("cst", [128, 4], F32)
    eye128 = E("eye128", [128, 128], BF16)
    eye8 = E("eye8", [8, 8], BF16)
    wgt0 = E("wgt0", [128, 64], F32)
    w1r0 = E("w1r0", [8, 32, 128, 1024], F32)
    w2r0 = E("w2r0", [8, 8, 128, 4096], F32)
    b1c0 = E("b1c0", [128, 256], F32)
    b2c0 = E("b2c0", [128, 64], F32)
    nwc = E("nwc", [128, 8], F32)
    w_xp = E("w_xp", [1024, 192], F32)
    w_dt1 = E("w_dt1", [64, 1024], F32)
    b_dt1 = E("b_dt1", [128, 8], F32)
    a1in = E("a1in", [128, 512], F32)
    d1c = E("d1c", [128, 8], F32)
    w_op1r = E("w_op1r", [8, 128, 1024], F32)
    wgt1 = E("wgt1", [128, 64], F32)
    w1r1 = E("w1r1", [8, 32, 128, 1024], F32)
    w2r1 = E("w2r1", [8, 8, 128, 4096], F32)
    b1c1 = E("b1c1", [128, 256], F32)
    b2c1 = E("b2c1", [128, 64], F32)
    bgr = E("bgr", [128, 8], F32)
    out = nc.dram_tensor("out", [1024, 2048], F32, kind="ExternalOutput").ap()
    cc_in = nc.dram_tensor("cc_in", [2048, 2048], F32).ap()
    cc_out = nc.dram_tensor("cc_out", [1024, 2048], F32).ap()
    xn_dr = nc.dram_tensor("xn_dr", [1024, 4096], BF16).ap()
    x1_dr = nc.dram_tensor("x1_dr", [1024, 2048], F32).ap()
    s1_dr = nc.dram_tensor("s1_dr", [1024, 2048], BF16).ap()
    xn1_dr = nc.dram_tensor("xn1_dr", [1024, 2048], BF16).ap()

    p = Prog()
    with contextlib.ExitStack() as es0:
        ioB = {"pref": "B_", "xb": xb, "w_ip": w_ip, "w_conv": w_conv, "b_conv": b_conv,
               "w_dt": w_dt, "b_dt": b_dt, "a_log": a_log, "d_prm": d_prm, "w_op": w_op,
               "eye64": eye64, "cst": cst, "xn_out": xn_dr,
               "spar_st": lambda dcol, t0: cc_in[
                   (t0 // 2048) * 1024 + dcol * 128:(t0 // 2048) * 1024 + (dcol + 1) * 128,
                   (t0 % 2048):(t0 % 2048) + TC]}
        _emit_B(nc, p, ioB)
        out_after_B = [("B_O_sp", p.v("B_O_sp"))]
        p.barrier()
        p.add("gpsimd", lambda g: g.collective_compute(
            "ReduceScatter", ALU.add, replica_groups=[[0, 1], [2, 3], [4, 5], [6, 7]],
            ins=[cc_in], outs=[cc_out]),
            inc=("CC", 1), wait=out_after_B)
        xnc_dr = nc.dram_tensor("xnc_dr", [1024, 2048], BF16).ap()
        w1bd0 = nc.dram_tensor("w1bd0", [8, 32, 128, 1024], BF16).ap()
        w2bd0 = nc.dram_tensor("w2bd0", [8, 8, 128, 4096], BF16).ap()
        ioC = {"pref": "C_", "lay": 0, "s_mode": "single_f32",
               "ln_x": x_own, "ln_cst": cst,
               "xn_st": lambda k, t0: xnc_dr[k * 128:(k + 1) * 128, t0:t0 + 512],
               "xn_ld": lambda k, tm0: xnc_dr[k * 128:(k + 1) * 128, tm0:tm0 + TCM],
               "x_ld": lambda j, tm0: x_own[j * 128:(j + 1) * 128, tm0:tm0 + TCM],
               "s_ld": lambda j, tm0: cc_out[j * 128:(j + 1) * 128, tm0:tm0 + TCM],
               "s_wait": [("CC", 1)],
               "out_st": lambda j, tm0: x1_dr[j * 128:(j + 1) * 128, tm0:tm0 + TCM],
               "wgt": wgt0, "w1r": w1r0, "w2r": w2r0, "b1c": b1c0, "b2c": b2c0,
               "w1b_dr": w1bd0, "w2b_dr": w2bd0,
               "eye128": eye128, "eye8": eye8}
        if phases in ("p1", "p1x"):
            ioC["out_st"] = lambda j, tm0: out[j * 128:(j + 1) * 128, tm0:tm0 + TCM]
        if phases == "p1x":
            xn_dbg = nc.dram_tensor("xn_dbg", [1024, 2048], BF16, kind="ExternalInput").ap()
            ioC.pop("ln_x"); ioC.pop("ln_cst"); ioC.pop("xn_st")
            ioC["xn_ld"] = lambda k, tm0: xn_dbg[k * 128:(k + 1) * 128, tm0:tm0 + TCM]
        _emit_tail(nc, p, ioC)
        if phases == "p1x":
            finish(p, es0, nc)
            return nc
        if phases == "p1":
            finish(p, es0, nc)
            return nc
        p.barrier()
        ioD1 = {"pref": "E_", "x1in": x1_dr, "nwc": nwc, "w_xp": w_xp, "w_dt1": w_dt1,
                "b_dt1": b_dt1, "a1in": a1in, "d1c": d1c, "w_op1r": w_op1r,
                "eye64": eye64, "cst": cst, "s1o": s1_dr, "xn1o": xn1_dr}
        _emit_D1(nc, p, ioD1)
        p.barrier()
        w1bd1 = nc.dram_tensor("w1bd1", [8, 32, 128, 1024], BF16).ap()
        w2bd1 = nc.dram_tensor("w2bd1", [8, 8, 128, 4096], BF16).ap()
        ioD2 = {"pref": "F_", "lay": 1, "s_mode": "single_bf16",
                "xn_ld": lambda k, tm0: xn1_dr[k * 128:(k + 1) * 128, tm0:tm0 + TCM],
                "x_ld": lambda j, tm0: x1_dr[j * 128:(j + 1) * 128, tm0:tm0 + TCM],
                "s_ld": lambda j, tm0: s1_dr[j * 128:(j + 1) * 128, tm0:tm0 + TCM],
                "out_st": lambda j, tm0: out[j * 128:(j + 1) * 128, tm0:tm0 + TCM],
                "bgr": bgr, "wgt": wgt1, "w1r": w1r1, "w2r": w2r1, "b1c": b1c1,
                "b2c": b2c1, "w1b_dr": w1bd1, "w2b_dr": w2bd1,
                "eye128": eye128, "eye8": eye8}
        _emit_tail(nc, p, ioD2)
        finish(p, es0, nc)
    return nc


def stage_fused(d):
    mapsB = stage_B(d)
    x = np.asarray(d["L0_attn_x"])
    w1 = np.asarray(d["L0_ffn_W1"]); w2 = np.asarray(d["L0_ffn_W2"])
    w1r0 = np.ascontiguousarray(w1.reshape(8, 8, 128, 32, 128).transpose(0, 3, 2, 1, 4).reshape(8, 32, 128, 1024))
    w2r0 = np.ascontiguousarray(w2.reshape(8, 32, 128, 8, 128).transpose(0, 3, 2, 1, 4).reshape(8, 8, 128, 4096))
    b1c0 = np.ascontiguousarray(np.asarray(d["L0_ffn_b1"]).reshape(8, 32, 128).transpose(2, 0, 1).reshape(128, 256))
    b2c0 = np.ascontiguousarray(np.asarray(d["L0_ffn_b2"]).reshape(8, 8, 128).transpose(2, 0, 1).reshape(128, 64))
    wgt0 = np.ascontiguousarray(np.asarray(d["L0_ffn_W_gate"]).reshape(8, 128, 8).transpose(1, 0, 2).reshape(128, 64))
    w11 = np.asarray(d["L1_ffn_W1"]); w21 = np.asarray(d["L1_ffn_W2"])
    w1r1 = np.ascontiguousarray(w11.reshape(8, 8, 128, 32, 128).transpose(0, 3, 2, 1, 4).reshape(8, 32, 128, 1024))
    w2r1 = np.ascontiguousarray(w21.reshape(8, 32, 128, 8, 128).transpose(0, 3, 2, 1, 4).reshape(8, 8, 128, 4096))
    b1c1 = np.ascontiguousarray(np.asarray(d["L1_ffn_b1"]).reshape(8, 32, 128).transpose(2, 0, 1).reshape(128, 256))
    b2c1 = np.ascontiguousarray(np.asarray(d["L1_ffn_b2"]).reshape(8, 8, 128).transpose(2, 0, 1).reshape(128, 64))
    wgt1 = np.ascontiguousarray(np.asarray(d["L1_ffn_Wg"]).reshape(8, 128, 8).transpose(1, 0, 2).reshape(128, 64))
    bgrv = np.ascontiguousarray(np.broadcast_to(np.asarray(d["L1_ffn_bg"])[None, :], (128, 8)))
    dtw1 = np.asarray(d["L1_attn_dt_proj_w"])
    opT1 = np.ascontiguousarray(np.asarray(d["L1_attn_out_proj_w"]).T)
    wop1r = np.ascontiguousarray(opT1.reshape(8, 128, 8, 128).transpose(2, 1, 0, 3).reshape(8, 128, 1024))
    cstv = np.ascontiguousarray(np.broadcast_to(
        np.array([1e-5, -0.5, 1e-6, 1e-8], np.float32)[None, :], (128, 4)))
    e128 = np.eye(128).astype(BF)
    e8 = np.eye(8).astype(BF)
    shared = {
        "eye128": e128, "eye8": e8,
        "wgt0": wgt0, "w1r0": w1r0, "w2r0": w2r0, "b1c0": b1c0, "b2c0": b2c0,
        "nwc": _col(np.asarray(d["L1_attn_norm_w"])),
        "w_xp": np.ascontiguousarray(np.asarray(d["L1_attn_x_proj_w"]).T),
        "w_dt1": np.ascontiguousarray(dtw1.T),
        "b_dt1": _col(np.asarray(d["L1_attn_dt_proj_b"])),
        "a1in": np.ascontiguousarray(np.asarray(d["L1_attn_A_log"]).reshape(8, 128, 64).transpose(1, 0, 2).reshape(128, 512)),
        "d1c": _col(np.asarray(d["L1_attn_D"])),
        "w_op1r": wop1r,
        "wgt1": wgt1, "w1r1": w1r1, "w2r1": w2r1, "b1c1": b1c1, "b2c1": b2c1,
        "bgr": bgrv,
    }
    maps = []
    for c in range(8):
        b, ih = c // 2, c % 2
        m = dict(mapsB[c])
        m["cst"] = cstv
        m["x_own"] = np.ascontiguousarray(x[b].T[:, ih * 2048:(ih + 1) * 2048])
        m.update(shared)
        maps.append(m)
    return maps


def kernel(**inputs):
    d = {k: np.asarray(v) for k, v in inputs.items()}
    nc = build_fused()
    res = run_bass_kernel_spmd(nc, stage_fused(d), core_ids=list(range(8)))
    out = np.empty((4, 4096, 1024), np.float32)
    for c in range(8):
        b, ih = c // 2, c % 2
        out[b, ih * 2048:(ih + 1) * 2048, :] = np.asarray(res.results[c]["out"]).T
    return out
